# version 10
# baseline (speedup 1.0000x reference)
import math
import numpy as np
import ml_dtypes
import concourse.bass as bass
import concourse.mybir as mybir
from concourse.alu_op_type import AluOpType
from concourse.bass_utils import run_bass_kernel_spmd

F32 = mybir.dt.float32
BF16 = mybir.dt.bfloat16
I32 = mybir.dt.int32
AF = mybir.ActivationFunctionType
OP = AluOpType
AX = mybir.AxisListType.X
BF = ml_dtypes.bfloat16

B = 2
S = 8192
D = 1024
TOK = 2048
NCORES = 8
HEADS = 16
HD = 64
AH = 4
FFN = 4096
NH = 32
SH = 8
P_ = 64
NST = 64
CH = 64
DMLP = 2560
DFF = 4096
NB = 32
EVLEN = 1152
NEAR_D0 = [-384, -256, -128, 0, 128]

_REL_PERM = np.array([i * 2 for i in range(32)] + [i * 2 + 1 for i in range(32)])


def _bucket_of(d):
    if d < 16:
        return d
    v = np.float32(np.log(np.float32(d) / np.float32(16.0)))
    v = v / np.float32(math.log(128 / 16)) * np.float32(16)
    b = 16 + int(v)
    return min(b, 31)


def _host_constants():
    C = {}
    sel = np.zeros((NB, EVLEN), np.float32)
    for idx in range(EVLEN):
        d = idx - 511
        if d >= 1:
            sel[_bucket_of(d), idx] = 1.0
    C["c_sel"] = sel.astype(BF)
    mask = np.zeros((1, EVLEN), np.float32)
    mask[0, : 511 + 1] = -30000.0
    C["c_maskrow"] = mask.astype(BF)
    C["c_J"] = np.eye(128, dtype=BF)[::-1].copy()
    C["c_I"] = np.eye(128, dtype=BF)
    C["c_If"] = np.eye(128, dtype=np.float32)
    C["c_onescol_f"] = np.ones((128, 1), np.float32)
    C["c_onescol_b"] = np.ones((128, 1), BF)
    C["c_onesrow_f"] = np.ones((1, 128), np.float32)
    C["c_onesrow_b"] = np.ones((1, 128), BF)
    s2 = np.zeros((128, 2), BF)
    s2[0:64, 0] = 1
    s2[64:128, 1] = 1
    C["c_sel64"] = s2
    e2 = np.zeros((2, 128), BF)
    e2[0, 0:64] = 1
    e2[1, 64:128] = 1
    C["c_exp2"] = e2
    e8 = np.zeros((4, 8, 128), BF)
    for j in range(4):
        e8[j, 2 * j, 0:64] = 1
        e8[j, 2 * j + 1, 64:128] = 1
    C["c_exppair"] = e8.reshape(32, 128)
    eA = np.zeros((8, 128), BF)
    eB = np.zeros((8, 128), BF)
    for p in range(128):
        eA[p // 32, p] = 1
        eB[4 + p // 32, p] = 1
    C["c_exp8A"] = eA
    C["c_exp8B"] = eB
    C["c_tril"] = np.triu(np.ones((128, 128), np.float32))
    C["c_ramp"] = (64.0 - (np.arange(512) % 64)).astype(np.float32)[None, :]
    m8 = np.zeros((8, 256), np.float32)
    for h in range(8):
        m8[h, (h // 2) * 64:(h // 2) * 64 + 64] = 1
    C["c_mask8"] = m8
    ax = np.zeros((8, 128), BF)
    for h in range(8):
        ax[h, (h % 2) * 64:(h % 2) * 64 + 64] = 1
    C["c_aexp"] = ax
    return C


def _stage_inputs(inp):
    consts = _host_constants()
    f32 = np.float32
    AS = np.ascontiguousarray

    def vec8(v):
        return AS(np.asarray(v, f32).reshape(8, 128).T)

    x = np.asarray(inp["L0_attn_x"], f32)
    wq = np.asarray(inp["L0_attn_Wq"], f32)
    wk = np.asarray(inp["L0_attn_Wk"], f32)
    wv = np.asarray(inp["L0_attn_Wv"], f32)
    relb = np.asarray(inp["L0_attn_rel_bias"], f32)
    scale = np.asarray(inp["L0_attn_scale"], f32)
    win = np.asarray(inp["L1_attn_W_in"], f32)
    bb = np.asarray(inp["L1_attn_B_bias"], f32)[0]
    cb = np.asarray(inp["L1_attn_C_bias"], f32)[0]
    wupT = AS(np.asarray(inp["L1_ffn_w_up"], f32).T)
    wdnT = AS(np.asarray(inp["L1_ffn_w_down"], f32).T)
    cbt = AS(np.asarray(inp["L0_ffn_CB"], f32).T)
    z0, x0, b0, c0, d0, t0 = 0, 2048, 4096, 6144, 8192, 8224

    maps = []
    for c in range(NCORES):
        b = c // 4
        q = c % 4
        h4 = list(range(4 * q, 4 * q + 4))
        h8 = list(range(8 * q, 8 * q + 8))
        m = dict(consts)
        m["xq"] = AS(x[b, TOK * q: TOK * (q + 1), :].T)
        m["ln1w8"] = vec8(inp["L0_attn_ln1_w"])
        m["ln2w8"] = vec8(inp["L0_attn_ln2_w"])
        m["n1w8"] = vec8(inp["L1_attn_norm1_w"])
        m["n2w8"] = vec8(inp["L1_attn_norm2_w"])
        m["scb8"] = vec8(inp["L0_ffn_SCB"])
        m["bias8"] = vec8(inp["L0_ffn_bias"])
        cols4 = np.concatenate([np.arange(h * HD, (h + 1) * HD) for h in h4])
        m["wq4"] = AS(wq[:, cols4])
        m["wk4"] = AS(wk[:, cols4])
        m["wv4"] = AS(wv[:, cols4])
        m["scale4"] = AS(scale[h4][:, None])
        m["scale22"] = AS(scale[h4].reshape(2, 2).T)
        m["scale4r"] = AS(scale[h4][None, :])
        m["relb4"] = AS(relb[:, h4])
        def slabify(w, njobs):
            K_, M_ = w.shape
            kch = K_ // 128
            t4 = w.reshape(kch, 128, njobs, 128).transpose(2, 1, 0, 3)
            return AS(t4.reshape(njobs * 128, kch * 128))
        m["wo"] = slabify(np.asarray(inp["L0_attn_Wo"], f32), 8)
        m["w1"] = slabify(np.asarray(inp["L0_attn_W1"], f32), 32)
        m["w2"] = slabify(np.asarray(inp["L0_attn_W2"], f32), 8)
        m["cbt"] = slabify(cbt, 8)
        m["wout"] = slabify(np.asarray(inp["L1_attn_W_out"], f32), 8)
        m["wg"] = slabify(np.asarray(inp["L1_attn_Wg"], f32), 20)
        m["wu"] = slabify(np.asarray(inp["L1_attn_Wu"], f32), 20)
        m["wd"] = slabify(np.asarray(inp["L1_attn_Wd"], f32), 8)
        zc = np.concatenate([np.arange(z0 + h * P_, z0 + (h + 1) * P_) for h in h8])
        xc = np.concatenate([np.arange(x0 + h * P_, x0 + (h + 1) * P_) for h in h8])
        bcol = np.concatenate([b0 + h * NST + _REL_PERM for h in h8])
        ccol = np.concatenate([c0 + h * NST + _REL_PERM for h in h8])
        dtc = d0 + np.array(h8)
        thc = np.concatenate([np.arange(t0 + h * 32, t0 + (h + 1) * 32) for h in h8])
        m["win_z"] = AS(win[:, zc])
        m["win_x"] = AS(win[:, xc])
        m["win_B"] = AS(win[:, bcol])
        m["win_C"] = AS(win[:, ccol])
        m["win_dt"] = AS(win[:, dtc])
        m["win_th"] = AS(win[:, thc])
        m["alog8"] = AS(np.asarray(inp["L1_attn_A_log"], f32)[h8][:, None])
        m["dtb8"] = AS(np.asarray(inp["L1_attn_dt_bias"], f32)[h8][:, None])
        m["d8"] = AS(np.asarray(inp["L1_attn_D_param"], f32)[h8][None, :])
        m["bnw"] = AS(np.asarray(inp["L1_attn_Bn_w"], f32)[_REL_PERM][None, :])
        m["cnw"] = AS(np.asarray(inp["L1_attn_Cn_w"], f32)[_REL_PERM][None, :])
        m["bbias"] = AS(bb[h8][:, _REL_PERM].reshape(1, 512))
        m["cbias"] = AS(cb[h8][:, _REL_PERM].reshape(1, 512))
        wq_ = wupT[:, 1024 * q: 1024 * (q + 1)]
        m["wupT_sl"] = AS(wq_.reshape(8, 128, 8, 128).transpose(2, 1, 0, 3)
                          .reshape(1024, 1024))
        wdq_ = wdnT[:, 256 * q: 256 * (q + 1)]
        m["wdnT_sl"] = AS(wdq_.reshape(32, 128, 2, 128).transpose(2, 1, 0, 3)
                          .reshape(256, 4096))
        maps.append(m)
    return maps


class Ctx:

    def __init__(self, nc):
        self.nc = nc
        self.cnt = {}
        self.sems = {}
        self._guards = []
        self.consts = {}

    def sem(self, name):
        if name not in self.sems:
            g = self.nc.semaphore(name)
            self.sems[name] = g.__enter__()
            self._guards.append(g)
            self.cnt[name] = 0
        return self.sems[name]

    def dma(self, queue, out, in_, semname):
        s = self.sem(semname)
        eng = self.nc.sync if queue == "sq" else self.nc.gpsimd
        eng.dma_start(out=out, in_=in_).then_inc(s, 16)
        self.cnt[semname] += 16

    def tag(self, instr, semname, v=1):
        s = self.sem(semname)
        instr.then_inc(s, v)
        self.cnt[semname] += v

    def wait(self, eng, semname):
        eng.wait_ge(self.sem(semname), self.cnt[semname])

    def lagwait(self, eng, semname, lag):
        tgt = self.cnt.get(semname, 0) - lag
        if tgt > 0:
            eng.wait_ge(self.sem(semname), tgt)

    def bar(self):
        self.nc.all_engine_barrier()

    def make_pool(self, reserve=512):
        rem = self.nc.sbuf_bytes_remaining - reserve
        self.pool_els = (rem // 2) & ~15
        self.pool_t = self.nc.alloc_sbuf_tensor("ctxpool", [128, self.pool_els], BF16)
        self.pool_off = 0

    def pool_reset(self):
        self.pool_off = 0

    def sb(self, name, shape, dtype):
        p, n = shape
        els = n if dtype == BF16 else n * 2
        off = self.pool_off
        self.pool_off = (off + els + 15) & ~15
        assert self.pool_off <= self.pool_els,             f"pool overflow at {name}: {self.pool_off} > {self.pool_els}"
        v = self.pool_t[:, off:off + els]
        if dtype != BF16:
            v = v.bitcast(dtype)
        if p < 128:
            v = v[0:p, :]
        return v

    def prep_consts(self, vals):
        self.ccol = getattr(self, "ccol", {})
        for v in vals:
            if v not in self.ccol:
                tname = "cc%d" % len(self.ccol)
                tile = self.nc.alloc_sbuf_tensor(tname, [128, 1], F32)
                self.nc.vector.memset(tile[:], float(v))
                self.ccol[v] = tile

    def cc(self, v):
        return self.ccol[v][:, 0:1]

    def load_consts(self, names):
        nc = self.nc
        shp = {"c_sel": ([NB, EVLEN], BF16), "c_maskrow": ([1, EVLEN], BF16),
               "c_J": ([128, 128], BF16), "c_I": ([128, 128], BF16),
               "c_If": ([128, 128], F32),
               "c_onescol_f": ([128, 1], F32), "c_onescol_b": ([128, 1], BF16),
               "c_onesrow_f": ([1, 128], F32), "c_onesrow_b": ([1, 128], BF16),
               "c_sel64": ([128, 2], BF16), "c_exp2": ([2, 128], BF16),
               "c_exppair": ([32, 128], BF16), "c_exp8A": ([8, 128], BF16),
               "c_exp8B": ([8, 128], BF16), "c_tril": ([128, 128], F32),
               "c_ramp": ([1, 512], F32), "c_mask8": ([8, 256], F32),
               "c_aexp": ([8, 128], BF16)}
        self.cdram = getattr(self, "cdram", {})
        for n in names:
            sh, dt = shp[n]
            ap = self.nc.dram_tensor(n, sh, dt, kind="ExternalInput").ap()
            self.cdram[n] = ap
            sb = nc.alloc_sbuf_tensor("s" + n, sh, dt)
            self.dma("sq", sb[:], ap, "ldc")
            self.consts[n] = sb
        self.wait(nc.sync, "ldc")

    def c(self, name):
        return self.consts[name]


def _mm_stream(ctx, jobs, load_fn, mm_fn, tag, lag=1):
    import os
    if os.environ.get("MM_SERIAL"):
        nc = ctx.nc
        for i, j in enumerate(jobs):
            if ctx.cnt.get(tag, 0) > 0:
                ctx.wait(nc.gpsimd, tag)
            w = load_fn(j, i % 2)
            nc.tensor.wait_ge(ctx.sem("wl%d" % (i % 2)), w)
            mm_fn(j, i % 2)
        return
    """Depth-2 pipelined weight streaming: load_fn(job, buf)->wl_count issues the
    DMA(s) for `job` into slab half `buf`; mm_fn(job, buf) emits consumers and
    tags its last consumer instruction with `tag`."""
    nc = ctx.nc
    wl = {}
    for i, j in enumerate(jobs):
        if i == 0:
            ctx.lagwait(nc.gpsimd, tag, 1)
            wl[0] = load_fn(j, 0)
        if i + 1 < len(jobs):
            ctx.lagwait(nc.gpsimd, tag, 0)
            wl[i + 1] = load_fn(jobs[i + 1], (i + 1) % 2)
        nc.tensor.wait_ge(ctx.sem("wl%d" % (i % 2)), wl[i])
        mm_fn(j, i % 2)


def _rows_chain_ln(ctx, psSx, psSxx, rO, extra=None):
    nc = ctx.nc
    sl = rO["sl"]
    nc.vector.wait_ge(ctx.sem("mm_ev"), ctx.cnt.get("mm_ev", 0))
    nc.vector.tensor_scalar_mul(rO["mean"][0:1, sl], psSx[0:1, 0:512], 1.0 / D)
    lv = nc.vector.tensor_scalar_mul(rO["var"][0:1, sl], psSxx[0:1, 0:512], 1.0 / D)
    ctx.tag(lv, "gv")
    nc.tensor.wait_ge(ctx.sem("gv"), ctx.cnt["gv"])
    nc.vector.tensor_mul(rO["tmp"][0:1, sl], rO["mean"][0:1, sl], rO["mean"][0:1, sl])
    lv = nc.vector.tensor_sub(rO["var"][0:1, sl], rO["var"][0:1, sl], rO["tmp"][0:1, sl])
    ctx.tag(lv, "gv")
    nc.scalar.wait_ge(ctx.sem("gv"), ctx.cnt["gv"])
    la = nc.scalar.activation(rO["tmp"][0:1, sl], rO["var"][0:1, sl], AF.Sqrt,
                              bias=ctx.cc(1e-5)[0:1, :], scale=1.0)
    ctx.tag(la, "gs")
    nc.vector.wait_ge(ctx.sem("gs"), ctx.cnt["gs"])
    lv = nc.vector.reciprocal(rO["r1"][0:1, sl], rO["tmp"][0:1, sl])
    ctx.tag(lv, "gv")
    nc.vector.wait_ge(ctx.sem("gv"), ctx.cnt["gv"])


def _emit_ln_sums(ctx, xsb, fchunks, tok, scr, psSx, psSxx, tslice, src_off=0):
    nc = ctx.nc
    ones = ctx.c("c_onescol_f")
    for fc in range(fchunks):
        xs = xsb[:, src_off + fc * tok + tslice.start: src_off + fc * tok + tslice.stop]
        a = nc.scalar.activation(scr[:, 0:512], xs, AF.Square, bias=0.0, scale=1.0)
        ctx.tag(a, "sq_ev")
        nc.tensor.wait_ge(ctx.sem("sq_ev"), ctx.cnt["sq_ev"])
        m1 = nc.tensor.matmul(psSx[0:1, 0:512], ones[:, 0:1], xs,
                              start=(fc == 0), stop=(fc == fchunks - 1))
        m2 = nc.tensor.matmul(psSxx[0:1, 0:512], ones[:, 0:1], scr[:, 0:512],
                              start=(fc == 0), stop=(fc == fchunks - 1))
        ctx.tag(m2, "mm_ev")
        nc.scalar.wait_ge(ctx.sem("mm_ev"), ctx.cnt["mm_ev"])


def emit_p1(ctx, EX):
    nc = ctx.nc
    xq = EX["xq"]
    ln1w8 = EX["ln1w8"]
    wup_sl = EX["wupT_sl"]
    wdn_sl = EX["wdnT_sl"]
    o_xn1 = EX["o_xn1"]
    o_rows = EX["o_rows"]
    o_absp = EX["o_absp"]

    xsb = ctx.sb("xsb", [128, 8 * TOK], F32)
    scr = ctx.sb("scr", [128, 512], F32)
    w8 = ctx.sb("w8", [128, 8], F32)
    xn1 = ctx.sb("xn1", [128, 8 * TOK], BF16)
    r_mean = ctx.sb("r_mean", [1, TOK], F32)
    r_var = ctx.sb("r_var", [1, TOK], F32)
    r_r1 = ctx.sb("r_r1", [1, TOK], F32)
    r_tmp = ctx.sb("r_tmp", [1, TOK], F32)
    r_s = ctx.sb("r_s", [1, TOK], F32)
    r_ms = ctx.sb("r_ms", [1, TOK], F32)
    r_mr1 = ctx.sb("r_mr1", [1, TOK], F32)
    psSx = ctx.pbig2[0:1, 0:512]
    psSxx = ctx.pbig2[0:1, 512:1024]
    psB = ctx.pbig2[0:128, 1024:1536]

    ctx.dma("sq", xsb[:].rearrange("p (f t) -> p f t", f=8),
            xq.rearrange("(f p) t -> p f t", p=128), "ld")
    ctx.dma("sq", w8[:], ln1w8, "ld")
    ctx.wait(nc.sync, "ld")
    ctx.bar()

    rO = {"mean": r_mean, "var": r_var, "r1": r_r1, "tmp": r_tmp}
    for tt in range(TOK // 512):
        tsl = slice(tt * 512, (tt + 1) * 512)
        _emit_ln_sums(ctx, xsb, 8, TOK, scr, psSx, psSxx, tsl)
        rO["sl"] = tsl
        _rows_chain_ln(ctx, psSx, psSxx, rO)
    ctx.bar()
    nc.vector.tensor_mul(r_tmp[0:1, :], r_r1[0:1, :], r_r1[0:1, :])
    ctx.bar()
    nc.vector.tensor_mul(r_tmp[0:1, :], r_tmp[0:1, :], r_var[0:1, :])
    ctx.bar()
    nc.scalar.activation(r_tmp[0:1, :], r_tmp[0:1, :], AF.Sqrt,
                         bias=ctx.cc(1e-6)[0:1, :], scale=1.0)
    ctx.bar()
    nc.vector.reciprocal(r_s[0:1, :], r_tmp[0:1, :])
    ctx.bar()
    nc.vector.tensor_mul(r_s[0:1, :], r_s[0:1, :], r_r1[0:1, :])
    ctx.bar()
    nc.vector.tensor_mul(r_ms[0:1, :], r_s[0:1, :], r_mean[0:1, :])
    nc.vector.tensor_mul(r_mr1[0:1, :], r_r1[0:1, :], r_mean[0:1, :])
    ctx.bar()
    onesrow = ctx.c("c_onesrow_f")
    for tt in range(TOK // 512):
        tsl = slice(tt * 512, (tt + 1) * 512)
        m = nc.tensor.matmul(psB[:, 0:512], onesrow[0:1, :], r_s[0:1, tsl],
                             start=True, stop=True)
        ctx.tag(m, "bc_ev")
        nc.vector.wait_ge(ctx.sem("bc_ev"), ctx.cnt["bc_ev"])
        last = None
        for fc in range(8):
            off = fc * TOK + tt * 512
            nc.vector.tensor_tensor(scr[:, 0:512], xsb[:, off:off + 512],
                                    psB[:, 0:512], OP.mult)
            last = nc.vector.tensor_scalar(xn1[:, off:off + 512], scr[:, 0:512],
                                           w8[:, fc:fc + 1], None, OP.mult)
        ctx.tag(last, "xn_ev")
        nc.tensor.wait_ge(ctx.sem("xn_ev"), ctx.cnt["xn_ev"])
    ctx.bar()
    ctx.dma("sq", o_xn1.rearrange("(f p) t -> p f t", p=128),
            xn1[:].rearrange("p (f t) -> p f t", f=8), "st")
    ctx.dma("sq", o_rows[0:1, :], r_s[0:1, :], "st")
    ctx.dma("sq", o_rows[1:2, :], r_ms[0:1, :], "st")
    ctx.dma("sq", o_rows[2:3, :], r_r1[0:1, :], "st")
    ctx.dma("sq", o_rows[3:4, :], r_mr1[0:1, :], "st")

    wld = ctx.sb("wld", [128, 1024], F32)
    absw = ctx.sb("absw", [128, 1024], F32)
    psS2 = ctx.pbig[0:1, 0:1024]
    accr = ctx.sb("accr", [1, 4], F32)
    ones = ctx.c("c_onescol_f")
    for mi, (ap, po) in enumerate([(wup_sl, 0), (wdn_sl, 512)]):
        nr, ncol = ap.shape[0] // 128, ap.shape[1] // 1024
        blocks = [ap[r * 128:(r + 1) * 128, c * 1024:(c + 1) * 1024]
                  for r in range(nr) for c in range(ncol)]
        ntile, cols = len(blocks), 1024
        for k in range(ntile):
            ctx.dma("sq", wld[:, 0:cols], blocks[k], "wl")
            ctx.wait(nc.scalar, "wl")
            a = nc.scalar.activation(absw[:, 0:cols], wld[:, 0:cols], AF.Abs,
                                     bias=0.0, scale=1.0)
            ctx.tag(a, "ab_ev")
            nc.tensor.wait_ge(ctx.sem("ab_ev"), ctx.cnt["ab_ev"])
            last_mm = None
            for j in range(max(1, cols // 512)):
                w = min(512, cols)
                last_mm = nc.tensor.matmul(
                    psS2[0:1, po + 0:po + w], ones[:, 0:1],
                    absw[:, j * 512:j * 512 + w],
                    start=(k == 0 and j == 0), stop=(k == ntile - 1 and j == max(0, cols // 512 - 1)))
            ctx.tag(last_mm, "wm_ev")
            nc.sync.wait_ge(ctx.sem("ab_ev"), ctx.cnt["ab_ev"])
            nc.scalar.wait_ge(ctx.sem("wm_ev"), ctx.cnt["wm_ev"])
    ctx.bar()
    nc.vector.tensor_reduce(accr[0:1, 0:1], psS2[0:1, 0:512], AX, OP.add)
    nc.vector.tensor_reduce(accr[0:1, 1:2], psS2[0:1, 512:1024], AX, OP.add)
    ctx.bar()
    ctx.dma("sq", o_absp, accr[0:1, 0:2], "st")
    ctx.wait(nc.sync, "st")
    ctx.bar()
    return nc


def emit_p2(ctx, EX):
    import os
    _UPTO = int(os.environ.get("P2_UPTO", "9"))
    nc = ctx.nc
    xn1f = EX["xn1f"]
    msf4 = EX["msf4"]
    wq4 = EX["wq4"]
    wk4 = EX["wk4"]
    wv4 = EX["wv4"]
    ln1w8 = EX["ln1w8"]
    scale4 = EX["scale4"]
    scale22 = EX["scale22"]
    scale4r = EX["scale4r"]
    relb4 = EX["relb4"]
    o_y = EX["o_y"]
    ev_hbm = nc.dram_tensor("ev_hbm", [4, EVLEN], BF16, kind="Internal")


    wqb = ctx.sb("wqb", [128, 8 * 256], BF16)
    wkb = ctx.sb("wkb", [128, 8 * 256], BF16)
    wvb = ctx.sb("wvb", [128, 8 * 256], BF16)
    ln1b = ctx.sb("ln1b", [128, 8], BF16)
    ln1f = ctx.sb("ln1f", [128, 8], F32)
    pool = ctx.sb("pool", [128, 18 * 1024], BF16)
    cwsb = ctx.sb("cwsb", [128, 6], BF16)
    cwrows = ctx.sb("cwrows", [1, 6 * 128], BF16)
    xt = [pool[:, 0:4096], pool[:, 4096:8192]]
    msneg = pool[0:1, 8192:16384]
    qsb = ctx.sb("qsb", [128, 2 * S], BF16)
    ksb = ctx.sb("ksb", [128, 2 * S], BF16)
    vtok = ctx.sb("vtok", [128, 64 * 260], BF16)
    sqsb = ctx.sb("sqsb", [128, 512], BF16)
    rsqb = ctx.sb("rsqb", [2, 512], BF16)
    rsqf = ctx.sb("rsqf", [2, 512], F32)
    bcq = ctx.sb("bcq", [128, 512], BF16)
    vtmp = ctx.sb("vtmp", [128, 512], BF16)
    scl4 = ctx.sb("scl4", [4, 1], F32)
    scl22 = ctx.sb("scl22", [2, 2], F32)
    scl4r = ctx.sb("scl4r", [1, 4], F32)
    rb4 = ctx.sb("rb4", [NB, 4], F32)
    srel = ctx.sb("srel", [NB, 4], F32)
    srelb = ctx.sb("srelb", [NB, 4], BF16)
    srelT = ctx.sb("srelT", [4, NB], F32)
    mrow = ctx.sb("mrow", [4, 4], F32)
    mnegbc = ctx.sb("mnegbc", [128, 8], F32)
    rowt = ctx.sb("rowt", [1, 4], F32)
    evsb = ctx.sb("evsb", [4, EVLEN], BF16)
    l2tmp = ctx.sb("l2tmp", [128, 512], BF16)
    esb = pool[:, 0:10240]
    phat = pool[:, 10240:11776]
    phat2 = pool[:, 11776:13312]
    vmean = ctx.sb("vmean", [64, 4], F32)
    srow = ctx.sb("srow", [128, 512], F32)
    ones64 = ctx.sb("ones64", [128, 64], F32)
    bcd = ctx.sb("bcd", [64, 512], F32)
    ysb = ctx.sb("ysb", [128, S], BF16)

    pbig = ctx.pbig
    pbig2 = ctx.pbig2
    ptb = ctx.ptb

    ctx.dma("gq", wqb[:].rearrange("p (kc m) -> p kc m", kc=8),
            wq4.rearrange("(kc p) m -> p kc m", p=128), "gl")
    ctx.dma("gq", wkb[:].rearrange("p (kc m) -> p kc m", kc=8),
            wk4.rearrange("(kc p) m -> p kc m", p=128), "gl")
    ctx.dma("gq", wvb[:].rearrange("p (kc m) -> p kc m", kc=8),
            wv4.rearrange("(kc p) m -> p kc m", p=128), "gl")
    ctx.dma("sq", ln1f[:], ln1w8, "ld")
    for rr in range(4):
        ctx.dma("sq", msneg[0:1, rr * TOK:(rr + 1) * TOK], msf4[rr], "ld")
    ctx.dma("sq", scl4[:], scale4, "ld")
    ctx.dma("sq", scl22[:], scale22, "ld")
    ctx.dma("sq", scl4r[:], scale4r, "ld")
    ctx.dma("sq", rb4[:], relb4, "ld")
    ctx.wait(nc.sync, "ld")
    ctx.wait(nc.gpsimd, "gl")
    ctx.bar()
    nc.vector.memset(vtok[:], 1.0)
    nc.vector.memset(ones64[64:65, :], 1.0)
    nc.vector.tensor_copy(ln1b[:], ln1f[:])
    nc.vector.tensor_scalar_mul(msneg, msneg, -1.0)
    ctx.bar()

    for wi, wb in enumerate([wqb, wkb, wvb]):
        for mi in range(2):
            for kc in range(8):
                nc.tensor.matmul(pbig[:, wi * 2 + mi: wi * 2 + mi + 1],
                                 wb[:, kc * 256 + mi * 128: kc * 256 + (mi + 1) * 128],
                                 ln1b[:, kc:kc + 1], start=(kc == 0), stop=(kc == 7))
    ctx.bar()
    nc.vector.tensor_copy(cwsb[:, 0:6], pbig[:, 0:6])
    ctx.bar()
    for j in range(6):
        nc.tensor.transpose(ptb[0:1, 0:128], cwsb[:, j:j + 1], ctx.c("c_I")[:])
        ctx.bar()
        nc.vector.tensor_copy(cwrows[0:1, j * 128:(j + 1) * 128], ptb[0:1, 0:128])
        ctx.bar()

    for tt in range(S // 512 if _UPTO >= 2 else 0):
        xb = xt[tt % 2]
        rank = (tt * 512) // TOK
        toff = (tt * 512) % TOK
        ctx.dma("sq", xb.rearrange("p (kc t) -> p kc t", kc=8),
                xn1f[rank, :, toff:toff + 512].rearrange("(kc p) t -> p kc t", p=128),
                "xl")
        ctx.wait(nc.sync, "xl")
        ctx.bar()
        for wi, wb in enumerate([wqb, wkb, wvb]):
            for mi in range(2):
                pb = pbig[:, (mi % 2) * 512:(mi % 2) * 512 + 512]
                for kc in range(8):
                    nc.tensor.matmul(pb, wb[:, kc * 256 + mi * 128: kc * 256 + (mi + 1) * 128],
                                     xb[:, kc * 512:(kc + 1) * 512],
                                     start=(kc == 0), stop=False)
                ci6 = wi * 2 + mi
                nc.tensor.matmul(pb, cwrows[0:1, ci6 * 128:(ci6 + 1) * 128],
                                 msneg[0:1, tt * 512:(tt + 1) * 512],
                                 start=False, stop=True)
                ctx.bar()
                if wi < 2:
                    nc.scalar.activation(sqsb[:, :], pb, AF.Square, bias=0.0, scale=1.0)
                    ctx.bar()
                    nc.tensor.matmul(pbig2[0:2, 0:512], ctx.c("c_sel64")[:, :], sqsb[:, :],
                                     start=True, stop=True)
                    ctx.bar()
                    nc.scalar.activation(rsqf[0:2, :], pbig2[0:2, 0:512], AF.Sqrt,
                                         bias=ctx.cc(1e-6)[0:2, :], scale=1.0)
                    ctx.bar()
                    nc.vector.reciprocal(rsqf[0:2, :], rsqf[0:2, :])
                    ctx.bar()
                    if wi == 0:
                        nc.vector.tensor_scalar(rsqf[0:2, :], rsqf[0:2, :],
                                                scl22[:, mi:mi + 1], None, OP.mult)
                        ctx.bar()
                    nc.vector.tensor_copy(rsqb[0:2, :], rsqf[0:2, :])
                    ctx.bar()
                    nc.tensor.matmul(pbig2[:, 512:1024], ctx.c("c_exp2")[:, :], rsqb[0:2, :],
                                     start=True, stop=True)
                    ctx.bar()
                    nc.vector.tensor_copy(bcq[:, :], pbig2[:, 512:1024])
                    ctx.bar()
                    dst = (qsb if wi == 0 else ksb)
                    nc.vector.tensor_tensor(dst[:, mi * S + tt * 512: mi * S + (tt + 1) * 512],
                                            pb, bcq[:, :], OP.mult)
                    ctx.bar()
                else:
                    nc.vector.tensor_copy(vtmp[:, :], pb)
                    ctx.bar()
                    for q4 in range(4):
                        tc = tt * 4 + q4
                        nc.tensor.transpose(ptb[:, 0:128], vtmp[:, q4 * 128:(q4 + 1) * 128],
                                            ctx.c("c_I")[:])
                        ctx.bar()
                        base = tc * 260
                        nc.vector.tensor_copy(vtok[:, base + (2 * mi) * 65: base + (2 * mi) * 65 + 64],
                                              ptb[:, 0:64])
                        nc.vector.tensor_copy(vtok[:, base + (2 * mi + 1) * 65: base + (2 * mi + 1) * 65 + 64],
                                              ptb[:, 64:128])
                        ctx.bar()

    if _UPTO < 3:
        return _p2_finish(ctx, o_y, ysb)
    nc.tensor.matmul(pbig[0:NB, 0:4], ctx.c("c_onesrow_f")[0:1, 0:NB], scl4r[0:1, :],
                     start=True, stop=True)
    ctx.bar()
    nc.vector.tensor_tensor(srel[:, :], rb4[:, :], pbig[0:NB, 0:4], OP.mult)
    ctx.bar()
    nc.vector.tensor_copy(srelb[:, :], srel[:, :])
    ctx.bar()
    nc.tensor.transpose(pbig[0:4, 0:NB], srel[:, :], ctx.c("c_If")[0:NB, 0:NB])
    ctx.bar()
    nc.vector.tensor_copy(srelT[:, :], pbig[0:4, 0:NB])
    ctx.bar()
    nc.vector.tensor_reduce(mrow[0:4, 0:1], srelT[:, :], AX, OP.max)
    nc.scalar.activation(mrow[0:4, 1:2], scl4[0:4, 0:1], AF.Abs, bias=0.0, scale=1.0)
    ctx.bar()
    nc.vector.tensor_scalar_mul(mrow[0:4, 1:2], mrow[0:4, 1:2], 1.02)
    ctx.bar()
    nc.vector.tensor_add(mrow[0:4, 2:3], mrow[0:4, 0:1], mrow[0:4, 1:2])
    ctx.bar()
    nc.vector.tensor_sub(mrow[0:4, 3:4], mrow[0:4, 2:3], srelT[:, 31:32])
    ctx.bar()
    for ci, col in enumerate([2, 3]):
        nc.tensor.transpose(pbig[0:1, 0:4], mrow[0:4, col:col + 1], ctx.c("c_If")[0:4, 0:4])
        ctx.bar()
        nc.vector.tensor_scalar_mul(rowt[0:1, :], pbig[0:1, 0:4], -1.0)
        ctx.bar()
        nc.tensor.matmul(pbig[:, 0:4], ctx.c("c_onesrow_f")[0:1, :], rowt[0:1, :],
                         start=True, stop=True)
        ctx.bar()
        nc.vector.tensor_copy(mnegbc[:, ci * 4:(ci + 1) * 4], pbig[:, 0:4])
        ctx.bar()
    for j, (o, w) in enumerate([(0, 512), (512, 512), (1024, 128)]):
        nc.tensor.matmul(pbig[0:4, o:o + w], srelb[:, :], ctx.c("c_sel")[:, o:o + w],
                         start=True, stop=False)
        nc.tensor.matmul(pbig[0:4, o:o + w], ctx.c("c_onesrow_b")[0:1, 0:4],
                         ctx.c("c_maskrow")[0:1, o:o + w], start=False, stop=True)
    ctx.bar()
    nc.scalar.activation(evsb[:, :], pbig[0:4, 0:EVLEN], AF.Exp, bias=0.0, scale=1.0)
    ctx.bar()
    ctx.dma("sq", ev_hbm.ap(), evsb[:, :], "ev")
    ctx.wait(nc.sync, "ev")
    ctx.bar()
    for h in range(4):
        for ci, d0 in enumerate(NEAR_D0):
            src = bass.AP(ev_hbm.ap().tensor, h * EVLEN + d0 + 384, [[1, 128], [1, 512]])
            ctx.dma("sq", l2tmp[:], src, "el")
            ctx.wait(nc.tensor, "el")
            mm = nc.tensor.matmul(pbig[:, 0:512], ctx.c("c_J")[:], l2tmp[:], start=True, stop=True)
            ctx.tag(mm, "elm")
            nc.vector.wait_ge(ctx.sem("elm"), ctx.cnt["elm"])
            vv = nc.vector.tensor_copy(esb[:, (h * 5 + ci) * 512:(h * 5 + ci + 1) * 512],
                                       pbig[:, 0:512])
            ctx.tag(vv, "elc")
            nc.tensor.wait_ge(ctx.sem("elc"), ctx.cnt["elc"])
            nc.sync.wait_ge(ctx.sem("elc"), ctx.cnt["elc"])

    if _UPTO < 4:
        return _p2_finish(ctx, o_y, ysb)
    for tc in range(64):
        for h in range(4):
            nc.tensor.matmul(pbig[0:64, h:h + 1],
                             vtok[:, tc * 260 + h * 65: tc * 260 + h * 65 + 64],
                             ctx.c("c_onescol_b")[:, 0:1],
                             start=(tc == 0 and h == 0), stop=(tc == 63 and h == 3))
    ctx.bar()
    nc.vector.tensor_scalar_mul(vmean[:, :], pbig[0:64, 0:4], 1.0 / S)
    ctx.bar()

    if _UPTO < 5:
        return _p2_finish(ctx, o_y, ysb)
    _NQT = int(os.environ.get("P2_NQT", "16"))
    _NH = int(os.environ.get("P2_NH", "4"))
    exc = {}
    avc = {}
    for h in range(_NH):
        prow = (h % 2) * 64
        fofs = (h // 2) * S
        if h in (2, 3):
            nc.vector.wait_ge(ctx.sem("st"), ctx.cnt["st"])
        for qt in range(_NQT):
            qs = qt * 512
            far = list(range(0, max(0, 4 * qt - 1)))
            near = [kc for kc in range(max(0, 4 * qt - 1), 4 * qt + 4)]
            groups = [far[i:i + 3] for i in range(0, len(far), 3)]
            groups = [(g, False) for g in groups if g] + [(near[0:3], True)]
            if len(near) > 3:
                groups.append((near[3:], True))
            ntot = len(far) + len(near)
            nav = 0
            first_av = True
            glist = []

            def sbank(gi, bi):
                if gi % 2 == 0:
                    return pbig[:, bi * 512:(bi + 1) * 512]
                return (pbig[:, 1536:2048] if bi == 0 else
                        pbig2[:, (bi - 1) * 512:bi * 512])

            def embank(gi, bi):
                return (phat if gi % 2 == 0 else phat2)[:, bi * 512:(bi + 1) * 512]

            def emit_av(gi):
                nonlocal nav, first_av
                g, is_near = groups[gi]
                nc.tensor.wait_ge(ctx.sem("ex"), exc[gi])
                if is_near:
                    nc.tensor.wait_ge(ctx.sem("em"), emc[gi])
                last = None
                for bi, kc in enumerate(g):
                    nav += 1
                    last = nc.tensor.matmul(
                        pbig2[0:65, 1024:1536],
                        vtok[:, kc * 260 + h * 65: kc * 260 + (h + 1) * 65],
                        embank(gi, bi), start=first_av, stop=(nav == ntot),
                        skip_group_check=True)
                    first_av = False
                ctx.tag(last, "av")
                avc[gi] = ctx.cnt["av"]

            emc = {}
            prev_ex = ctx.cnt.get("ex", 0)
            prev_av = ctx.cnt.get("av", 0)
            for gi, (g, is_near) in enumerate(groups):
                nc.tensor.wait_ge(ctx.sem("ex"), exc[gi - 2] if gi >= 2 else prev_ex)
                last = None
                for bi, kc in enumerate(g):
                    last = nc.tensor.matmul(
                        sbank(gi, bi),
                        ksb[prow:prow + 64, fofs + kc * 128: fofs + (kc + 1) * 128],
                        qsb[prow:prow + 64, fofs + qs: fofs + qs + 512],
                        start=True, stop=True)
                ctx.tag(last, "qk")
                nc.scalar.wait_ge(ctx.sem("qk"), ctx.cnt["qk"])
                nc.scalar.wait_ge(ctx.sem("av"), avc[gi - 2] if gi >= 2 else prev_av)
                mcol = h if is_near else 4 + h
                la = None
                if gi % 2 == 0:
                    la = nc.scalar.activation(
                        (phat if gi % 2 == 0 else phat2)[:, 0:len(g) * 512],
                        pbig[:, 0:len(g) * 512], AF.Exp,
                        bias=mnegbc[:, mcol:mcol + 1], scale=1.0)
                else:
                    la = nc.scalar.activation(phat2[:, 0:512], pbig[:, 1536:2048],
                                              AF.Exp, bias=mnegbc[:, mcol:mcol + 1],
                                              scale=1.0)
                    if len(g) > 1:
                        la = nc.scalar.activation(phat2[:, 512:len(g) * 512],
                                                  pbig2[:, 0:(len(g) - 1) * 512],
                                                  AF.Exp, bias=mnegbc[:, mcol:mcol + 1],
                                                  scale=1.0)
                ctx.tag(la, "ex")
                exc[gi] = ctx.cnt["ex"]
                if is_near:
                    nc.vector.wait_ge(ctx.sem("ex"), exc[gi])
                    lv = None
                    for bi, kc in enumerate(g):
                        d0 = qs - kc * 128
                        ci = NEAR_D0.index(d0)
                        lv = nc.vector.tensor_tensor(
                            embank(gi, bi), embank(gi, bi),
                            esb[:, (h * 5 + ci) * 512:(h * 5 + ci + 1) * 512], OP.mult)
                    ctx.tag(lv, "em")
                    emc[gi] = ctx.cnt["em"]
                if gi >= 1:
                    emit_av(gi - 1)
            emit_av(len(groups) - 1)
            if qt == 0:
                nc.vector.wait_ge(ctx.sem("av"), ctx.cnt["av"])
                nc.vector.tensor_copy(pbig2[0:64, 1024:1025], vmean[:, h:h + 1])
                lv = nc.vector.memset(pbig2[64:65, 1024:1025], 1.0)
                ctx.tag(lv, "q0")
            nc.vector.wait_ge(ctx.sem("av"), ctx.cnt["av"])
            lv = nc.vector.reciprocal(srow[64:65, 0:512], pbig2[64:65, 1024:1536])
            ctx.tag(lv, "rc")
            nc.tensor.wait_ge(ctx.sem("rc"), ctx.cnt["rc"])
            m = nc.tensor.matmul(pbig[0:64, 0:512], ones64[64:65, 0:64],
                                 srow[64:65, 0:512], start=True, stop=True)
            ctx.tag(m, "nb")
            nc.vector.wait_ge(ctx.sem("nb"), ctx.cnt["nb"])
            nc.vector.tensor_copy(bcd[:, :], pbig[0:64, 0:512])
            lv = nc.vector.tensor_tensor(ysb[prow:prow + 64, qs: qs + 512],
                                         pbig2[0:64, 1024:1536], bcd[:, :], OP.mult)
            ctx.tag(lv, "yw")
            nc.tensor.wait_ge(ctx.sem("yw"), ctx.cnt["yw"])
        if h % 2 == 1:
            nc.sync.wait_ge(ctx.sem("yw"), ctx.cnt["yw"])
            ctx.dma("sq", o_y[(h - 1) * 64:(h + 1) * 64, :], ysb[:, :], "st")
        ctx.wait(nc.sync, "st")
    ctx.bar()
    return nc


def emit_p3(ctx, EX):
    nc = ctx.nc
    yq = EX["yq"]
    xq = EX["xq"]
    rows1 = EX["rows1"]
    wo = EX["wo"]
    w1 = EX["w1"]
    w2 = EX["w2"]
    cbt = EX["cbt"]
    ln2w8 = EX["ln2w8"]
    scb8 = EX["scb8"]
    bias8 = EX["bias8"]
    o_x4 = EX["o_x4"]

    xsb = ctx.sb("xsb", [128, 8 * TOK], F32)
    yinner = ctx.sb("yinner", [128, 8 * 512], F32)
    hbuf = ctx.sb("hbuf3", [128, 16384], BF16)
    pool = ctx.sb("pool3", [128, 16384], BF16)
    wslab = ctx.sb("wslab", [128, 2 * 4096], BF16)
    scr = ctx.sb("scr3", [128, 512], F32)
    scr3b = ctx.sb("scr3b", [128, 1024], F32)
    cwrow = ctx.sb("cwrow", [1, 8 * 128], BF16)
    cwcol = ctx.sb("cwcol", [128, 8], BF16)
    w8a = ctx.sb("w8a", [128, 8], F32)
    w8b = ctx.sb("w8b", [128, 8], F32)
    w8c = ctx.sb("w8c", [128, 8], F32)
    r_r1 = ctx.sb("r_r1", [1, TOK], BF16)
    r_mr1n = ctx.sb("r_mr1n", [1, TOK], BF16)
    r_mbrb = ctx.sb("r_mbrb", [1, TOK], BF16)
    r_rbf = ctx.sb("r_rbf", [1, TOK], F32)
    t_mean = ctx.sb("t_mean", [1, 512], F32)
    t_var = ctx.sb("t_var", [1, 512], F32)
    t_tmp = ctx.sb("t_tmp", [1, 512], F32)
    t_r = ctx.sb("t_r", [1, 512], F32)
    pbig = ctx.pbig
    pbig2 = ctx.pbig2
    ptb = ctx.ptb

    ysb2 = pool[:, 0:16384]

    ctx.dma("sq", xsb[:].rearrange("p (f t) -> p f t", f=8),
            xq.rearrange("(f p) t -> p f t", p=128), "ld")
    ctx.dma("sq", ysb2.rearrange("p (f t) -> p f t", f=8),
            yq.rearrange("(f p) t -> p f t", p=128), "ld")
    ctx.dma("gq", r_r1[0:1, :], rows1[2:3, :], "gl")
    ctx.dma("gq", r_mr1n[0:1, :], rows1[3:4, :], "gl")
    ctx.dma("sq", w8a[:], ln2w8, "ld")
    ctx.dma("sq", w8b[:], scb8, "ld")
    ctx.dma("sq", w8c[:], bias8, "ld")
    ctx.wait(nc.sync, "ld")
    ctx.wait(nc.gpsimd, "gl")
    ctx.bar()
    nc.vector.tensor_scalar_mul(w8b[:, :], w8b[:, :], 1.0 / 127.0)
    nc.vector.tensor_scalar_mul(r_mr1n[0:1, :], r_mr1n[0:1, :], -1.0)
    ctx.bar()

    def stream_w(ap, job, kchunks, buf):
        ctx.dma("gq", wslab[:, buf * 4096: buf * 4096 + kchunks * 128],
                ap[job * 128:(job + 1) * 128, :], "wl%d" % buf)
        return ctx.cnt["wl%d" % buf]

    xn2 = pool[:, 0:16384]
    for tt in range(TOK // 512):
        ts = slice(tt * 512, (tt + 1) * 512)
        m = nc.tensor.matmul(pbig2[:, 1024:1536], ctx.c("c_onesrow_b")[0:1, :],
                             r_r1[0:1, ts], start=True, stop=True)
        def ld_wo(fc, buf):
            return stream_w(wo, fc, 8, buf)

        def mm_wo(fc, buf):
            bo = buf * 4096
            pb = pbig[:, (fc % 2) * 512:(fc % 2) * 512 + 512]
            ctx.lagwait(nc.tensor, "ev3", 1)
            for kc in range(8):
                nc.tensor.matmul(pb, wslab[:, bo + kc * 128: bo + (kc + 1) * 128],
                                 ysb2[:, kc * TOK + tt * 512: kc * TOK + (tt + 1) * 512],
                                 start=(kc == 0), stop=False)
            mmm = nc.tensor.matmul(pb, ctx.c("c_onesrow_b")[0:1, :], r_mr1n[0:1, ts],
                                   start=False, stop=True)
            ctx.tag(mmm, "mm3")
            nc.vector.wait_ge(ctx.sem("mm3"), ctx.cnt["mm3"])
            nc.vector.tensor_tensor(scr3b[:, (fc % 2) * 512:(fc % 2) * 512 + 512],
                                    xsb[:, fc * TOK + tt * 512: fc * TOK + (tt + 1) * 512],
                                    pbig2[:, 1024:1536], OP.mult)
            vv = nc.vector.tensor_add(yinner[:, fc * 512:(fc + 1) * 512],
                                      scr3b[:, (fc % 2) * 512:(fc % 2) * 512 + 512], pb)
            ctx.tag(vv, "ev3")

        _mm_stream(ctx, list(range(8)), ld_wo, mm_wo, "mm3")
        ctx.wait(nc.tensor, "ev3")
        ctx.bar()
        _emit_ln_sums(ctx, yinner, 8, 512, scr, pbig2[0:1, 0:512], pbig2[0:1, 512:1024],
                      slice(0, 512))
        nc.vector.wait_ge(ctx.sem("mm_ev"), ctx.cnt["mm_ev"])
        lv = nc.vector.tensor_scalar_mul(t_tmp[0:1, :], pbig2[0:1, 512:1024], 1.0 / D)
        ctx.tag(lv, "gv")
        nc.scalar.wait_ge(ctx.sem("gv"), ctx.cnt["gv"])
        nc.tensor.wait_ge(ctx.sem("gv"), ctx.cnt["gv"])
        la = nc.scalar.activation(t_tmp[0:1, :], t_tmp[0:1, :], AF.Sqrt,
                                  bias=ctx.cc(1e-6)[0:1, :], scale=1.0)
        ctx.tag(la, "gs")
        nc.vector.wait_ge(ctx.sem("gs"), ctx.cnt["gs"])
        lv = nc.vector.reciprocal(t_r[0:1, :], t_tmp[0:1, :])
        ctx.tag(lv, "gv")
        nc.tensor.wait_ge(ctx.sem("gv"), ctx.cnt["gv"])
        m = nc.tensor.matmul(pbig2[:, 1024:1536], ctx.c("c_onesrow_f")[0:1, :],
                             t_r[0:1, :], start=True, stop=True)
        ctx.tag(m, "bc3")
        nc.vector.wait_ge(ctx.sem("bc3"), ctx.cnt["bc3"])
        last = None
        for fc in range(8):
            off = fc * TOK + tt * 512
            yo = fc * 512
            nc.vector.tensor_tensor(scr[:, :], yinner[:, yo:yo + 512],
                                    pbig2[:, 1024:1536], OP.mult)
            nc.vector.tensor_scalar(xn2[:, off:off + 512], scr[:, :],
                                    w8a[:, fc:fc + 1], None, OP.mult)
            last = nc.vector.tensor_add(xsb[:, off:off + 512], xsb[:, off:off + 512],
                                        yinner[:, yo:yo + 512])
        ctx.tag(last, "x23")
        nc.tensor.wait_ge(ctx.sem("x23"), ctx.cnt["x23"])
        nc.scalar.wait_ge(ctx.sem("x23"), ctx.cnt["x23"])
    ctx.bar()

    for tt in range(TOK // 512):
        def ld_w1(hk, buf):
            return stream_w(w1, hk, 8, buf)

        def mm_w1(hk, buf):
            bo = buf * 4096
            pb = pbig[:, (hk % 2) * 512:(hk % 2) * 512 + 512]
            ctx.lagwait(nc.tensor, "f2", 1)
            last = None
            for kc in range(8):
                last = nc.tensor.matmul(pb, wslab[:, bo + kc * 128: bo + (kc + 1) * 128],
                                        xn2[:, kc * TOK + tt * 512: kc * TOK + (tt + 1) * 512],
                                        start=(kc == 0), stop=(kc == 7))
            ctx.tag(last, "f1")
            nc.scalar.wait_ge(ctx.sem("f1"), ctx.cnt["f1"])
            aa = nc.scalar.activation(hbuf[:, hk * 512:(hk + 1) * 512], pb, AF.Relu,
                                      bias=0.0, scale=1.0)
            ctx.tag(aa, "f2")

        _mm_stream(ctx, list(range(32)), ld_w1, mm_w1, "f1")
        nc.tensor.wait_ge(ctx.sem("f2"), ctx.cnt["f2"])
        ctx.bar()
        def ld_w2(fc, buf):
            return stream_w(w2, fc, 32, buf)

        def mm_w2(fc, buf):
            bo = buf * 4096
            pb = pbig[:, (fc % 2) * 512:(fc % 2) * 512 + 512]
            ctx.lagwait(nc.tensor, "f4", 1)
            last = None
            for kc in range(32):
                last = nc.tensor.matmul(pb, wslab[:, bo + kc * 128: bo + (kc + 1) * 128],
                                        hbuf[:, kc * 512: (kc + 1) * 512],
                                        start=(kc == 0), stop=(kc == 31))
            ctx.tag(last, "f3")
            nc.vector.wait_ge(ctx.sem("f3"), ctx.cnt["f3"])
            off = fc * TOK + tt * 512
            vv = nc.vector.tensor_add(xsb[:, off:off + 512], xsb[:, off:off + 512], pb)
            ctx.tag(vv, "f4")

        _mm_stream(ctx, list(range(8)), ld_w2, mm_w2, "f3")
        ctx.wait(nc.vector, "f4")
        ctx.bar()

    xnb = pool[:, 0:16384]
    for tt in range(TOK // 512):
        ts = slice(tt * 512, (tt + 1) * 512)
        _emit_ln_sums(ctx, xsb, 8, TOK, scr, pbig2[0:1, 0:512], pbig2[0:1, 512:1024], ts)
        rO = {"mean": t_mean, "var": t_var, "r1": t_r, "tmp": t_tmp, "sl": slice(0, 512)}
        _rows_chain_ln(ctx, pbig2[0:1, 0:512], pbig2[0:1, 512:1024], rO)
        nc.vector.tensor_copy(r_rbf[0:1, ts], t_r[0:1, :])
        nc.vector.tensor_mul(t_tmp[0:1, :], t_mean[0:1, :], t_r[0:1, :])
        ctx.bar()
        nc.vector.tensor_scalar_mul(r_mbrb[0:1, ts], t_tmp[0:1, :], -1.0)
        ctx.bar()
    ctx.bar()
    for fc in range(8):
        stream_w(cbt, fc, 8, 0)
        ctx.wait(nc.tensor, "wl0")
        for kc in range(8):
            nc.tensor.matmul(pbig[0:128, fc:fc + 1],
                             wslab[:, kc * 128:(kc + 1) * 128],
                             ctx.c("c_onescol_b")[:, 0:1], start=(kc == 0), stop=(kc == 7))
        ctx.bar()
    nc.vector.tensor_copy(cwcol[:, 0:8], pbig[:, 0:8])
    ctx.bar()
    for j in range(8):
        mt = nc.tensor.transpose(ptb[0:1, 0:128], cwcol[:, j:j + 1], ctx.c("c_I")[:])
        ctx.tag(mt, "cwt")
        nc.vector.wait_ge(ctx.sem("cwt"), ctx.cnt["cwt"])
        vv = nc.vector.tensor_copy(cwrow[0:1, j * 128:(j + 1) * 128], ptb[0:1, 0:128])
        ctx.tag(vv, "cwc8")
        nc.tensor.wait_ge(ctx.sem("cwc8"), ctx.cnt["cwc8"])
    ctx.bar()
    for tt in range(TOK // 512):
        ts = slice(tt * 512, (tt + 1) * 512)
        m = nc.tensor.matmul(pbig2[:, 1024:1536], ctx.c("c_onesrow_f")[0:1, :],
                             r_rbf[0:1, ts], start=True, stop=True)
        ctx.tag(m, "bcb")
        nc.vector.wait_ge(ctx.sem("bcb"), ctx.cnt["bcb"])
        last = None
        for fc in range(8):
            off = fc * TOK + tt * 512
            last = nc.vector.tensor_tensor(xnb[:, off:off + 512], xsb[:, off:off + 512],
                                           pbig2[:, 1024:1536], OP.mult)
        ctx.tag(last, "xnb")
        nc.tensor.wait_ge(ctx.sem("xnb"), ctx.cnt["xnb"])
    ctx.bar()
    for tt in range(TOK // 512):
        ts = slice(tt * 512, (tt + 1) * 512)
        def ld_cb(fc, buf):
            return stream_w(cbt, fc, 8, buf)

        def mm_cb(fc, buf):
            bo = buf * 4096
            pb = pbig[:, (fc % 2) * 512:(fc % 2) * 512 + 512]
            ctx.lagwait(nc.tensor, "cb2", 1)
            for kc in range(8):
                nc.tensor.matmul(pb, wslab[:, bo + kc * 128: bo + (kc + 1) * 128],
                                 xnb[:, kc * TOK + tt * 512: kc * TOK + (tt + 1) * 512],
                                 start=(kc == 0), stop=False)
            mmm = nc.tensor.matmul(pb, cwrow[0:1, fc * 128:(fc + 1) * 128],
                                   r_mbrb[0:1, ts], start=False, stop=True)
            ctx.tag(mmm, "cb1")
            nc.vector.wait_ge(ctx.sem("cb1"), ctx.cnt["cb1"])
            off = fc * TOK + tt * 512
            nc.vector.tensor_scalar(scr3b[:, (fc % 2) * 512:(fc % 2) * 512 + 512], pb,
                                    w8b[:, fc:fc + 1], w8c[:, fc:fc + 1],
                                    OP.mult, OP.add)
            vv = nc.vector.tensor_add(xsb[:, off:off + 512], xsb[:, off:off + 512],
                                      scr3b[:, (fc % 2) * 512:(fc % 2) * 512 + 512])
            ctx.tag(vv, "cb2")

        _mm_stream(ctx, list(range(8)), ld_cb, mm_cb, "cb1")
        ctx.wait(nc.vector, "cb2")
    ctx.bar()
    ctx.dma("sq", o_x4.rearrange("(f p) t -> p f t", p=128),
            xsb[:].rearrange("p (f t) -> p f t", f=8), "st")
    ctx.wait(nc.sync, "st")
    ctx.bar()
    return nc


def emit_p4(ctx, EX):
    nc = ctx.nc
    xq = EX["x4q"]
    n1w8 = EX["n1w8"]
    o_xn4 = EX["o_xn4"]
    o_rows = EX["o_rows4"]

    xsb = ctx.sb("xsb4", [128, 8 * TOK], F32)
    scr = ctx.sb("scr4", [128, 512], F32)
    w8 = ctx.sb("w84", [128, 8], F32)
    xn4 = ctx.sb("xn4", [128, 8 * TOK], BF16)
    r_mean = ctx.sb("r_mean4", [1, TOK], F32)
    r_var = ctx.sb("r_var4", [1, TOK], F32)
    r_r1 = ctx.sb("r_r14", [1, TOK], F32)
    r_tmp = ctx.sb("r_tmp4", [1, TOK], F32)
    r_s = ctx.sb("r_s4", [1, TOK], F32)
    r_ms = ctx.sb("r_ms4", [1, TOK], F32)
    r_mr1 = ctx.sb("r_mr14", [1, TOK], F32)
    psSx = ctx.pbig2[0:1, 0:512]
    psSxx = ctx.pbig2[0:1, 512:1024]
    psB = ctx.pbig2[0:128, 1024:1536]

    ctx.dma("sq", xsb[:].rearrange("p (f t) -> p f t", f=8),
            xq.rearrange("(f p) t -> p f t", p=128), "ld")
    ctx.dma("sq", w8[:], n1w8, "ld")
    ctx.wait(nc.sync, "ld")
    ctx.bar()
    rO = {"mean": r_mean, "var": r_var, "r1": r_r1, "tmp": r_tmp}
    for tt in range(TOK // 512):
        tsl = slice(tt * 512, (tt + 1) * 512)
        _emit_ln_sums(ctx, xsb, 8, TOK, scr, psSx, psSxx, tsl)
        rO["sl"] = tsl
        _rows_chain_ln(ctx, psSx, psSxx, rO)
    ctx.bar()
    nc.vector.tensor_mul(r_tmp[0:1, :], r_r1[0:1, :], r_r1[0:1, :])
    ctx.bar()
    nc.vector.tensor_mul(r_tmp[0:1, :], r_tmp[0:1, :], r_var[0:1, :])
    ctx.bar()
    nc.scalar.activation(r_tmp[0:1, :], r_tmp[0:1, :], AF.Sqrt,
                         bias=ctx.cc(1e-5)[0:1, :], scale=1.0)
    ctx.bar()
    nc.vector.reciprocal(r_s[0:1, :], r_tmp[0:1, :])
    ctx.bar()
    nc.vector.tensor_mul(r_s[0:1, :], r_s[0:1, :], r_r1[0:1, :])
    ctx.bar()
    nc.vector.tensor_mul(r_ms[0:1, :], r_s[0:1, :], r_mean[0:1, :])
    nc.vector.tensor_mul(r_mr1[0:1, :], r_r1[0:1, :], r_mean[0:1, :])
    ctx.bar()
    onesrow = ctx.c("c_onesrow_f")
    for tt in range(TOK // 512):
        tsl = slice(tt * 512, (tt + 1) * 512)
        m = nc.tensor.matmul(psB[:, 0:512], onesrow[0:1, :], r_s[0:1, tsl],
                             start=True, stop=True)
        ctx.tag(m, "bc_ev")
        nc.vector.wait_ge(ctx.sem("bc_ev"), ctx.cnt["bc_ev"])
        last = None
        for fc in range(8):
            off = fc * TOK + tt * 512
            nc.vector.tensor_tensor(scr[:, 0:512], xsb[:, off:off + 512],
                                    psB[:, 0:512], OP.mult)
            last = nc.vector.tensor_scalar(xn4[:, off:off + 512], scr[:, 0:512],
                                           w8[:, fc:fc + 1], None, OP.mult)
        ctx.tag(last, "xn_ev")
        nc.tensor.wait_ge(ctx.sem("xn_ev"), ctx.cnt["xn_ev"])
        nc.scalar.wait_ge(ctx.sem("xn_ev"), ctx.cnt["xn_ev"])
    ctx.bar()
    ctx.dma("sq", o_xn4.rearrange("(f p) t -> p f t", p=128),
            xn4[:].rearrange("p (f t) -> p f t", f=8), "st")
    ctx.dma("sq", o_rows[0:1, :], r_s[0:1, :], "st")
    ctx.dma("sq", o_rows[1:2, :], r_ms[0:1, :], "st")
    ctx.dma("sq", o_rows[2:3, :], r_r1[0:1, :], "st")
    ctx.dma("sq", o_rows[3:4, :], r_mr1[0:1, :], "st")
    ctx.wait(nc.sync, "st")
    ctx.bar()
    return nc


def emit_p5(ctx, EX):
    nc = ctx.nc
    xn4f = EX["xn4f"]
    mscf4 = EX["mscf4"]
    win_z = EX["win_z"]
    win_x = EX["win_x"]
    win_B = EX["win_B"]
    win_C = EX["win_C"]
    win_dt = EX["win_dt"]
    win_th = EX["win_th"]
    n1w8 = EX["n1w8"]
    alog8 = EX["alog8"]
    dtb8 = EX["dtb8"]
    d8 = EX["d8"]
    bnw = EX["bnw"]
    cnw = EX["cnw"]
    bbias = EX["bbias"]
    cbias = EX["cbias"]
    o_yg = EX["o_yg"]
    o_dbga = EX["o_dbga"]
    o_dbgb = EX["o_dbgb"]
    import os
    _DBGB = os.environ.get("P5_DBGB", "")

    wz = ctx.sb("wz", [128, 8 * 512], BF16)
    wx = ctx.sb("wx", [128, 8 * 512], BF16)
    wB = ctx.sb("wB", [128, 8 * 512], BF16)
    wC = ctx.sb("wC", [128, 8 * 512], BF16)
    wdt = ctx.sb("wdt", [128, 8 * 8], BF16)
    wth = ctx.sb("wth", [128, 8 * 256], BF16)
    n1b = ctx.sb("n1b", [128, 8], BF16)
    n1f = ctx.sb("n1f", [128, 8], F32)
    msneg = ctx.sb("msneg5", [1, S], BF16)
    xt = [ctx.sb(f"xt5{i}", [128, 8 * 512], BF16) for i in range(2)]
    btok = ctx.sb("btok", [128, 2048], BF16)
    ctok = ctx.sb("ctok", [128, 2048], BF16)
    cfm = ctx.sb("cfm", [128, 2048], BF16)
    xg = ctx.sb("xg", [128, 2048], BF16)
    xw = ctx.sb("xw", [128, 2048], BF16)
    zsil = ctx.sb("zsil", [128, 2048], BF16)
    ygt = ctx.sb("ygt", [128, 2048], BF16)
    rdt = ctx.sb("rdt", [8, 512], F32)
    rtmp = ctx.sb("rtmp", [8, 512], F32)
    rtmp2 = ctx.sb("rtmp2", [8, 512], F32)
    rA = ctx.sb("rA", [8, 512], F32)
    rgam = ctx.sb("rgam", [8, 512], F32)
    rF = ctx.sb("rF", [8, 512], F32)
    rG = ctx.sb("rG", [8, 512], F32)
    rea = ctx.sb("rea", [8, 512], F32)
    ren = ctx.sb("ren", [8, 512], F32)
    rvin = ctx.sb("rvin", [8, 512], F32)
    ru = ctx.sb("ru", [8, 512], F32)
    rub = ctx.sb("rub", [8, 512], BF16)
    reab = ctx.sb("reab", [8, 512], BF16)
    rdsv = ctx.sb("rdsv", [8, 512], F32)
    rones = ctx.sb("rones", [8, 512], F32)
    ramp8 = ctx.sb("ramp8", [8, 512], F32)
    eaend = ctx.sb("eaend", [8, 8], F32)
    prevend = ctx.sb("prevend", [8, 8], F32)
    carryF = ctx.sb("carryF", [8, 1], F32)
    expal = ctx.sb("expal", [8, 1], F32)
    dtbc = ctx.sb("dtbc", [8, 1], F32)
    dttm = ctx.sb("dttm", [128, 32], F32)
    gamtm = ctx.sb("gamtm", [128, 32], F32)
    dswtm = ctx.sb("dswtm", [128, 32], F32)
    skiptok = ctx.sb("skiptok", [128, 8], F32)
    skipfm = ctx.sb("skipfm", [8, 512], F32)
    d8bc = ctx.sb("d8bc", [128, 8], F32)
    bnw128 = ctx.sb("bnw128", [128, 64], F32)
    cnw128 = ctx.sb("cnw128", [128, 64], F32)
    bb128 = ctx.sb("bb128", [128, 512], F32)
    cb128 = ctx.sb("cb128", [128, 512], F32)
    dthe = ctx.sb("dthe", [128, 256], F32)
    scrA = ctx.sb("scrA5", [128, 512], F32)
    scrB = ctx.sb("scrB5", [128, 512], F32)
    ssin = ctx.sb("ssin", [128, 256], F32)
    scos = ctx.sb("scos", [128, 256], F32)
    carryTh = ctx.sb("carryTh", [1, 256], F32)
    rstd8 = ctx.sb("rstd8", [128, 8], F32)
    rstd8b = ctx.sb("rstd8b", [128, 8], F32)
    sq5 = ctx.sb("sq5", [128, 512], BF16)
    FS = ctx.sb("FS", [128, 256], F32)
    FSb = ctx.sb("FSb", [128, 256], BF16)
    T1sb = ctx.sb("T1sb", [128, 256], BF16)
    bva = ctx.sb("bva", [8, 2048], BF16)
    ubcA = ctx.sb("ubcA", [128, 2048], BF16)
    eabcA = ctx.sb("eabcA", [128, 2048], BF16)
    skbcA = ctx.sb("skbcA", [128, 2048], BF16)
    decA = ctx.sb("decA", [128, 2048], BF16)
    cwr = ctx.sb("cwr", [1, 14 * 128], BF16)
    cwz = ctx.sb("cwz", [1, 4 * 128], BF16)
    cwdt = ctx.sb("cwdt", [1, 8], BF16)
    cwc = ctx.sb("cwc", [128, 16], BF16)

    pA = ctx.pbig
    pB = ctx.pbig2
    ptb = pA[:, 1792:1856].bitcast(BF16)


    for w_, ap_, ncol in [(wz, win_z, 512), (wx, win_x, 512), (wB, win_B, 512),
                          (wC, win_C, 512), (wdt, win_dt, 8), (wth, win_th, 256)]:
        ctx.dma("gq", w_[:].rearrange("p (kc m) -> p kc m", kc=8),
                ap_.rearrange("(kc p) m -> p kc m", p=128), "gl")
    for rr in range(4):
        ctx.dma("sq", msneg[0:1, rr * TOK:(rr + 1) * TOK], mscf4[rr], "ld")
    ctx.dma("sq", n1f[:], n1w8, "ld")
    ctx.dma("sq", expal[:], alog8, "ld")
    ctx.dma("sq", dtbc[:], dtb8, "ld")
    ctx.wait(nc.sync, "ld")
    ctx.wait(nc.gpsimd, "gl")
    ctx.bar()
    ctx.dma("gq", ramp8[:], ctx.cdram["c_ramp"][0:1, :].partition_broadcast(8), "g2")
    ctx.dma("gq", d8bc[:], d8[0:1, :].partition_broadcast(128), "g2")
    ctx.dma("gq", bnw128[:], bnw[0:1, :].partition_broadcast(128), "g2")
    ctx.dma("gq", cnw128[:], cnw[0:1, :].partition_broadcast(128), "g2")
    ctx.dma("gq", bb128[:], bbias[0:1, :].partition_broadcast(128), "g2")
    ctx.dma("gq", cb128[:], cbias[0:1, :].partition_broadcast(128), "g2")
    ctx.wait(nc.gpsimd, "g2")
    ctx.bar()
    nc.vector.memset(rones[:], 1.0)
    nc.vector.memset(carryF[:], 0.0)
    nc.vector.memset(carryTh[:], 0.0)
    nc.vector.memset(FS[:], 0.0)
    nc.vector.tensor_copy(n1b[:], n1f[:])
    nc.vector.tensor_scalar_mul(msneg[0:1, :], msneg[0:1, :], -1.0)
    ctx.bar()
    nc.scalar.activation(expal[:], expal[:], AF.Exp, bias=0.0, scale=1.0)
    ctx.bar()
    nc.vector.tensor_scalar_mul(expal[:], expal[:], -1.0)
    ctx.bar()

    order = [(wx, 4, 0), (wB, 4, 4), (wC, 4, 8), (wth, 2, 12)]
    for w_, nch, base in order:
        for mi in range(nch):
            for kc in range(8):
                wid = 512 if w_ is not wth else 256
                nc.tensor.matmul(pA[0:128, base + mi: base + mi + 1],
                                 w_[:, kc * wid + mi * 128: kc * wid + (mi + 1) * 128],
                                 n1b[:, kc:kc + 1], start=(kc == 0), stop=(kc == 7))
    ctx.bar()
    nc.vector.tensor_copy(cwc[:, 0:14], pA[0:128, 0:14])
    ctx.bar()
    for j in range(14):
        nc.tensor.transpose(ptb[0:1, 0:128], cwc[:, j:j + 1], ctx.c("c_I")[:])
        ctx.bar()
        nc.vector.tensor_copy(cwr[0:1, j * 128:(j + 1) * 128], ptb[0:1, 0:128])
        ctx.bar()
    for mi in range(4):
        for kc in range(8):
            nc.tensor.matmul(pA[0:128, mi:mi + 1],
                             wz[:, kc * 512 + mi * 128: kc * 512 + (mi + 1) * 128],
                             n1b[:, kc:kc + 1], start=(kc == 0), stop=(kc == 7))
    ctx.bar()
    nc.vector.tensor_copy(cwc[:, 0:4], pA[0:128, 0:4])
    ctx.bar()
    for j in range(4):
        nc.tensor.transpose(ptb[0:1, 0:128], cwc[:, j:j + 1], ctx.c("c_I")[:])
        ctx.bar()
        nc.vector.tensor_copy(cwz[0:1, j * 128:(j + 1) * 128], ptb[0:1, 0:128])
        ctx.bar()
    for kc in range(8):
        nc.tensor.matmul(pA[0:8, 0:1], wdt[:, kc * 8:(kc + 1) * 8], n1b[:, kc:kc + 1],
                         start=(kc == 0), stop=(kc == 7))
    ctx.bar()
    nc.vector.tensor_copy(cwc[0:8, 0:1], pA[0:8, 0:1])
    ctx.bar()
    nc.tensor.transpose(ptb[0:1, 0:8], cwc[0:8, 0:1], ctx.c("c_I")[0:8, 0:8])
    ctx.bar()
    nc.vector.tensor_copy(cwdt[0:1, 0:8], ptb[0:1, 0:8])
    ctx.bar()

    TPI = 2 * math.pi

    for tile in range(16):
        gts = tile * 512
        rank = gts // TOK
        toff = gts % TOK
        xb = xt[tile % 2]
        ctx.dma("sq", xb[:].rearrange("p (kc t) -> p kc t", kc=8),
                xn4f[rank, :, toff:toff + 512].rearrange("(kc p) t -> p kc t", p=128),
                "xl")
        ctx.wait(nc.sync, "xl")
        ctx.bar()
        msl = msneg[0:1, gts:gts + 512]

        for kc in range(8):
            nc.tensor.matmul(pB[0:8, 0:512], wdt[:, kc * 8:(kc + 1) * 8],
                             xb[:, kc * 512:(kc + 1) * 512], start=(kc == 0), stop=False)
        _i = nc.tensor.matmul(pB[0:8, 0:512], cwdt[0:1, 0:8], msl, start=False, stop=True)
        ctx.tag(_i, "dtmm")
        nc.vector.wait_ge(ctx.sem("dtmm"), ctx.cnt["dtmm"])
        lv = nc.vector.tensor_scalar(rdt[:, :], pB[0:8, 0:512], dtbc[:, 0:1], None, OP.add)
        ctx.tag(lv, "gv")
        nc.scalar.wait_ge(ctx.sem("gv"), ctx.cnt["gv"])
        nc.scalar.activation(rtmp[:, :], rdt[:, :], AF.Abs, bias=0.0, scale=1.0)
        nc.scalar.activation(rtmp[:, :], rtmp[:, :], AF.Exp, bias=0.0, scale=-1.0)
        la = nc.scalar.activation(rtmp[:, :], rtmp[:, :], AF.Ln,
                                  bias=ctx.c("c_onescol_f")[0:8, 0:1], scale=1.0)
        ctx.tag(la, "gs")
        nc.vector.tensor_scalar_max(rtmp2[:, :], rdt[:, :], 0.0)
        nc.vector.wait_ge(ctx.sem("gs"), ctx.cnt["gs"])
        nc.vector.tensor_add(rdt[:, :], rtmp[:, :], rtmp2[:, :])
        lv = nc.vector.tensor_scalar(rA[:, :], rdt[:, :], expal[:, 0:1], None, OP.mult)
        ctx.tag(lv, "gv")
        nc.scalar.wait_ge(ctx.sem("gv"), ctx.cnt["gv"])
        la = nc.scalar.activation(rtmp[:, :], rA[:, :], AF.Exp, bias=0.0, scale=1.0)
        ctx.tag(la, "gs")
        nc.vector.wait_ge(ctx.sem("gs"), ctx.cnt["gs"])
        nc.vector.tensor_scalar(rtmp[:, :], rtmp[:, :], -1.0, None, OP.add)
        nc.vector.tensor_scalar(rtmp2[:, :], rA[:, :], 1e-6, None, OP.add)
        lv = nc.vector.reciprocal(rtmp2[:, :], rtmp2[:, :])
        ctx.tag(lv, "gv")
        nc.vector.wait_ge(ctx.sem("gv"), ctx.cnt["gv"])
        nc.vector.tensor_mul(rgam[:, :], rtmp[:, :], rtmp2[:, :])
        nc.vector.tensor_scalar(rgam[:, :], rgam[:, :], 0.5, 1.0, OP.mult, OP.add)
        nc.vector.tensor_tensor_scan(rF[:, :], rones[:, :], rA[:, :], carryF[:, 0:1],
                                     OP.mult, OP.add)
        nc.vector.tensor_copy(prevend[:, 0:1], carryF[:, 0:1])
        nc.vector.tensor_copy(prevend[:, 1:8],
                              rF[:, :].rearrange("p (c l) -> p c l", c=8)[:, 0:7, 63:64]
                              .rearrange("p c l -> p (c l)"))
        nc.vector.tensor_copy(carryF[:, 0:1], rF[:, 511:512])
        pv = prevend[:, :].unsqueeze(2).broadcast_to((8, 8, 64))
        lv = nc.vector.tensor_tensor(rG[:, :].rearrange("p (c l) -> p c l", c=8),
                                     rF[:, :].rearrange("p (c l) -> p c l", c=8), pv, OP.subtract)
        ctx.tag(lv, "gv")
        nc.scalar.wait_ge(ctx.sem("gv"), ctx.cnt["gv"])
        nc.scalar.activation(rea[:, :], rG[:, :], AF.Exp, bias=0.0, scale=1.0)
        la = nc.scalar.activation(ren[:, :], rG[:, :], AF.Exp, bias=0.0, scale=-1.0)
        ctx.tag(la, "gs")
        nc.vector.wait_ge(ctx.sem("gs"), ctx.cnt["gs"])
        nc.vector.tensor_copy(eaend[:, :],
                              rea[:, :].rearrange("p (c l) -> p c l", c=8)[:, :, 63:64]
                              .rearrange("p c l -> p (c l)"))
        for c in range(8):
            nc.vector.tensor_tensor_scan(rvin[:, c * 64:(c + 1) * 64],
                                         rones[:, 0:64], ren[:, c * 64:(c + 1) * 64],
                                         0.0, OP.mult, OP.add)
        nc.vector.tensor_sub(rtmp[:, :], rvin[:, :], ren[:, :])
        nc.vector.tensor_mul(rtmp[:, :], rtmp[:, :], rea[:, :])
        nc.vector.tensor_add(ru[:, :], rtmp[:, :], ramp8[:, :])
        nc.vector.tensor_copy(rub[:, :], ru[:, :])
        nc.vector.tensor_copy(reab[:, :], rea[:, :])
        ev_ = eaend[:, :].unsqueeze(2).broadcast_to((8, 8, 64))
        lv = nc.vector.tensor_tensor(rdsv[:, :].rearrange("p (c l) -> p c l", c=8),
                                     ren[:, :].rearrange("p (c l) -> p c l", c=8), ev_, OP.mult)
        ctx.tag(lv, "gv")
        nc.tensor.wait_ge(ctx.sem("gv"), ctx.cnt["gv"])
        if tile == 0 and _DBGB == "rows":
            for ri, rr in enumerate([rdt, rA, rF, rG, rea, ren, rvin, ru, rgam, rdsv]):
                ctx.dma("sq", o_dbga[ri * 8:(ri + 1) * 8, :], rr[:, :], "dbg")
            ctx.dma("sq", o_dbga[80:88, :], eaend[:, 0:8].broadcast_to((8, 8)) if False else ru[:, :], "dbg")
            ctx.wait(nc.sync, "dbg")
            ctx.bar()
        for tc in range(4):
            for src, dst in [(rdt, dttm), (rgam, gamtm), (rdsv, dswtm)]:
                mt = nc.tensor.transpose(pB[0:128, 512:520].bitcast(F32)[:, 0:8] if False else pA[0:128, 1024:1032],
                                         src[:, tc * 128:(tc + 1) * 128], ctx.c("c_If")[0:8, 0:8])
                ctx.tag(mt, "ttm")
                nc.vector.wait_ge(ctx.sem("ttm"), ctx.cnt["ttm"])
                vv = nc.vector.tensor_copy(dst[:, tc * 8:(tc + 1) * 8], pA[0:128, 1024:1032])
                ctx.tag(vv, "ttc")
                nc.tensor.wait_ge(ctx.sem("ttc"), ctx.cnt["ttc"])
        ctx.bar()

        for mz in range(4):
            pb = pA[:, (mz % 2) * 512:(mz % 2) * 512 + 512]
            for kc in range(8):
                nc.tensor.matmul(pb, wz[:, kc * 512 + mz * 128: kc * 512 + (mz + 1) * 128],
                                 xb[:, kc * 512:(kc + 1) * 512], start=(kc == 0), stop=False)
            _i = nc.tensor.matmul(pb, cwz[0:1, mz * 128:(mz + 1) * 128], msl,
                                  start=False, stop=True)
            ctx.tag(_i, "zmm")
            nc.scalar.wait_ge(ctx.sem("zmm"), ctx.cnt["zmm"])
            aa = nc.scalar.activation(zsil[:, mz * 512:(mz + 1) * 512], pb, AF.Silu,
                                      bias=0.0, scale=1.0)
            ctx.tag(aa, "zsl")
            nc.tensor.wait_ge(ctx.sem("zsl"), ctx.cnt["zsl"])
        ctx.bar()

        for tc in range(4):
            tsl = slice(tc * 128, (tc + 1) * 128)
            if "gs" in ctx.cnt:
                nc.tensor.wait_ge(ctx.sem("gs"), ctx.cnt["gs"])
            for kc in range(8):
                st = (kc == 0)
                sp = (kc == 7)
                lhs = xb[:, kc * 512 + tc * 128: kc * 512 + (tc + 1) * 128]
                nc.tensor.matmul(pA[:, 0:512], lhs, wx[:, kc * 512:(kc + 1) * 512], start=st, stop=False)
                nc.tensor.matmul(pA[:, 512:1024], lhs, wB[:, kc * 512:(kc + 1) * 512], start=st, stop=False)
                nc.tensor.matmul(pA[:, 1024:1536], lhs, wC[:, kc * 512:(kc + 1) * 512], start=st, stop=False)
                nc.tensor.matmul(pA[:, 1536:1792], lhs, wth[:, kc * 256:(kc + 1) * 256], start=st, stop=False)
            mstat = msneg[0:1, gts + tc * 128: gts + (tc + 1) * 128]
            for regi, (off, cwbase, wd) in enumerate([(0, 0, 512), (512, 4, 512), (1024, 8, 512)]):
                nc.tensor.matmul(pA[:, off:off + wd], mstat,
                                 cwr[0:1, cwbase * 128:(cwbase + 4) * 128][0:1, 0:wd],
                                 start=False, stop=True)
            _i = nc.tensor.matmul(pA[:, 1536:1792], mstat, cwr[0:1, 12 * 128:14 * 128],
                                  start=False, stop=True)
            ctx.tag(_i, "swm")
            nc.vector.wait_ge(ctx.sem("swm"), ctx.cnt["swm"])
            if tile == 0 and _DBGB in ("braw", "craw", "xraw", "thraw"):
                srcp = {"braw": pA[:, 512:1024], "craw": pA[:, 1024:1536],
                        "xraw": pA[:, 0:512], "thraw": pA[:, 1536:1792]}[_DBGB]
                wd = 256 if _DBGB == "thraw" else 512
                nc.vector.tensor_copy(ygt[:, tc * 512: tc * 512 + wd].bitcast(F32)[:, 0:wd] if False
                                      else scrB[:, 0:wd], srcp)
                ctx.bar()
                ctx.dma("sq", o_dbgb[:, tc * 512: tc * 512 + wd], scrB[:, 0:wd], "dbg3")
                ctx.wait(nc.sync, "dbg3")
                ctx.bar()

            dtv_ = dttm[:, tc * 8:(tc + 1) * 8].unsqueeze(2).broadcast_to((128, 8, 32))
            lv = nc.vector.tensor_tensor(dthe[:, :].rearrange("p (h j) -> p h j", h=8),
                                         pA[:, 1536:1792].rearrange("p (h j) -> p h j", h=8),
                                         dtv_, OP.mult)
            ctx.tag(lv, "dth")
            nc.tensor.wait_ge(ctx.sem("dth"), ctx.cnt["dth"])
            m1 = nc.tensor.matmul(pB[:, 512:768], ctx.c("c_tril")[:], dthe[:, :],
                                  start=True, stop=False)
            m2 = nc.tensor.matmul(pB[:, 512:768], ctx.c("c_onesrow_f")[0:1, :],
                                  carryTh[0:1, :], start=False, stop=True)
            m3 = nc.tensor.matmul(pB[0:1, 768:1024], ctx.c("c_onescol_f")[:, 0:1],
                                  dthe[:, :], start=True, stop=True)
            ctx.tag(m3, "cum")
            nc.vector.wait_ge(ctx.sem("cum"), ctx.cnt["cum"])
            if tile == 0 and _DBGB == "cum":
                nc.vector.tensor_copy(scrB[:, 0:256], pB[:, 512:768])
                ctx.bar()
                ctx.dma("sq", o_dbgb[:, tc * 512: tc * 512 + 256], scrB[:, 0:256], "dbg3")
                ctx.wait(nc.sync, "dbg3")
                ctx.bar()
            nc.vector.tensor_add(carryTh[0:1, :], carryTh[0:1, :], pB[0:1, 768:1024])
            K_ = 1.5 * (2 ** 23)
            nc.vector.tensor_scalar(scrA[:, 0:256], pB[:, 512:768], 1.0 / TPI, K_,
                                    OP.mult, OP.add)
            nc.vector.tensor_scalar(scrA[:, 0:256], scrA[:, 0:256], -K_, None, OP.add)
            nc.vector.scalar_tensor_tensor(scrB[:, 0:256], scrA[:, 0:256], -TPI,
                                           pB[:, 512:768], OP.mult, OP.add)
            nc.vector.tensor_scalar_min(scrB[:, 0:256], scrB[:, 0:256], float(math.pi))
            lv = nc.vector.tensor_scalar_max(scrB[:, 0:256], scrB[:, 0:256], float(-math.pi))
            ctx.tag(lv, "gv")
            nc.scalar.wait_ge(ctx.sem("gv"), ctx.cnt["gv"])
            la = nc.scalar.activation(ssin[:, :], scrB[:, 0:256], AF.Sin, bias=0.0, scale=1.0)
            ctx.tag(la, "gs")
            nc.vector.tensor_scalar(scrA[:, 0:256], scrB[:, 0:256], float(math.pi / 2), None, OP.add)
            nc.vector.wait_ge(ctx.sem("gs"), ctx.cnt["gs"])
            nc.vector.tensor_scalar(scrB[:, 0:256], scrA[:, 0:256], float(math.pi), None, OP.is_gt)
            lv = nc.vector.scalar_tensor_tensor(scrA[:, 0:256], scrB[:, 0:256], -TPI,
                                                scrA[:, 0:256], OP.mult, OP.add)
            ctx.tag(lv, "gv")
            nc.scalar.wait_ge(ctx.sem("gv"), ctx.cnt["gv"])
            la = nc.scalar.activation(scos[:, :], scrA[:, 0:256], AF.Sin, bias=0.0, scale=1.0)
            ctx.tag(la, "gs")

            for (pso, w128, bias128, dstt) in [(512, bnw128, bb128, btok),
                                               (1024, cnw128, cb128, ctok)]:
                ps_ = pA[:, pso:pso + 512]
                la = nc.scalar.activation(sq5[:, :], ps_, AF.Square, bias=0.0, scale=1.0)
                ctx.tag(la, "gs")
                nc.vector.wait_ge(ctx.sem("gs"), ctx.cnt["gs"])
                lv = nc.vector.tensor_reduce(rstd8[:, :].rearrange("p (h o) -> p h o", h=8),
                                              sq5[:, :].rearrange("p (h n) -> p h n", h=8),
                                              AX, OP.add)
                ctx.tag(lv, "gv")
                nc.vector.wait_ge(ctx.sem("gv"), ctx.cnt["gv"])
                lv = nc.vector.tensor_scalar_mul(rstd8[:, :], rstd8[:, :], 1.0 / 64.0)
                ctx.tag(lv, "gv")
                nc.scalar.wait_ge(ctx.sem("gv"), ctx.cnt["gv"])
                la = nc.scalar.activation(rstd8b[:, :], rstd8[:, :], AF.Sqrt,
                                           bias=ctx.cc(1e-5)[0:128, 0:1], scale=1.0)
                ctx.tag(la, "gs")
                nc.vector.wait_ge(ctx.sem("gs"), ctx.cnt["gs"])
                lv = nc.vector.reciprocal(rstd8[:, :], rstd8b[:, :])
                ctx.tag(lv, "gv")
                nc.vector.wait_ge(ctx.sem("gv"), ctx.cnt["gv"])
                rsv = rstd8[:, :].unsqueeze(2).broadcast_to((128, 8, 64))
                nc.vector.tensor_tensor(scrA[:, :].rearrange("p (h n) -> p h n", h=8),
                                        ps_.rearrange("p (h n) -> p h n", h=8), rsv, OP.mult)
                wv_ = w128[:, :].unsqueeze(1).broadcast_to((128, 8, 64))
                nc.vector.tensor_tensor(scrA[:, :].rearrange("p (h n) -> p h n", h=8),
                                        scrA[:, :].rearrange("p (h n) -> p h n", h=8), wv_, OP.mult)
                nc.vector.tensor_add(scrA[:, :], scrA[:, :], bias128[:, :])
                bev = scrA[:, :].rearrange("p (h t) -> p h t", h=8)[:, :, 0:32]
                bod = scrA[:, :].rearrange("p (h t) -> p h t", h=8)[:, :, 32:64]
                cv = scos[:, :].rearrange("p (h j) -> p h j", h=8)
                sv = ssin[:, :].rearrange("p (h j) -> p h j", h=8)
                dev = dstt[:, tc * 512:(tc + 1) * 512].rearrange("p (h t) -> p h t", h=8)[:, :, 0:32]
                dod = dstt[:, tc * 512:(tc + 1) * 512].rearrange("p (h t) -> p h t", h=8)[:, :, 32:64]
                nc.vector.tensor_tensor(scrB[:, 0:256].rearrange("p (h j) -> p h j", h=8), bev, cv, OP.mult)
                nc.vector.tensor_tensor(scrB[:, 256:512].rearrange("p (h j) -> p h j", h=8), bod, sv, OP.mult)
                nc.vector.tensor_sub(scrB[:, 0:256], scrB[:, 0:256], scrB[:, 256:512])
                nc.vector.tensor_copy(dev, scrB[:, 0:256].rearrange("p (h j) -> p h j", h=8))
                nc.vector.tensor_tensor(scrB[:, 0:256].rearrange("p (h j) -> p h j", h=8), bev, sv, OP.mult)
                nc.vector.tensor_tensor(scrB[:, 256:512].rearrange("p (h j) -> p h j", h=8), bod, cv, OP.mult)
                nc.vector.tensor_add(scrB[:, 0:256], scrB[:, 0:256], scrB[:, 256:512])
                nc.vector.tensor_copy(dod, scrB[:, 0:256].rearrange("p (h j) -> p h j", h=8))

            gv_ = gamtm[:, tc * 8:(tc + 1) * 8].unsqueeze(2).broadcast_to((128, 8, 64))
            nc.vector.tensor_tensor(xg[:, tc * 512:(tc + 1) * 512].rearrange("p (h q) -> p h q", h=8),
                                    pA[:, 0:512].rearrange("p (h q) -> p h q", h=8), gv_, OP.mult)
            dv_ = dswtm[:, tc * 8:(tc + 1) * 8].unsqueeze(2).broadcast_to((128, 8, 64))
            nc.vector.tensor_tensor(xw[:, tc * 512:(tc + 1) * 512].rearrange("p (h q) -> p h q", h=8),
                                    xg[:, tc * 512:(tc + 1) * 512].rearrange("p (h q) -> p h q", h=8),
                                    dv_, OP.mult)
            lv = nc.vector.tensor_reduce(skiptok[:, :].rearrange("p (h o) -> p h o", h=8),
                                         pA[:, 0:512].rearrange("p (h q) -> p h q", h=8), AX, OP.add)
            ctx.tag(lv, "gv")
            nc.vector.wait_ge(ctx.sem("gv"), ctx.cnt["gv"])
            lv = nc.vector.tensor_mul(skiptok[:, :], skiptok[:, :], d8bc[:, :])
            ctx.tag(lv, "gv")
            nc.tensor.wait_ge(ctx.sem("gv"), ctx.cnt["gv"])
            mt = nc.tensor.transpose(pA[0:8, 1792:1920], skiptok[:, :], ctx.c("c_If")[0:128, 0:128])
            ctx.tag(mt, "skt")
            nc.vector.wait_ge(ctx.sem("skt"), ctx.cnt["skt"])
            lv = nc.vector.tensor_copy(skipfm[0:8, tc * 128:(tc + 1) * 128], pA[0:8, 1792:1920])
            ctx.tag(lv, "gv")
            nc.tensor.wait_ge(ctx.sem("gv"), ctx.cnt["gv"])
            for j in range(4):
                mt = nc.tensor.transpose(ptb[:, 0:128],
                                         ctok[:, tc * 512 + j * 128: tc * 512 + (j + 1) * 128],
                                         ctx.c("c_I")[:])
                ctx.tag(mt, "cft")
                nc.vector.wait_ge(ctx.sem("cft"), ctx.cnt["cft"])
                vv = nc.vector.tensor_copy(cfm[:, (j * 4 + tc) * 128:(j * 4 + tc + 1) * 128],
                                           ptb[:, 0:128])
                ctx.tag(vv, "cfc")
                nc.tensor.wait_ge(ctx.sem("cfc"), ctx.cnt["cfc"])
        ctx.bar()

        mv = ctx.c("c_mask8")[:, :].rearrange("p (j l) -> p j l", j=4) \
            .unsqueeze(1).broadcast_to((8, 8, 4, 64))
        bva4 = bva[:, :].rearrange("p (c j l) -> p c j l", c=8, j=4)
        lv = nc.vector.tensor_tensor(bva4,
                                     rub[:, :].rearrange("p (c l) -> p c l", c=8)
                                     .unsqueeze(2).broadcast_to((8, 8, 4, 64)),
                                     mv, OP.mult)
        ctx.tag(lv, "bv")
        nc.tensor.wait_ge(ctx.sem("bv"), ctx.cnt["bv"])
        for j4 in range(4):
            mm = nc.tensor.matmul(pA[:, j4 * 512:(j4 + 1) * 512], ctx.c("c_aexp")[:, :],
                                  bva[:, j4 * 512:(j4 + 1) * 512], start=True, stop=True)
        ctx.tag(mm, "bvm")
        nc.vector.wait_ge(ctx.sem("bvm"), ctx.cnt["bvm"])
        lv = nc.vector.tensor_copy(ubcA[:, :], pA[:, 0:2048])
        ctx.tag(lv, "bvc")
        lv = nc.vector.tensor_tensor(bva4,
                                     reab[:, :].rearrange("p (c l) -> p c l", c=8)
                                     .unsqueeze(2).broadcast_to((8, 8, 4, 64)),
                                     mv, OP.mult)
        ctx.tag(lv, "bv")
        nc.tensor.wait_ge(ctx.sem("bv"), ctx.cnt["bv"])
        nc.tensor.wait_ge(ctx.sem("bvc"), ctx.cnt["bvc"])
        for j4 in range(4):
            mm = nc.tensor.matmul(pA[:, j4 * 512:(j4 + 1) * 512], ctx.c("c_aexp")[:, :],
                                  bva[:, j4 * 512:(j4 + 1) * 512], start=True, stop=True)
        ctx.tag(mm, "bvm")
        nc.vector.wait_ge(ctx.sem("bvm"), ctx.cnt["bvm"])
        lv = nc.vector.tensor_copy(eabcA[:, :], pA[:, 0:2048])
        ctx.tag(lv, "bvc")
        lv = nc.vector.tensor_tensor(bva4,
                                     skipfm[:, :].rearrange("p (c l) -> p c l", c=8)
                                     .unsqueeze(2).broadcast_to((8, 8, 4, 64)),
                                     mv, OP.mult)
        ctx.tag(lv, "bv")
        nc.tensor.wait_ge(ctx.sem("bv"), ctx.cnt["bv"])
        nc.tensor.wait_ge(ctx.sem("bvc"), ctx.cnt["bvc"])
        for j4 in range(4):
            mm = nc.tensor.matmul(pA[:, j4 * 512:(j4 + 1) * 512], ctx.c("c_aexp")[:, :],
                                  bva[:, j4 * 512:(j4 + 1) * 512], start=True, stop=True)
        ctx.tag(mm, "bvm")
        nc.vector.wait_ge(ctx.sem("bvm"), ctx.cnt["bvm"])
        lv = nc.vector.tensor_copy(skbcA[:, :], pA[:, 0:2048])
        ctx.tag(lv, "bvc")
        lv = nc.vector.tensor_tensor(bva4,
                                     eaend[:, :].unsqueeze(2).unsqueeze(3)
                                     .broadcast_to((8, 8, 4, 64)),
                                     mv, OP.mult)
        ctx.tag(lv, "bv")
        nc.tensor.wait_ge(ctx.sem("bv"), ctx.cnt["bv"])
        nc.tensor.wait_ge(ctx.sem("bvc"), ctx.cnt["bvc"])
        for j4 in range(4):
            mm = nc.tensor.matmul(pA[:, j4 * 512:(j4 + 1) * 512], ctx.c("c_aexp")[:, :],
                                  bva[:, j4 * 512:(j4 + 1) * 512], start=True, stop=True)
        ctx.tag(mm, "bvm")
        nc.vector.wait_ge(ctx.sem("bvm"), ctx.cnt["bvm"])
        lv = nc.vector.tensor_copy(decA[:, :], pA[:, 0:2048])
        ctx.tag(lv, "bvc")
        nc.tensor.wait_ge(ctx.sem("bvc"), ctx.cnt["bvc"])

        for c in range(8):
            tcb = c // 2
            pr = (c % 2) * 64
            if c > 0:
                nc.tensor.wait_ge(ctx.sem("fsu"), ctx.cnt["fsu"])
            last = None
            for h in range(8):
                lhs = btok[pr:pr + 64, tcb * 512 + h * 64: tcb * 512 + (h + 1) * 64]
                o1 = pB[(h % 2) * 64:(h % 2) * 64 + 64, 0 + (h // 2) * 64: 0 + (h // 2) * 64 + 64]
                o2 = pB[(h % 2) * 64:(h % 2) * 64 + 64, 256 + (h // 2) * 64: 256 + (h // 2) * 64 + 64]
                nc.tensor.matmul(o1, lhs, xg[pr:pr + 64, tcb * 512 + h * 64: tcb * 512 + (h + 1) * 64],
                                 start=(h < 2), stop=True, skip_group_check=True)
                last = nc.tensor.matmul(o2, lhs, xw[pr:pr + 64, tcb * 512 + h * 64: tcb * 512 + (h + 1) * 64],
                                        start=(h < 2), stop=True, skip_group_check=True)
            ctx.tag(last, "t1s")
            nc.vector.wait_ge(ctx.sem("t1s"), ctx.cnt["t1s"])
            nc.vector.tensor_copy(T1sb[:, :], pB[:, 0:256])
            nc.vector.tensor_mul(FS[:, :], FS[:, :], decA[:, c * 256:(c + 1) * 256])
            nc.vector.tensor_add(FS[:, :], FS[:, :], pB[:, 256:512])
            lv = nc.vector.tensor_copy(FSb[:, :], FS[:, :])
            ctx.tag(lv, "fsu")
            nc.tensor.wait_ge(ctx.sem("fsu"), ctx.cnt["fsu"])
            if c > 0:
                nc.tensor.wait_ge(ctx.sem("ycb"), ctx.cnt["ycb"])
            last = None
            for h in range(8):
                rhs = cfm[(h % 2) * 64:(h % 2) * 64 + 64,
                          ((h // 2) * 4 + tcb) * 128 + pr: ((h // 2) * 4 + tcb) * 128 + pr + 64]
                oy1 = pB[(h % 2) * 64:(h % 2) * 64 + 64, 512 + (h // 2) * 64: 512 + (h // 2) * 64 + 64]
                oy2 = pB[(h % 2) * 64:(h % 2) * 64 + 64, 768 + (h // 2) * 64: 768 + (h // 2) * 64 + 64]
                nc.tensor.matmul(oy1, T1sb[(h % 2) * 64:(h % 2) * 64 + 64, (h // 2) * 64:(h // 2) * 64 + 64],
                                 rhs, start=(h < 2), stop=True, skip_group_check=True)
                last = nc.tensor.matmul(oy2, FSb[(h % 2) * 64:(h % 2) * 64 + 64, (h // 2) * 64:(h // 2) * 64 + 64],
                                        rhs, start=(h < 2), stop=True, skip_group_check=True)
            ctx.tag(last, "ymm")
            nc.vector.wait_ge(ctx.sem("ymm"), ctx.cnt["ymm"])
            nc.vector.tensor_tensor(scrA[:, 0:256], pB[:, 512:768],
                                    ubcA[:, c * 256:(c + 1) * 256], OP.mult)
            nc.vector.tensor_tensor(scrB[:, 0:256], pB[:, 768:1024],
                                    eabcA[:, c * 256:(c + 1) * 256], OP.mult)
            lv = nc.vector.tensor_add(scrA[:, 0:256], scrA[:, 0:256], scrB[:, 0:256])
            ctx.tag(lv, "ycb")
            nc.vector.tensor_add(scrA[:, 0:256], scrA[:, 0:256],
                                 skbcA[:, c * 256:(c + 1) * 256])
            zv = zsil[:, :].rearrange("p (j t) -> p j t", j=4)[:, :, c * 64:(c + 1) * 64]
            lv = nc.vector.tensor_tensor(ygt[:, :].rearrange("p (j t) -> p j t", j=4)[:, :, c * 64:(c + 1) * 64],
                                         scrA[:, 0:256].rearrange("p (j l) -> p j l", j=4),
                                         zv, OP.mult)
            ctx.tag(lv, "ygd")
        nc.sync.wait_ge(ctx.sem("ygd"), ctx.cnt["ygd"])
        ctx.bar()
        if tile == 0 and _DBGB in ("btok", "ctok", "cfm", "xg", "xw", "zsil", "ygt"):
            srcmap = {"btok": btok, "ctok": ctok, "cfm": cfm, "xg": xg, "xw": xw,
                      "zsil": zsil, "ygt": ygt}
            ctx.dma("gq", o_dbgb, srcmap[_DBGB][:, 0:2048], "dbg2")
            ctx.wait(nc.gpsimd, "dbg2")
            ctx.bar()
        ctx.dma("sq", o_yg.rearrange("(j q) t -> q j t", j=4)[:, :, gts:gts + 512],
                ygt[:].rearrange("q (j t) -> q j t", j=4), "st")
        ctx.wait(nc.sync, "st")
        ctx.bar()
    return nc


def emit_p6(ctx, EX):
    nc = ctx.nc
    ygq = EX["ygq"]
    xq = EX["x4q"]
    rows4 = EX["rows4"]
    wout = EX["wout"]
    wg = EX["wg"]
    wu = EX["wu"]
    wd = EX["wd"]
    n2w8 = EX["n2w8"]
    o_x5 = EX["o_x5"]

    xsb = ctx.sb("xsb6", [128, 8 * TOK], F32)
    yin2 = ctx.sb("yin2", [128, 8 * 512], F32)
    tbuf = ctx.sb("tbuf6", [128, 20 * 512], BF16)
    pool = ctx.sb("pool6", [128, 16384], BF16)
    ygt = ctx.sb("ygt6", [128, 16 * 512], BF16)
    wslab = ctx.sb("wslab6", [128, 2 * 4096], BF16)
    scr = ctx.sb("scr6", [128, 512], F32)
    scr6b = ctx.sb("scr6b", [128, 1024], F32)
    gtile = ctx.sb("gtile", [128, 1024], BF16)
    w8a = ctx.sb("w8a6", [128, 8], F32)
    r_r1 = ctx.sb("r_r16", [1, TOK], F32)
    r_mr1n = ctx.sb("r_mr1n6", [1, TOK], BF16)
    t_tmp = ctx.sb("t_tmp6", [1, 512], F32)
    t_r = ctx.sb("t_r6", [1, 512], F32)
    pbig = ctx.pbig
    pbig2 = ctx.pbig2

    ctx.dma("sq", xsb[:].rearrange("p (f t) -> p f t", f=8),
            xq.rearrange("(f p) t -> p f t", p=128), "ld")
    ctx.dma("sq", r_r1[0:1, :], rows4[2:3, :], "ld")
    ctx.dma("gq", r_mr1n[0:1, :], rows4[3:4, :], "gl")
    ctx.dma("sq", w8a[:], n2w8, "ld")
    ctx.wait(nc.sync, "ld")
    ctx.wait(nc.gpsimd, "gl")
    ctx.bar()
    nc.vector.tensor_scalar_mul(r_mr1n[0:1, :], r_mr1n[0:1, :], -1.0)
    ctx.bar()

    def stream_w(ap, job, kchunks, buf, off2=0):
        ctx.dma("gq", wslab[:, buf * 4096 + off2: buf * 4096 + off2 + kchunks * 128],
                ap[job * 128:(job + 1) * 128, :], "wl%d" % buf)
        return ctx.cnt["wl%d" % buf]

    xn5 = pool[:, 0:16384]
    for tt in range(TOK // 512):
        ts = slice(tt * 512, (tt + 1) * 512)
        ctx.dma("sq", ygt[:].rearrange("p (kc t) -> p kc t", kc=16),
                ygq[:, ts].rearrange("(kc p) t -> p kc t", p=128), "yl")
        ctx.wait(nc.tensor, "yl")
        m = nc.tensor.matmul(pbig2[:, 1024:1536], ctx.c("c_onesrow_f")[0:1, :],
                             r_r1[0:1, ts], start=True, stop=True)
        def ld_mo(fc, buf):
            return stream_w(wout, fc, 16, buf)

        def mm_mo(fc, buf):
            bo = buf * 4096
            pb = pbig[:, (fc % 2) * 512:(fc % 2) * 512 + 512]
            ctx.lagwait(nc.tensor, "ev6", 1)
            for kc in range(16):
                nc.tensor.matmul(pb, wslab[:, bo + kc * 128: bo + (kc + 1) * 128],
                                 ygt[:, kc * 512:(kc + 1) * 512],
                                 start=(kc == 0), stop=False)
            mmm = nc.tensor.matmul(pb, ctx.c("c_onesrow_b")[0:1, :], r_mr1n[0:1, ts],
                                   start=False, stop=True)
            ctx.tag(mmm, "mo")
            nc.vector.wait_ge(ctx.sem("mo"), ctx.cnt["mo"])
            off = fc * TOK + tt * 512
            nc.vector.tensor_tensor(scr6b[:, (fc % 2) * 512:(fc % 2) * 512 + 512],
                                    xsb[:, off:off + 512], pbig2[:, 1024:1536], OP.mult)
            vv = nc.vector.tensor_add(yin2[:, fc * 512:(fc + 1) * 512],
                                      scr6b[:, (fc % 2) * 512:(fc % 2) * 512 + 512], pb)
            ctx.tag(vv, "ev6")

        _mm_stream(ctx, list(range(8)), ld_mo, mm_mo, "mo")
        ctx.bar()
        _emit_ln_sums(ctx, yin2, 8, 512, scr, pbig2[0:1, 0:512], pbig2[0:1, 512:1024],
                      slice(0, 512))
        nc.vector.wait_ge(ctx.sem("mm_ev"), ctx.cnt["mm_ev"])
        lv = nc.vector.tensor_scalar_mul(t_tmp[0:1, :], pbig2[0:1, 512:1024], 1.0 / D)
        ctx.tag(lv, "gv")
        nc.scalar.wait_ge(ctx.sem("gv"), ctx.cnt["gv"])
        nc.tensor.wait_ge(ctx.sem("gv"), ctx.cnt["gv"])
        la = nc.scalar.activation(t_tmp[0:1, :], t_tmp[0:1, :], AF.Sqrt,
                                  bias=ctx.cc(1e-5)[0:1, :], scale=1.0)
        ctx.tag(la, "gs")
        nc.vector.wait_ge(ctx.sem("gs"), ctx.cnt["gs"])
        lv = nc.vector.reciprocal(t_r[0:1, :], t_tmp[0:1, :])
        ctx.tag(lv, "gv")
        nc.tensor.wait_ge(ctx.sem("gv"), ctx.cnt["gv"])
        m = nc.tensor.matmul(pbig2[:, 1024:1536], ctx.c("c_onesrow_f")[0:1, :],
                             t_r[0:1, :], start=True, stop=True)
        ctx.tag(m, "bc6")
        nc.vector.wait_ge(ctx.sem("bc6"), ctx.cnt["bc6"])
        last = None
        for fc in range(8):
            off = fc * TOK + tt * 512
            yo = fc * 512
            nc.vector.tensor_tensor(scr[:, :], yin2[:, yo:yo + 512],
                                    pbig2[:, 1024:1536], OP.mult)
            nc.vector.tensor_scalar(xn5[:, off:off + 512], scr[:, :],
                                    w8a[:, fc:fc + 1], None, OP.mult)
            last = nc.vector.tensor_add(xsb[:, off:off + 512], xsb[:, off:off + 512],
                                        yin2[:, yo:yo + 512])
        ctx.tag(last, "x26")
        nc.tensor.wait_ge(ctx.sem("x26"), ctx.cnt["x26"])
        nc.scalar.wait_ge(ctx.sem("x26"), ctx.cnt["x26"])
    ctx.bar()

    for tt in range(TOK // 512):
        def ld_gu(hk, buf):
            stream_w(wg, hk, 8, buf)
            return stream_w(wu, hk, 8, buf, off2=1024)

        def mm_gu(hk, buf):
            bo = buf * 4096
            pb = pbig[:, (hk % 2) * 512:(hk % 2) * 512 + 512]
            pb2 = pbig[:, 1024 + (hk % 2) * 512: 1024 + (hk % 2) * 512 + 512]
            ctx.lagwait(nc.tensor, "g4", 1)
            last = None
            for kc in range(8):
                last = nc.tensor.matmul(pb, wslab[:, bo + kc * 128: bo + (kc + 1) * 128],
                                        xn5[:, kc * TOK + tt * 512: kc * TOK + (tt + 1) * 512],
                                        start=(kc == 0), stop=(kc == 7))
            ctx.tag(last, "g1")
            last = None
            for kc in range(8):
                last = nc.tensor.matmul(pb2, wslab[:, bo + 1024 + kc * 128: bo + 1024 + (kc + 1) * 128],
                                        xn5[:, kc * TOK + tt * 512: kc * TOK + (tt + 1) * 512],
                                        start=(kc == 0), stop=(kc == 7))
            ctx.tag(last, "g2")
            nc.scalar.wait_ge(ctx.sem("g1"), ctx.cnt["g1"])
            aa = nc.scalar.activation(gtile[:, (hk % 2) * 512:(hk % 2) * 512 + 512], pb,
                                      AF.Silu, bias=0.0, scale=1.0)
            ctx.tag(aa, "g3")
            nc.vector.wait_ge(ctx.sem("g3"), ctx.cnt["g3"])
            nc.vector.wait_ge(ctx.sem("g2"), ctx.cnt["g2"])
            vv = nc.vector.tensor_tensor(tbuf[:, hk * 512:(hk + 1) * 512], pb2,
                                         gtile[:, (hk % 2) * 512:(hk % 2) * 512 + 512], OP.mult)
            ctx.tag(vv, "g4")

        _mm_stream(ctx, list(range(20)), ld_gu, mm_gu, "g4")
        ctx.bar()
        nc.tensor.wait_ge(ctx.sem("g4"), ctx.cnt["g4"])

        def ld_wd(fc, buf):
            return stream_w(wd, fc, 20, buf)

        def mm_wd(fc, buf):
            bo = buf * 4096
            pb = pbig[:, (fc % 2) * 512:(fc % 2) * 512 + 512]
            ctx.lagwait(nc.tensor, "g6", 1)
            last = None
            for kc in range(20):
                last = nc.tensor.matmul(pb, wslab[:, bo + kc * 128: bo + (kc + 1) * 128],
                                        tbuf[:, kc * 512:(kc + 1) * 512],
                                        start=(kc == 0), stop=(kc == 19))
            ctx.tag(last, "g5")
            nc.vector.wait_ge(ctx.sem("g5"), ctx.cnt["g5"])
            off = fc * TOK + tt * 512
            vv = nc.vector.tensor_add(xsb[:, off:off + 512], xsb[:, off:off + 512], pb)
            ctx.tag(vv, "g6")

        _mm_stream(ctx, list(range(8)), ld_wd, mm_wd, "g5")
        ctx.bar()
    ctx.dma("sq", o_x5.rearrange("(f p) t -> p f t", p=128),
            xsb[:].rearrange("p (f t) -> p f t", f=8), "st")
    ctx.wait(nc.sync, "st")
    ctx.bar()
    return nc


def emit_p7(ctx, EX):
    nc = ctx.nc
    absp = EX["absp_all"]
    wup_sl = EX["wupT_sl"]
    wdn_sl = EX["wdnT_sl"]
    o_up = EX["o_wtup"]
    o_dn = EX["o_wtdn"]

    ab = ctx.sb("ab", [8, 2], F32)
    thr = ctx.sb("thr", [1, 2], F32)
    thrc = ctx.sb("thrc", [128, 2], F32)
    wld = ctx.sb("wld7", [128, 1024], F32)
    aw = ctx.sb("aw7", [128, 1024], F32)
    sg = ctx.sb("sg7", [128, 1024], F32)
    msk = ctx.sb("msk7", [128, 1024], F32)
    wt = ctx.sb("wt7", [128, 1024], BF16)
    ps = ctx.pbig[0:128, 0:512]

    ctx.dma("sq", ab[:], absp, "ld")
    ctx.wait(nc.sync, "ld")
    ctx.bar()
    nc.vector.tensor_reduce(thr[0:1, 0:2].rearrange("p (a b) -> p a b", a=2) if False
                            else thr[0:1, 0:2], ab[0:1, 0:2], AX, OP.add) if False else None
    m = nc.tensor.matmul(ps[0:1, 0:2], ctx.c("c_onescol_f")[0:8, 0:1], ab[:, :],
                         start=True, stop=True)
    ctx.tag(m, "s7")
    nc.vector.wait_ge(ctx.sem("s7"), ctx.cnt["s7"])
    nc.vector.tensor_copy(thr[0:1, 0:2], ps[0:1, 0:2])
    ctx.bar()
    nc.vector.tensor_scalar_mul(thr[0:1, 0:1], thr[0:1, 0:1], 0.5 * 0.5 / (D * DFF))
    nc.vector.tensor_scalar_mul(thr[0:1, 1:2], thr[0:1, 1:2], 0.5 * 0.5 / (D * DFF))
    ctx.bar()
    m = nc.tensor.matmul(ps[:, 0:2], ctx.c("c_onesrow_f")[0:1, :], thr[0:1, 0:2],
                         start=True, stop=True)
    ctx.tag(m, "s8")
    nc.vector.wait_ge(ctx.sem("s8"), ctx.cnt["s8"])
    nc.vector.tensor_copy(thrc[:, 0:2], ps[:, 0:2])
    ctx.bar()
    for mi, (ap, oap, col) in enumerate(
            [(wup_sl, o_up, 0), (wdn_sl, o_dn, 1)]):
        nr, ncol = ap.shape[0] // 128, ap.shape[1] // 1024
        blocks = [(ap[r * 128:(r + 1) * 128, c * 1024:(c + 1) * 1024],
                   oap[r * 128:(r + 1) * 128, c * 1024:(c + 1) * 1024])
                  for r in range(nr) for c in range(ncol)]
        cols = 1024
        for k in range(len(blocks)):
            ctx.dma("sq", wld[:, 0:cols], blocks[k][0], "wl")
            ctx.wait(nc.scalar, "wl")
            ctx.wait(nc.vector, "wl")
            a1 = nc.scalar.activation(aw[:, 0:cols], wld[:, 0:cols], AF.Abs, bias=0.0, scale=1.0)
            a2 = nc.scalar.activation(sg[:, 0:cols], wld[:, 0:cols], AF.Sign, bias=0.0, scale=1.0)
            ctx.tag(a2, "t7")
            nc.vector.wait_ge(ctx.sem("t7"), ctx.cnt["t7"])
            nc.vector.tensor_scalar(msk[:, 0:cols], aw[:, 0:cols], thrc[:, col:col + 1],
                                    None, OP.is_gt)
            vv = nc.vector.tensor_tensor(wt[:, 0:cols], msk[:, 0:cols], sg[:, 0:cols], OP.mult)
            ctx.tag(vv, "t8")
            nc.sync.wait_ge(ctx.sem("t8"), ctx.cnt["t8"])
            nc.scalar.wait_ge(ctx.sem("t8"), ctx.cnt["t8"])
            ctx.dma("sq", blocks[k][1], wt[:, 0:cols], "st")
    ctx.wait(nc.sync, "st")
    ctx.bar()
    return nc


def emit_p8(ctx, EX):
    nc = ctx.nc
    xq = EX["x5q"]
    wtup = EX["wtupT"]
    wtdn = EX["wtdnT"]
    o_out = EX["o_out"]

    xsb = ctx.sb("xsb8", [128, 8 * TOK], F32)
    hbuf = ctx.sb("hbuf8", [128, 32 * 512], BF16)
    pool = ctx.sb("pool8", [128, 16384], BF16)
    wslab = ctx.sb("wslab8", [128, 2 * 4096], BF16)
    scr = ctx.sb("scr8", [128, 512], F32)
    cwrow = ctx.sb("cwrow8", [1, 32 * 128], BF16)
    cwcol = ctx.sb("cwcol8", [128, 32], BF16)
    r_sd = ctx.sb("r_sd", [1, TOK], F32)
    r_msdn = ctx.sb("r_msdn", [1, TOK], BF16)
    t_mean = ctx.sb("t_mean8", [1, 512], F32)
    t_var = ctx.sb("t_var8", [1, 512], F32)
    t_tmp = ctx.sb("t_tmp8", [1, 512], F32)
    t_r = ctx.sb("t_r8", [1, 512], F32)
    pbig = ctx.pbig
    pbig2 = ctx.pbig2
    ptb = ctx.ptb

    ctx.dma("sq", xsb[:].rearrange("p (f t) -> p f t", f=8),
            xq.rearrange("(f p) t -> p f t", p=128), "ld")
    ctx.wait(nc.sync, "ld")
    ctx.bar()

    def stream_w(ap, job, width, buf):
        ctx.dma("gq", wslab[:, buf * 4096: buf * 4096 + width],
                ap[job * 128:(job + 1) * 128, :], "wl%d" % buf)
        return ctx.cnt["wl%d" % buf]

    xnd = pool[:, 0:16384]
    for tt in range(TOK // 512):
        ts = slice(tt * 512, (tt + 1) * 512)
        nc.scalar.wait_ge(ctx.sem("xn8"), ctx.cnt.get("xn8", 0))
        _emit_ln_sums(ctx, xsb, 8, TOK, scr, pbig2[0:1, 0:512], pbig2[0:1, 512:1024], ts)
        rO = {"mean": t_mean, "var": t_var, "r1": t_r, "tmp": t_tmp, "sl": slice(0, 512)}
        _rows_chain_ln(ctx, pbig2[0:1, 0:512], pbig2[0:1, 512:1024], rO)
        nc.vector.tensor_mul(t_tmp[0:1, :], t_r[0:1, :], t_r[0:1, :])
        lv = nc.vector.tensor_mul(t_tmp[0:1, :], t_tmp[0:1, :], t_var[0:1, :])
        ctx.tag(lv, "gv")
        nc.scalar.wait_ge(ctx.sem("gv"), ctx.cnt["gv"])
        la = nc.scalar.activation(t_tmp[0:1, :], t_tmp[0:1, :], AF.Sqrt, bias=0.0, scale=1.0)
        ctx.tag(la, "gs")
        nc.vector.wait_ge(ctx.sem("gs"), ctx.cnt["gs"])
        nc.vector.tensor_scalar(t_tmp[0:1, :], t_tmp[0:1, :], 1e-8, None, OP.add)
        lv = nc.vector.reciprocal(t_tmp[0:1, :], t_tmp[0:1, :])
        ctx.tag(lv, "gv")
        nc.vector.wait_ge(ctx.sem("gv"), ctx.cnt["gv"])
        nc.vector.tensor_mul(t_tmp[0:1, :], t_tmp[0:1, :], t_r[0:1, :])
        nc.vector.tensor_copy(r_sd[0:1, ts], t_tmp[0:1, :])
        nc.vector.tensor_mul(t_tmp[0:1, :], t_tmp[0:1, :], t_mean[0:1, :])
        lv = nc.vector.tensor_scalar_mul(r_msdn[0:1, ts], t_tmp[0:1, :], -1.0)
        ctx.tag(lv, "gv")
        nc.tensor.wait_ge(ctx.sem("gv"), ctx.cnt["gv"])
        m = nc.tensor.matmul(pbig2[:, 1024:1536], ctx.c("c_onesrow_f")[0:1, :],
                             r_sd[0:1, ts], start=True, stop=True)
        ctx.tag(m, "bc8")
        nc.vector.wait_ge(ctx.sem("bc8"), ctx.cnt["bc8"])
        last = None
        for fc in range(8):
            off = fc * TOK + tt * 512
            last = nc.vector.tensor_tensor(xnd[:, off:off + 512], xsb[:, off:off + 512],
                                           pbig2[:, 1024:1536], OP.mult)
        ctx.tag(last, "xn8")
        nc.tensor.wait_ge(ctx.sem("xn8"), ctx.cnt["xn8"])
    ctx.bar()
    def ld_cs(fc, buf):
        return stream_w(wtup, fc, 1024, buf)

    def mm_cs(fc, buf):
        bo = buf * 4096
        last = None
        for kc in range(8):
            last = nc.tensor.matmul(pbig[0:128, fc % 16: fc % 16 + 1],
                                    wslab[:, bo + kc * 128: bo + (kc + 1) * 128],
                                    ctx.c("c_onescol_b")[:, 0:1],
                                    start=(kc == 0 and fc % 16 == 0), stop=(kc == 7),
                                    skip_group_check=True)
        ctx.tag(last, "cs8")
        if fc % 16 == 15:
            nc.vector.wait_ge(ctx.sem("cs8"), ctx.cnt["cs8"])
            nc.vector.tensor_copy(cwcol[:, (fc // 16) * 16:(fc // 16) * 16 + 16],
                                  pbig[:, 0:16])
            ctx.tag(nc.vector.tensor_copy(scr[0:1, 0:1], cwcol[0:1, 0:1]), "cv8")
            nc.tensor.wait_ge(ctx.sem("cv8"), ctx.cnt["cv8"])

    _mm_stream(ctx, list(range(32)), ld_cs, mm_cs, "cs8")
    ctx.bar()
    for j in range(32):
        mt = nc.tensor.transpose(ptb[0:1, 0:128], cwcol[:, j:j + 1], ctx.c("c_I")[:])
        ctx.tag(mt, "cwt")
        nc.vector.wait_ge(ctx.sem("cwt"), ctx.cnt["cwt"])
        vv = nc.vector.tensor_copy(cwrow[0:1, j * 128:(j + 1) * 128], ptb[0:1, 0:128])
        ctx.tag(vv, "cwc8")
        nc.tensor.wait_ge(ctx.sem("cwc8"), ctx.cnt["cwc8"])
    ctx.bar()

    for tt in range(TOK // 512):
        ts = slice(tt * 512, (tt + 1) * 512)
        def ld_up(hk, buf):
            return stream_w(wtup, hk, 1024, buf)

        def mm_up(hk, buf):
            bo = buf * 4096
            pb = pbig[:, (hk % 2) * 512:(hk % 2) * 512 + 512]
            ctx.lagwait(nc.tensor, "u2", 1)
            for kc in range(8):
                nc.tensor.matmul(pb, wslab[:, bo + kc * 128: bo + (kc + 1) * 128],
                                 xnd[:, kc * TOK + tt * 512: kc * TOK + (tt + 1) * 512],
                                 start=(kc == 0), stop=False)
            last = nc.tensor.matmul(pb, cwrow[0:1, hk * 128:(hk + 1) * 128],
                                    r_msdn[0:1, ts], start=False, stop=True)
            ctx.tag(last, "u1")
            nc.scalar.wait_ge(ctx.sem("u1"), ctx.cnt["u1"])
            aa = nc.scalar.activation(hbuf[:, hk * 512:(hk + 1) * 512], pb, AF.Gelu,
                                      bias=0.0, scale=1.0)
            ctx.tag(aa, "u2")

        _mm_stream(ctx, list(range(32)), ld_up, mm_up, "u1")
        nc.tensor.wait_ge(ctx.sem("u2"), ctx.cnt["u2"])
        ctx.bar()
        for i32 in range(1):
            pass
        for hk in range(32):
            aa = nc.scalar.activation(scr[:, :], hbuf[:, hk * 512:(hk + 1) * 512],
                                      AF.Square, bias=0.0, scale=1.0)
            ctx.tag(aa, "h1")
            nc.tensor.wait_ge(ctx.sem("h1"), ctx.cnt["h1"])
            last = nc.tensor.matmul(pbig2[0:1, 0:512], ctx.c("c_onescol_f")[:, 0:1],
                                    scr[:, :], start=(hk == 0), stop=(hk == 31))
            ctx.tag(last, "h2")
            nc.scalar.wait_ge(ctx.sem("h2"), ctx.cnt["h2"])
        ctx.bar()
        nc.vector.tensor_scalar_mul(t_tmp[0:1, :], pbig2[0:1, 0:512], 1.0 / DFF)
        ctx.bar()
        nc.scalar.activation(t_tmp[0:1, :], t_tmp[0:1, :], AF.Sqrt, bias=0.0, scale=1.0)
        ctx.bar()
        nc.vector.tensor_scalar(t_tmp[0:1, :], t_tmp[0:1, :], 1e-8, None, OP.add)
        ctx.bar()
        nc.vector.reciprocal(t_r[0:1, :], t_tmp[0:1, :])
        ctx.bar()
        m = nc.tensor.matmul(pbig2[:, 1024:1536], ctx.c("c_onesrow_f")[0:1, :],
                             t_r[0:1, :], start=True, stop=True)
        ctx.tag(m, "h3")
        nc.vector.wait_ge(ctx.sem("h3"), ctx.cnt["h3"])
        last = None
        for hk in range(32):
            last = nc.vector.tensor_tensor(hbuf[:, hk * 512:(hk + 1) * 512],
                                           hbuf[:, hk * 512:(hk + 1) * 512],
                                           pbig2[:, 1024:1536], OP.mult)
        ctx.tag(last, "h4")
        nc.tensor.wait_ge(ctx.sem("h4"), ctx.cnt["h4"])
        ctx.bar()
        def ld_dn(fc, buf):
            return stream_w(wtdn, fc, 4096, buf)

        def mm_dn(fc, buf):
            bo = buf * 4096
            pb = pbig[:, (fc % 2) * 512:(fc % 2) * 512 + 512]
            ctx.lagwait(nc.tensor, "d2", 1)
            last = None
            for kc in range(32):
                last = nc.tensor.matmul(pb, wslab[:, bo + kc * 128: bo + (kc + 1) * 128],
                                        hbuf[:, kc * 512:(kc + 1) * 512],
                                        start=(kc == 0), stop=(kc == 31))
            ctx.tag(last, "d1")
            nc.vector.wait_ge(ctx.sem("d1"), ctx.cnt["d1"])
            off = fc * TOK + tt * 512
            vv = nc.vector.tensor_add(xsb[:, off:off + 512], xsb[:, off:off + 512], pb)
            ctx.tag(vv, "d2")

        _mm_stream(ctx, list(range(8)), ld_dn, mm_dn, "d1")
        ctx.bar()
    ctx.dma("sq", o_out.rearrange("(f p) t -> p f t", p=128),
            xsb[:].rearrange("p (f t) -> p f t", f=8), "st")
    ctx.wait(nc.sync, "st")
    ctx.bar()
    return nc

_EXT_SHAPES = {
    "xq": ([D, TOK], F32), "ln1w8": ([128, 8], F32), "ln2w8": ([128, 8], F32),
    "n1w8": ([128, 8], F32), "n2w8": ([128, 8], F32), "scb8": ([128, 8], F32),
    "bias8": ([128, 8], F32), "wq4": ([D, 256], F32), "wk4": ([D, 256], F32),
    "wv4": ([D, 256], F32), "scale4": ([4, 1], F32), "scale22": ([2, 2], F32),
    "scale4r": ([1, 4], F32), "relb4": ([NB, 4], F32), "wo": ([8 * 128, D], F32),
    "w1": ([32 * 128, D], F32), "w2": ([8 * 128, FFN], F32), "cbt": ([8 * 128, D], F32),
    "win_z": ([D, 512], F32), "win_x": ([D, 512], F32), "win_B": ([D, 512], F32),
    "win_C": ([D, 512], F32), "win_dt": ([D, 8], F32), "win_th": ([D, 256], F32),
    "alog8": ([8, 1], F32), "dtb8": ([8, 1], F32), "d8": ([1, 8], F32),
    "bnw": ([1, 64], F32), "cnw": ([1, 64], F32), "bbias": ([1, 512], F32),
    "cbias": ([1, 512], F32), "wout": ([8 * 128, 2048], F32), "wg": ([20 * 128, D], F32),
    "wu": ([20 * 128, D], F32), "wd": ([8 * 128, DMLP], F32),
    "wupT_sl": ([D, 1024], F32), "wdnT_sl": ([256, DFF], F32),
}
_ALL_CONSTS = ["c_sel", "c_maskrow", "c_J", "c_I", "c_If", "c_onescol_f",
               "c_onescol_b", "c_onesrow_f", "c_onesrow_b", "c_sel64", "c_exp2",
               "c_tril", "c_ramp", "c_mask8", "c_aexp"]


def build_fused():
    import os
    _UPTO = int(os.environ.get("FUSE_UPTO", "99"))
    nc = bass.Bass("TRN2", target_bir_lowering=False, debug=False,
                   detect_race_conections=False) if False else bass.Bass(
        "TRN2", target_bir_lowering=False, debug=False, detect_race_conditions=False)
    ctx = Ctx(nc)
    E = {}
    for n, (sh, dt) in _EXT_SHAPES.items():
        E[n] = nc.dram_tensor(n, sh, dt, kind="ExternalInput").ap()
    o_out = nc.dram_tensor("o_out", [D, TOK], F32, kind="ExternalOutput").ap()

    ti = lambda n, sh, dt: nc.dram_tensor(n, sh, dt, kind="Internal")
    s_ag1 = ti("s_ag1", [1032, TOK], BF16)
    g_ag1 = ti("g_ag1", [4 * 1032, TOK], BF16)
    s_absp = ti("s_absp", [1, 2], F32)
    g_absp = ti("g_absp", [8, 2], F32)
    s_rows1 = ti("s_rows1", [4, TOK], F32)
    s_wtup = ti("s_wtup", [D, 1024], BF16)
    s_wtdn = ti("s_wtdn", [256, DFF], BF16)
    g_wtup = ti("g_wtup", [4 * D, 1024], BF16)
    g_wtdn = ti("g_wtdn", [4 * 256, DFF], BF16)
    s_y = ti("s_y", [256, S], BF16)
    g_y = ti("g_y", [1024, S], BF16)
    s_yq = ti("s_yq", [1024, TOK], BF16)
    s_x4 = ti("s_x4", [D, TOK], F32)
    s_rows4 = ti("s_rows4", [4, TOK], F32)
    s_ag4 = ti("s_ag4", [1032, TOK], BF16)
    g_ag4 = ti("g_ag4", [4 * 1032, TOK], BF16)
    s_yg = ti("s_yg", [512, S], BF16)
    g_yg = ti("g_yg", [2048, S], BF16)
    s_ygq = ti("s_ygq", [2048, TOK], BF16)
    s_x5 = ti("s_x5", [D, TOK], F32)
    s_dbga = ti("s_dbga", [88, 512], F32)
    s_dbgb = ti("s_dbgb", [128, 2048], F32)

    ctx.load_consts(_ALL_CONSTS)
    ctx.prep_consts([1e-5, 1e-6, 1e-8])
    ctx.pbig = nc.alloc_psum_tensor("pbig", [128, 2048], F32)
    ctx.pbig2 = nc.alloc_psum_tensor("pbig2", [128, 1536], F32)
    ctx.ptb = nc.alloc_psum_tensor("ptb", [128, 256], BF16)
    ctx.bar()
    ctx.make_pool()
    QUAD = [[0, 1, 2, 3], [4, 5, 6, 7]]
    GLOB = [list(range(8))]
    ncoll = [0]

    def coll(kind, groups, in_ap, out_ap):
        ncoll[0] += 1
        cc = ctx.sem(f"cc{ncoll[0]}")
        nc.gpsimd.collective_compute(kind, OP.bypass, replica_groups=groups,
                                     ins=[in_ap], outs=[out_ap]).then_inc(cc)
        nc.gpsimd.wait_ge(cc, 1)
        ctx.bar()

    ag1v = g_ag1.ap().rearrange("(r d) t -> r d t", d=1032)

    ctx.pool_reset()
    emit_p1(ctx, {"xq": E["xq"], "ln1w8": E["ln1w8"], "wupT_sl": E["wupT_sl"],
                  "wdnT_sl": E["wdnT_sl"], "o_xn1": s_ag1.ap()[0:1024, :],
                  "o_rows": s_rows1.ap(), "o_absp": s_absp.ap()})
    ctx.bar()
    ctx.dma("gq", s_ag1.ap()[1024:1025, :], s_rows1.ap()[1:2, :], "gms")
    ctx.wait(nc.gpsimd, "gms")
    ctx.bar()
    coll("AllGather", QUAD, s_ag1.ap(), g_ag1.ap())
    coll("AllGather", GLOB, s_absp.ap(), g_absp.ap())
    if _UPTO <= 1:
        ctx.dma("sq", o_out.rearrange("(f p) t -> p f t", p=128),
                ctx.pool_t[:, 0:16 * TOK].bitcast(F32).rearrange("p (f t) -> p f t", f=8), "fin")
        ctx.wait(nc.sync, "fin")
        ctx.bar()
        return nc

    ctx.pool_reset()
    emit_p7(ctx, {"absp_all": g_absp.ap(), "wupT_sl": E["wupT_sl"],
                  "wdnT_sl": E["wdnT_sl"], "o_wtup": s_wtup.ap(),
                  "o_wtdn": s_wtdn.ap()})
    ctx.bar()
    coll("AllGather", QUAD, s_wtup.ap(), g_wtup.ap())
    coll("AllGather", QUAD, s_wtdn.ap(), g_wtdn.ap())
    if _UPTO == 15:
        ctx.dma("sq", o_out.rearrange("(f p) t -> p f t", p=128),
                ctx.pool_t[:, 0:16 * TOK].bitcast(F32).rearrange("p (f t) -> p f t", f=8), "fin")
        ctx.wait(nc.sync, "fin")
        ctx.bar()
        return nc

    ctx.pool_reset()
    emit_p2(ctx, {"xn1f": ag1v[:, 0:1024, :], "msf4": ag1v[:, 1024:1025, :],
                  "wq4": E["wq4"], "wk4": E["wk4"], "wv4": E["wv4"],
                  "ln1w8": E["ln1w8"], "scale4": E["scale4"],
                  "scale22": E["scale22"], "scale4r": E["scale4r"],
                  "relb4": E["relb4"], "o_y": s_y.ap()})
    ctx.bar()
    coll("AllGather", QUAD, s_y.ap(), g_y.ap())
    pid = nc.gpsimd.partition_id()
    qoff = (pid % 4) * TOK
    src = bass.AP(g_y.ap().tensor, qoff, [[S, 1024], [1, TOK]])
    ctx.dma("gq", s_yq.ap(), src, "gsl")
    ctx.wait(nc.gpsimd, "gsl")
    ctx.bar()
    if _UPTO <= 2:
        ctx.dma("sq", o_out.rearrange("(f p) t -> p f t", p=128),
                ctx.pool_t[:, 0:16 * TOK].bitcast(F32).rearrange("p (f t) -> p f t", f=8), "fin")
        ctx.wait(nc.sync, "fin")
        ctx.bar()
        return nc

    ctx.pool_reset()
    emit_p3(ctx, {"yq": s_yq.ap(), "xq": E["xq"], "rows1": s_rows1.ap(),
                  "wo": E["wo"], "w1": E["w1"], "w2": E["w2"], "cbt": E["cbt"],
                  "ln2w8": E["ln2w8"], "scb8": E["scb8"], "bias8": E["bias8"],
                  "o_x4": s_x4.ap()})
    ctx.bar()
    if _UPTO <= 3:
        ctx.dma("sq", o_out.rearrange("(f p) t -> p f t", p=128),
                ctx.pool_t[:, 0:16 * TOK].bitcast(F32).rearrange("p (f t) -> p f t", f=8), "fin")
        ctx.wait(nc.sync, "fin")
        ctx.bar()
        return nc

    ctx.pool_reset()
    emit_p4(ctx, {"x4q": s_x4.ap(), "n1w8": E["n1w8"],
                  "o_xn4": s_ag4.ap()[0:1024, :], "o_rows4": s_rows4.ap()})
    ctx.bar()
    ctx.dma("gq", s_ag4.ap()[1024:1025, :], s_rows4.ap()[1:2, :], "gms")
    ctx.wait(nc.gpsimd, "gms")
    ctx.bar()
    coll("AllGather", QUAD, s_ag4.ap(), g_ag4.ap())
    if _UPTO <= 4:
        ctx.dma("sq", o_out.rearrange("(f p) t -> p f t", p=128),
                ctx.pool_t[:, 0:16 * TOK].bitcast(F32).rearrange("p (f t) -> p f t", f=8), "fin")
        ctx.wait(nc.sync, "fin")
        ctx.bar()
        return nc
    ag4v = g_ag4.ap().rearrange("(r d) t -> r d t", d=1032)

    ctx.pool_reset()
    emit_p5(ctx, {"xn4f": ag4v[:, 0:1024, :], "mscf4": ag4v[:, 1024:1025, :],
                  "win_z": E["win_z"], "win_x": E["win_x"], "win_B": E["win_B"],
                  "win_C": E["win_C"], "win_dt": E["win_dt"], "win_th": E["win_th"],
                  "n1w8": E["n1w8"], "alog8": E["alog8"], "dtb8": E["dtb8"],
                  "d8": E["d8"], "bnw": E["bnw"], "cnw": E["cnw"],
                  "bbias": E["bbias"], "cbias": E["cbias"], "o_yg": s_yg.ap(),
                  "o_dbga": s_dbga.ap(), "o_dbgb": s_dbgb.ap()})
    ctx.bar()
    coll("AllGather", QUAD, s_yg.ap(), g_yg.ap())
    src = bass.AP(g_yg.ap().tensor, qoff, [[S, 2048], [1, TOK]])
    ctx.dma("gq", s_ygq.ap(), src, "gsl")
    ctx.wait(nc.gpsimd, "gsl")
    ctx.bar()
    if _UPTO == 55:
        ctx.dma("sq", o_out.rearrange("a b -> (a b)"),
                s_ygq.ap().bitcast(F32).rearrange("a b -> (a b)"), "fin")
        ctx.wait(nc.sync, "fin")
        ctx.bar()
        return nc
    if _UPTO <= 5:
        ctx.dma("sq", o_out.rearrange("(f p) t -> p f t", p=128),
                ctx.pool_t[:, 0:16 * TOK].bitcast(F32).rearrange("p (f t) -> p f t", f=8), "fin")
        ctx.wait(nc.sync, "fin")
        ctx.bar()
        return nc

    ctx.pool_reset()
    emit_p6(ctx, {"ygq": s_ygq.ap(), "x4q": s_x4.ap(), "rows4": s_rows4.ap(),
                  "wout": E["wout"], "wg": E["wg"], "wu": E["wu"], "wd": E["wd"],
                  "n2w8": E["n2w8"], "o_x5": s_x5.ap()})
    ctx.bar()
    if _UPTO <= 6:
        ctx.dma("sq", o_out, s_x5.ap(), "fin")
        ctx.wait(nc.sync, "fin")
        ctx.bar()
        return nc

    ctx.pool_reset()
    emit_p8(ctx, {"x5q": s_x5.ap(),
                  "wtupT": g_wtup.ap(),
                  "wtdnT": g_wtdn.ap(),
                  "o_out": o_out})
    ctx.bar()
    return nc


def kernel(**inputs):
    maps = _stage_inputs(inputs)
    nc = build_fused()
    keys = list(_EXT_SHAPES) + _ALL_CONSTS
    ins = [{k: m[k] for k in keys} for m in maps]
    res = run_bass_kernel_spmd(nc, ins, core_ids=list(range(NCORES)))
    out = np.empty((B, S, D), np.float32)
    for c in range(NCORES):
        b, q = c // 4, c % 4
        out[b, TOK * q: TOK * (q + 1), :] = np.asarray(res.results[c]["o_out"]).T
    return out


# version 11
# speedup vs baseline: 1.0024x; 1.0024x over previous
import math
import numpy as np
import ml_dtypes
import concourse.bass as bass
import concourse.mybir as mybir
from concourse.alu_op_type import AluOpType
from concourse.bass_utils import run_bass_kernel_spmd

F32 = mybir.dt.float32
BF16 = mybir.dt.bfloat16
I32 = mybir.dt.int32
AF = mybir.ActivationFunctionType
OP = AluOpType
AX = mybir.AxisListType.X
BF = ml_dtypes.bfloat16

B = 2
S = 8192
D = 1024
TOK = 2048
NCORES = 8
HEADS = 16
HD = 64
AH = 4
FFN = 4096
NH = 32
SH = 8
P_ = 64
NST = 64
CH = 64
DMLP = 2560
DFF = 4096
NB = 32
EVLEN = 1152
NEAR_D0 = [-384, -256, -128, 0, 128]

_REL_PERM = np.array([i * 2 for i in range(32)] + [i * 2 + 1 for i in range(32)])


def _bucket_of(d):
    if d < 16:
        return d
    v = np.float32(np.log(np.float32(d) / np.float32(16.0)))
    v = v / np.float32(math.log(128 / 16)) * np.float32(16)
    b = 16 + int(v)
    return min(b, 31)


def _host_constants():
    C = {}
    sel = np.zeros((NB, EVLEN), np.float32)
    for idx in range(EVLEN):
        d = idx - 511
        if d >= 1:
            sel[_bucket_of(d), idx] = 1.0
    C["c_sel"] = sel.astype(BF)
    mask = np.zeros((1, EVLEN), np.float32)
    mask[0, : 511 + 1] = -30000.0
    C["c_maskrow"] = mask.astype(BF)
    C["c_J"] = np.eye(128, dtype=BF)[::-1].copy()
    C["c_I"] = np.eye(128, dtype=BF)
    C["c_If"] = np.eye(128, dtype=np.float32)
    C["c_onescol_f"] = np.ones((128, 1), np.float32)
    C["c_onescol_b"] = np.ones((128, 1), BF)
    C["c_onesrow_f"] = np.ones((1, 128), np.float32)
    C["c_onesrow_b"] = np.ones((1, 128), BF)
    s2 = np.zeros((128, 2), BF)
    s2[0:64, 0] = 1
    s2[64:128, 1] = 1
    C["c_sel64"] = s2
    e2 = np.zeros((2, 128), BF)
    e2[0, 0:64] = 1
    e2[1, 64:128] = 1
    C["c_exp2"] = e2
    e8 = np.zeros((4, 8, 128), BF)
    for j in range(4):
        e8[j, 2 * j, 0:64] = 1
        e8[j, 2 * j + 1, 64:128] = 1
    C["c_exppair"] = e8.reshape(32, 128)
    eA = np.zeros((8, 128), BF)
    eB = np.zeros((8, 128), BF)
    for p in range(128):
        eA[p // 32, p] = 1
        eB[4 + p // 32, p] = 1
    C["c_exp8A"] = eA
    C["c_exp8B"] = eB
    C["c_tril"] = np.triu(np.ones((128, 128), np.float32))
    C["c_ramp"] = (64.0 - (np.arange(512) % 64)).astype(np.float32)[None, :]
    m8 = np.zeros((8, 256), np.float32)
    for h in range(8):
        m8[h, (h // 2) * 64:(h // 2) * 64 + 64] = 1
    C["c_mask8"] = m8
    ax = np.zeros((8, 128), BF)
    for h in range(8):
        ax[h, (h % 2) * 64:(h % 2) * 64 + 64] = 1
    C["c_aexp"] = ax
    return C


def _stage_inputs(inp):
    consts = _host_constants()
    f32 = np.float32
    AS = np.ascontiguousarray

    def vec8(v):
        return AS(np.asarray(v, f32).reshape(8, 128).T)

    x = np.asarray(inp["L0_attn_x"], f32)
    wq = np.asarray(inp["L0_attn_Wq"], f32)
    wk = np.asarray(inp["L0_attn_Wk"], f32)
    wv = np.asarray(inp["L0_attn_Wv"], f32)
    relb = np.asarray(inp["L0_attn_rel_bias"], f32)
    scale = np.asarray(inp["L0_attn_scale"], f32)
    win = np.asarray(inp["L1_attn_W_in"], f32)
    bb = np.asarray(inp["L1_attn_B_bias"], f32)[0]
    cb = np.asarray(inp["L1_attn_C_bias"], f32)[0]
    wupT = AS(np.asarray(inp["L1_ffn_w_up"], f32).T)
    wdnT = AS(np.asarray(inp["L1_ffn_w_down"], f32).T)
    cbt = AS(np.asarray(inp["L0_ffn_CB"], f32).T)
    z0, x0, b0, c0, d0, t0 = 0, 2048, 4096, 6144, 8192, 8224

    maps = []
    for c in range(NCORES):
        b = c // 4
        q = c % 4
        h4 = list(range(4 * q, 4 * q + 4))
        h8 = list(range(8 * q, 8 * q + 8))
        m = dict(consts)
        m["xq"] = AS(x[b, TOK * q: TOK * (q + 1), :].T)
        m["ln1w8"] = vec8(inp["L0_attn_ln1_w"])
        m["ln2w8"] = vec8(inp["L0_attn_ln2_w"])
        m["n1w8"] = vec8(inp["L1_attn_norm1_w"])
        m["n2w8"] = vec8(inp["L1_attn_norm2_w"])
        m["scb8"] = vec8(inp["L0_ffn_SCB"])
        m["bias8"] = vec8(inp["L0_ffn_bias"])
        cols4 = np.concatenate([np.arange(h * HD, (h + 1) * HD) for h in h4])
        m["wq4"] = AS(wq[:, cols4])
        m["wk4"] = AS(wk[:, cols4])
        m["wv4"] = AS(wv[:, cols4])
        m["scale4"] = AS(scale[h4][:, None])
        m["scale22"] = AS(scale[h4].reshape(2, 2).T)
        m["scale4r"] = AS(scale[h4][None, :])
        m["relb4"] = AS(relb[:, h4])
        def slabify(w, njobs):
            K_, M_ = w.shape
            kch = K_ // 128
            t4 = w.reshape(kch, 128, njobs, 128).transpose(2, 1, 0, 3)
            return AS(t4.reshape(njobs * 128, kch * 128))
        m["wo"] = slabify(np.asarray(inp["L0_attn_Wo"], f32), 8)
        m["w1"] = slabify(np.asarray(inp["L0_attn_W1"], f32), 32)
        m["w2"] = slabify(np.asarray(inp["L0_attn_W2"], f32), 8)
        m["cbt"] = slabify(cbt, 8)
        m["wout"] = slabify(np.asarray(inp["L1_attn_W_out"], f32), 8)
        m["wg"] = slabify(np.asarray(inp["L1_attn_Wg"], f32), 20)
        m["wu"] = slabify(np.asarray(inp["L1_attn_Wu"], f32), 20)
        m["wd"] = slabify(np.asarray(inp["L1_attn_Wd"], f32), 8)
        zc = np.concatenate([np.arange(z0 + h * P_, z0 + (h + 1) * P_) for h in h8])
        xc = np.concatenate([np.arange(x0 + h * P_, x0 + (h + 1) * P_) for h in h8])
        bcol = np.concatenate([b0 + h * NST + _REL_PERM for h in h8])
        ccol = np.concatenate([c0 + h * NST + _REL_PERM for h in h8])
        dtc = d0 + np.array(h8)
        thc = np.concatenate([np.arange(t0 + h * 32, t0 + (h + 1) * 32) for h in h8])
        m["win_z"] = AS(win[:, zc])
        m["win_x"] = AS(win[:, xc])
        m["win_B"] = AS(win[:, bcol])
        m["win_C"] = AS(win[:, ccol])
        m["win_dt"] = AS(win[:, dtc])
        m["win_th"] = AS(win[:, thc])
        m["alog8"] = AS(np.asarray(inp["L1_attn_A_log"], f32)[h8][:, None])
        m["dtb8"] = AS(np.asarray(inp["L1_attn_dt_bias"], f32)[h8][:, None])
        m["d8"] = AS(np.asarray(inp["L1_attn_D_param"], f32)[h8][None, :])
        m["bnw"] = AS(np.asarray(inp["L1_attn_Bn_w"], f32)[_REL_PERM][None, :])
        m["cnw"] = AS(np.asarray(inp["L1_attn_Cn_w"], f32)[_REL_PERM][None, :])
        m["bbias"] = AS(bb[h8][:, _REL_PERM].reshape(1, 512))
        m["cbias"] = AS(cb[h8][:, _REL_PERM].reshape(1, 512))
        wq_ = wupT[:, 1024 * q: 1024 * (q + 1)]
        m["wupT_sl"] = AS(wq_.reshape(8, 128, 8, 128).transpose(2, 1, 0, 3)
                          .reshape(1024, 1024))
        wdq_ = wdnT[:, 256 * q: 256 * (q + 1)]
        m["wdnT_sl"] = AS(wdq_.reshape(32, 128, 2, 128).transpose(2, 1, 0, 3)
                          .reshape(256, 4096))
        maps.append(m)
    return maps


class Ctx:

    def __init__(self, nc):
        self.nc = nc
        self.cnt = {}
        self.sems = {}
        self._guards = []
        self.consts = {}

    def sem(self, name):
        if name not in self.sems:
            g = self.nc.semaphore(name)
            self.sems[name] = g.__enter__()
            self._guards.append(g)
            self.cnt[name] = 0
        return self.sems[name]

    def dma(self, queue, out, in_, semname):
        s = self.sem(semname)
        eng = self.nc.sync if queue == "sq" else self.nc.gpsimd
        eng.dma_start(out=out, in_=in_).then_inc(s, 16)
        self.cnt[semname] += 16

    def tag(self, instr, semname, v=1):
        s = self.sem(semname)
        instr.then_inc(s, v)
        self.cnt[semname] += v

    def wait(self, eng, semname):
        eng.wait_ge(self.sem(semname), self.cnt[semname])

    def lagwait(self, eng, semname, lag):
        tgt = self.cnt.get(semname, 0) - lag
        if tgt > 0:
            eng.wait_ge(self.sem(semname), tgt)

    def bar(self):
        self.nc.all_engine_barrier()

    def make_pool(self, reserve=512):
        rem = self.nc.sbuf_bytes_remaining - reserve
        self.pool_els = (rem // 2) & ~15
        self.pool_t = self.nc.alloc_sbuf_tensor("ctxpool", [128, self.pool_els], BF16)
        self.pool_off = 0

    def pool_reset(self):
        self.pool_off = 0

    def sb(self, name, shape, dtype):
        p, n = shape
        els = n if dtype == BF16 else n * 2
        off = self.pool_off
        self.pool_off = (off + els + 15) & ~15
        assert self.pool_off <= self.pool_els,             f"pool overflow at {name}: {self.pool_off} > {self.pool_els}"
        v = self.pool_t[:, off:off + els]
        if dtype != BF16:
            v = v.bitcast(dtype)
        if p < 128:
            v = v[0:p, :]
        return v

    def prep_consts(self, vals):
        self.ccol = getattr(self, "ccol", {})
        for v in vals:
            if v not in self.ccol:
                tname = "cc%d" % len(self.ccol)
                tile = self.nc.alloc_sbuf_tensor(tname, [128, 1], F32)
                self.nc.vector.memset(tile[:], float(v))
                self.ccol[v] = tile

    def cc(self, v):
        return self.ccol[v][:, 0:1]

    def load_consts(self, names):
        nc = self.nc
        shp = {"c_sel": ([NB, EVLEN], BF16), "c_maskrow": ([1, EVLEN], BF16),
               "c_J": ([128, 128], BF16), "c_I": ([128, 128], BF16),
               "c_If": ([128, 128], F32),
               "c_onescol_f": ([128, 1], F32), "c_onescol_b": ([128, 1], BF16),
               "c_onesrow_f": ([1, 128], F32), "c_onesrow_b": ([1, 128], BF16),
               "c_sel64": ([128, 2], BF16), "c_exp2": ([2, 128], BF16),
               "c_exppair": ([32, 128], BF16), "c_exp8A": ([8, 128], BF16),
               "c_exp8B": ([8, 128], BF16), "c_tril": ([128, 128], F32),
               "c_ramp": ([1, 512], F32), "c_mask8": ([8, 256], F32),
               "c_aexp": ([8, 128], BF16)}
        self.cdram = getattr(self, "cdram", {})
        for n in names:
            sh, dt = shp[n]
            ap = self.nc.dram_tensor(n, sh, dt, kind="ExternalInput").ap()
            self.cdram[n] = ap
            sb = nc.alloc_sbuf_tensor("s" + n, sh, dt)
            self.dma("sq", sb[:], ap, "ldc")
            self.consts[n] = sb
        self.wait(nc.sync, "ldc")

    def c(self, name):
        return self.consts[name]


def _mm_stream(ctx, jobs, load_fn, mm_fn, tag, lag=1):
    import os
    if os.environ.get("MM_SERIAL"):
        nc = ctx.nc
        for i, j in enumerate(jobs):
            if ctx.cnt.get(tag, 0) > 0:
                ctx.wait(nc.gpsimd, tag)
            w = load_fn(j, i % 2)
            nc.tensor.wait_ge(ctx.sem("wl%d" % (i % 2)), w)
            mm_fn(j, i % 2)
        return
    """Depth-2 pipelined weight streaming: load_fn(job, buf)->wl_count issues the
    DMA(s) for `job` into slab half `buf`; mm_fn(job, buf) emits consumers and
    tags its last consumer instruction with `tag`."""
    nc = ctx.nc
    wl = {}
    for i, j in enumerate(jobs):
        if i == 0:
            ctx.lagwait(nc.gpsimd, tag, 1)
            wl[0] = load_fn(j, 0)
        if i + 1 < len(jobs):
            ctx.lagwait(nc.gpsimd, tag, 0)
            wl[i + 1] = load_fn(jobs[i + 1], (i + 1) % 2)
        nc.tensor.wait_ge(ctx.sem("wl%d" % (i % 2)), wl[i])
        mm_fn(j, i % 2)


def _rows_chain_ln(ctx, psSx, psSxx, rO, extra=None):
    nc = ctx.nc
    sl = rO["sl"]
    nc.vector.wait_ge(ctx.sem("mm_ev"), ctx.cnt.get("mm_ev", 0))
    nc.vector.tensor_scalar_mul(rO["mean"][0:1, sl], psSx[0:1, 0:512], 1.0 / D)
    lv = nc.vector.tensor_scalar_mul(rO["var"][0:1, sl], psSxx[0:1, 0:512], 1.0 / D)
    ctx.tag(lv, "gv")
    nc.tensor.wait_ge(ctx.sem("gv"), ctx.cnt["gv"])
    nc.vector.tensor_mul(rO["tmp"][0:1, sl], rO["mean"][0:1, sl], rO["mean"][0:1, sl])
    lv = nc.vector.tensor_sub(rO["var"][0:1, sl], rO["var"][0:1, sl], rO["tmp"][0:1, sl])
    ctx.tag(lv, "gv")
    nc.scalar.wait_ge(ctx.sem("gv"), ctx.cnt["gv"])
    la = nc.scalar.activation(rO["tmp"][0:1, sl], rO["var"][0:1, sl], AF.Sqrt,
                              bias=ctx.cc(1e-5)[0:1, :], scale=1.0)
    ctx.tag(la, "gs")
    nc.vector.wait_ge(ctx.sem("gs"), ctx.cnt["gs"])
    lv = nc.vector.reciprocal(rO["r1"][0:1, sl], rO["tmp"][0:1, sl])
    ctx.tag(lv, "gv")
    nc.vector.wait_ge(ctx.sem("gv"), ctx.cnt["gv"])


def _emit_ln_sums(ctx, xsb, fchunks, tok, scr, psSx, psSxx, tslice, src_off=0):
    nc = ctx.nc
    ones = ctx.c("c_onescol_f")
    for fc in range(fchunks):
        xs = xsb[:, src_off + fc * tok + tslice.start: src_off + fc * tok + tslice.stop]
        a = nc.scalar.activation(scr[:, 0:512], xs, AF.Square, bias=0.0, scale=1.0)
        ctx.tag(a, "sq_ev")
        nc.tensor.wait_ge(ctx.sem("sq_ev"), ctx.cnt["sq_ev"])
        m1 = nc.tensor.matmul(psSx[0:1, 0:512], ones[:, 0:1], xs,
                              start=(fc == 0), stop=(fc == fchunks - 1))
        m2 = nc.tensor.matmul(psSxx[0:1, 0:512], ones[:, 0:1], scr[:, 0:512],
                              start=(fc == 0), stop=(fc == fchunks - 1))
        ctx.tag(m2, "mm_ev")
        nc.scalar.wait_ge(ctx.sem("mm_ev"), ctx.cnt["mm_ev"])


def emit_p1(ctx, EX):
    nc = ctx.nc
    xq = EX["xq"]
    ln1w8 = EX["ln1w8"]
    wup_sl = EX["wupT_sl"]
    wdn_sl = EX["wdnT_sl"]
    o_xn1 = EX["o_xn1"]
    o_rows = EX["o_rows"]
    o_absp = EX["o_absp"]

    xsb = ctx.sb("xsb", [128, 8 * TOK], F32)
    scr = ctx.sb("scr", [128, 512], F32)
    w8 = ctx.sb("w8", [128, 8], F32)
    xn1 = ctx.sb("xn1", [128, 8 * TOK], BF16)
    r_mean = ctx.sb("r_mean", [1, TOK], F32)
    r_var = ctx.sb("r_var", [1, TOK], F32)
    r_r1 = ctx.sb("r_r1", [1, TOK], F32)
    r_tmp = ctx.sb("r_tmp", [1, TOK], F32)
    r_s = ctx.sb("r_s", [1, TOK], F32)
    r_ms = ctx.sb("r_ms", [1, TOK], F32)
    r_mr1 = ctx.sb("r_mr1", [1, TOK], F32)
    psSx = ctx.pbig2[0:1, 0:512]
    psSxx = ctx.pbig2[0:1, 512:1024]
    psB = ctx.pbig2[0:128, 1024:1536]

    ctx.dma("sq", xsb[:].rearrange("p (f t) -> p f t", f=8),
            xq.rearrange("(f p) t -> p f t", p=128), "ld")
    ctx.dma("sq", w8[:], ln1w8, "ld")
    ctx.wait(nc.sync, "ld")
    ctx.bar()

    rO = {"mean": r_mean, "var": r_var, "r1": r_r1, "tmp": r_tmp}
    for tt in range(TOK // 512):
        tsl = slice(tt * 512, (tt + 1) * 512)
        _emit_ln_sums(ctx, xsb, 8, TOK, scr, psSx, psSxx, tsl)
        rO["sl"] = tsl
        _rows_chain_ln(ctx, psSx, psSxx, rO)
    ctx.bar()
    nc.vector.tensor_mul(r_tmp[0:1, :], r_r1[0:1, :], r_r1[0:1, :])
    ctx.bar()
    nc.vector.tensor_mul(r_tmp[0:1, :], r_tmp[0:1, :], r_var[0:1, :])
    ctx.bar()
    nc.scalar.activation(r_tmp[0:1, :], r_tmp[0:1, :], AF.Sqrt,
                         bias=ctx.cc(1e-6)[0:1, :], scale=1.0)
    ctx.bar()
    nc.vector.reciprocal(r_s[0:1, :], r_tmp[0:1, :])
    ctx.bar()
    nc.vector.tensor_mul(r_s[0:1, :], r_s[0:1, :], r_r1[0:1, :])
    ctx.bar()
    nc.vector.tensor_mul(r_ms[0:1, :], r_s[0:1, :], r_mean[0:1, :])
    nc.vector.tensor_mul(r_mr1[0:1, :], r_r1[0:1, :], r_mean[0:1, :])
    ctx.bar()
    onesrow = ctx.c("c_onesrow_f")
    for tt in range(TOK // 512):
        tsl = slice(tt * 512, (tt + 1) * 512)
        m = nc.tensor.matmul(psB[:, 0:512], onesrow[0:1, :], r_s[0:1, tsl],
                             start=True, stop=True)
        ctx.tag(m, "bc_ev")
        nc.vector.wait_ge(ctx.sem("bc_ev"), ctx.cnt["bc_ev"])
        last = None
        for fc in range(8):
            off = fc * TOK + tt * 512
            nc.vector.tensor_tensor(scr[:, 0:512], xsb[:, off:off + 512],
                                    psB[:, 0:512], OP.mult)
            last = nc.vector.tensor_scalar(xn1[:, off:off + 512], scr[:, 0:512],
                                           w8[:, fc:fc + 1], None, OP.mult)
        ctx.tag(last, "xn_ev")
        nc.tensor.wait_ge(ctx.sem("xn_ev"), ctx.cnt["xn_ev"])
    ctx.bar()
    ctx.dma("sq", o_xn1.rearrange("(f p) t -> p f t", p=128),
            xn1[:].rearrange("p (f t) -> p f t", f=8), "st")
    ctx.dma("sq", o_rows[0:1, :], r_s[0:1, :], "st")
    ctx.dma("sq", o_rows[1:2, :], r_ms[0:1, :], "st")
    ctx.dma("sq", o_rows[2:3, :], r_r1[0:1, :], "st")
    ctx.dma("sq", o_rows[3:4, :], r_mr1[0:1, :], "st")

    wld = ctx.sb("wld", [128, 1024], F32)
    absw = ctx.sb("absw", [128, 1024], F32)
    psS2 = ctx.pbig[0:1, 0:1024]
    accr = ctx.sb("accr", [1, 4], F32)
    ones = ctx.c("c_onescol_f")
    for mi, (ap, po) in enumerate([(wup_sl, 0), (wdn_sl, 512)]):
        nr, ncol = ap.shape[0] // 128, ap.shape[1] // 1024
        blocks = [ap[r * 128:(r + 1) * 128, c * 1024:(c + 1) * 1024]
                  for r in range(nr) for c in range(ncol)]
        ntile, cols = len(blocks), 1024
        for k in range(ntile):
            ctx.dma("sq", wld[:, 0:cols], blocks[k], "wl")
            ctx.wait(nc.scalar, "wl")
            a = nc.scalar.activation(absw[:, 0:cols], wld[:, 0:cols], AF.Abs,
                                     bias=0.0, scale=1.0)
            ctx.tag(a, "ab_ev")
            nc.tensor.wait_ge(ctx.sem("ab_ev"), ctx.cnt["ab_ev"])
            last_mm = None
            for j in range(max(1, cols // 512)):
                w = min(512, cols)
                last_mm = nc.tensor.matmul(
                    psS2[0:1, po + 0:po + w], ones[:, 0:1],
                    absw[:, j * 512:j * 512 + w],
                    start=(k == 0 and j == 0), stop=(k == ntile - 1 and j == max(0, cols // 512 - 1)))
            ctx.tag(last_mm, "wm_ev")
            nc.sync.wait_ge(ctx.sem("ab_ev"), ctx.cnt["ab_ev"])
            nc.scalar.wait_ge(ctx.sem("wm_ev"), ctx.cnt["wm_ev"])
    ctx.bar()
    nc.vector.tensor_reduce(accr[0:1, 0:1], psS2[0:1, 0:512], AX, OP.add)
    nc.vector.tensor_reduce(accr[0:1, 1:2], psS2[0:1, 512:1024], AX, OP.add)
    ctx.bar()
    ctx.dma("sq", o_absp, accr[0:1, 0:2], "st")
    ctx.wait(nc.sync, "st")
    ctx.bar()
    return nc


def emit_p2(ctx, EX):
    import os
    _UPTO = int(os.environ.get("P2_UPTO", "9"))
    nc = ctx.nc
    xn1f = EX["xn1f"]
    msf4 = EX["msf4"]
    wq4 = EX["wq4"]
    wk4 = EX["wk4"]
    wv4 = EX["wv4"]
    ln1w8 = EX["ln1w8"]
    scale4 = EX["scale4"]
    scale22 = EX["scale22"]
    scale4r = EX["scale4r"]
    relb4 = EX["relb4"]
    o_y = EX["o_y"]
    ev_hbm = nc.dram_tensor("ev_hbm", [4, EVLEN], BF16, kind="Internal")


    wqb = ctx.sb("wqb", [128, 8 * 256], BF16)
    wkb = ctx.sb("wkb", [128, 8 * 256], BF16)
    wvb = ctx.sb("wvb", [128, 8 * 256], BF16)
    ln1b = ctx.sb("ln1b", [128, 8], BF16)
    ln1f = ctx.sb("ln1f", [128, 8], F32)
    pool = ctx.sb("pool", [128, 18 * 1024], BF16)
    cwsb = ctx.sb("cwsb", [128, 6], BF16)
    cwrows = ctx.sb("cwrows", [1, 6 * 128], BF16)
    xt = [pool[:, 0:4096], pool[:, 4096:8192]]
    msneg = pool[0:1, 8192:16384]
    qsb = ctx.sb("qsb", [128, 2 * S], BF16)
    ksb = ctx.sb("ksb", [128, 2 * S], BF16)
    vtok = ctx.sb("vtok", [128, 64 * 260], BF16)
    sqsb = ctx.sb("sqsb", [128, 512], BF16)
    rsqb = ctx.sb("rsqb", [2, 512], BF16)
    rsqf = ctx.sb("rsqf", [2, 512], F32)
    bcq = ctx.sb("bcq", [128, 512], BF16)
    vtmp = ctx.sb("vtmp", [128, 512], BF16)
    scl4 = ctx.sb("scl4", [4, 1], F32)
    scl22 = ctx.sb("scl22", [2, 2], F32)
    scl4r = ctx.sb("scl4r", [1, 4], F32)
    rb4 = ctx.sb("rb4", [NB, 4], F32)
    srel = ctx.sb("srel", [NB, 4], F32)
    srelb = ctx.sb("srelb", [NB, 4], BF16)
    srelT = ctx.sb("srelT", [4, NB], F32)
    mrow = ctx.sb("mrow", [4, 4], F32)
    mnegbc = ctx.sb("mnegbc", [128, 8], F32)
    rowt = ctx.sb("rowt", [1, 4], F32)
    evsb = ctx.sb("evsb", [4, EVLEN], BF16)
    l2tmp = ctx.sb("l2tmp", [128, 512], BF16)
    esb = pool[:, 0:10240]
    phat = pool[:, 10240:11776]
    phat2 = pool[:, 11776:13312]
    vmean = ctx.sb("vmean", [64, 4], F32)
    srow = ctx.sb("srow", [128, 512], F32)
    ones64 = ctx.sb("ones64", [128, 64], F32)
    bcd = ctx.sb("bcd", [64, 512], F32)
    ysb = ctx.sb("ysb", [128, S], BF16)

    pbig = ctx.pbig
    pbig2 = ctx.pbig2
    ptb = ctx.ptb

    ctx.dma("gq", wqb[:].rearrange("p (kc m) -> p kc m", kc=8),
            wq4.rearrange("(kc p) m -> p kc m", p=128), "gl")
    ctx.dma("gq", wkb[:].rearrange("p (kc m) -> p kc m", kc=8),
            wk4.rearrange("(kc p) m -> p kc m", p=128), "gl")
    ctx.dma("gq", wvb[:].rearrange("p (kc m) -> p kc m", kc=8),
            wv4.rearrange("(kc p) m -> p kc m", p=128), "gl")
    ctx.dma("sq", ln1f[:], ln1w8, "ld")
    for rr in range(4):
        ctx.dma("sq", msneg[0:1, rr * TOK:(rr + 1) * TOK], msf4[rr], "ld")
    ctx.dma("sq", scl4[:], scale4, "ld")
    ctx.dma("sq", scl22[:], scale22, "ld")
    ctx.dma("sq", scl4r[:], scale4r, "ld")
    ctx.dma("sq", rb4[:], relb4, "ld")
    ctx.wait(nc.sync, "ld")
    ctx.wait(nc.gpsimd, "gl")
    ctx.bar()
    nc.vector.memset(vtok[:], 1.0)
    nc.vector.memset(ones64[64:65, :], 1.0)
    nc.vector.tensor_copy(ln1b[:], ln1f[:])
    nc.vector.tensor_scalar_mul(msneg, msneg, -1.0)
    ctx.bar()

    for wi, wb in enumerate([wqb, wkb, wvb]):
        for mi in range(2):
            for kc in range(8):
                nc.tensor.matmul(pbig[:, wi * 2 + mi: wi * 2 + mi + 1],
                                 wb[:, kc * 256 + mi * 128: kc * 256 + (mi + 1) * 128],
                                 ln1b[:, kc:kc + 1], start=(kc == 0), stop=(kc == 7))
    ctx.bar()
    nc.vector.tensor_copy(cwsb[:, 0:6], pbig[:, 0:6])
    ctx.bar()
    for j in range(6):
        nc.tensor.transpose(ptb[0:1, 0:128], cwsb[:, j:j + 1], ctx.c("c_I")[:])
        ctx.bar()
        nc.vector.tensor_copy(cwrows[0:1, j * 128:(j + 1) * 128], ptb[0:1, 0:128])
        ctx.bar()

    for tt in range(S // 512 if _UPTO >= 2 else 0):
        xb = xt[tt % 2]
        rank = (tt * 512) // TOK
        toff = (tt * 512) % TOK
        ctx.dma("sq", xb.rearrange("p (kc t) -> p kc t", kc=8),
                xn1f[rank, :, toff:toff + 512].rearrange("(kc p) t -> p kc t", p=128),
                "xl")
        ctx.wait(nc.sync, "xl")
        ctx.bar()
        for wi, wb in enumerate([wqb, wkb, wvb]):
            for mi in range(2):
                pb = pbig[:, (mi % 2) * 512:(mi % 2) * 512 + 512]
                for kc in range(8):
                    nc.tensor.matmul(pb, wb[:, kc * 256 + mi * 128: kc * 256 + (mi + 1) * 128],
                                     xb[:, kc * 512:(kc + 1) * 512],
                                     start=(kc == 0), stop=False)
                ci6 = wi * 2 + mi
                nc.tensor.matmul(pb, cwrows[0:1, ci6 * 128:(ci6 + 1) * 128],
                                 msneg[0:1, tt * 512:(tt + 1) * 512],
                                 start=False, stop=True)
                ctx.bar()
                if wi < 2:
                    nc.scalar.activation(sqsb[:, :], pb, AF.Square, bias=0.0, scale=1.0)
                    ctx.bar()
                    nc.tensor.matmul(pbig2[0:2, 0:512], ctx.c("c_sel64")[:, :], sqsb[:, :],
                                     start=True, stop=True)
                    ctx.bar()
                    nc.scalar.activation(rsqf[0:2, :], pbig2[0:2, 0:512], AF.Sqrt,
                                         bias=ctx.cc(1e-6)[0:2, :], scale=1.0)
                    ctx.bar()
                    nc.vector.reciprocal(rsqf[0:2, :], rsqf[0:2, :])
                    ctx.bar()
                    if wi == 0:
                        nc.vector.tensor_scalar(rsqf[0:2, :], rsqf[0:2, :],
                                                scl22[:, mi:mi + 1], None, OP.mult)
                        ctx.bar()
                    nc.vector.tensor_copy(rsqb[0:2, :], rsqf[0:2, :])
                    ctx.bar()
                    nc.tensor.matmul(pbig2[:, 512:1024], ctx.c("c_exp2")[:, :], rsqb[0:2, :],
                                     start=True, stop=True)
                    ctx.bar()
                    nc.vector.tensor_copy(bcq[:, :], pbig2[:, 512:1024])
                    ctx.bar()
                    dst = (qsb if wi == 0 else ksb)
                    nc.vector.tensor_tensor(dst[:, mi * S + tt * 512: mi * S + (tt + 1) * 512],
                                            pb, bcq[:, :], OP.mult)
                    ctx.bar()
                else:
                    nc.vector.tensor_copy(vtmp[:, :], pb)
                    ctx.bar()
                    for q4 in range(4):
                        tc = tt * 4 + q4
                        nc.tensor.transpose(ptb[:, 0:128], vtmp[:, q4 * 128:(q4 + 1) * 128],
                                            ctx.c("c_I")[:])
                        ctx.bar()
                        base = tc * 260
                        nc.vector.tensor_copy(vtok[:, base + (2 * mi) * 65: base + (2 * mi) * 65 + 64],
                                              ptb[:, 0:64])
                        nc.vector.tensor_copy(vtok[:, base + (2 * mi + 1) * 65: base + (2 * mi + 1) * 65 + 64],
                                              ptb[:, 64:128])
                        ctx.bar()

    if _UPTO < 3:
        return _p2_finish(ctx, o_y, ysb)
    nc.tensor.matmul(pbig[0:NB, 0:4], ctx.c("c_onesrow_f")[0:1, 0:NB], scl4r[0:1, :],
                     start=True, stop=True)
    ctx.bar()
    nc.vector.tensor_tensor(srel[:, :], rb4[:, :], pbig[0:NB, 0:4], OP.mult)
    ctx.bar()
    nc.vector.tensor_copy(srelb[:, :], srel[:, :])
    ctx.bar()
    nc.tensor.transpose(pbig[0:4, 0:NB], srel[:, :], ctx.c("c_If")[0:NB, 0:NB])
    ctx.bar()
    nc.vector.tensor_copy(srelT[:, :], pbig[0:4, 0:NB])
    ctx.bar()
    nc.vector.tensor_reduce(mrow[0:4, 0:1], srelT[:, :], AX, OP.max)
    nc.scalar.activation(mrow[0:4, 1:2], scl4[0:4, 0:1], AF.Abs, bias=0.0, scale=1.0)
    ctx.bar()
    nc.vector.tensor_scalar_mul(mrow[0:4, 1:2], mrow[0:4, 1:2], 1.02)
    ctx.bar()
    nc.vector.tensor_add(mrow[0:4, 2:3], mrow[0:4, 0:1], mrow[0:4, 1:2])
    ctx.bar()
    nc.vector.tensor_sub(mrow[0:4, 3:4], mrow[0:4, 2:3], srelT[:, 31:32])
    ctx.bar()
    for ci, col in enumerate([2, 3]):
        nc.tensor.transpose(pbig[0:1, 0:4], mrow[0:4, col:col + 1], ctx.c("c_If")[0:4, 0:4])
        ctx.bar()
        nc.vector.tensor_scalar_mul(rowt[0:1, :], pbig[0:1, 0:4], -1.0)
        ctx.bar()
        nc.tensor.matmul(pbig[:, 0:4], ctx.c("c_onesrow_f")[0:1, :], rowt[0:1, :],
                         start=True, stop=True)
        ctx.bar()
        nc.vector.tensor_copy(mnegbc[:, ci * 4:(ci + 1) * 4], pbig[:, 0:4])
        ctx.bar()
    for j, (o, w) in enumerate([(0, 512), (512, 512), (1024, 128)]):
        nc.tensor.matmul(pbig[0:4, o:o + w], srelb[:, :], ctx.c("c_sel")[:, o:o + w],
                         start=True, stop=False)
        nc.tensor.matmul(pbig[0:4, o:o + w], ctx.c("c_onesrow_b")[0:1, 0:4],
                         ctx.c("c_maskrow")[0:1, o:o + w], start=False, stop=True)
    ctx.bar()
    nc.scalar.activation(evsb[:, :], pbig[0:4, 0:EVLEN], AF.Exp, bias=0.0, scale=1.0)
    ctx.bar()
    ctx.dma("sq", ev_hbm.ap(), evsb[:, :], "ev")
    ctx.wait(nc.sync, "ev")
    ctx.bar()
    for h in range(4):
        for ci, d0 in enumerate(NEAR_D0):
            src = bass.AP(ev_hbm.ap().tensor, h * EVLEN + d0 + 384, [[1, 128], [1, 512]])
            ctx.dma("sq", l2tmp[:], src, "el")
            ctx.wait(nc.tensor, "el")
            mm = nc.tensor.matmul(pbig[:, 0:512], ctx.c("c_J")[:], l2tmp[:], start=True, stop=True)
            ctx.tag(mm, "elm")
            nc.vector.wait_ge(ctx.sem("elm"), ctx.cnt["elm"])
            vv = nc.vector.tensor_copy(esb[:, (h * 5 + ci) * 512:(h * 5 + ci + 1) * 512],
                                       pbig[:, 0:512])
            ctx.tag(vv, "elc")
            nc.tensor.wait_ge(ctx.sem("elc"), ctx.cnt["elc"])
            nc.sync.wait_ge(ctx.sem("elc"), ctx.cnt["elc"])

    if _UPTO < 4:
        return _p2_finish(ctx, o_y, ysb)
    for tc in range(64):
        for h in range(4):
            nc.tensor.matmul(pbig[0:64, h:h + 1],
                             vtok[:, tc * 260 + h * 65: tc * 260 + h * 65 + 64],
                             ctx.c("c_onescol_b")[:, 0:1],
                             start=(tc == 0 and h == 0), stop=(tc == 63 and h == 3))
    ctx.bar()
    nc.vector.tensor_scalar_mul(vmean[:, :], pbig[0:64, 0:4], 1.0 / S)
    ctx.bar()

    if _UPTO < 5:
        return _p2_finish(ctx, o_y, ysb)
    _NQT = int(os.environ.get("P2_NQT", "16"))
    _NH = int(os.environ.get("P2_NH", "4"))
    exc = {}
    avc = {}
    for h in range(_NH):
        prow = (h % 2) * 64
        fofs = (h // 2) * S
        if h in (2, 3):
            nc.vector.wait_ge(ctx.sem("st"), ctx.cnt["st"])
        for qt in range(_NQT):
            qs = qt * 512
            far = list(range(0, max(0, 4 * qt - 1)))
            near = [kc for kc in range(max(0, 4 * qt - 1), 4 * qt + 4)]
            groups = [far[i:i + 3] for i in range(0, len(far), 3)]
            groups = [(g, False) for g in groups if g] + [(near[0:3], True)]
            if len(near) > 3:
                groups.append((near[3:], True))
            ntot = len(far) + len(near)
            nav = 0
            first_av = True
            glist = []

            def sbank(gi, bi):
                if gi % 2 == 0:
                    return pbig[:, bi * 512:(bi + 1) * 512]
                return (pbig[:, 1536:2048] if bi == 0 else
                        pbig2[:, (bi - 1) * 512:bi * 512])

            def embank(gi, bi):
                return (phat if gi % 2 == 0 else phat2)[:, bi * 512:(bi + 1) * 512]

            def emit_av(gi):
                nonlocal nav, first_av
                g, is_near = groups[gi]
                nc.tensor.wait_ge(ctx.sem("ex"), exc[gi])
                if is_near:
                    nc.tensor.wait_ge(ctx.sem("em"), emc[gi])
                last = None
                for bi, kc in enumerate(g):
                    nav += 1
                    last = nc.tensor.matmul(
                        pbig2[0:65, 1024:1536],
                        vtok[:, kc * 260 + h * 65: kc * 260 + (h + 1) * 65],
                        embank(gi, bi), start=first_av, stop=(nav == ntot),
                        skip_group_check=True)
                    first_av = False
                ctx.tag(last, "av")
                avc[gi] = ctx.cnt["av"]

            emc = {}
            prev_ex = ctx.cnt.get("ex", 0)
            prev_av = ctx.cnt.get("av", 0)
            for gi, (g, is_near) in enumerate(groups):
                nc.tensor.wait_ge(ctx.sem("ex"), exc[gi - 2] if gi >= 2 else prev_ex)
                last = None
                for bi, kc in enumerate(g):
                    last = nc.tensor.matmul(
                        sbank(gi, bi),
                        ksb[prow:prow + 64, fofs + kc * 128: fofs + (kc + 1) * 128],
                        qsb[prow:prow + 64, fofs + qs: fofs + qs + 512],
                        start=True, stop=True)
                ctx.tag(last, "qk")
                nc.scalar.wait_ge(ctx.sem("qk"), ctx.cnt["qk"])
                nc.scalar.wait_ge(ctx.sem("av"), avc[gi - 2] if gi >= 2 else prev_av)
                mcol = h if is_near else 4 + h
                la = None
                if gi % 2 == 0:
                    la = nc.scalar.activation(
                        (phat if gi % 2 == 0 else phat2)[:, 0:len(g) * 512],
                        pbig[:, 0:len(g) * 512], AF.Exp,
                        bias=mnegbc[:, mcol:mcol + 1], scale=1.0)
                else:
                    la = nc.scalar.activation(phat2[:, 0:512], pbig[:, 1536:2048],
                                              AF.Exp, bias=mnegbc[:, mcol:mcol + 1],
                                              scale=1.0)
                    if len(g) > 1:
                        la = nc.scalar.activation(phat2[:, 512:len(g) * 512],
                                                  pbig2[:, 0:(len(g) - 1) * 512],
                                                  AF.Exp, bias=mnegbc[:, mcol:mcol + 1],
                                                  scale=1.0)
                ctx.tag(la, "ex")
                exc[gi] = ctx.cnt["ex"]
                if is_near:
                    nc.vector.wait_ge(ctx.sem("ex"), exc[gi])
                    lv = None
                    for bi, kc in enumerate(g):
                        d0 = qs - kc * 128
                        ci = NEAR_D0.index(d0)
                        lv = nc.vector.tensor_tensor(
                            embank(gi, bi), embank(gi, bi),
                            esb[:, (h * 5 + ci) * 512:(h * 5 + ci + 1) * 512], OP.mult)
                    ctx.tag(lv, "em")
                    emc[gi] = ctx.cnt["em"]
                if gi >= 1:
                    emit_av(gi - 1)
            emit_av(len(groups) - 1)
            if qt == 0:
                nc.vector.wait_ge(ctx.sem("av"), ctx.cnt["av"])
                nc.vector.tensor_copy(pbig2[0:64, 1024:1025], vmean[:, h:h + 1])
                lv = nc.vector.memset(pbig2[64:65, 1024:1025], 1.0)
                ctx.tag(lv, "q0")
            nc.vector.wait_ge(ctx.sem("av"), ctx.cnt["av"])
            lv = nc.vector.reciprocal(srow[64:65, 0:512], pbig2[64:65, 1024:1536])
            ctx.tag(lv, "rc")
            nc.tensor.wait_ge(ctx.sem("rc"), ctx.cnt["rc"])
            m = nc.tensor.matmul(pbig[0:64, 0:512], ones64[64:65, 0:64],
                                 srow[64:65, 0:512], start=True, stop=True)
            ctx.tag(m, "nb")
            nc.vector.wait_ge(ctx.sem("nb"), ctx.cnt["nb"])
            nc.vector.tensor_copy(bcd[:, :], pbig[0:64, 0:512])
            lv = nc.vector.tensor_tensor(ysb[prow:prow + 64, qs: qs + 512],
                                         pbig2[0:64, 1024:1536], bcd[:, :], OP.mult)
            ctx.tag(lv, "yw")
            nc.tensor.wait_ge(ctx.sem("yw"), ctx.cnt["yw"])
        if h % 2 == 1:
            nc.sync.wait_ge(ctx.sem("yw"), ctx.cnt["yw"])
            ctx.dma("sq", o_y[(h - 1) * 64:(h + 1) * 64, :], ysb[:, :], "st")
        if h == _NH - 1:
            ctx.wait(nc.sync, "st")
    ctx.bar()
    return nc


def emit_p3(ctx, EX):
    nc = ctx.nc
    yq = EX["yq"]
    xq = EX["xq"]
    rows1 = EX["rows1"]
    wo = EX["wo"]
    w1 = EX["w1"]
    w2 = EX["w2"]
    cbt = EX["cbt"]
    ln2w8 = EX["ln2w8"]
    scb8 = EX["scb8"]
    bias8 = EX["bias8"]
    o_x4 = EX["o_x4"]

    xsb = ctx.sb("xsb", [128, 8 * TOK], F32)
    yinner = ctx.sb("yinner", [128, 8 * 512], F32)
    hbuf = ctx.sb("hbuf3", [128, 16384], BF16)
    pool = ctx.sb("pool3", [128, 16384], BF16)
    wslab = ctx.sb("wslab", [128, 2 * 4096], BF16)
    scr = ctx.sb("scr3", [128, 512], F32)
    scr3b = ctx.sb("scr3b", [128, 1024], F32)
    cwrow = ctx.sb("cwrow", [1, 8 * 128], BF16)
    cwcol = ctx.sb("cwcol", [128, 8], BF16)
    w8a = ctx.sb("w8a", [128, 8], F32)
    w8b = ctx.sb("w8b", [128, 8], F32)
    w8c = ctx.sb("w8c", [128, 8], F32)
    r_r1 = ctx.sb("r_r1", [1, TOK], BF16)
    r_mr1n = ctx.sb("r_mr1n", [1, TOK], BF16)
    r_mbrb = ctx.sb("r_mbrb", [1, TOK], BF16)
    r_rbf = ctx.sb("r_rbf", [1, TOK], F32)
    t_mean = ctx.sb("t_mean", [1, 512], F32)
    t_var = ctx.sb("t_var", [1, 512], F32)
    t_tmp = ctx.sb("t_tmp", [1, 512], F32)
    t_r = ctx.sb("t_r", [1, 512], F32)
    pbig = ctx.pbig
    pbig2 = ctx.pbig2
    ptb = ctx.ptb

    ysb2 = pool[:, 0:16384]

    ctx.dma("sq", xsb[:].rearrange("p (f t) -> p f t", f=8),
            xq.rearrange("(f p) t -> p f t", p=128), "ld")
    ctx.dma("sq", ysb2.rearrange("p (f t) -> p f t", f=8),
            yq.rearrange("(f p) t -> p f t", p=128), "ld")
    ctx.dma("gq", r_r1[0:1, :], rows1[2:3, :], "gl")
    ctx.dma("gq", r_mr1n[0:1, :], rows1[3:4, :], "gl")
    ctx.dma("sq", w8a[:], ln2w8, "ld")
    ctx.dma("sq", w8b[:], scb8, "ld")
    ctx.dma("sq", w8c[:], bias8, "ld")
    ctx.wait(nc.sync, "ld")
    ctx.wait(nc.gpsimd, "gl")
    ctx.bar()
    nc.vector.tensor_scalar_mul(w8b[:, :], w8b[:, :], 1.0 / 127.0)
    nc.vector.tensor_scalar_mul(r_mr1n[0:1, :], r_mr1n[0:1, :], -1.0)
    ctx.bar()

    def stream_w(ap, job, kchunks, buf):
        ctx.dma("gq", wslab[:, buf * 4096: buf * 4096 + kchunks * 128],
                ap[job * 128:(job + 1) * 128, :], "wl%d" % buf)
        return ctx.cnt["wl%d" % buf]

    xn2 = pool[:, 0:16384]
    for tt in range(TOK // 512):
        ts = slice(tt * 512, (tt + 1) * 512)
        m = nc.tensor.matmul(pbig2[:, 1024:1536], ctx.c("c_onesrow_b")[0:1, :],
                             r_r1[0:1, ts], start=True, stop=True)
        def ld_wo(fc, buf):
            return stream_w(wo, fc, 8, buf)

        def mm_wo(fc, buf):
            bo = buf * 4096
            pb = pbig[:, (fc % 2) * 512:(fc % 2) * 512 + 512]
            ctx.lagwait(nc.tensor, "ev3", 1)
            for kc in range(8):
                nc.tensor.matmul(pb, wslab[:, bo + kc * 128: bo + (kc + 1) * 128],
                                 ysb2[:, kc * TOK + tt * 512: kc * TOK + (tt + 1) * 512],
                                 start=(kc == 0), stop=False)
            mmm = nc.tensor.matmul(pb, ctx.c("c_onesrow_b")[0:1, :], r_mr1n[0:1, ts],
                                   start=False, stop=True)
            ctx.tag(mmm, "mm3")
            nc.vector.wait_ge(ctx.sem("mm3"), ctx.cnt["mm3"])
            nc.vector.tensor_tensor(scr3b[:, (fc % 2) * 512:(fc % 2) * 512 + 512],
                                    xsb[:, fc * TOK + tt * 512: fc * TOK + (tt + 1) * 512],
                                    pbig2[:, 1024:1536], OP.mult)
            vv = nc.vector.tensor_add(yinner[:, fc * 512:(fc + 1) * 512],
                                      scr3b[:, (fc % 2) * 512:(fc % 2) * 512 + 512], pb)
            ctx.tag(vv, "ev3")

        _mm_stream(ctx, list(range(8)), ld_wo, mm_wo, "mm3")
        ctx.wait(nc.tensor, "ev3")
        ctx.bar()
        _emit_ln_sums(ctx, yinner, 8, 512, scr, pbig2[0:1, 0:512], pbig2[0:1, 512:1024],
                      slice(0, 512))
        nc.vector.wait_ge(ctx.sem("mm_ev"), ctx.cnt["mm_ev"])
        lv = nc.vector.tensor_scalar_mul(t_tmp[0:1, :], pbig2[0:1, 512:1024], 1.0 / D)
        ctx.tag(lv, "gv")
        nc.scalar.wait_ge(ctx.sem("gv"), ctx.cnt["gv"])
        nc.tensor.wait_ge(ctx.sem("gv"), ctx.cnt["gv"])
        la = nc.scalar.activation(t_tmp[0:1, :], t_tmp[0:1, :], AF.Sqrt,
                                  bias=ctx.cc(1e-6)[0:1, :], scale=1.0)
        ctx.tag(la, "gs")
        nc.vector.wait_ge(ctx.sem("gs"), ctx.cnt["gs"])
        lv = nc.vector.reciprocal(t_r[0:1, :], t_tmp[0:1, :])
        ctx.tag(lv, "gv")
        nc.tensor.wait_ge(ctx.sem("gv"), ctx.cnt["gv"])
        m = nc.tensor.matmul(pbig2[:, 1024:1536], ctx.c("c_onesrow_f")[0:1, :],
                             t_r[0:1, :], start=True, stop=True)
        ctx.tag(m, "bc3")
        nc.vector.wait_ge(ctx.sem("bc3"), ctx.cnt["bc3"])
        last = None
        for fc in range(8):
            off = fc * TOK + tt * 512
            yo = fc * 512
            nc.vector.tensor_tensor(scr[:, :], yinner[:, yo:yo + 512],
                                    pbig2[:, 1024:1536], OP.mult)
            nc.vector.tensor_scalar(xn2[:, off:off + 512], scr[:, :],
                                    w8a[:, fc:fc + 1], None, OP.mult)
            last = nc.vector.tensor_add(xsb[:, off:off + 512], xsb[:, off:off + 512],
                                        yinner[:, yo:yo + 512])
        ctx.tag(last, "x23")
        nc.tensor.wait_ge(ctx.sem("x23"), ctx.cnt["x23"])
        nc.scalar.wait_ge(ctx.sem("x23"), ctx.cnt["x23"])
    ctx.bar()

    for tt in range(TOK // 512):
        def ld_w1(hk, buf):
            return stream_w(w1, hk, 8, buf)

        def mm_w1(hk, buf):
            bo = buf * 4096
            pb = pbig[:, (hk % 2) * 512:(hk % 2) * 512 + 512]
            ctx.lagwait(nc.tensor, "f2", 1)
            last = None
            for kc in range(8):
                last = nc.tensor.matmul(pb, wslab[:, bo + kc * 128: bo + (kc + 1) * 128],
                                        xn2[:, kc * TOK + tt * 512: kc * TOK + (tt + 1) * 512],
                                        start=(kc == 0), stop=(kc == 7))
            ctx.tag(last, "f1")
            nc.scalar.wait_ge(ctx.sem("f1"), ctx.cnt["f1"])
            aa = nc.scalar.activation(hbuf[:, hk * 512:(hk + 1) * 512], pb, AF.Relu,
                                      bias=0.0, scale=1.0)
            ctx.tag(aa, "f2")

        _mm_stream(ctx, list(range(32)), ld_w1, mm_w1, "f1")
        nc.tensor.wait_ge(ctx.sem("f2"), ctx.cnt["f2"])
        ctx.bar()
        def ld_w2(fc, buf):
            return stream_w(w2, fc, 32, buf)

        def mm_w2(fc, buf):
            bo = buf * 4096
            pb = pbig[:, (fc % 2) * 512:(fc % 2) * 512 + 512]
            ctx.lagwait(nc.tensor, "f4", 1)
            last = None
            for kc in range(32):
                last = nc.tensor.matmul(pb, wslab[:, bo + kc * 128: bo + (kc + 1) * 128],
                                        hbuf[:, kc * 512: (kc + 1) * 512],
                                        start=(kc == 0), stop=(kc == 31))
            ctx.tag(last, "f3")
            nc.vector.wait_ge(ctx.sem("f3"), ctx.cnt["f3"])
            off = fc * TOK + tt * 512
            vv = nc.vector.tensor_add(xsb[:, off:off + 512], xsb[:, off:off + 512], pb)
            ctx.tag(vv, "f4")

        _mm_stream(ctx, list(range(8)), ld_w2, mm_w2, "f3")
        ctx.wait(nc.vector, "f4")
        ctx.bar()

    xnb = pool[:, 0:16384]
    for tt in range(TOK // 512):
        ts = slice(tt * 512, (tt + 1) * 512)
        _emit_ln_sums(ctx, xsb, 8, TOK, scr, pbig2[0:1, 0:512], pbig2[0:1, 512:1024], ts)
        rO = {"mean": t_mean, "var": t_var, "r1": t_r, "tmp": t_tmp, "sl": slice(0, 512)}
        _rows_chain_ln(ctx, pbig2[0:1, 0:512], pbig2[0:1, 512:1024], rO)
        nc.vector.tensor_copy(r_rbf[0:1, ts], t_r[0:1, :])
        nc.vector.tensor_mul(t_tmp[0:1, :], t_mean[0:1, :], t_r[0:1, :])
        ctx.bar()
        nc.vector.tensor_scalar_mul(r_mbrb[0:1, ts], t_tmp[0:1, :], -1.0)
        ctx.bar()
    ctx.bar()
    for fc in range(8):
        stream_w(cbt, fc, 8, 0)
        ctx.wait(nc.tensor, "wl0")
        for kc in range(8):
            nc.tensor.matmul(pbig[0:128, fc:fc + 1],
                             wslab[:, kc * 128:(kc + 1) * 128],
                             ctx.c("c_onescol_b")[:, 0:1], start=(kc == 0), stop=(kc == 7))
        ctx.bar()
    nc.vector.tensor_copy(cwcol[:, 0:8], pbig[:, 0:8])
    ctx.bar()
    for j in range(8):
        mt = nc.tensor.transpose(ptb[0:1, 0:128], cwcol[:, j:j + 1], ctx.c("c_I")[:])
        ctx.tag(mt, "cwt")
        nc.vector.wait_ge(ctx.sem("cwt"), ctx.cnt["cwt"])
        vv = nc.vector.tensor_copy(cwrow[0:1, j * 128:(j + 1) * 128], ptb[0:1, 0:128])
        ctx.tag(vv, "cwc8")
        nc.tensor.wait_ge(ctx.sem("cwc8"), ctx.cnt["cwc8"])
    ctx.bar()
    for tt in range(TOK // 512):
        ts = slice(tt * 512, (tt + 1) * 512)
        m = nc.tensor.matmul(pbig2[:, 1024:1536], ctx.c("c_onesrow_f")[0:1, :],
                             r_rbf[0:1, ts], start=True, stop=True)
        ctx.tag(m, "bcb")
        nc.vector.wait_ge(ctx.sem("bcb"), ctx.cnt["bcb"])
        last = None
        for fc in range(8):
            off = fc * TOK + tt * 512
            last = nc.vector.tensor_tensor(xnb[:, off:off + 512], xsb[:, off:off + 512],
                                           pbig2[:, 1024:1536], OP.mult)
        ctx.tag(last, "xnb")
        nc.tensor.wait_ge(ctx.sem("xnb"), ctx.cnt["xnb"])
    ctx.bar()
    for tt in range(TOK // 512):
        ts = slice(tt * 512, (tt + 1) * 512)
        def ld_cb(fc, buf):
            return stream_w(cbt, fc, 8, buf)

        def mm_cb(fc, buf):
            bo = buf * 4096
            pb = pbig[:, (fc % 2) * 512:(fc % 2) * 512 + 512]
            ctx.lagwait(nc.tensor, "cb2", 1)
            for kc in range(8):
                nc.tensor.matmul(pb, wslab[:, bo + kc * 128: bo + (kc + 1) * 128],
                                 xnb[:, kc * TOK + tt * 512: kc * TOK + (tt + 1) * 512],
                                 start=(kc == 0), stop=False)
            mmm = nc.tensor.matmul(pb, cwrow[0:1, fc * 128:(fc + 1) * 128],
                                   r_mbrb[0:1, ts], start=False, stop=True)
            ctx.tag(mmm, "cb1")
            nc.vector.wait_ge(ctx.sem("cb1"), ctx.cnt["cb1"])
            off = fc * TOK + tt * 512
            nc.vector.tensor_scalar(scr3b[:, (fc % 2) * 512:(fc % 2) * 512 + 512], pb,
                                    w8b[:, fc:fc + 1], w8c[:, fc:fc + 1],
                                    OP.mult, OP.add)
            vv = nc.vector.tensor_add(xsb[:, off:off + 512], xsb[:, off:off + 512],
                                      scr3b[:, (fc % 2) * 512:(fc % 2) * 512 + 512])
            ctx.tag(vv, "cb2")

        _mm_stream(ctx, list(range(8)), ld_cb, mm_cb, "cb1")
        ctx.wait(nc.vector, "cb2")
    ctx.bar()
    ctx.dma("sq", o_x4.rearrange("(f p) t -> p f t", p=128),
            xsb[:].rearrange("p (f t) -> p f t", f=8), "st")
    ctx.wait(nc.sync, "st")
    ctx.bar()
    return nc


def emit_p4(ctx, EX):
    nc = ctx.nc
    xq = EX["x4q"]
    n1w8 = EX["n1w8"]
    o_xn4 = EX["o_xn4"]
    o_rows = EX["o_rows4"]

    xsb = ctx.sb("xsb4", [128, 8 * TOK], F32)
    scr = ctx.sb("scr4", [128, 512], F32)
    w8 = ctx.sb("w84", [128, 8], F32)
    xn4 = ctx.sb("xn4", [128, 8 * TOK], BF16)
    r_mean = ctx.sb("r_mean4", [1, TOK], F32)
    r_var = ctx.sb("r_var4", [1, TOK], F32)
    r_r1 = ctx.sb("r_r14", [1, TOK], F32)
    r_tmp = ctx.sb("r_tmp4", [1, TOK], F32)
    r_s = ctx.sb("r_s4", [1, TOK], F32)
    r_ms = ctx.sb("r_ms4", [1, TOK], F32)
    r_mr1 = ctx.sb("r_mr14", [1, TOK], F32)
    psSx = ctx.pbig2[0:1, 0:512]
    psSxx = ctx.pbig2[0:1, 512:1024]
    psB = ctx.pbig2[0:128, 1024:1536]

    ctx.dma("sq", xsb[:].rearrange("p (f t) -> p f t", f=8),
            xq.rearrange("(f p) t -> p f t", p=128), "ld")
    ctx.dma("sq", w8[:], n1w8, "ld")
    ctx.wait(nc.sync, "ld")
    ctx.bar()
    rO = {"mean": r_mean, "var": r_var, "r1": r_r1, "tmp": r_tmp}
    for tt in range(TOK // 512):
        tsl = slice(tt * 512, (tt + 1) * 512)
        _emit_ln_sums(ctx, xsb, 8, TOK, scr, psSx, psSxx, tsl)
        rO["sl"] = tsl
        _rows_chain_ln(ctx, psSx, psSxx, rO)
    ctx.bar()
    nc.vector.tensor_mul(r_tmp[0:1, :], r_r1[0:1, :], r_r1[0:1, :])
    ctx.bar()
    nc.vector.tensor_mul(r_tmp[0:1, :], r_tmp[0:1, :], r_var[0:1, :])
    ctx.bar()
    nc.scalar.activation(r_tmp[0:1, :], r_tmp[0:1, :], AF.Sqrt,
                         bias=ctx.cc(1e-5)[0:1, :], scale=1.0)
    ctx.bar()
    nc.vector.reciprocal(r_s[0:1, :], r_tmp[0:1, :])
    ctx.bar()
    nc.vector.tensor_mul(r_s[0:1, :], r_s[0:1, :], r_r1[0:1, :])
    ctx.bar()
    nc.vector.tensor_mul(r_ms[0:1, :], r_s[0:1, :], r_mean[0:1, :])
    nc.vector.tensor_mul(r_mr1[0:1, :], r_r1[0:1, :], r_mean[0:1, :])
    ctx.bar()
    onesrow = ctx.c("c_onesrow_f")
    for tt in range(TOK // 512):
        tsl = slice(tt * 512, (tt + 1) * 512)
        m = nc.tensor.matmul(psB[:, 0:512], onesrow[0:1, :], r_s[0:1, tsl],
                             start=True, stop=True)
        ctx.tag(m, "bc_ev")
        nc.vector.wait_ge(ctx.sem("bc_ev"), ctx.cnt["bc_ev"])
        last = None
        for fc in range(8):
            off = fc * TOK + tt * 512
            nc.vector.tensor_tensor(scr[:, 0:512], xsb[:, off:off + 512],
                                    psB[:, 0:512], OP.mult)
            last = nc.vector.tensor_scalar(xn4[:, off:off + 512], scr[:, 0:512],
                                           w8[:, fc:fc + 1], None, OP.mult)
        ctx.tag(last, "xn_ev")
        nc.tensor.wait_ge(ctx.sem("xn_ev"), ctx.cnt["xn_ev"])
        nc.scalar.wait_ge(ctx.sem("xn_ev"), ctx.cnt["xn_ev"])
    ctx.bar()
    ctx.dma("sq", o_xn4.rearrange("(f p) t -> p f t", p=128),
            xn4[:].rearrange("p (f t) -> p f t", f=8), "st")
    ctx.dma("sq", o_rows[0:1, :], r_s[0:1, :], "st")
    ctx.dma("sq", o_rows[1:2, :], r_ms[0:1, :], "st")
    ctx.dma("sq", o_rows[2:3, :], r_r1[0:1, :], "st")
    ctx.dma("sq", o_rows[3:4, :], r_mr1[0:1, :], "st")
    ctx.wait(nc.sync, "st")
    ctx.bar()
    return nc


def emit_p5(ctx, EX):
    nc = ctx.nc
    xn4f = EX["xn4f"]
    mscf4 = EX["mscf4"]
    win_z = EX["win_z"]
    win_x = EX["win_x"]
    win_B = EX["win_B"]
    win_C = EX["win_C"]
    win_dt = EX["win_dt"]
    win_th = EX["win_th"]
    n1w8 = EX["n1w8"]
    alog8 = EX["alog8"]
    dtb8 = EX["dtb8"]
    d8 = EX["d8"]
    bnw = EX["bnw"]
    cnw = EX["cnw"]
    bbias = EX["bbias"]
    cbias = EX["cbias"]
    o_yg = EX["o_yg"]
    o_dbga = EX["o_dbga"]
    o_dbgb = EX["o_dbgb"]
    import os
    _DBGB = os.environ.get("P5_DBGB", "")

    wz = ctx.sb("wz", [128, 8 * 512], BF16)
    wx = ctx.sb("wx", [128, 8 * 512], BF16)
    wB = ctx.sb("wB", [128, 8 * 512], BF16)
    wC = ctx.sb("wC", [128, 8 * 512], BF16)
    wdt = ctx.sb("wdt", [128, 8 * 8], BF16)
    wth = ctx.sb("wth", [128, 8 * 256], BF16)
    n1b = ctx.sb("n1b", [128, 8], BF16)
    n1f = ctx.sb("n1f", [128, 8], F32)
    msneg = ctx.sb("msneg5", [1, S], BF16)
    xt = [ctx.sb(f"xt5{i}", [128, 8 * 512], BF16) for i in range(2)]
    btok = ctx.sb("btok", [128, 2048], BF16)
    ctok = ctx.sb("ctok", [128, 2048], BF16)
    cfm = ctx.sb("cfm", [128, 2048], BF16)
    xg = ctx.sb("xg", [128, 2048], BF16)
    xw = ctx.sb("xw", [128, 2048], BF16)
    zsil = ctx.sb("zsil", [128, 2048], BF16)
    ygt = ctx.sb("ygt", [128, 2048], BF16)
    rdt = ctx.sb("rdt", [8, 512], F32)
    rtmp = ctx.sb("rtmp", [8, 512], F32)
    rtmp2 = ctx.sb("rtmp2", [8, 512], F32)
    rA = ctx.sb("rA", [8, 512], F32)
    rgam = ctx.sb("rgam", [8, 512], F32)
    rF = ctx.sb("rF", [8, 512], F32)
    rG = ctx.sb("rG", [8, 512], F32)
    rea = ctx.sb("rea", [8, 512], F32)
    ren = ctx.sb("ren", [8, 512], F32)
    rvin = ctx.sb("rvin", [8, 512], F32)
    ru = ctx.sb("ru", [8, 512], F32)
    rub = ctx.sb("rub", [8, 512], BF16)
    reab = ctx.sb("reab", [8, 512], BF16)
    rdsv = ctx.sb("rdsv", [8, 512], F32)
    rones = ctx.sb("rones", [8, 512], F32)
    ramp8 = ctx.sb("ramp8", [8, 512], F32)
    eaend = ctx.sb("eaend", [8, 8], F32)
    prevend = ctx.sb("prevend", [8, 8], F32)
    carryF = ctx.sb("carryF", [8, 1], F32)
    expal = ctx.sb("expal", [8, 1], F32)
    dtbc = ctx.sb("dtbc", [8, 1], F32)
    dttm = ctx.sb("dttm", [128, 32], F32)
    gamtm = ctx.sb("gamtm", [128, 32], F32)
    dswtm = ctx.sb("dswtm", [128, 32], F32)
    skiptok = ctx.sb("skiptok", [128, 8], F32)
    skipfm = ctx.sb("skipfm", [8, 512], F32)
    d8bc = ctx.sb("d8bc", [128, 8], F32)
    bnw128 = ctx.sb("bnw128", [128, 64], F32)
    cnw128 = ctx.sb("cnw128", [128, 64], F32)
    bb128 = ctx.sb("bb128", [128, 512], F32)
    cb128 = ctx.sb("cb128", [128, 512], F32)
    dthe = ctx.sb("dthe", [128, 256], F32)
    scrA = ctx.sb("scrA5", [128, 512], F32)
    scrB = ctx.sb("scrB5", [128, 512], F32)
    ssin = ctx.sb("ssin", [128, 256], F32)
    scos = ctx.sb("scos", [128, 256], F32)
    carryTh = ctx.sb("carryTh", [1, 256], F32)
    rstd8 = ctx.sb("rstd8", [128, 8], F32)
    rstd8b = ctx.sb("rstd8b", [128, 8], F32)
    sq5 = ctx.sb("sq5", [128, 512], BF16)
    FS = ctx.sb("FS", [128, 256], F32)
    FSb = ctx.sb("FSb", [128, 256], BF16)
    T1sb = ctx.sb("T1sb", [128, 256], BF16)
    bva = ctx.sb("bva", [8, 2048], BF16)
    ubcA = ctx.sb("ubcA", [128, 2048], BF16)
    eabcA = ctx.sb("eabcA", [128, 2048], BF16)
    skbcA = ctx.sb("skbcA", [128, 2048], BF16)
    decA = ctx.sb("decA", [128, 2048], BF16)
    cwr = ctx.sb("cwr", [1, 14 * 128], BF16)
    cwz = ctx.sb("cwz", [1, 4 * 128], BF16)
    cwdt = ctx.sb("cwdt", [1, 8], BF16)
    cwc = ctx.sb("cwc", [128, 16], BF16)

    pA = ctx.pbig
    pB = ctx.pbig2
    ptb = pA[:, 1792:1856].bitcast(BF16)


    for w_, ap_, ncol in [(wz, win_z, 512), (wx, win_x, 512), (wB, win_B, 512),
                          (wC, win_C, 512), (wdt, win_dt, 8), (wth, win_th, 256)]:
        ctx.dma("gq", w_[:].rearrange("p (kc m) -> p kc m", kc=8),
                ap_.rearrange("(kc p) m -> p kc m", p=128), "gl")
    for rr in range(4):
        ctx.dma("sq", msneg[0:1, rr * TOK:(rr + 1) * TOK], mscf4[rr], "ld")
    ctx.dma("sq", n1f[:], n1w8, "ld")
    ctx.dma("sq", expal[:], alog8, "ld")
    ctx.dma("sq", dtbc[:], dtb8, "ld")
    ctx.wait(nc.sync, "ld")
    ctx.wait(nc.gpsimd, "gl")
    ctx.bar()
    ctx.dma("gq", ramp8[:], ctx.cdram["c_ramp"][0:1, :].partition_broadcast(8), "g2")
    ctx.dma("gq", d8bc[:], d8[0:1, :].partition_broadcast(128), "g2")
    ctx.dma("gq", bnw128[:], bnw[0:1, :].partition_broadcast(128), "g2")
    ctx.dma("gq", cnw128[:], cnw[0:1, :].partition_broadcast(128), "g2")
    ctx.dma("gq", bb128[:], bbias[0:1, :].partition_broadcast(128), "g2")
    ctx.dma("gq", cb128[:], cbias[0:1, :].partition_broadcast(128), "g2")
    ctx.wait(nc.gpsimd, "g2")
    ctx.bar()
    nc.vector.memset(rones[:], 1.0)
    nc.vector.memset(carryF[:], 0.0)
    nc.vector.memset(carryTh[:], 0.0)
    nc.vector.memset(FS[:], 0.0)
    nc.vector.tensor_copy(n1b[:], n1f[:])
    nc.vector.tensor_scalar_mul(msneg[0:1, :], msneg[0:1, :], -1.0)
    ctx.bar()
    nc.scalar.activation(expal[:], expal[:], AF.Exp, bias=0.0, scale=1.0)
    ctx.bar()
    nc.vector.tensor_scalar_mul(expal[:], expal[:], -1.0)
    ctx.bar()

    order = [(wx, 4, 0), (wB, 4, 4), (wC, 4, 8), (wth, 2, 12)]
    for w_, nch, base in order:
        for mi in range(nch):
            for kc in range(8):
                wid = 512 if w_ is not wth else 256
                nc.tensor.matmul(pA[0:128, base + mi: base + mi + 1],
                                 w_[:, kc * wid + mi * 128: kc * wid + (mi + 1) * 128],
                                 n1b[:, kc:kc + 1], start=(kc == 0), stop=(kc == 7))
    ctx.bar()
    nc.vector.tensor_copy(cwc[:, 0:14], pA[0:128, 0:14])
    ctx.bar()
    for j in range(14):
        nc.tensor.transpose(ptb[0:1, 0:128], cwc[:, j:j + 1], ctx.c("c_I")[:])
        ctx.bar()
        nc.vector.tensor_copy(cwr[0:1, j * 128:(j + 1) * 128], ptb[0:1, 0:128])
        ctx.bar()
    for mi in range(4):
        for kc in range(8):
            nc.tensor.matmul(pA[0:128, mi:mi + 1],
                             wz[:, kc * 512 + mi * 128: kc * 512 + (mi + 1) * 128],
                             n1b[:, kc:kc + 1], start=(kc == 0), stop=(kc == 7))
    ctx.bar()
    nc.vector.tensor_copy(cwc[:, 0:4], pA[0:128, 0:4])
    ctx.bar()
    for j in range(4):
        nc.tensor.transpose(ptb[0:1, 0:128], cwc[:, j:j + 1], ctx.c("c_I")[:])
        ctx.bar()
        nc.vector.tensor_copy(cwz[0:1, j * 128:(j + 1) * 128], ptb[0:1, 0:128])
        ctx.bar()
    for kc in range(8):
        nc.tensor.matmul(pA[0:8, 0:1], wdt[:, kc * 8:(kc + 1) * 8], n1b[:, kc:kc + 1],
                         start=(kc == 0), stop=(kc == 7))
    ctx.bar()
    nc.vector.tensor_copy(cwc[0:8, 0:1], pA[0:8, 0:1])
    ctx.bar()
    nc.tensor.transpose(ptb[0:1, 0:8], cwc[0:8, 0:1], ctx.c("c_I")[0:8, 0:8])
    ctx.bar()
    nc.vector.tensor_copy(cwdt[0:1, 0:8], ptb[0:1, 0:8])
    ctx.bar()

    TPI = 2 * math.pi

    for tile in range(16):
        gts = tile * 512
        rank = gts // TOK
        toff = gts % TOK
        xb = xt[tile % 2]
        ctx.dma("sq", xb[:].rearrange("p (kc t) -> p kc t", kc=8),
                xn4f[rank, :, toff:toff + 512].rearrange("(kc p) t -> p kc t", p=128),
                "xl")
        ctx.wait(nc.sync, "xl")
        ctx.bar()
        msl = msneg[0:1, gts:gts + 512]

        for kc in range(8):
            nc.tensor.matmul(pB[0:8, 0:512], wdt[:, kc * 8:(kc + 1) * 8],
                             xb[:, kc * 512:(kc + 1) * 512], start=(kc == 0), stop=False)
        _i = nc.tensor.matmul(pB[0:8, 0:512], cwdt[0:1, 0:8], msl, start=False, stop=True)
        ctx.tag(_i, "dtmm")
        nc.vector.wait_ge(ctx.sem("dtmm"), ctx.cnt["dtmm"])
        lv = nc.vector.tensor_scalar(rdt[:, :], pB[0:8, 0:512], dtbc[:, 0:1], None, OP.add)
        ctx.tag(lv, "gv")
        nc.scalar.wait_ge(ctx.sem("gv"), ctx.cnt["gv"])
        nc.scalar.activation(rtmp[:, :], rdt[:, :], AF.Abs, bias=0.0, scale=1.0)
        nc.scalar.activation(rtmp[:, :], rtmp[:, :], AF.Exp, bias=0.0, scale=-1.0)
        la = nc.scalar.activation(rtmp[:, :], rtmp[:, :], AF.Ln,
                                  bias=ctx.c("c_onescol_f")[0:8, 0:1], scale=1.0)
        ctx.tag(la, "gs")
        nc.vector.tensor_scalar_max(rtmp2[:, :], rdt[:, :], 0.0)
        nc.vector.wait_ge(ctx.sem("gs"), ctx.cnt["gs"])
        nc.vector.tensor_add(rdt[:, :], rtmp[:, :], rtmp2[:, :])
        lv = nc.vector.tensor_scalar(rA[:, :], rdt[:, :], expal[:, 0:1], None, OP.mult)
        ctx.tag(lv, "gv")
        nc.scalar.wait_ge(ctx.sem("gv"), ctx.cnt["gv"])
        la = nc.scalar.activation(rtmp[:, :], rA[:, :], AF.Exp, bias=0.0, scale=1.0)
        ctx.tag(la, "gs")
        nc.vector.wait_ge(ctx.sem("gs"), ctx.cnt["gs"])
        nc.vector.tensor_scalar(rtmp[:, :], rtmp[:, :], -1.0, None, OP.add)
        nc.vector.tensor_scalar(rtmp2[:, :], rA[:, :], 1e-6, None, OP.add)
        lv = nc.vector.reciprocal(rtmp2[:, :], rtmp2[:, :])
        ctx.tag(lv, "gv")
        nc.vector.wait_ge(ctx.sem("gv"), ctx.cnt["gv"])
        nc.vector.tensor_mul(rgam[:, :], rtmp[:, :], rtmp2[:, :])
        nc.vector.tensor_scalar(rgam[:, :], rgam[:, :], 0.5, 1.0, OP.mult, OP.add)
        nc.vector.tensor_tensor_scan(rF[:, :], rones[:, :], rA[:, :], carryF[:, 0:1],
                                     OP.mult, OP.add)
        nc.vector.tensor_copy(prevend[:, 0:1], carryF[:, 0:1])
        nc.vector.tensor_copy(prevend[:, 1:8],
                              rF[:, :].rearrange("p (c l) -> p c l", c=8)[:, 0:7, 63:64]
                              .rearrange("p c l -> p (c l)"))
        nc.vector.tensor_copy(carryF[:, 0:1], rF[:, 511:512])
        pv = prevend[:, :].unsqueeze(2).broadcast_to((8, 8, 64))
        lv = nc.vector.tensor_tensor(rG[:, :].rearrange("p (c l) -> p c l", c=8),
                                     rF[:, :].rearrange("p (c l) -> p c l", c=8), pv, OP.subtract)
        ctx.tag(lv, "gv")
        nc.scalar.wait_ge(ctx.sem("gv"), ctx.cnt["gv"])
        nc.scalar.activation(rea[:, :], rG[:, :], AF.Exp, bias=0.0, scale=1.0)
        la = nc.scalar.activation(ren[:, :], rG[:, :], AF.Exp, bias=0.0, scale=-1.0)
        ctx.tag(la, "gs")
        nc.vector.wait_ge(ctx.sem("gs"), ctx.cnt["gs"])
        nc.vector.tensor_copy(eaend[:, :],
                              rea[:, :].rearrange("p (c l) -> p c l", c=8)[:, :, 63:64]
                              .rearrange("p c l -> p (c l)"))
        for c in range(8):
            nc.vector.tensor_tensor_scan(rvin[:, c * 64:(c + 1) * 64],
                                         rones[:, 0:64], ren[:, c * 64:(c + 1) * 64],
                                         0.0, OP.mult, OP.add)
        nc.vector.tensor_sub(rtmp[:, :], rvin[:, :], ren[:, :])
        nc.vector.tensor_mul(rtmp[:, :], rtmp[:, :], rea[:, :])
        nc.vector.tensor_add(ru[:, :], rtmp[:, :], ramp8[:, :])
        nc.vector.tensor_copy(rub[:, :], ru[:, :])
        nc.vector.tensor_copy(reab[:, :], rea[:, :])
        ev_ = eaend[:, :].unsqueeze(2).broadcast_to((8, 8, 64))
        lv = nc.vector.tensor_tensor(rdsv[:, :].rearrange("p (c l) -> p c l", c=8),
                                     ren[:, :].rearrange("p (c l) -> p c l", c=8), ev_, OP.mult)
        ctx.tag(lv, "gv")
        nc.tensor.wait_ge(ctx.sem("gv"), ctx.cnt["gv"])
        if tile == 0 and _DBGB == "rows":
            for ri, rr in enumerate([rdt, rA, rF, rG, rea, ren, rvin, ru, rgam, rdsv]):
                ctx.dma("sq", o_dbga[ri * 8:(ri + 1) * 8, :], rr[:, :], "dbg")
            ctx.dma("sq", o_dbga[80:88, :], eaend[:, 0:8].broadcast_to((8, 8)) if False else ru[:, :], "dbg")
            ctx.wait(nc.sync, "dbg")
            ctx.bar()
        for tc in range(4):
            for src, dst in [(rdt, dttm), (rgam, gamtm), (rdsv, dswtm)]:
                mt = nc.tensor.transpose(pB[0:128, 512:520].bitcast(F32)[:, 0:8] if False else pA[0:128, 1024:1032],
                                         src[:, tc * 128:(tc + 1) * 128], ctx.c("c_If")[0:8, 0:8])
                ctx.tag(mt, "ttm")
                nc.vector.wait_ge(ctx.sem("ttm"), ctx.cnt["ttm"])
                vv = nc.vector.tensor_copy(dst[:, tc * 8:(tc + 1) * 8], pA[0:128, 1024:1032])
                ctx.tag(vv, "ttc")
                nc.tensor.wait_ge(ctx.sem("ttc"), ctx.cnt["ttc"])
        ctx.bar()

        for mz in range(4):
            pb = pA[:, (mz % 2) * 512:(mz % 2) * 512 + 512]
            for kc in range(8):
                nc.tensor.matmul(pb, wz[:, kc * 512 + mz * 128: kc * 512 + (mz + 1) * 128],
                                 xb[:, kc * 512:(kc + 1) * 512], start=(kc == 0), stop=False)
            _i = nc.tensor.matmul(pb, cwz[0:1, mz * 128:(mz + 1) * 128], msl,
                                  start=False, stop=True)
            ctx.tag(_i, "zmm")
            nc.scalar.wait_ge(ctx.sem("zmm"), ctx.cnt["zmm"])
            aa = nc.scalar.activation(zsil[:, mz * 512:(mz + 1) * 512], pb, AF.Silu,
                                      bias=0.0, scale=1.0)
            ctx.tag(aa, "zsl")
            nc.tensor.wait_ge(ctx.sem("zsl"), ctx.cnt["zsl"])
        ctx.bar()

        for tc in range(4):
            tsl = slice(tc * 128, (tc + 1) * 128)
            if "gs" in ctx.cnt:
                nc.tensor.wait_ge(ctx.sem("gs"), ctx.cnt["gs"])
            for kc in range(8):
                st = (kc == 0)
                sp = (kc == 7)
                lhs = xb[:, kc * 512 + tc * 128: kc * 512 + (tc + 1) * 128]
                nc.tensor.matmul(pA[:, 0:512], lhs, wx[:, kc * 512:(kc + 1) * 512], start=st, stop=False)
                nc.tensor.matmul(pA[:, 512:1024], lhs, wB[:, kc * 512:(kc + 1) * 512], start=st, stop=False)
                nc.tensor.matmul(pA[:, 1024:1536], lhs, wC[:, kc * 512:(kc + 1) * 512], start=st, stop=False)
                nc.tensor.matmul(pA[:, 1536:1792], lhs, wth[:, kc * 256:(kc + 1) * 256], start=st, stop=False)
            mstat = msneg[0:1, gts + tc * 128: gts + (tc + 1) * 128]
            for regi, (off, cwbase, wd) in enumerate([(0, 0, 512), (512, 4, 512), (1024, 8, 512)]):
                nc.tensor.matmul(pA[:, off:off + wd], mstat,
                                 cwr[0:1, cwbase * 128:(cwbase + 4) * 128][0:1, 0:wd],
                                 start=False, stop=True)
            _i = nc.tensor.matmul(pA[:, 1536:1792], mstat, cwr[0:1, 12 * 128:14 * 128],
                                  start=False, stop=True)
            ctx.tag(_i, "swm")
            nc.vector.wait_ge(ctx.sem("swm"), ctx.cnt["swm"])
            if tile == 0 and _DBGB in ("braw", "craw", "xraw", "thraw"):
                srcp = {"braw": pA[:, 512:1024], "craw": pA[:, 1024:1536],
                        "xraw": pA[:, 0:512], "thraw": pA[:, 1536:1792]}[_DBGB]
                wd = 256 if _DBGB == "thraw" else 512
                nc.vector.tensor_copy(ygt[:, tc * 512: tc * 512 + wd].bitcast(F32)[:, 0:wd] if False
                                      else scrB[:, 0:wd], srcp)
                ctx.bar()
                ctx.dma("sq", o_dbgb[:, tc * 512: tc * 512 + wd], scrB[:, 0:wd], "dbg3")
                ctx.wait(nc.sync, "dbg3")
                ctx.bar()

            dtv_ = dttm[:, tc * 8:(tc + 1) * 8].unsqueeze(2).broadcast_to((128, 8, 32))
            lv = nc.vector.tensor_tensor(dthe[:, :].rearrange("p (h j) -> p h j", h=8),
                                         pA[:, 1536:1792].rearrange("p (h j) -> p h j", h=8),
                                         dtv_, OP.mult)
            ctx.tag(lv, "dth")
            nc.tensor.wait_ge(ctx.sem("dth"), ctx.cnt["dth"])
            m1 = nc.tensor.matmul(pB[:, 512:768], ctx.c("c_tril")[:], dthe[:, :],
                                  start=True, stop=False)
            m2 = nc.tensor.matmul(pB[:, 512:768], ctx.c("c_onesrow_f")[0:1, :],
                                  carryTh[0:1, :], start=False, stop=True)
            m3 = nc.tensor.matmul(pB[0:1, 768:1024], ctx.c("c_onescol_f")[:, 0:1],
                                  dthe[:, :], start=True, stop=True)
            ctx.tag(m3, "cum")
            nc.vector.wait_ge(ctx.sem("cum"), ctx.cnt["cum"])
            if tile == 0 and _DBGB == "cum":
                nc.vector.tensor_copy(scrB[:, 0:256], pB[:, 512:768])
                ctx.bar()
                ctx.dma("sq", o_dbgb[:, tc * 512: tc * 512 + 256], scrB[:, 0:256], "dbg3")
                ctx.wait(nc.sync, "dbg3")
                ctx.bar()
            nc.vector.tensor_add(carryTh[0:1, :], carryTh[0:1, :], pB[0:1, 768:1024])
            K_ = 1.5 * (2 ** 23)
            nc.vector.tensor_scalar(scrA[:, 0:256], pB[:, 512:768], 1.0 / TPI, K_,
                                    OP.mult, OP.add)
            nc.vector.tensor_scalar(scrA[:, 0:256], scrA[:, 0:256], -K_, None, OP.add)
            nc.vector.scalar_tensor_tensor(scrB[:, 0:256], scrA[:, 0:256], -TPI,
                                           pB[:, 512:768], OP.mult, OP.add)
            nc.vector.tensor_scalar_min(scrB[:, 0:256], scrB[:, 0:256], float(math.pi))
            lv = nc.vector.tensor_scalar_max(scrB[:, 0:256], scrB[:, 0:256], float(-math.pi))
            ctx.tag(lv, "gv")
            nc.scalar.wait_ge(ctx.sem("gv"), ctx.cnt["gv"])
            la = nc.scalar.activation(ssin[:, :], scrB[:, 0:256], AF.Sin, bias=0.0, scale=1.0)
            ctx.tag(la, "gs")
            nc.vector.tensor_scalar(scrA[:, 0:256], scrB[:, 0:256], float(math.pi / 2), None, OP.add)
            nc.vector.wait_ge(ctx.sem("gs"), ctx.cnt["gs"])
            nc.vector.tensor_scalar(scrB[:, 0:256], scrA[:, 0:256], float(math.pi), None, OP.is_gt)
            lv = nc.vector.scalar_tensor_tensor(scrA[:, 0:256], scrB[:, 0:256], -TPI,
                                                scrA[:, 0:256], OP.mult, OP.add)
            ctx.tag(lv, "gv")
            nc.scalar.wait_ge(ctx.sem("gv"), ctx.cnt["gv"])
            la = nc.scalar.activation(scos[:, :], scrA[:, 0:256], AF.Sin, bias=0.0, scale=1.0)
            ctx.tag(la, "gs")

            for (pso, w128, bias128, dstt) in [(512, bnw128, bb128, btok),
                                               (1024, cnw128, cb128, ctok)]:
                ps_ = pA[:, pso:pso + 512]
                la = nc.scalar.activation(sq5[:, :], ps_, AF.Square, bias=0.0, scale=1.0)
                ctx.tag(la, "gs")
                nc.vector.wait_ge(ctx.sem("gs"), ctx.cnt["gs"])
                lv = nc.vector.tensor_reduce(rstd8[:, :].rearrange("p (h o) -> p h o", h=8),
                                              sq5[:, :].rearrange("p (h n) -> p h n", h=8),
                                              AX, OP.add)
                ctx.tag(lv, "gv")
                nc.vector.wait_ge(ctx.sem("gv"), ctx.cnt["gv"])
                lv = nc.vector.tensor_scalar_mul(rstd8[:, :], rstd8[:, :], 1.0 / 64.0)
                ctx.tag(lv, "gv")
                nc.scalar.wait_ge(ctx.sem("gv"), ctx.cnt["gv"])
                la = nc.scalar.activation(rstd8b[:, :], rstd8[:, :], AF.Sqrt,
                                           bias=ctx.cc(1e-5)[0:128, 0:1], scale=1.0)
                ctx.tag(la, "gs")
                nc.vector.wait_ge(ctx.sem("gs"), ctx.cnt["gs"])
                lv = nc.vector.reciprocal(rstd8[:, :], rstd8b[:, :])
                ctx.tag(lv, "gv")
                nc.vector.wait_ge(ctx.sem("gv"), ctx.cnt["gv"])
                rsv = rstd8[:, :].unsqueeze(2).broadcast_to((128, 8, 64))
                nc.vector.tensor_tensor(scrA[:, :].rearrange("p (h n) -> p h n", h=8),
                                        ps_.rearrange("p (h n) -> p h n", h=8), rsv, OP.mult)
                wv_ = w128[:, :].unsqueeze(1).broadcast_to((128, 8, 64))
                nc.vector.tensor_tensor(scrA[:, :].rearrange("p (h n) -> p h n", h=8),
                                        scrA[:, :].rearrange("p (h n) -> p h n", h=8), wv_, OP.mult)
                nc.vector.tensor_add(scrA[:, :], scrA[:, :], bias128[:, :])
                bev = scrA[:, :].rearrange("p (h t) -> p h t", h=8)[:, :, 0:32]
                bod = scrA[:, :].rearrange("p (h t) -> p h t", h=8)[:, :, 32:64]
                cv = scos[:, :].rearrange("p (h j) -> p h j", h=8)
                sv = ssin[:, :].rearrange("p (h j) -> p h j", h=8)
                dev = dstt[:, tc * 512:(tc + 1) * 512].rearrange("p (h t) -> p h t", h=8)[:, :, 0:32]
                dod = dstt[:, tc * 512:(tc + 1) * 512].rearrange("p (h t) -> p h t", h=8)[:, :, 32:64]
                nc.vector.tensor_tensor(scrB[:, 0:256].rearrange("p (h j) -> p h j", h=8), bev, cv, OP.mult)
                nc.vector.tensor_tensor(scrB[:, 256:512].rearrange("p (h j) -> p h j", h=8), bod, sv, OP.mult)
                nc.vector.tensor_sub(scrB[:, 0:256], scrB[:, 0:256], scrB[:, 256:512])
                nc.vector.tensor_copy(dev, scrB[:, 0:256].rearrange("p (h j) -> p h j", h=8))
                nc.vector.tensor_tensor(scrB[:, 0:256].rearrange("p (h j) -> p h j", h=8), bev, sv, OP.mult)
                nc.vector.tensor_tensor(scrB[:, 256:512].rearrange("p (h j) -> p h j", h=8), bod, cv, OP.mult)
                nc.vector.tensor_add(scrB[:, 0:256], scrB[:, 0:256], scrB[:, 256:512])
                nc.vector.tensor_copy(dod, scrB[:, 0:256].rearrange("p (h j) -> p h j", h=8))

            gv_ = gamtm[:, tc * 8:(tc + 1) * 8].unsqueeze(2).broadcast_to((128, 8, 64))
            nc.vector.tensor_tensor(xg[:, tc * 512:(tc + 1) * 512].rearrange("p (h q) -> p h q", h=8),
                                    pA[:, 0:512].rearrange("p (h q) -> p h q", h=8), gv_, OP.mult)
            dv_ = dswtm[:, tc * 8:(tc + 1) * 8].unsqueeze(2).broadcast_to((128, 8, 64))
            nc.vector.tensor_tensor(xw[:, tc * 512:(tc + 1) * 512].rearrange("p (h q) -> p h q", h=8),
                                    xg[:, tc * 512:(tc + 1) * 512].rearrange("p (h q) -> p h q", h=8),
                                    dv_, OP.mult)
            lv = nc.vector.tensor_reduce(skiptok[:, :].rearrange("p (h o) -> p h o", h=8),
                                         pA[:, 0:512].rearrange("p (h q) -> p h q", h=8), AX, OP.add)
            ctx.tag(lv, "gv")
            nc.vector.wait_ge(ctx.sem("gv"), ctx.cnt["gv"])
            lv = nc.vector.tensor_mul(skiptok[:, :], skiptok[:, :], d8bc[:, :])
            ctx.tag(lv, "gv")
            nc.tensor.wait_ge(ctx.sem("gv"), ctx.cnt["gv"])
            mt = nc.tensor.transpose(pA[0:8, 1792:1920], skiptok[:, :], ctx.c("c_If")[0:128, 0:128])
            ctx.tag(mt, "skt")
            nc.vector.wait_ge(ctx.sem("skt"), ctx.cnt["skt"])
            lv = nc.vector.tensor_copy(skipfm[0:8, tc * 128:(tc + 1) * 128], pA[0:8, 1792:1920])
            ctx.tag(lv, "gv")
            nc.tensor.wait_ge(ctx.sem("gv"), ctx.cnt["gv"])
            for j in range(4):
                mt = nc.tensor.transpose(ptb[:, 0:128],
                                         ctok[:, tc * 512 + j * 128: tc * 512 + (j + 1) * 128],
                                         ctx.c("c_I")[:])
                ctx.tag(mt, "cft")
                nc.vector.wait_ge(ctx.sem("cft"), ctx.cnt["cft"])
                vv = nc.vector.tensor_copy(cfm[:, (j * 4 + tc) * 128:(j * 4 + tc + 1) * 128],
                                           ptb[:, 0:128])
                ctx.tag(vv, "cfc")
                nc.tensor.wait_ge(ctx.sem("cfc"), ctx.cnt["cfc"])
        ctx.bar()

        mv = ctx.c("c_mask8")[:, :].rearrange("p (j l) -> p j l", j=4) \
            .unsqueeze(1).broadcast_to((8, 8, 4, 64))
        bva4 = bva[:, :].rearrange("p (c j l) -> p c j l", c=8, j=4)
        lv = nc.vector.tensor_tensor(bva4,
                                     rub[:, :].rearrange("p (c l) -> p c l", c=8)
                                     .unsqueeze(2).broadcast_to((8, 8, 4, 64)),
                                     mv, OP.mult)
        ctx.tag(lv, "bv")
        nc.tensor.wait_ge(ctx.sem("bv"), ctx.cnt["bv"])
        for j4 in range(4):
            mm = nc.tensor.matmul(pA[:, j4 * 512:(j4 + 1) * 512], ctx.c("c_aexp")[:, :],
                                  bva[:, j4 * 512:(j4 + 1) * 512], start=True, stop=True)
        ctx.tag(mm, "bvm")
        nc.vector.wait_ge(ctx.sem("bvm"), ctx.cnt["bvm"])
        lv = nc.vector.tensor_copy(ubcA[:, :], pA[:, 0:2048])
        ctx.tag(lv, "bvc")
        lv = nc.vector.tensor_tensor(bva4,
                                     reab[:, :].rearrange("p (c l) -> p c l", c=8)
                                     .unsqueeze(2).broadcast_to((8, 8, 4, 64)),
                                     mv, OP.mult)
        ctx.tag(lv, "bv")
        nc.tensor.wait_ge(ctx.sem("bv"), ctx.cnt["bv"])
        nc.tensor.wait_ge(ctx.sem("bvc"), ctx.cnt["bvc"])
        for j4 in range(4):
            mm = nc.tensor.matmul(pA[:, j4 * 512:(j4 + 1) * 512], ctx.c("c_aexp")[:, :],
                                  bva[:, j4 * 512:(j4 + 1) * 512], start=True, stop=True)
        ctx.tag(mm, "bvm")
        nc.vector.wait_ge(ctx.sem("bvm"), ctx.cnt["bvm"])
        lv = nc.vector.tensor_copy(eabcA[:, :], pA[:, 0:2048])
        ctx.tag(lv, "bvc")
        lv = nc.vector.tensor_tensor(bva4,
                                     skipfm[:, :].rearrange("p (c l) -> p c l", c=8)
                                     .unsqueeze(2).broadcast_to((8, 8, 4, 64)),
                                     mv, OP.mult)
        ctx.tag(lv, "bv")
        nc.tensor.wait_ge(ctx.sem("bv"), ctx.cnt["bv"])
        nc.tensor.wait_ge(ctx.sem("bvc"), ctx.cnt["bvc"])
        for j4 in range(4):
            mm = nc.tensor.matmul(pA[:, j4 * 512:(j4 + 1) * 512], ctx.c("c_aexp")[:, :],
                                  bva[:, j4 * 512:(j4 + 1) * 512], start=True, stop=True)
        ctx.tag(mm, "bvm")
        nc.vector.wait_ge(ctx.sem("bvm"), ctx.cnt["bvm"])
        lv = nc.vector.tensor_copy(skbcA[:, :], pA[:, 0:2048])
        ctx.tag(lv, "bvc")
        lv = nc.vector.tensor_tensor(bva4,
                                     eaend[:, :].unsqueeze(2).unsqueeze(3)
                                     .broadcast_to((8, 8, 4, 64)),
                                     mv, OP.mult)
        ctx.tag(lv, "bv")
        nc.tensor.wait_ge(ctx.sem("bv"), ctx.cnt["bv"])
        nc.tensor.wait_ge(ctx.sem("bvc"), ctx.cnt["bvc"])
        for j4 in range(4):
            mm = nc.tensor.matmul(pA[:, j4 * 512:(j4 + 1) * 512], ctx.c("c_aexp")[:, :],
                                  bva[:, j4 * 512:(j4 + 1) * 512], start=True, stop=True)
        ctx.tag(mm, "bvm")
        nc.vector.wait_ge(ctx.sem("bvm"), ctx.cnt["bvm"])
        lv = nc.vector.tensor_copy(decA[:, :], pA[:, 0:2048])
        ctx.tag(lv, "bvc")
        nc.tensor.wait_ge(ctx.sem("bvc"), ctx.cnt["bvc"])

        for c in range(8):
            tcb = c // 2
            pr = (c % 2) * 64
            if c > 0:
                nc.tensor.wait_ge(ctx.sem("fsu"), ctx.cnt["fsu"])
            last = None
            for h in range(8):
                lhs = btok[pr:pr + 64, tcb * 512 + h * 64: tcb * 512 + (h + 1) * 64]
                o1 = pB[(h % 2) * 64:(h % 2) * 64 + 64, 0 + (h // 2) * 64: 0 + (h // 2) * 64 + 64]
                o2 = pB[(h % 2) * 64:(h % 2) * 64 + 64, 256 + (h // 2) * 64: 256 + (h // 2) * 64 + 64]
                nc.tensor.matmul(o1, lhs, xg[pr:pr + 64, tcb * 512 + h * 64: tcb * 512 + (h + 1) * 64],
                                 start=(h < 2), stop=True, skip_group_check=True)
                last = nc.tensor.matmul(o2, lhs, xw[pr:pr + 64, tcb * 512 + h * 64: tcb * 512 + (h + 1) * 64],
                                        start=(h < 2), stop=True, skip_group_check=True)
            ctx.tag(last, "t1s")
            nc.vector.wait_ge(ctx.sem("t1s"), ctx.cnt["t1s"])
            nc.vector.tensor_copy(T1sb[:, :], pB[:, 0:256])
            nc.vector.tensor_mul(FS[:, :], FS[:, :], decA[:, c * 256:(c + 1) * 256])
            nc.vector.tensor_add(FS[:, :], FS[:, :], pB[:, 256:512])
            lv = nc.vector.tensor_copy(FSb[:, :], FS[:, :])
            ctx.tag(lv, "fsu")
            nc.tensor.wait_ge(ctx.sem("fsu"), ctx.cnt["fsu"])
            if c > 0:
                nc.tensor.wait_ge(ctx.sem("ycb"), ctx.cnt["ycb"])
            last = None
            for h in range(8):
                rhs = cfm[(h % 2) * 64:(h % 2) * 64 + 64,
                          ((h // 2) * 4 + tcb) * 128 + pr: ((h // 2) * 4 + tcb) * 128 + pr + 64]
                oy1 = pB[(h % 2) * 64:(h % 2) * 64 + 64, 512 + (h // 2) * 64: 512 + (h // 2) * 64 + 64]
                oy2 = pB[(h % 2) * 64:(h % 2) * 64 + 64, 768 + (h // 2) * 64: 768 + (h // 2) * 64 + 64]
                nc.tensor.matmul(oy1, T1sb[(h % 2) * 64:(h % 2) * 64 + 64, (h // 2) * 64:(h // 2) * 64 + 64],
                                 rhs, start=(h < 2), stop=True, skip_group_check=True)
                last = nc.tensor.matmul(oy2, FSb[(h % 2) * 64:(h % 2) * 64 + 64, (h // 2) * 64:(h // 2) * 64 + 64],
                                        rhs, start=(h < 2), stop=True, skip_group_check=True)
            ctx.tag(last, "ymm")
            nc.vector.wait_ge(ctx.sem("ymm"), ctx.cnt["ymm"])
            nc.vector.tensor_tensor(scrA[:, 0:256], pB[:, 512:768],
                                    ubcA[:, c * 256:(c + 1) * 256], OP.mult)
            nc.vector.tensor_tensor(scrB[:, 0:256], pB[:, 768:1024],
                                    eabcA[:, c * 256:(c + 1) * 256], OP.mult)
            lv = nc.vector.tensor_add(scrA[:, 0:256], scrA[:, 0:256], scrB[:, 0:256])
            ctx.tag(lv, "ycb")
            nc.vector.tensor_add(scrA[:, 0:256], scrA[:, 0:256],
                                 skbcA[:, c * 256:(c + 1) * 256])
            zv = zsil[:, :].rearrange("p (j t) -> p j t", j=4)[:, :, c * 64:(c + 1) * 64]
            lv = nc.vector.tensor_tensor(ygt[:, :].rearrange("p (j t) -> p j t", j=4)[:, :, c * 64:(c + 1) * 64],
                                         scrA[:, 0:256].rearrange("p (j l) -> p j l", j=4),
                                         zv, OP.mult)
            ctx.tag(lv, "ygd")
        nc.sync.wait_ge(ctx.sem("ygd"), ctx.cnt["ygd"])
        ctx.bar()
        if tile == 0 and _DBGB in ("btok", "ctok", "cfm", "xg", "xw", "zsil", "ygt"):
            srcmap = {"btok": btok, "ctok": ctok, "cfm": cfm, "xg": xg, "xw": xw,
                      "zsil": zsil, "ygt": ygt}
            ctx.dma("gq", o_dbgb, srcmap[_DBGB][:, 0:2048], "dbg2")
            ctx.wait(nc.gpsimd, "dbg2")
            ctx.bar()
        ctx.dma("sq", o_yg.rearrange("(j q) t -> q j t", j=4)[:, :, gts:gts + 512],
                ygt[:].rearrange("q (j t) -> q j t", j=4), "st")
        ctx.wait(nc.sync, "st")
        ctx.bar()
    return nc


def emit_p6(ctx, EX):
    nc = ctx.nc
    ygq = EX["ygq"]
    xq = EX["x4q"]
    rows4 = EX["rows4"]
    wout = EX["wout"]
    wg = EX["wg"]
    wu = EX["wu"]
    wd = EX["wd"]
    n2w8 = EX["n2w8"]
    o_x5 = EX["o_x5"]

    xsb = ctx.sb("xsb6", [128, 8 * TOK], F32)
    yin2 = ctx.sb("yin2", [128, 8 * 512], F32)
    tbuf = ctx.sb("tbuf6", [128, 20 * 512], BF16)
    pool = ctx.sb("pool6", [128, 16384], BF16)
    ygt = ctx.sb("ygt6", [128, 16 * 512], BF16)
    wslab = ctx.sb("wslab6", [128, 2 * 4096], BF16)
    scr = ctx.sb("scr6", [128, 512], F32)
    scr6b = ctx.sb("scr6b", [128, 1024], F32)
    gtile = ctx.sb("gtile", [128, 1024], BF16)
    w8a = ctx.sb("w8a6", [128, 8], F32)
    r_r1 = ctx.sb("r_r16", [1, TOK], F32)
    r_mr1n = ctx.sb("r_mr1n6", [1, TOK], BF16)
    t_tmp = ctx.sb("t_tmp6", [1, 512], F32)
    t_r = ctx.sb("t_r6", [1, 512], F32)
    pbig = ctx.pbig
    pbig2 = ctx.pbig2

    ctx.dma("sq", xsb[:].rearrange("p (f t) -> p f t", f=8),
            xq.rearrange("(f p) t -> p f t", p=128), "ld")
    ctx.dma("sq", r_r1[0:1, :], rows4[2:3, :], "ld")
    ctx.dma("gq", r_mr1n[0:1, :], rows4[3:4, :], "gl")
    ctx.dma("sq", w8a[:], n2w8, "ld")
    ctx.wait(nc.sync, "ld")
    ctx.wait(nc.gpsimd, "gl")
    ctx.bar()
    nc.vector.tensor_scalar_mul(r_mr1n[0:1, :], r_mr1n[0:1, :], -1.0)
    ctx.bar()

    def stream_w(ap, job, kchunks, buf, off2=0):
        ctx.dma("gq", wslab[:, buf * 4096 + off2: buf * 4096 + off2 + kchunks * 128],
                ap[job * 128:(job + 1) * 128, :], "wl%d" % buf)
        return ctx.cnt["wl%d" % buf]

    xn5 = pool[:, 0:16384]
    for tt in range(TOK // 512):
        ts = slice(tt * 512, (tt + 1) * 512)
        ctx.dma("sq", ygt[:].rearrange("p (kc t) -> p kc t", kc=16),
                ygq[:, ts].rearrange("(kc p) t -> p kc t", p=128), "yl")
        ctx.wait(nc.tensor, "yl")
        m = nc.tensor.matmul(pbig2[:, 1024:1536], ctx.c("c_onesrow_f")[0:1, :],
                             r_r1[0:1, ts], start=True, stop=True)
        def ld_mo(fc, buf):
            return stream_w(wout, fc, 16, buf)

        def mm_mo(fc, buf):
            bo = buf * 4096
            pb = pbig[:, (fc % 2) * 512:(fc % 2) * 512 + 512]
            ctx.lagwait(nc.tensor, "ev6", 1)
            for kc in range(16):
                nc.tensor.matmul(pb, wslab[:, bo + kc * 128: bo + (kc + 1) * 128],
                                 ygt[:, kc * 512:(kc + 1) * 512],
                                 start=(kc == 0), stop=False)
            mmm = nc.tensor.matmul(pb, ctx.c("c_onesrow_b")[0:1, :], r_mr1n[0:1, ts],
                                   start=False, stop=True)
            ctx.tag(mmm, "mo")
            nc.vector.wait_ge(ctx.sem("mo"), ctx.cnt["mo"])
            off = fc * TOK + tt * 512
            nc.vector.tensor_tensor(scr6b[:, (fc % 2) * 512:(fc % 2) * 512 + 512],
                                    xsb[:, off:off + 512], pbig2[:, 1024:1536], OP.mult)
            vv = nc.vector.tensor_add(yin2[:, fc * 512:(fc + 1) * 512],
                                      scr6b[:, (fc % 2) * 512:(fc % 2) * 512 + 512], pb)
            ctx.tag(vv, "ev6")

        _mm_stream(ctx, list(range(8)), ld_mo, mm_mo, "mo")
        ctx.bar()
        _emit_ln_sums(ctx, yin2, 8, 512, scr, pbig2[0:1, 0:512], pbig2[0:1, 512:1024],
                      slice(0, 512))
        nc.vector.wait_ge(ctx.sem("mm_ev"), ctx.cnt["mm_ev"])
        lv = nc.vector.tensor_scalar_mul(t_tmp[0:1, :], pbig2[0:1, 512:1024], 1.0 / D)
        ctx.tag(lv, "gv")
        nc.scalar.wait_ge(ctx.sem("gv"), ctx.cnt["gv"])
        nc.tensor.wait_ge(ctx.sem("gv"), ctx.cnt["gv"])
        la = nc.scalar.activation(t_tmp[0:1, :], t_tmp[0:1, :], AF.Sqrt,
                                  bias=ctx.cc(1e-5)[0:1, :], scale=1.0)
        ctx.tag(la, "gs")
        nc.vector.wait_ge(ctx.sem("gs"), ctx.cnt["gs"])
        lv = nc.vector.reciprocal(t_r[0:1, :], t_tmp[0:1, :])
        ctx.tag(lv, "gv")
        nc.tensor.wait_ge(ctx.sem("gv"), ctx.cnt["gv"])
        m = nc.tensor.matmul(pbig2[:, 1024:1536], ctx.c("c_onesrow_f")[0:1, :],
                             t_r[0:1, :], start=True, stop=True)
        ctx.tag(m, "bc6")
        nc.vector.wait_ge(ctx.sem("bc6"), ctx.cnt["bc6"])
        last = None
        for fc in range(8):
            off = fc * TOK + tt * 512
            yo = fc * 512
            nc.vector.tensor_tensor(scr[:, :], yin2[:, yo:yo + 512],
                                    pbig2[:, 1024:1536], OP.mult)
            nc.vector.tensor_scalar(xn5[:, off:off + 512], scr[:, :],
                                    w8a[:, fc:fc + 1], None, OP.mult)
            last = nc.vector.tensor_add(xsb[:, off:off + 512], xsb[:, off:off + 512],
                                        yin2[:, yo:yo + 512])
        ctx.tag(last, "x26")
        nc.tensor.wait_ge(ctx.sem("x26"), ctx.cnt["x26"])
        nc.scalar.wait_ge(ctx.sem("x26"), ctx.cnt["x26"])
    ctx.bar()

    for tt in range(TOK // 512):
        def ld_gu(hk, buf):
            stream_w(wg, hk, 8, buf)
            return stream_w(wu, hk, 8, buf, off2=1024)

        def mm_gu(hk, buf):
            bo = buf * 4096
            pb = pbig[:, (hk % 2) * 512:(hk % 2) * 512 + 512]
            pb2 = pbig[:, 1024 + (hk % 2) * 512: 1024 + (hk % 2) * 512 + 512]
            ctx.lagwait(nc.tensor, "g4", 1)
            last = None
            for kc in range(8):
                last = nc.tensor.matmul(pb, wslab[:, bo + kc * 128: bo + (kc + 1) * 128],
                                        xn5[:, kc * TOK + tt * 512: kc * TOK + (tt + 1) * 512],
                                        start=(kc == 0), stop=(kc == 7))
            ctx.tag(last, "g1")
            last = None
            for kc in range(8):
                last = nc.tensor.matmul(pb2, wslab[:, bo + 1024 + kc * 128: bo + 1024 + (kc + 1) * 128],
                                        xn5[:, kc * TOK + tt * 512: kc * TOK + (tt + 1) * 512],
                                        start=(kc == 0), stop=(kc == 7))
            ctx.tag(last, "g2")
            nc.scalar.wait_ge(ctx.sem("g1"), ctx.cnt["g1"])
            aa = nc.scalar.activation(gtile[:, (hk % 2) * 512:(hk % 2) * 512 + 512], pb,
                                      AF.Silu, bias=0.0, scale=1.0)
            ctx.tag(aa, "g3")
            nc.vector.wait_ge(ctx.sem("g3"), ctx.cnt["g3"])
            nc.vector.wait_ge(ctx.sem("g2"), ctx.cnt["g2"])
            vv = nc.vector.tensor_tensor(tbuf[:, hk * 512:(hk + 1) * 512], pb2,
                                         gtile[:, (hk % 2) * 512:(hk % 2) * 512 + 512], OP.mult)
            ctx.tag(vv, "g4")

        _mm_stream(ctx, list(range(20)), ld_gu, mm_gu, "g4")
        ctx.bar()
        nc.tensor.wait_ge(ctx.sem("g4"), ctx.cnt["g4"])

        def ld_wd(fc, buf):
            return stream_w(wd, fc, 20, buf)

        def mm_wd(fc, buf):
            bo = buf * 4096
            pb = pbig[:, (fc % 2) * 512:(fc % 2) * 512 + 512]
            ctx.lagwait(nc.tensor, "g6", 1)
            last = None
            for kc in range(20):
                last = nc.tensor.matmul(pb, wslab[:, bo + kc * 128: bo + (kc + 1) * 128],
                                        tbuf[:, kc * 512:(kc + 1) * 512],
                                        start=(kc == 0), stop=(kc == 19))
            ctx.tag(last, "g5")
            nc.vector.wait_ge(ctx.sem("g5"), ctx.cnt["g5"])
            off = fc * TOK + tt * 512
            vv = nc.vector.tensor_add(xsb[:, off:off + 512], xsb[:, off:off + 512], pb)
            ctx.tag(vv, "g6")

        _mm_stream(ctx, list(range(8)), ld_wd, mm_wd, "g5")
        ctx.bar()
    ctx.dma("sq", o_x5.rearrange("(f p) t -> p f t", p=128),
            xsb[:].rearrange("p (f t) -> p f t", f=8), "st")
    ctx.wait(nc.sync, "st")
    ctx.bar()
    return nc


def emit_p7(ctx, EX):
    nc = ctx.nc
    absp = EX["absp_all"]
    wup_sl = EX["wupT_sl"]
    wdn_sl = EX["wdnT_sl"]
    o_up = EX["o_wtup"]
    o_dn = EX["o_wtdn"]

    ab = ctx.sb("ab", [8, 2], F32)
    thr = ctx.sb("thr", [1, 2], F32)
    thrc = ctx.sb("thrc", [128, 2], F32)
    wld = ctx.sb("wld7", [128, 1024], F32)
    aw = ctx.sb("aw7", [128, 1024], F32)
    sg = ctx.sb("sg7", [128, 1024], F32)
    msk = ctx.sb("msk7", [128, 1024], F32)
    wt = ctx.sb("wt7", [128, 1024], BF16)
    ps = ctx.pbig[0:128, 0:512]

    ctx.dma("sq", ab[:], absp, "ld")
    ctx.wait(nc.sync, "ld")
    ctx.bar()
    nc.vector.tensor_reduce(thr[0:1, 0:2].rearrange("p (a b) -> p a b", a=2) if False
                            else thr[0:1, 0:2], ab[0:1, 0:2], AX, OP.add) if False else None
    m = nc.tensor.matmul(ps[0:1, 0:2], ctx.c("c_onescol_f")[0:8, 0:1], ab[:, :],
                         start=True, stop=True)
    ctx.tag(m, "s7")
    nc.vector.wait_ge(ctx.sem("s7"), ctx.cnt["s7"])
    nc.vector.tensor_copy(thr[0:1, 0:2], ps[0:1, 0:2])
    ctx.bar()
    nc.vector.tensor_scalar_mul(thr[0:1, 0:1], thr[0:1, 0:1], 0.5 * 0.5 / (D * DFF))
    nc.vector.tensor_scalar_mul(thr[0:1, 1:2], thr[0:1, 1:2], 0.5 * 0.5 / (D * DFF))
    ctx.bar()
    m = nc.tensor.matmul(ps[:, 0:2], ctx.c("c_onesrow_f")[0:1, :], thr[0:1, 0:2],
                         start=True, stop=True)
    ctx.tag(m, "s8")
    nc.vector.wait_ge(ctx.sem("s8"), ctx.cnt["s8"])
    nc.vector.tensor_copy(thrc[:, 0:2], ps[:, 0:2])
    ctx.bar()
    for mi, (ap, oap, col) in enumerate(
            [(wup_sl, o_up, 0), (wdn_sl, o_dn, 1)]):
        nr, ncol = ap.shape[0] // 128, ap.shape[1] // 1024
        blocks = [(ap[r * 128:(r + 1) * 128, c * 1024:(c + 1) * 1024],
                   oap[r * 128:(r + 1) * 128, c * 1024:(c + 1) * 1024])
                  for r in range(nr) for c in range(ncol)]
        cols = 1024
        for k in range(len(blocks)):
            ctx.dma("sq", wld[:, 0:cols], blocks[k][0], "wl")
            ctx.wait(nc.scalar, "wl")
            ctx.wait(nc.vector, "wl")
            a1 = nc.scalar.activation(aw[:, 0:cols], wld[:, 0:cols], AF.Abs, bias=0.0, scale=1.0)
            a2 = nc.scalar.activation(sg[:, 0:cols], wld[:, 0:cols], AF.Sign, bias=0.0, scale=1.0)
            ctx.tag(a2, "t7")
            nc.vector.wait_ge(ctx.sem("t7"), ctx.cnt["t7"])
            nc.vector.tensor_scalar(msk[:, 0:cols], aw[:, 0:cols], thrc[:, col:col + 1],
                                    None, OP.is_gt)
            vv = nc.vector.tensor_tensor(wt[:, 0:cols], msk[:, 0:cols], sg[:, 0:cols], OP.mult)
            ctx.tag(vv, "t8")
            nc.sync.wait_ge(ctx.sem("t8"), ctx.cnt["t8"])
            nc.scalar.wait_ge(ctx.sem("t8"), ctx.cnt["t8"])
            ctx.dma("sq", blocks[k][1], wt[:, 0:cols], "st")
    ctx.wait(nc.sync, "st")
    ctx.bar()
    return nc


def emit_p8(ctx, EX):
    nc = ctx.nc
    xq = EX["x5q"]
    wtup = EX["wtupT"]
    wtdn = EX["wtdnT"]
    o_out = EX["o_out"]

    xsb = ctx.sb("xsb8", [128, 8 * TOK], F32)
    hbuf = ctx.sb("hbuf8", [128, 32 * 512], BF16)
    pool = ctx.sb("pool8", [128, 16384], BF16)
    wslab = ctx.sb("wslab8", [128, 2 * 4096], BF16)
    scr = ctx.sb("scr8", [128, 512], F32)
    cwrow = ctx.sb("cwrow8", [1, 32 * 128], BF16)
    cwcol = ctx.sb("cwcol8", [128, 32], BF16)
    r_sd = ctx.sb("r_sd", [1, TOK], F32)
    r_msdn = ctx.sb("r_msdn", [1, TOK], BF16)
    t_mean = ctx.sb("t_mean8", [1, 512], F32)
    t_var = ctx.sb("t_var8", [1, 512], F32)
    t_tmp = ctx.sb("t_tmp8", [1, 512], F32)
    t_r = ctx.sb("t_r8", [1, 512], F32)
    pbig = ctx.pbig
    pbig2 = ctx.pbig2
    ptb = ctx.ptb

    ctx.dma("sq", xsb[:].rearrange("p (f t) -> p f t", f=8),
            xq.rearrange("(f p) t -> p f t", p=128), "ld")
    ctx.wait(nc.sync, "ld")
    ctx.bar()

    def stream_w(ap, job, width, buf):
        ctx.dma("gq", wslab[:, buf * 4096: buf * 4096 + width],
                ap[job * 128:(job + 1) * 128, :], "wl%d" % buf)
        return ctx.cnt["wl%d" % buf]

    xnd = pool[:, 0:16384]
    for tt in range(TOK // 512):
        ts = slice(tt * 512, (tt + 1) * 512)
        nc.scalar.wait_ge(ctx.sem("xn8"), ctx.cnt.get("xn8", 0))
        _emit_ln_sums(ctx, xsb, 8, TOK, scr, pbig2[0:1, 0:512], pbig2[0:1, 512:1024], ts)
        rO = {"mean": t_mean, "var": t_var, "r1": t_r, "tmp": t_tmp, "sl": slice(0, 512)}
        _rows_chain_ln(ctx, pbig2[0:1, 0:512], pbig2[0:1, 512:1024], rO)
        nc.vector.tensor_mul(t_tmp[0:1, :], t_r[0:1, :], t_r[0:1, :])
        lv = nc.vector.tensor_mul(t_tmp[0:1, :], t_tmp[0:1, :], t_var[0:1, :])
        ctx.tag(lv, "gv")
        nc.scalar.wait_ge(ctx.sem("gv"), ctx.cnt["gv"])
        la = nc.scalar.activation(t_tmp[0:1, :], t_tmp[0:1, :], AF.Sqrt, bias=0.0, scale=1.0)
        ctx.tag(la, "gs")
        nc.vector.wait_ge(ctx.sem("gs"), ctx.cnt["gs"])
        nc.vector.tensor_scalar(t_tmp[0:1, :], t_tmp[0:1, :], 1e-8, None, OP.add)
        lv = nc.vector.reciprocal(t_tmp[0:1, :], t_tmp[0:1, :])
        ctx.tag(lv, "gv")
        nc.vector.wait_ge(ctx.sem("gv"), ctx.cnt["gv"])
        nc.vector.tensor_mul(t_tmp[0:1, :], t_tmp[0:1, :], t_r[0:1, :])
        nc.vector.tensor_copy(r_sd[0:1, ts], t_tmp[0:1, :])
        nc.vector.tensor_mul(t_tmp[0:1, :], t_tmp[0:1, :], t_mean[0:1, :])
        lv = nc.vector.tensor_scalar_mul(r_msdn[0:1, ts], t_tmp[0:1, :], -1.0)
        ctx.tag(lv, "gv")
        nc.tensor.wait_ge(ctx.sem("gv"), ctx.cnt["gv"])
        m = nc.tensor.matmul(pbig2[:, 1024:1536], ctx.c("c_onesrow_f")[0:1, :],
                             r_sd[0:1, ts], start=True, stop=True)
        ctx.tag(m, "bc8")
        nc.vector.wait_ge(ctx.sem("bc8"), ctx.cnt["bc8"])
        last = None
        for fc in range(8):
            off = fc * TOK + tt * 512
            last = nc.vector.tensor_tensor(xnd[:, off:off + 512], xsb[:, off:off + 512],
                                           pbig2[:, 1024:1536], OP.mult)
        ctx.tag(last, "xn8")
        nc.tensor.wait_ge(ctx.sem("xn8"), ctx.cnt["xn8"])
    ctx.bar()
    def ld_cs(fc, buf):
        return stream_w(wtup, fc, 1024, buf)

    def mm_cs(fc, buf):
        bo = buf * 4096
        last = None
        for kc in range(8):
            last = nc.tensor.matmul(pbig[0:128, fc % 16: fc % 16 + 1],
                                    wslab[:, bo + kc * 128: bo + (kc + 1) * 128],
                                    ctx.c("c_onescol_b")[:, 0:1],
                                    start=(kc == 0 and fc % 16 == 0), stop=(kc == 7),
                                    skip_group_check=True)
        ctx.tag(last, "cs8")
        if fc % 16 == 15:
            nc.vector.wait_ge(ctx.sem("cs8"), ctx.cnt["cs8"])
            nc.vector.tensor_copy(cwcol[:, (fc // 16) * 16:(fc // 16) * 16 + 16],
                                  pbig[:, 0:16])
            ctx.tag(nc.vector.tensor_copy(scr[0:1, 0:1], cwcol[0:1, 0:1]), "cv8")
            nc.tensor.wait_ge(ctx.sem("cv8"), ctx.cnt["cv8"])

    _mm_stream(ctx, list(range(32)), ld_cs, mm_cs, "cs8")
    ctx.bar()
    for j in range(32):
        mt = nc.tensor.transpose(ptb[0:1, 0:128], cwcol[:, j:j + 1], ctx.c("c_I")[:])
        ctx.tag(mt, "cwt")
        nc.vector.wait_ge(ctx.sem("cwt"), ctx.cnt["cwt"])
        vv = nc.vector.tensor_copy(cwrow[0:1, j * 128:(j + 1) * 128], ptb[0:1, 0:128])
        ctx.tag(vv, "cwc8")
        nc.tensor.wait_ge(ctx.sem("cwc8"), ctx.cnt["cwc8"])
    ctx.bar()

    for tt in range(TOK // 512):
        ts = slice(tt * 512, (tt + 1) * 512)
        def ld_up(hk, buf):
            return stream_w(wtup, hk, 1024, buf)

        def mm_up(hk, buf):
            bo = buf * 4096
            pb = pbig[:, (hk % 2) * 512:(hk % 2) * 512 + 512]
            ctx.lagwait(nc.tensor, "u2", 1)
            for kc in range(8):
                nc.tensor.matmul(pb, wslab[:, bo + kc * 128: bo + (kc + 1) * 128],
                                 xnd[:, kc * TOK + tt * 512: kc * TOK + (tt + 1) * 512],
                                 start=(kc == 0), stop=False)
            last = nc.tensor.matmul(pb, cwrow[0:1, hk * 128:(hk + 1) * 128],
                                    r_msdn[0:1, ts], start=False, stop=True)
            ctx.tag(last, "u1")
            nc.scalar.wait_ge(ctx.sem("u1"), ctx.cnt["u1"])
            aa = nc.scalar.activation(hbuf[:, hk * 512:(hk + 1) * 512], pb, AF.Gelu,
                                      bias=0.0, scale=1.0)
            ctx.tag(aa, "u2")

        _mm_stream(ctx, list(range(32)), ld_up, mm_up, "u1")
        nc.tensor.wait_ge(ctx.sem("u2"), ctx.cnt["u2"])
        ctx.bar()
        for i32 in range(1):
            pass
        for hk in range(32):
            aa = nc.scalar.activation(scr[:, :], hbuf[:, hk * 512:(hk + 1) * 512],
                                      AF.Square, bias=0.0, scale=1.0)
            ctx.tag(aa, "h1")
            nc.tensor.wait_ge(ctx.sem("h1"), ctx.cnt["h1"])
            last = nc.tensor.matmul(pbig2[0:1, 0:512], ctx.c("c_onescol_f")[:, 0:1],
                                    scr[:, :], start=(hk == 0), stop=(hk == 31))
            ctx.tag(last, "h2")
            nc.scalar.wait_ge(ctx.sem("h2"), ctx.cnt["h2"])
        ctx.bar()
        nc.vector.tensor_scalar_mul(t_tmp[0:1, :], pbig2[0:1, 0:512], 1.0 / DFF)
        ctx.bar()
        nc.scalar.activation(t_tmp[0:1, :], t_tmp[0:1, :], AF.Sqrt, bias=0.0, scale=1.0)
        ctx.bar()
        nc.vector.tensor_scalar(t_tmp[0:1, :], t_tmp[0:1, :], 1e-8, None, OP.add)
        ctx.bar()
        nc.vector.reciprocal(t_r[0:1, :], t_tmp[0:1, :])
        ctx.bar()
        m = nc.tensor.matmul(pbig2[:, 1024:1536], ctx.c("c_onesrow_f")[0:1, :],
                             t_r[0:1, :], start=True, stop=True)
        ctx.tag(m, "h3")
        nc.vector.wait_ge(ctx.sem("h3"), ctx.cnt["h3"])
        last = None
        for hk in range(32):
            last = nc.vector.tensor_tensor(hbuf[:, hk * 512:(hk + 1) * 512],
                                           hbuf[:, hk * 512:(hk + 1) * 512],
                                           pbig2[:, 1024:1536], OP.mult)
        ctx.tag(last, "h4")
        nc.tensor.wait_ge(ctx.sem("h4"), ctx.cnt["h4"])
        ctx.bar()
        def ld_dn(fc, buf):
            return stream_w(wtdn, fc, 4096, buf)

        def mm_dn(fc, buf):
            bo = buf * 4096
            pb = pbig[:, (fc % 2) * 512:(fc % 2) * 512 + 512]
            ctx.lagwait(nc.tensor, "d2", 1)
            last = None
            for kc in range(32):
                last = nc.tensor.matmul(pb, wslab[:, bo + kc * 128: bo + (kc + 1) * 128],
                                        hbuf[:, kc * 512:(kc + 1) * 512],
                                        start=(kc == 0), stop=(kc == 31))
            ctx.tag(last, "d1")
            nc.vector.wait_ge(ctx.sem("d1"), ctx.cnt["d1"])
            off = fc * TOK + tt * 512
            vv = nc.vector.tensor_add(xsb[:, off:off + 512], xsb[:, off:off + 512], pb)
            ctx.tag(vv, "d2")

        _mm_stream(ctx, list(range(8)), ld_dn, mm_dn, "d1")
        ctx.bar()
    ctx.dma("sq", o_out.rearrange("(f p) t -> p f t", p=128),
            xsb[:].rearrange("p (f t) -> p f t", f=8), "st")
    ctx.wait(nc.sync, "st")
    ctx.bar()
    return nc

_EXT_SHAPES = {
    "xq": ([D, TOK], F32), "ln1w8": ([128, 8], F32), "ln2w8": ([128, 8], F32),
    "n1w8": ([128, 8], F32), "n2w8": ([128, 8], F32), "scb8": ([128, 8], F32),
    "bias8": ([128, 8], F32), "wq4": ([D, 256], F32), "wk4": ([D, 256], F32),
    "wv4": ([D, 256], F32), "scale4": ([4, 1], F32), "scale22": ([2, 2], F32),
    "scale4r": ([1, 4], F32), "relb4": ([NB, 4], F32), "wo": ([8 * 128, D], F32),
    "w1": ([32 * 128, D], F32), "w2": ([8 * 128, FFN], F32), "cbt": ([8 * 128, D], F32),
    "win_z": ([D, 512], F32), "win_x": ([D, 512], F32), "win_B": ([D, 512], F32),
    "win_C": ([D, 512], F32), "win_dt": ([D, 8], F32), "win_th": ([D, 256], F32),
    "alog8": ([8, 1], F32), "dtb8": ([8, 1], F32), "d8": ([1, 8], F32),
    "bnw": ([1, 64], F32), "cnw": ([1, 64], F32), "bbias": ([1, 512], F32),
    "cbias": ([1, 512], F32), "wout": ([8 * 128, 2048], F32), "wg": ([20 * 128, D], F32),
    "wu": ([20 * 128, D], F32), "wd": ([8 * 128, DMLP], F32),
    "wupT_sl": ([D, 1024], F32), "wdnT_sl": ([256, DFF], F32),
}
_ALL_CONSTS = ["c_sel", "c_maskrow", "c_J", "c_I", "c_If", "c_onescol_f",
               "c_onescol_b", "c_onesrow_f", "c_onesrow_b", "c_sel64", "c_exp2",
               "c_tril", "c_ramp", "c_mask8", "c_aexp"]


def build_fused():
    import os
    _UPTO = int(os.environ.get("FUSE_UPTO", "99"))
    nc = bass.Bass("TRN2", target_bir_lowering=False, debug=False,
                   detect_race_conections=False) if False else bass.Bass(
        "TRN2", target_bir_lowering=False, debug=False, detect_race_conditions=False)
    ctx = Ctx(nc)
    E = {}
    for n, (sh, dt) in _EXT_SHAPES.items():
        E[n] = nc.dram_tensor(n, sh, dt, kind="ExternalInput").ap()
    o_out = nc.dram_tensor("o_out", [D, TOK], F32, kind="ExternalOutput").ap()

    ti = lambda n, sh, dt: nc.dram_tensor(n, sh, dt, kind="Internal")
    s_ag1 = ti("s_ag1", [1032, TOK], BF16)
    g_ag1 = ti("g_ag1", [4 * 1032, TOK], BF16)
    s_absp = ti("s_absp", [1, 2], F32)
    g_absp = ti("g_absp", [8, 2], F32)
    s_rows1 = ti("s_rows1", [4, TOK], F32)
    s_wtup = ti("s_wtup", [D, 1024], BF16)
    s_wtdn = ti("s_wtdn", [256, DFF], BF16)
    g_wtup = ti("g_wtup", [4 * D, 1024], BF16)
    g_wtdn = ti("g_wtdn", [4 * 256, DFF], BF16)
    s_y = ti("s_y", [256, S], BF16)
    g_y = ti("g_y", [1024, S], BF16)
    s_yq = ti("s_yq", [1024, TOK], BF16)
    s_x4 = ti("s_x4", [D, TOK], F32)
    s_rows4 = ti("s_rows4", [4, TOK], F32)
    s_ag4 = ti("s_ag4", [1032, TOK], BF16)
    g_ag4 = ti("g_ag4", [4 * 1032, TOK], BF16)
    s_yg = ti("s_yg", [512, S], BF16)
    g_yg = ti("g_yg", [2048, S], BF16)
    s_ygq = ti("s_ygq", [2048, TOK], BF16)
    s_x5 = ti("s_x5", [D, TOK], F32)
    s_dbga = ti("s_dbga", [88, 512], F32)
    s_dbgb = ti("s_dbgb", [128, 2048], F32)

    ctx.load_consts(_ALL_CONSTS)
    ctx.prep_consts([1e-5, 1e-6, 1e-8])
    ctx.pbig = nc.alloc_psum_tensor("pbig", [128, 2048], F32)
    ctx.pbig2 = nc.alloc_psum_tensor("pbig2", [128, 1536], F32)
    ctx.ptb = nc.alloc_psum_tensor("ptb", [128, 256], BF16)
    ctx.bar()
    ctx.make_pool()
    QUAD = [[0, 1, 2, 3], [4, 5, 6, 7]]
    GLOB = [list(range(8))]
    ncoll = [0]

    def coll(kind, groups, in_ap, out_ap):
        ncoll[0] += 1
        cc = ctx.sem(f"cc{ncoll[0]}")
        nc.gpsimd.collective_compute(kind, OP.bypass, replica_groups=groups,
                                     ins=[in_ap], outs=[out_ap]).then_inc(cc)
        nc.gpsimd.wait_ge(cc, 1)
        ctx.bar()

    ag1v = g_ag1.ap().rearrange("(r d) t -> r d t", d=1032)

    ctx.pool_reset()
    emit_p1(ctx, {"xq": E["xq"], "ln1w8": E["ln1w8"], "wupT_sl": E["wupT_sl"],
                  "wdnT_sl": E["wdnT_sl"], "o_xn1": s_ag1.ap()[0:1024, :],
                  "o_rows": s_rows1.ap(), "o_absp": s_absp.ap()})
    ctx.bar()
    ctx.dma("gq", s_ag1.ap()[1024:1025, :], s_rows1.ap()[1:2, :], "gms")
    ctx.wait(nc.gpsimd, "gms")
    ctx.bar()
    coll("AllGather", QUAD, s_ag1.ap(), g_ag1.ap())
    coll("AllGather", GLOB, s_absp.ap(), g_absp.ap())
    if _UPTO <= 1:
        ctx.dma("sq", o_out.rearrange("(f p) t -> p f t", p=128),
                ctx.pool_t[:, 0:16 * TOK].bitcast(F32).rearrange("p (f t) -> p f t", f=8), "fin")
        ctx.wait(nc.sync, "fin")
        ctx.bar()
        return nc

    ctx.pool_reset()
    emit_p7(ctx, {"absp_all": g_absp.ap(), "wupT_sl": E["wupT_sl"],
                  "wdnT_sl": E["wdnT_sl"], "o_wtup": s_wtup.ap(),
                  "o_wtdn": s_wtdn.ap()})
    ctx.bar()
    coll("AllGather", QUAD, s_wtup.ap(), g_wtup.ap())
    coll("AllGather", QUAD, s_wtdn.ap(), g_wtdn.ap())
    if _UPTO == 15:
        ctx.dma("sq", o_out.rearrange("(f p) t -> p f t", p=128),
                ctx.pool_t[:, 0:16 * TOK].bitcast(F32).rearrange("p (f t) -> p f t", f=8), "fin")
        ctx.wait(nc.sync, "fin")
        ctx.bar()
        return nc

    ctx.pool_reset()
    emit_p2(ctx, {"xn1f": ag1v[:, 0:1024, :], "msf4": ag1v[:, 1024:1025, :],
                  "wq4": E["wq4"], "wk4": E["wk4"], "wv4": E["wv4"],
                  "ln1w8": E["ln1w8"], "scale4": E["scale4"],
                  "scale22": E["scale22"], "scale4r": E["scale4r"],
                  "relb4": E["relb4"], "o_y": s_y.ap()})
    ctx.bar()
    coll("AllGather", QUAD, s_y.ap(), g_y.ap())
    pid = nc.gpsimd.partition_id()
    qoff = (pid % 4) * TOK
    src = bass.AP(g_y.ap().tensor, qoff, [[S, 1024], [1, TOK]])
    ctx.dma("gq", s_yq.ap(), src, "gsl")
    ctx.wait(nc.gpsimd, "gsl")
    ctx.bar()
    if _UPTO <= 2:
        ctx.dma("sq", o_out.rearrange("(f p) t -> p f t", p=128),
                ctx.pool_t[:, 0:16 * TOK].bitcast(F32).rearrange("p (f t) -> p f t", f=8), "fin")
        ctx.wait(nc.sync, "fin")
        ctx.bar()
        return nc

    ctx.pool_reset()
    emit_p3(ctx, {"yq": s_yq.ap(), "xq": E["xq"], "rows1": s_rows1.ap(),
                  "wo": E["wo"], "w1": E["w1"], "w2": E["w2"], "cbt": E["cbt"],
                  "ln2w8": E["ln2w8"], "scb8": E["scb8"], "bias8": E["bias8"],
                  "o_x4": s_x4.ap()})
    ctx.bar()
    if _UPTO <= 3:
        ctx.dma("sq", o_out.rearrange("(f p) t -> p f t", p=128),
                ctx.pool_t[:, 0:16 * TOK].bitcast(F32).rearrange("p (f t) -> p f t", f=8), "fin")
        ctx.wait(nc.sync, "fin")
        ctx.bar()
        return nc

    ctx.pool_reset()
    emit_p4(ctx, {"x4q": s_x4.ap(), "n1w8": E["n1w8"],
                  "o_xn4": s_ag4.ap()[0:1024, :], "o_rows4": s_rows4.ap()})
    ctx.bar()
    ctx.dma("gq", s_ag4.ap()[1024:1025, :], s_rows4.ap()[1:2, :], "gms")
    ctx.wait(nc.gpsimd, "gms")
    ctx.bar()
    coll("AllGather", QUAD, s_ag4.ap(), g_ag4.ap())
    if _UPTO <= 4:
        ctx.dma("sq", o_out.rearrange("(f p) t -> p f t", p=128),
                ctx.pool_t[:, 0:16 * TOK].bitcast(F32).rearrange("p (f t) -> p f t", f=8), "fin")
        ctx.wait(nc.sync, "fin")
        ctx.bar()
        return nc
    ag4v = g_ag4.ap().rearrange("(r d) t -> r d t", d=1032)

    ctx.pool_reset()
    emit_p5(ctx, {"xn4f": ag4v[:, 0:1024, :], "mscf4": ag4v[:, 1024:1025, :],
                  "win_z": E["win_z"], "win_x": E["win_x"], "win_B": E["win_B"],
                  "win_C": E["win_C"], "win_dt": E["win_dt"], "win_th": E["win_th"],
                  "n1w8": E["n1w8"], "alog8": E["alog8"], "dtb8": E["dtb8"],
                  "d8": E["d8"], "bnw": E["bnw"], "cnw": E["cnw"],
                  "bbias": E["bbias"], "cbias": E["cbias"], "o_yg": s_yg.ap(),
                  "o_dbga": s_dbga.ap(), "o_dbgb": s_dbgb.ap()})
    ctx.bar()
    coll("AllGather", QUAD, s_yg.ap(), g_yg.ap())
    src = bass.AP(g_yg.ap().tensor, qoff, [[S, 2048], [1, TOK]])
    ctx.dma("gq", s_ygq.ap(), src, "gsl")
    ctx.wait(nc.gpsimd, "gsl")
    ctx.bar()
    if _UPTO == 55:
        ctx.dma("sq", o_out.rearrange("a b -> (a b)"),
                s_ygq.ap().bitcast(F32).rearrange("a b -> (a b)"), "fin")
        ctx.wait(nc.sync, "fin")
        ctx.bar()
        return nc
    if _UPTO <= 5:
        ctx.dma("sq", o_out.rearrange("(f p) t -> p f t", p=128),
                ctx.pool_t[:, 0:16 * TOK].bitcast(F32).rearrange("p (f t) -> p f t", f=8), "fin")
        ctx.wait(nc.sync, "fin")
        ctx.bar()
        return nc

    ctx.pool_reset()
    emit_p6(ctx, {"ygq": s_ygq.ap(), "x4q": s_x4.ap(), "rows4": s_rows4.ap(),
                  "wout": E["wout"], "wg": E["wg"], "wu": E["wu"], "wd": E["wd"],
                  "n2w8": E["n2w8"], "o_x5": s_x5.ap()})
    ctx.bar()
    if _UPTO <= 6:
        ctx.dma("sq", o_out, s_x5.ap(), "fin")
        ctx.wait(nc.sync, "fin")
        ctx.bar()
        return nc

    ctx.pool_reset()
    emit_p8(ctx, {"x5q": s_x5.ap(),
                  "wtupT": g_wtup.ap(),
                  "wtdnT": g_wtdn.ap(),
                  "o_out": o_out})
    ctx.bar()
    return nc


def kernel(**inputs):
    maps = _stage_inputs(inputs)
    nc = build_fused()
    keys = list(_EXT_SHAPES) + _ALL_CONSTS
    ins = [{k: m[k] for k in keys} for m in maps]
    res = run_bass_kernel_spmd(nc, ins, core_ids=list(range(NCORES)))
    out = np.empty((B, S, D), np.float32)
    for c in range(NCORES):
        b, q = c // 4, c % 4
        out[b, TOK * q: TOK * (q + 1), :] = np.asarray(res.results[c]["o_out"]).T
    return out


# version 12 (final)
# speedup vs baseline: 1.0055x; 1.0031x over previous
import math
import numpy as np
import ml_dtypes
import concourse.bass as bass
import concourse.mybir as mybir
from concourse.alu_op_type import AluOpType
from concourse.bass_utils import run_bass_kernel_spmd

F32 = mybir.dt.float32
BF16 = mybir.dt.bfloat16
I32 = mybir.dt.int32
AF = mybir.ActivationFunctionType
OP = AluOpType
AX = mybir.AxisListType.X
BF = ml_dtypes.bfloat16

B = 2
S = 8192
D = 1024
TOK = 2048
NCORES = 8
HEADS = 16
HD = 64
AH = 4
FFN = 4096
NH = 32
SH = 8
P_ = 64
NST = 64
CH = 64
DMLP = 2560
DFF = 4096
NB = 32
EVLEN = 1152
NEAR_D0 = [-384, -256, -128, 0, 128]

_REL_PERM = np.array([i * 2 for i in range(32)] + [i * 2 + 1 for i in range(32)])


def _bucket_of(d):
    if d < 16:
        return d
    v = np.float32(np.log(np.float32(d) / np.float32(16.0)))
    v = v / np.float32(math.log(128 / 16)) * np.float32(16)
    b = 16 + int(v)
    return min(b, 31)


def _host_constants():
    C = {}
    sel = np.zeros((NB, EVLEN), np.float32)
    for idx in range(EVLEN):
        d = idx - 511
        if d >= 1:
            sel[_bucket_of(d), idx] = 1.0
    C["c_sel"] = sel.astype(BF)
    mask = np.zeros((1, EVLEN), np.float32)
    mask[0, : 511 + 1] = -30000.0
    C["c_maskrow"] = mask.astype(BF)
    C["c_J"] = np.eye(128, dtype=BF)[::-1].copy()
    C["c_I"] = np.eye(128, dtype=BF)
    C["c_If"] = np.eye(128, dtype=np.float32)
    C["c_onescol_f"] = np.ones((128, 1), np.float32)
    C["c_onescol_b"] = np.ones((128, 1), BF)
    C["c_onesrow_f"] = np.ones((1, 128), np.float32)
    C["c_onesrow_b"] = np.ones((1, 128), BF)
    s2 = np.zeros((128, 2), BF)
    s2[0:64, 0] = 1
    s2[64:128, 1] = 1
    C["c_sel64"] = s2
    e2 = np.zeros((2, 128), BF)
    e2[0, 0:64] = 1
    e2[1, 64:128] = 1
    C["c_exp2"] = e2
    e8 = np.zeros((4, 8, 128), BF)
    for j in range(4):
        e8[j, 2 * j, 0:64] = 1
        e8[j, 2 * j + 1, 64:128] = 1
    C["c_exppair"] = e8.reshape(32, 128)
    eA = np.zeros((8, 128), BF)
    eB = np.zeros((8, 128), BF)
    for p in range(128):
        eA[p // 32, p] = 1
        eB[4 + p // 32, p] = 1
    C["c_exp8A"] = eA
    C["c_exp8B"] = eB
    C["c_tril"] = np.triu(np.ones((128, 128), np.float32))
    C["c_ramp"] = (64.0 - (np.arange(512) % 64)).astype(np.float32)[None, :]
    m8 = np.zeros((8, 256), np.float32)
    for h in range(8):
        m8[h, (h // 2) * 64:(h // 2) * 64 + 64] = 1
    C["c_mask8"] = m8
    ax = np.zeros((8, 128), BF)
    for h in range(8):
        ax[h, (h % 2) * 64:(h % 2) * 64 + 64] = 1
    C["c_aexp"] = ax
    return C


def _stage_inputs(inp):
    consts = _host_constants()
    f32 = np.float32
    AS = np.ascontiguousarray

    def vec8(v):
        return AS(np.asarray(v, f32).reshape(8, 128).T)

    x = np.asarray(inp["L0_attn_x"], f32)
    wq = np.asarray(inp["L0_attn_Wq"], f32)
    wk = np.asarray(inp["L0_attn_Wk"], f32)
    wv = np.asarray(inp["L0_attn_Wv"], f32)
    relb = np.asarray(inp["L0_attn_rel_bias"], f32)
    scale = np.asarray(inp["L0_attn_scale"], f32)
    win = np.asarray(inp["L1_attn_W_in"], f32)
    bb = np.asarray(inp["L1_attn_B_bias"], f32)[0]
    cb = np.asarray(inp["L1_attn_C_bias"], f32)[0]
    wupT = AS(np.asarray(inp["L1_ffn_w_up"], f32).T)
    wdnT = AS(np.asarray(inp["L1_ffn_w_down"], f32).T)
    cbt = AS(np.asarray(inp["L0_ffn_CB"], f32).T)
    z0, x0, b0, c0, d0, t0 = 0, 2048, 4096, 6144, 8192, 8224

    maps = []
    for c in range(NCORES):
        b = c // 4
        q = c % 4
        h4 = list(range(4 * q, 4 * q + 4))
        h8 = list(range(8 * q, 8 * q + 8))
        m = dict(consts)
        m["xq"] = AS(x[b, TOK * q: TOK * (q + 1), :].T)
        m["ln1w8"] = vec8(inp["L0_attn_ln1_w"])
        m["ln2w8"] = vec8(inp["L0_attn_ln2_w"])
        m["n1w8"] = vec8(inp["L1_attn_norm1_w"])
        m["n2w8"] = vec8(inp["L1_attn_norm2_w"])
        m["scb8"] = vec8(inp["L0_ffn_SCB"])
        m["bias8"] = vec8(inp["L0_ffn_bias"])
        cols4 = np.concatenate([np.arange(h * HD, (h + 1) * HD) for h in h4])
        m["wq4"] = AS(wq[:, cols4])
        m["wk4"] = AS(wk[:, cols4])
        m["wv4"] = AS(wv[:, cols4])
        m["scale4"] = AS(scale[h4][:, None])
        m["scale22"] = AS(scale[h4].reshape(2, 2).T)
        m["scale4r"] = AS(scale[h4][None, :])
        m["relb4"] = AS(relb[:, h4])
        def slabify(w, njobs):
            K_, M_ = w.shape
            kch = K_ // 128
            t4 = w.reshape(kch, 128, njobs, 128).transpose(2, 1, 0, 3)
            return AS(t4.reshape(njobs * 128, kch * 128))
        m["wo"] = slabify(np.asarray(inp["L0_attn_Wo"], f32), 8)
        m["w1"] = slabify(np.asarray(inp["L0_attn_W1"], f32), 32)
        m["w2"] = slabify(np.asarray(inp["L0_attn_W2"], f32), 8)
        m["cbt"] = slabify(cbt, 8)
        m["wout"] = slabify(np.asarray(inp["L1_attn_W_out"], f32), 8)
        m["wg"] = slabify(np.asarray(inp["L1_attn_Wg"], f32), 20)
        m["wu"] = slabify(np.asarray(inp["L1_attn_Wu"], f32), 20)
        m["wd"] = slabify(np.asarray(inp["L1_attn_Wd"], f32), 8)
        zc = np.concatenate([np.arange(z0 + h * P_, z0 + (h + 1) * P_) for h in h8])
        xc = np.concatenate([np.arange(x0 + h * P_, x0 + (h + 1) * P_) for h in h8])
        bcol = np.concatenate([b0 + h * NST + _REL_PERM for h in h8])
        ccol = np.concatenate([c0 + h * NST + _REL_PERM for h in h8])
        dtc = d0 + np.array(h8)
        thc = np.concatenate([np.arange(t0 + h * 32, t0 + (h + 1) * 32) for h in h8])
        m["win_z"] = AS(win[:, zc])
        m["win_x"] = AS(win[:, xc])
        m["win_B"] = AS(win[:, bcol])
        m["win_C"] = AS(win[:, ccol])
        m["win_dt"] = AS(win[:, dtc])
        m["win_th"] = AS(win[:, thc])
        m["alog8"] = AS(np.asarray(inp["L1_attn_A_log"], f32)[h8][:, None])
        m["dtb8"] = AS(np.asarray(inp["L1_attn_dt_bias"], f32)[h8][:, None])
        m["d8"] = AS(np.asarray(inp["L1_attn_D_param"], f32)[h8][None, :])
        m["bnw"] = AS(np.asarray(inp["L1_attn_Bn_w"], f32)[_REL_PERM][None, :])
        m["cnw"] = AS(np.asarray(inp["L1_attn_Cn_w"], f32)[_REL_PERM][None, :])
        m["bbias"] = AS(bb[h8][:, _REL_PERM].reshape(1, 512))
        m["cbias"] = AS(cb[h8][:, _REL_PERM].reshape(1, 512))
        wq_ = wupT[:, 1024 * q: 1024 * (q + 1)]
        m["wupT_sl"] = AS(wq_.reshape(8, 128, 8, 128).transpose(2, 1, 0, 3)
                          .reshape(1024, 1024))
        wdq_ = wdnT[:, 256 * q: 256 * (q + 1)]
        m["wdnT_sl"] = AS(wdq_.reshape(32, 128, 2, 128).transpose(2, 1, 0, 3)
                          .reshape(256, 4096))
        maps.append(m)
    return maps


class Ctx:

    def __init__(self, nc):
        self.nc = nc
        self.cnt = {}
        self.sems = {}
        self._guards = []
        self.consts = {}

    def sem(self, name):
        if name not in self.sems:
            g = self.nc.semaphore(name)
            self.sems[name] = g.__enter__()
            self._guards.append(g)
            self.cnt[name] = 0
        return self.sems[name]

    def dma(self, queue, out, in_, semname):
        s = self.sem(semname)
        eng = self.nc.sync if queue == "sq" else self.nc.gpsimd
        eng.dma_start(out=out, in_=in_).then_inc(s, 16)
        self.cnt[semname] += 16

    def tag(self, instr, semname, v=1):
        s = self.sem(semname)
        instr.then_inc(s, v)
        self.cnt[semname] += v

    def wait(self, eng, semname):
        eng.wait_ge(self.sem(semname), self.cnt[semname])

    def lagwait(self, eng, semname, lag):
        tgt = self.cnt.get(semname, 0) - lag
        if tgt > 0:
            eng.wait_ge(self.sem(semname), tgt)

    def bar(self):
        self.nc.all_engine_barrier()

    def make_pool(self, reserve=512):
        rem = self.nc.sbuf_bytes_remaining - reserve
        self.pool_els = (rem // 2) & ~15
        self.pool_t = self.nc.alloc_sbuf_tensor("ctxpool", [128, self.pool_els], BF16)
        self.pool_off = 0

    def pool_reset(self):
        self.pool_off = 0

    def sb(self, name, shape, dtype):
        p, n = shape
        els = n if dtype == BF16 else n * 2
        off = self.pool_off
        self.pool_off = (off + els + 15) & ~15
        assert self.pool_off <= self.pool_els,             f"pool overflow at {name}: {self.pool_off} > {self.pool_els}"
        v = self.pool_t[:, off:off + els]
        if dtype != BF16:
            v = v.bitcast(dtype)
        if p < 128:
            v = v[0:p, :]
        return v

    def prep_consts(self, vals):
        self.ccol = getattr(self, "ccol", {})
        for v in vals:
            if v not in self.ccol:
                tname = "cc%d" % len(self.ccol)
                tile = self.nc.alloc_sbuf_tensor(tname, [128, 1], F32)
                self.nc.vector.memset(tile[:], float(v))
                self.ccol[v] = tile

    def cc(self, v):
        return self.ccol[v][:, 0:1]

    def load_consts(self, names):
        nc = self.nc
        shp = {"c_sel": ([NB, EVLEN], BF16), "c_maskrow": ([1, EVLEN], BF16),
               "c_J": ([128, 128], BF16), "c_I": ([128, 128], BF16),
               "c_If": ([128, 128], F32),
               "c_onescol_f": ([128, 1], F32), "c_onescol_b": ([128, 1], BF16),
               "c_onesrow_f": ([1, 128], F32), "c_onesrow_b": ([1, 128], BF16),
               "c_sel64": ([128, 2], BF16), "c_exp2": ([2, 128], BF16),
               "c_exppair": ([32, 128], BF16), "c_exp8A": ([8, 128], BF16),
               "c_exp8B": ([8, 128], BF16), "c_tril": ([128, 128], F32),
               "c_ramp": ([1, 512], F32), "c_mask8": ([8, 256], F32),
               "c_aexp": ([8, 128], BF16)}
        self.cdram = getattr(self, "cdram", {})
        for n in names:
            sh, dt = shp[n]
            ap = self.nc.dram_tensor(n, sh, dt, kind="ExternalInput").ap()
            self.cdram[n] = ap
            sb = nc.alloc_sbuf_tensor("s" + n, sh, dt)
            self.dma("sq", sb[:], ap, "ldc")
            self.consts[n] = sb
        self.wait(nc.sync, "ldc")

    def c(self, name):
        return self.consts[name]


def _mm_stream(ctx, jobs, load_fn, mm_fn, tag, lag=1):
    import os
    if os.environ.get("MM_SERIAL"):
        nc = ctx.nc
        for i, j in enumerate(jobs):
            if ctx.cnt.get(tag, 0) > 0:
                ctx.wait(nc.gpsimd, tag)
            w = load_fn(j, i % 2)
            nc.tensor.wait_ge(ctx.sem("wl%d" % (i % 2)), w)
            mm_fn(j, i % 2)
        return
    """Depth-2 pipelined weight streaming: load_fn(job, buf)->wl_count issues the
    DMA(s) for `job` into slab half `buf`; mm_fn(job, buf) emits consumers and
    tags its last consumer instruction with `tag`."""
    nc = ctx.nc
    wl = {}
    for i, j in enumerate(jobs):
        if i == 0:
            ctx.lagwait(nc.gpsimd, tag, 1)
            wl[0] = load_fn(j, 0)
        if i + 1 < len(jobs):
            ctx.lagwait(nc.gpsimd, tag, 0)
            wl[i + 1] = load_fn(jobs[i + 1], (i + 1) % 2)
        nc.tensor.wait_ge(ctx.sem("wl%d" % (i % 2)), wl[i])
        mm_fn(j, i % 2)


def _rows_chain_ln(ctx, psSx, psSxx, rO, extra=None):
    nc = ctx.nc
    sl = rO["sl"]
    nc.vector.wait_ge(ctx.sem("mm_ev"), ctx.cnt.get("mm_ev", 0))
    nc.vector.tensor_scalar_mul(rO["mean"][0:1, sl], psSx[0:1, 0:512], 1.0 / D)
    lv = nc.vector.tensor_scalar_mul(rO["var"][0:1, sl], psSxx[0:1, 0:512], 1.0 / D)
    ctx.tag(lv, "gv")
    nc.tensor.wait_ge(ctx.sem("gv"), ctx.cnt["gv"])
    nc.vector.tensor_mul(rO["tmp"][0:1, sl], rO["mean"][0:1, sl], rO["mean"][0:1, sl])
    lv = nc.vector.tensor_sub(rO["var"][0:1, sl], rO["var"][0:1, sl], rO["tmp"][0:1, sl])
    ctx.tag(lv, "gv")
    nc.scalar.wait_ge(ctx.sem("gv"), ctx.cnt["gv"])
    la = nc.scalar.activation(rO["tmp"][0:1, sl], rO["var"][0:1, sl], AF.Sqrt,
                              bias=ctx.cc(1e-5)[0:1, :], scale=1.0)
    ctx.tag(la, "gs")
    nc.vector.wait_ge(ctx.sem("gs"), ctx.cnt["gs"])
    lv = nc.vector.reciprocal(rO["r1"][0:1, sl], rO["tmp"][0:1, sl])
    ctx.tag(lv, "gv")
    nc.vector.wait_ge(ctx.sem("gv"), ctx.cnt["gv"])


def _emit_ln_sums(ctx, xsb, fchunks, tok, scr, psSx, psSxx, tslice, src_off=0):
    nc = ctx.nc
    ones = ctx.c("c_onescol_f")
    for fc in range(fchunks):
        xs = xsb[:, src_off + fc * tok + tslice.start: src_off + fc * tok + tslice.stop]
        a = nc.scalar.activation(scr[:, 0:512], xs, AF.Square, bias=0.0, scale=1.0)
        ctx.tag(a, "sq_ev")
        nc.tensor.wait_ge(ctx.sem("sq_ev"), ctx.cnt["sq_ev"])
        m1 = nc.tensor.matmul(psSx[0:1, 0:512], ones[:, 0:1], xs,
                              start=(fc == 0), stop=(fc == fchunks - 1))
        m2 = nc.tensor.matmul(psSxx[0:1, 0:512], ones[:, 0:1], scr[:, 0:512],
                              start=(fc == 0), stop=(fc == fchunks - 1))
        ctx.tag(m2, "mm_ev")
        nc.scalar.wait_ge(ctx.sem("mm_ev"), ctx.cnt["mm_ev"])


def emit_p1(ctx, EX):
    nc = ctx.nc
    xq = EX["xq"]
    ln1w8 = EX["ln1w8"]
    wup_sl = EX["wupT_sl"]
    wdn_sl = EX["wdnT_sl"]
    o_xn1 = EX["o_xn1"]
    o_rows = EX["o_rows"]
    o_absp = EX["o_absp"]

    xsb = ctx.sb("xsb", [128, 8 * TOK], F32)
    scr = ctx.sb("scr", [128, 512], F32)
    w8 = ctx.sb("w8", [128, 8], F32)
    xn1 = ctx.sb("xn1", [128, 8 * TOK], BF16)
    r_mean = ctx.sb("r_mean", [1, TOK], F32)
    r_var = ctx.sb("r_var", [1, TOK], F32)
    r_r1 = ctx.sb("r_r1", [1, TOK], F32)
    r_tmp = ctx.sb("r_tmp", [1, TOK], F32)
    r_s = ctx.sb("r_s", [1, TOK], F32)
    r_ms = ctx.sb("r_ms", [1, TOK], F32)
    r_mr1 = ctx.sb("r_mr1", [1, TOK], F32)
    psSx = ctx.pbig2[0:1, 0:512]
    psSxx = ctx.pbig2[0:1, 512:1024]
    psB = ctx.pbig2[0:128, 1024:1536]

    ctx.dma("sq", xsb[:].rearrange("p (f t) -> p f t", f=8),
            xq.rearrange("(f p) t -> p f t", p=128), "ld")
    ctx.dma("sq", w8[:], ln1w8, "ld")
    ctx.wait(nc.sync, "ld")
    ctx.bar()

    rO = {"mean": r_mean, "var": r_var, "r1": r_r1, "tmp": r_tmp}
    for tt in range(TOK // 512):
        tsl = slice(tt * 512, (tt + 1) * 512)
        _emit_ln_sums(ctx, xsb, 8, TOK, scr, psSx, psSxx, tsl)
        rO["sl"] = tsl
        _rows_chain_ln(ctx, psSx, psSxx, rO)
    ctx.bar()
    nc.vector.tensor_mul(r_tmp[0:1, :], r_r1[0:1, :], r_r1[0:1, :])
    ctx.bar()
    nc.vector.tensor_mul(r_tmp[0:1, :], r_tmp[0:1, :], r_var[0:1, :])
    ctx.bar()
    nc.scalar.activation(r_tmp[0:1, :], r_tmp[0:1, :], AF.Sqrt,
                         bias=ctx.cc(1e-6)[0:1, :], scale=1.0)
    ctx.bar()
    nc.vector.reciprocal(r_s[0:1, :], r_tmp[0:1, :])
    ctx.bar()
    nc.vector.tensor_mul(r_s[0:1, :], r_s[0:1, :], r_r1[0:1, :])
    ctx.bar()
    nc.vector.tensor_mul(r_ms[0:1, :], r_s[0:1, :], r_mean[0:1, :])
    nc.vector.tensor_mul(r_mr1[0:1, :], r_r1[0:1, :], r_mean[0:1, :])
    ctx.bar()
    onesrow = ctx.c("c_onesrow_f")
    for tt in range(TOK // 512):
        tsl = slice(tt * 512, (tt + 1) * 512)
        m = nc.tensor.matmul(psB[:, 0:512], onesrow[0:1, :], r_s[0:1, tsl],
                             start=True, stop=True)
        ctx.tag(m, "bc_ev")
        nc.vector.wait_ge(ctx.sem("bc_ev"), ctx.cnt["bc_ev"])
        last = None
        for fc in range(8):
            off = fc * TOK + tt * 512
            nc.vector.tensor_tensor(scr[:, 0:512], xsb[:, off:off + 512],
                                    psB[:, 0:512], OP.mult)
            last = nc.vector.tensor_scalar(xn1[:, off:off + 512], scr[:, 0:512],
                                           w8[:, fc:fc + 1], None, OP.mult)
        ctx.tag(last, "xn_ev")
        nc.tensor.wait_ge(ctx.sem("xn_ev"), ctx.cnt["xn_ev"])
    ctx.bar()
    ctx.dma("sq", o_xn1.rearrange("(f p) t -> p f t", p=128),
            xn1[:].rearrange("p (f t) -> p f t", f=8), "st")
    ctx.dma("sq", o_rows[0:1, :], r_s[0:1, :], "st")
    ctx.dma("sq", o_rows[1:2, :], r_ms[0:1, :], "st")
    ctx.dma("sq", o_rows[2:3, :], r_r1[0:1, :], "st")
    ctx.dma("sq", o_rows[3:4, :], r_mr1[0:1, :], "st")

    wld = ctx.sb("wld", [128, 1024], F32)
    absw = ctx.sb("absw", [128, 1024], F32)
    psS2 = ctx.pbig[0:1, 0:1024]
    accr = ctx.sb("accr", [1, 4], F32)
    ones = ctx.c("c_onescol_f")
    for mi, (ap, po) in enumerate([(wup_sl, 0), (wdn_sl, 512)]):
        nr, ncol = ap.shape[0] // 128, ap.shape[1] // 1024
        blocks = [ap[r * 128:(r + 1) * 128, c * 1024:(c + 1) * 1024]
                  for r in range(nr) for c in range(ncol)]
        ntile, cols = len(blocks), 1024
        for k in range(ntile):
            ctx.dma("sq", wld[:, 0:cols], blocks[k], "wl")
            ctx.wait(nc.scalar, "wl")
            a = nc.scalar.activation(absw[:, 0:cols], wld[:, 0:cols], AF.Abs,
                                     bias=0.0, scale=1.0)
            ctx.tag(a, "ab_ev")
            nc.tensor.wait_ge(ctx.sem("ab_ev"), ctx.cnt["ab_ev"])
            last_mm = None
            for j in range(max(1, cols // 512)):
                w = min(512, cols)
                last_mm = nc.tensor.matmul(
                    psS2[0:1, po + 0:po + w], ones[:, 0:1],
                    absw[:, j * 512:j * 512 + w],
                    start=(k == 0 and j == 0), stop=(k == ntile - 1 and j == max(0, cols // 512 - 1)))
            ctx.tag(last_mm, "wm_ev")
            nc.sync.wait_ge(ctx.sem("ab_ev"), ctx.cnt["ab_ev"])
            nc.scalar.wait_ge(ctx.sem("wm_ev"), ctx.cnt["wm_ev"])
    ctx.bar()
    nc.vector.tensor_reduce(accr[0:1, 0:1], psS2[0:1, 0:512], AX, OP.add)
    nc.vector.tensor_reduce(accr[0:1, 1:2], psS2[0:1, 512:1024], AX, OP.add)
    ctx.bar()
    ctx.dma("sq", o_absp, accr[0:1, 0:2], "st")
    ctx.wait(nc.sync, "st")
    ctx.bar()
    return nc


def emit_p2(ctx, EX):
    import os
    _UPTO = int(os.environ.get("P2_UPTO", "9"))
    nc = ctx.nc
    xn1f = EX["xn1f"]
    msf4 = EX["msf4"]
    wq4 = EX["wq4"]
    wk4 = EX["wk4"]
    wv4 = EX["wv4"]
    ln1w8 = EX["ln1w8"]
    scale4 = EX["scale4"]
    scale22 = EX["scale22"]
    scale4r = EX["scale4r"]
    relb4 = EX["relb4"]
    o_y = EX["o_y"]
    ev_hbm = nc.dram_tensor("ev_hbm", [4, EVLEN], BF16, kind="Internal")


    wqb = ctx.sb("wqb", [128, 8 * 256], BF16)
    wkb = ctx.sb("wkb", [128, 8 * 256], BF16)
    wvb = ctx.sb("wvb", [128, 8 * 256], BF16)
    ln1b = ctx.sb("ln1b", [128, 8], BF16)
    ln1f = ctx.sb("ln1f", [128, 8], F32)
    pool = ctx.sb("pool", [128, 18 * 1024], BF16)
    cwsb = ctx.sb("cwsb", [128, 6], BF16)
    cwrows = ctx.sb("cwrows", [1, 6 * 128], BF16)
    xt = [pool[:, 0:4096], pool[:, 4096:8192]]
    msneg = pool[0:1, 8192:16384]
    qsb = ctx.sb("qsb", [128, 2 * S], BF16)
    ksb = ctx.sb("ksb", [128, 2 * S], BF16)
    vtok = ctx.sb("vtok", [128, 64 * 260], BF16)
    sqsb = ctx.sb("sqsb", [128, 512], BF16)
    rsqb = ctx.sb("rsqb", [2, 512], BF16)
    rsqf = ctx.sb("rsqf", [2, 512], F32)
    bcq = ctx.sb("bcq", [128, 512], BF16)
    vtmp = ctx.sb("vtmp", [128, 512], BF16)
    scl4 = ctx.sb("scl4", [4, 1], F32)
    scl22 = ctx.sb("scl22", [2, 2], F32)
    scl4r = ctx.sb("scl4r", [1, 4], F32)
    rb4 = ctx.sb("rb4", [NB, 4], F32)
    srel = ctx.sb("srel", [NB, 4], F32)
    srelb = ctx.sb("srelb", [NB, 4], BF16)
    srelT = ctx.sb("srelT", [4, NB], F32)
    mrow = ctx.sb("mrow", [4, 4], F32)
    mnegbc = ctx.sb("mnegbc", [128, 8], F32)
    rowt = ctx.sb("rowt", [1, 4], F32)
    evsb = ctx.sb("evsb", [4, EVLEN], BF16)
    l2tmp = ctx.sb("l2tmp", [128, 512], BF16)
    esb = pool[:, 0:10240]
    phat = pool[:, 10240:11776]
    phat2 = pool[:, 11776:13312]
    vmean = ctx.sb("vmean", [64, 4], F32)
    srow = ctx.sb("srow", [128, 512], F32)
    ones64 = ctx.sb("ones64", [128, 64], F32)
    bcd = ctx.sb("bcd", [64, 512], F32)
    ysb = ctx.sb("ysb", [128, S], BF16)

    pbig = ctx.pbig
    pbig2 = ctx.pbig2
    ptb = ctx.ptb

    ctx.dma("gq", wqb[:].rearrange("p (kc m) -> p kc m", kc=8),
            wq4.rearrange("(kc p) m -> p kc m", p=128), "gl")
    ctx.dma("gq", wkb[:].rearrange("p (kc m) -> p kc m", kc=8),
            wk4.rearrange("(kc p) m -> p kc m", p=128), "gl")
    ctx.dma("gq", wvb[:].rearrange("p (kc m) -> p kc m", kc=8),
            wv4.rearrange("(kc p) m -> p kc m", p=128), "gl")
    ctx.dma("sq", ln1f[:], ln1w8, "ld")
    for rr in range(4):
        ctx.dma("sq", msneg[0:1, rr * TOK:(rr + 1) * TOK], msf4[rr], "ld")
    ctx.dma("sq", scl4[:], scale4, "ld")
    ctx.dma("sq", scl22[:], scale22, "ld")
    ctx.dma("sq", scl4r[:], scale4r, "ld")
    ctx.dma("sq", rb4[:], relb4, "ld")
    ctx.wait(nc.sync, "ld")
    ctx.wait(nc.gpsimd, "gl")
    ctx.bar()
    nc.vector.memset(vtok[:], 1.0)
    nc.vector.memset(ones64[64:65, :], 1.0)
    nc.vector.tensor_copy(ln1b[:], ln1f[:])
    nc.vector.tensor_scalar_mul(msneg, msneg, -1.0)
    ctx.bar()

    for wi, wb in enumerate([wqb, wkb, wvb]):
        for mi in range(2):
            for kc in range(8):
                nc.tensor.matmul(pbig[:, wi * 2 + mi: wi * 2 + mi + 1],
                                 wb[:, kc * 256 + mi * 128: kc * 256 + (mi + 1) * 128],
                                 ln1b[:, kc:kc + 1], start=(kc == 0), stop=(kc == 7))
    ctx.bar()
    nc.vector.tensor_copy(cwsb[:, 0:6], pbig[:, 0:6])
    ctx.bar()
    for j in range(6):
        nc.tensor.transpose(ptb[0:1, 0:128], cwsb[:, j:j + 1], ctx.c("c_I")[:])
        ctx.bar()
        nc.vector.tensor_copy(cwrows[0:1, j * 128:(j + 1) * 128], ptb[0:1, 0:128])
        ctx.bar()

    for tt in range(S // 512 if _UPTO >= 2 else 0):
        xb = xt[tt % 2]
        rank = (tt * 512) // TOK
        toff = (tt * 512) % TOK
        ctx.dma("sq", xb.rearrange("p (kc t) -> p kc t", kc=8),
                xn1f[rank, :, toff:toff + 512].rearrange("(kc p) t -> p kc t", p=128),
                "xl")
        ctx.wait(nc.sync, "xl")
        ctx.bar()
        for wi, wb in enumerate([wqb, wkb, wvb]):
            for mi in range(2):
                pb = pbig[:, (mi % 2) * 512:(mi % 2) * 512 + 512]
                for kc in range(8):
                    nc.tensor.matmul(pb, wb[:, kc * 256 + mi * 128: kc * 256 + (mi + 1) * 128],
                                     xb[:, kc * 512:(kc + 1) * 512],
                                     start=(kc == 0), stop=False)
                ci6 = wi * 2 + mi
                nc.tensor.matmul(pb, cwrows[0:1, ci6 * 128:(ci6 + 1) * 128],
                                 msneg[0:1, tt * 512:(tt + 1) * 512],
                                 start=False, stop=True)
                ctx.bar()
                if wi < 2:
                    nc.scalar.activation(sqsb[:, :], pb, AF.Square, bias=0.0, scale=1.0)
                    ctx.bar()
                    nc.tensor.matmul(pbig2[0:2, 0:512], ctx.c("c_sel64")[:, :], sqsb[:, :],
                                     start=True, stop=True)
                    ctx.bar()
                    nc.scalar.activation(rsqf[0:2, :], pbig2[0:2, 0:512], AF.Sqrt,
                                         bias=ctx.cc(1e-6)[0:2, :], scale=1.0)
                    ctx.bar()
                    nc.vector.reciprocal(rsqf[0:2, :], rsqf[0:2, :])
                    ctx.bar()
                    if wi == 0:
                        nc.vector.tensor_scalar(rsqf[0:2, :], rsqf[0:2, :],
                                                scl22[:, mi:mi + 1], None, OP.mult)
                        ctx.bar()
                    nc.vector.tensor_copy(rsqb[0:2, :], rsqf[0:2, :])
                    ctx.bar()
                    nc.tensor.matmul(pbig2[:, 512:1024], ctx.c("c_exp2")[:, :], rsqb[0:2, :],
                                     start=True, stop=True)
                    ctx.bar()
                    nc.vector.tensor_copy(bcq[:, :], pbig2[:, 512:1024])
                    ctx.bar()
                    dst = (qsb if wi == 0 else ksb)
                    nc.vector.tensor_tensor(dst[:, mi * S + tt * 512: mi * S + (tt + 1) * 512],
                                            pb, bcq[:, :], OP.mult)
                    ctx.bar()
                else:
                    nc.vector.tensor_copy(vtmp[:, :], pb)
                    ctx.bar()
                    for q4 in range(4):
                        tc = tt * 4 + q4
                        nc.tensor.transpose(ptb[:, 0:128], vtmp[:, q4 * 128:(q4 + 1) * 128],
                                            ctx.c("c_I")[:])
                        ctx.bar()
                        base = tc * 260
                        nc.vector.tensor_copy(vtok[:, base + (2 * mi) * 65: base + (2 * mi) * 65 + 64],
                                              ptb[:, 0:64])
                        nc.vector.tensor_copy(vtok[:, base + (2 * mi + 1) * 65: base + (2 * mi + 1) * 65 + 64],
                                              ptb[:, 64:128])
                        ctx.bar()

    if _UPTO < 3:
        return _p2_finish(ctx, o_y, ysb)
    nc.tensor.matmul(pbig[0:NB, 0:4], ctx.c("c_onesrow_f")[0:1, 0:NB], scl4r[0:1, :],
                     start=True, stop=True)
    ctx.bar()
    nc.vector.tensor_tensor(srel[:, :], rb4[:, :], pbig[0:NB, 0:4], OP.mult)
    ctx.bar()
    nc.vector.tensor_copy(srelb[:, :], srel[:, :])
    ctx.bar()
    nc.tensor.transpose(pbig[0:4, 0:NB], srel[:, :], ctx.c("c_If")[0:NB, 0:NB])
    ctx.bar()
    nc.vector.tensor_copy(srelT[:, :], pbig[0:4, 0:NB])
    ctx.bar()
    nc.vector.tensor_reduce(mrow[0:4, 0:1], srelT[:, :], AX, OP.max)
    nc.scalar.activation(mrow[0:4, 1:2], scl4[0:4, 0:1], AF.Abs, bias=0.0, scale=1.0)
    ctx.bar()
    nc.vector.tensor_scalar_mul(mrow[0:4, 1:2], mrow[0:4, 1:2], 1.02)
    ctx.bar()
    nc.vector.tensor_add(mrow[0:4, 2:3], mrow[0:4, 0:1], mrow[0:4, 1:2])
    ctx.bar()
    nc.vector.tensor_sub(mrow[0:4, 3:4], mrow[0:4, 2:3], srelT[:, 31:32])
    ctx.bar()
    for ci, col in enumerate([2, 3]):
        nc.tensor.transpose(pbig[0:1, 0:4], mrow[0:4, col:col + 1], ctx.c("c_If")[0:4, 0:4])
        ctx.bar()
        nc.vector.tensor_scalar_mul(rowt[0:1, :], pbig[0:1, 0:4], -1.0)
        ctx.bar()
        nc.tensor.matmul(pbig[:, 0:4], ctx.c("c_onesrow_f")[0:1, :], rowt[0:1, :],
                         start=True, stop=True)
        ctx.bar()
        nc.vector.tensor_copy(mnegbc[:, ci * 4:(ci + 1) * 4], pbig[:, 0:4])
        ctx.bar()
    for j, (o, w) in enumerate([(0, 512), (512, 512), (1024, 128)]):
        nc.tensor.matmul(pbig[0:4, o:o + w], srelb[:, :], ctx.c("c_sel")[:, o:o + w],
                         start=True, stop=False)
        nc.tensor.matmul(pbig[0:4, o:o + w], ctx.c("c_onesrow_b")[0:1, 0:4],
                         ctx.c("c_maskrow")[0:1, o:o + w], start=False, stop=True)
    ctx.bar()
    nc.scalar.activation(evsb[:, :], pbig[0:4, 0:EVLEN], AF.Exp, bias=0.0, scale=1.0)
    ctx.bar()
    ctx.dma("sq", ev_hbm.ap(), evsb[:, :], "ev")
    ctx.wait(nc.sync, "ev")
    ctx.bar()
    for h in range(4):
        for ci, d0 in enumerate(NEAR_D0):
            src = bass.AP(ev_hbm.ap().tensor, h * EVLEN + d0 + 384, [[1, 128], [1, 512]])
            ctx.dma("sq", l2tmp[:], src, "el")
            ctx.wait(nc.tensor, "el")
            mm = nc.tensor.matmul(pbig[:, 0:512], ctx.c("c_J")[:], l2tmp[:], start=True, stop=True)
            ctx.tag(mm, "elm")
            nc.vector.wait_ge(ctx.sem("elm"), ctx.cnt["elm"])
            vv = nc.vector.tensor_copy(esb[:, (h * 5 + ci) * 512:(h * 5 + ci + 1) * 512],
                                       pbig[:, 0:512])
            ctx.tag(vv, "elc")
            nc.tensor.wait_ge(ctx.sem("elc"), ctx.cnt["elc"])
            nc.sync.wait_ge(ctx.sem("elc"), ctx.cnt["elc"])

    if _UPTO < 4:
        return _p2_finish(ctx, o_y, ysb)
    for tc in range(64):
        for h in range(4):
            nc.tensor.matmul(pbig[0:64, h:h + 1],
                             vtok[:, tc * 260 + h * 65: tc * 260 + h * 65 + 64],
                             ctx.c("c_onescol_b")[:, 0:1],
                             start=(tc == 0 and h == 0), stop=(tc == 63 and h == 3))
    ctx.bar()
    nc.vector.tensor_scalar_mul(vmean[:, :], pbig[0:64, 0:4], 1.0 / S)
    ctx.bar()

    if _UPTO < 5:
        return _p2_finish(ctx, o_y, ysb)
    _NQT = int(os.environ.get("P2_NQT", "16"))
    _NH = int(os.environ.get("P2_NH", "4"))
    exc = {}
    avc = {}
    for h in range(_NH):
        prow = (h % 2) * 64
        fofs = (h // 2) * S
        if h in (2, 3):
            nc.vector.wait_ge(ctx.sem("st"), ctx.cnt["st"])
        for qt in range(_NQT):
            qs = qt * 512
            far = list(range(0, max(0, 4 * qt - 1)))
            near = [kc for kc in range(max(0, 4 * qt - 1), 4 * qt + 4)]
            groups = [far[i:i + 3] for i in range(0, len(far), 3)]
            groups = [(g, False) for g in groups if g] + [(near[0:3], True)]
            if len(near) > 3:
                groups.append((near[3:], True))
            ntot = len(far) + len(near)
            nav = 0
            first_av = True
            glist = []

            def sbank(gi, bi):
                if gi % 2 == 0:
                    return pbig[:, bi * 512:(bi + 1) * 512]
                return (pbig[:, 1536:2048] if bi == 0 else
                        pbig2[:, (bi - 1) * 512:bi * 512])

            def embank(gi, bi):
                return (phat if gi % 2 == 0 else phat2)[:, bi * 512:(bi + 1) * 512]

            def emit_av(gi):
                nonlocal nav, first_av
                g, is_near = groups[gi]
                nc.tensor.wait_ge(ctx.sem("ex"), exc[gi])
                if is_near:
                    nc.tensor.wait_ge(ctx.sem("em"), emc[gi])
                last = None
                for bi, kc in enumerate(g):
                    nav += 1
                    last = nc.tensor.matmul(
                        pbig2[0:65, 1024:1536],
                        vtok[:, kc * 260 + h * 65: kc * 260 + (h + 1) * 65],
                        embank(gi, bi), start=first_av, stop=(nav == ntot),
                        skip_group_check=True)
                    first_av = False
                ctx.tag(last, "av")
                avc[gi] = ctx.cnt["av"]

            emc = {}
            prev_ex = ctx.cnt.get("ex", 0)
            prev_av = ctx.cnt.get("av", 0)
            for gi, (g, is_near) in enumerate(groups):
                nc.tensor.wait_ge(ctx.sem("ex"), exc[gi - 2] if gi >= 2 else prev_ex)
                last = None
                for bi, kc in enumerate(g):
                    last = nc.tensor.matmul(
                        sbank(gi, bi),
                        ksb[prow:prow + 64, fofs + kc * 128: fofs + (kc + 1) * 128],
                        qsb[prow:prow + 64, fofs + qs: fofs + qs + 512],
                        start=True, stop=True)
                ctx.tag(last, "qk")
                nc.scalar.wait_ge(ctx.sem("qk"), ctx.cnt["qk"])
                nc.scalar.wait_ge(ctx.sem("av"), avc[gi - 2] if gi >= 2 else prev_av)
                mcol = h if is_near else 4 + h
                la = None
                if gi % 2 == 0:
                    la = nc.scalar.activation(
                        (phat if gi % 2 == 0 else phat2)[:, 0:len(g) * 512],
                        pbig[:, 0:len(g) * 512], AF.Exp,
                        bias=mnegbc[:, mcol:mcol + 1], scale=1.0)
                else:
                    la = nc.scalar.activation(phat2[:, 0:512], pbig[:, 1536:2048],
                                              AF.Exp, bias=mnegbc[:, mcol:mcol + 1],
                                              scale=1.0)
                    if len(g) > 1:
                        la = nc.scalar.activation(phat2[:, 512:len(g) * 512],
                                                  pbig2[:, 0:(len(g) - 1) * 512],
                                                  AF.Exp, bias=mnegbc[:, mcol:mcol + 1],
                                                  scale=1.0)
                ctx.tag(la, "ex")
                exc[gi] = ctx.cnt["ex"]
                if is_near:
                    nc.vector.wait_ge(ctx.sem("ex"), exc[gi])
                    lv = None
                    for bi, kc in enumerate(g):
                        d0 = qs - kc * 128
                        ci = NEAR_D0.index(d0)
                        lv = nc.vector.tensor_tensor(
                            embank(gi, bi), embank(gi, bi),
                            esb[:, (h * 5 + ci) * 512:(h * 5 + ci + 1) * 512], OP.mult)
                    ctx.tag(lv, "em")
                    emc[gi] = ctx.cnt["em"]
                if gi >= 1:
                    emit_av(gi - 1)
            emit_av(len(groups) - 1)
            if qt == 0:
                nc.vector.wait_ge(ctx.sem("av"), ctx.cnt["av"])
                nc.vector.tensor_copy(pbig2[0:64, 1024:1025], vmean[:, h:h + 1])
                lv = nc.vector.memset(pbig2[64:65, 1024:1025], 1.0)
                ctx.tag(lv, "q0")
            nc.vector.wait_ge(ctx.sem("av"), ctx.cnt["av"])
            lv = nc.vector.reciprocal(srow[64:65, 0:512], pbig2[64:65, 1024:1536])
            ctx.tag(lv, "rc")
            nc.tensor.wait_ge(ctx.sem("rc"), ctx.cnt["rc"])
            m = nc.tensor.matmul(pbig[0:64, 0:512], ones64[64:65, 0:64],
                                 srow[64:65, 0:512], start=True, stop=True)
            ctx.tag(m, "nb")
            nc.vector.wait_ge(ctx.sem("nb"), ctx.cnt["nb"])
            nc.vector.tensor_copy(bcd[:, :], pbig[0:64, 0:512])
            lv = nc.vector.tensor_tensor(ysb[prow:prow + 64, qs: qs + 512],
                                         pbig2[0:64, 1024:1536], bcd[:, :], OP.mult)
            ctx.tag(lv, "yw")
            nc.tensor.wait_ge(ctx.sem("yw"), ctx.cnt["yw"])
        if h % 2 == 1:
            nc.sync.wait_ge(ctx.sem("yw"), ctx.cnt["yw"])
            ctx.dma("sq", o_y[(h - 1) * 64:(h + 1) * 64, :], ysb[:, :], "st")
        if h == _NH - 1:
            ctx.wait(nc.sync, "st")
    ctx.bar()
    return nc


def emit_p3(ctx, EX):
    nc = ctx.nc
    yq = EX["yq"]
    xq = EX["xq"]
    rows1 = EX["rows1"]
    wo = EX["wo"]
    w1 = EX["w1"]
    w2 = EX["w2"]
    cbt = EX["cbt"]
    ln2w8 = EX["ln2w8"]
    scb8 = EX["scb8"]
    bias8 = EX["bias8"]
    o_x4 = EX["o_x4"]

    xsb = ctx.sb("xsb", [128, 8 * TOK], F32)
    yinner = ctx.sb("yinner", [128, 8 * 512], F32)
    hbuf = ctx.sb("hbuf3", [128, 16384], BF16)
    pool = ctx.sb("pool3", [128, 16384], BF16)
    wslab = ctx.sb("wslab", [128, 2 * 4096], BF16)
    scr = ctx.sb("scr3", [128, 512], F32)
    scr3b = ctx.sb("scr3b", [128, 1024], F32)
    cwrow = ctx.sb("cwrow", [1, 8 * 128], BF16)
    cwcol = ctx.sb("cwcol", [128, 8], BF16)
    w8a = ctx.sb("w8a", [128, 8], F32)
    w8b = ctx.sb("w8b", [128, 8], F32)
    w8c = ctx.sb("w8c", [128, 8], F32)
    r_r1 = ctx.sb("r_r1", [1, TOK], BF16)
    r_mr1n = ctx.sb("r_mr1n", [1, TOK], BF16)
    r_mbrb = ctx.sb("r_mbrb", [1, TOK], BF16)
    r_rbf = ctx.sb("r_rbf", [1, TOK], F32)
    t_mean = ctx.sb("t_mean", [1, 512], F32)
    t_var = ctx.sb("t_var", [1, 512], F32)
    t_tmp = ctx.sb("t_tmp", [1, 512], F32)
    t_r = ctx.sb("t_r", [1, 512], F32)
    pbig = ctx.pbig
    pbig2 = ctx.pbig2
    ptb = ctx.ptb

    ysb2 = pool[:, 0:16384]

    ctx.dma("sq", xsb[:].rearrange("p (f t) -> p f t", f=8),
            xq.rearrange("(f p) t -> p f t", p=128), "ld")
    ctx.dma("sq", ysb2.rearrange("p (f t) -> p f t", f=8),
            yq.rearrange("(f p) t -> p f t", p=128), "ld")
    ctx.dma("gq", r_r1[0:1, :], rows1[2:3, :], "gl")
    ctx.dma("gq", r_mr1n[0:1, :], rows1[3:4, :], "gl")
    ctx.dma("sq", w8a[:], ln2w8, "ld")
    ctx.dma("sq", w8b[:], scb8, "ld")
    ctx.dma("sq", w8c[:], bias8, "ld")
    ctx.wait(nc.sync, "ld")
    ctx.wait(nc.gpsimd, "gl")
    ctx.bar()
    nc.vector.tensor_scalar_mul(w8b[:, :], w8b[:, :], 1.0 / 127.0)
    nc.vector.tensor_scalar_mul(r_mr1n[0:1, :], r_mr1n[0:1, :], -1.0)
    ctx.bar()

    def stream_w(ap, job, kchunks, buf):
        ctx.dma("gq", wslab[:, buf * 4096: buf * 4096 + kchunks * 128],
                ap[job * 128:(job + 1) * 128, :], "wl%d" % buf)
        return ctx.cnt["wl%d" % buf]

    xn2 = pool[:, 0:16384]
    for tt in range(TOK // 512):
        ts = slice(tt * 512, (tt + 1) * 512)
        m = nc.tensor.matmul(pbig2[:, 1024:1536], ctx.c("c_onesrow_b")[0:1, :],
                             r_r1[0:1, ts], start=True, stop=True)
        def ld_wo(fc, buf):
            return stream_w(wo, fc, 8, buf)

        def mm_wo(fc, buf):
            bo = buf * 4096
            pb = pbig[:, (fc % 2) * 512:(fc % 2) * 512 + 512]
            ctx.lagwait(nc.tensor, "ev3", 1)
            for kc in range(8):
                nc.tensor.matmul(pb, wslab[:, bo + kc * 128: bo + (kc + 1) * 128],
                                 ysb2[:, kc * TOK + tt * 512: kc * TOK + (tt + 1) * 512],
                                 start=(kc == 0), stop=False)
            mmm = nc.tensor.matmul(pb, ctx.c("c_onesrow_b")[0:1, :], r_mr1n[0:1, ts],
                                   start=False, stop=True)
            ctx.tag(mmm, "mm3")
            nc.vector.wait_ge(ctx.sem("mm3"), ctx.cnt["mm3"])
            nc.vector.tensor_tensor(scr3b[:, (fc % 2) * 512:(fc % 2) * 512 + 512],
                                    xsb[:, fc * TOK + tt * 512: fc * TOK + (tt + 1) * 512],
                                    pbig2[:, 1024:1536], OP.mult)
            vv = nc.vector.tensor_add(yinner[:, fc * 512:(fc + 1) * 512],
                                      scr3b[:, (fc % 2) * 512:(fc % 2) * 512 + 512], pb)
            ctx.tag(vv, "ev3")

        _mm_stream(ctx, list(range(8)), ld_wo, mm_wo, "mm3")
        ctx.wait(nc.tensor, "ev3")
        ctx.bar()
        _emit_ln_sums(ctx, yinner, 8, 512, scr, pbig2[0:1, 0:512], pbig2[0:1, 512:1024],
                      slice(0, 512))
        nc.vector.wait_ge(ctx.sem("mm_ev"), ctx.cnt["mm_ev"])
        lv = nc.vector.tensor_scalar_mul(t_tmp[0:1, :], pbig2[0:1, 512:1024], 1.0 / D)
        ctx.tag(lv, "gv")
        nc.scalar.wait_ge(ctx.sem("gv"), ctx.cnt["gv"])
        nc.tensor.wait_ge(ctx.sem("gv"), ctx.cnt["gv"])
        la = nc.scalar.activation(t_tmp[0:1, :], t_tmp[0:1, :], AF.Sqrt,
                                  bias=ctx.cc(1e-6)[0:1, :], scale=1.0)
        ctx.tag(la, "gs")
        nc.vector.wait_ge(ctx.sem("gs"), ctx.cnt["gs"])
        lv = nc.vector.reciprocal(t_r[0:1, :], t_tmp[0:1, :])
        ctx.tag(lv, "gv")
        nc.tensor.wait_ge(ctx.sem("gv"), ctx.cnt["gv"])
        m = nc.tensor.matmul(pbig2[:, 1024:1536], ctx.c("c_onesrow_f")[0:1, :],
                             t_r[0:1, :], start=True, stop=True)
        ctx.tag(m, "bc3")
        nc.vector.wait_ge(ctx.sem("bc3"), ctx.cnt["bc3"])
        last = None
        for fc in range(8):
            off = fc * TOK + tt * 512
            yo = fc * 512
            nc.vector.tensor_tensor(scr[:, :], yinner[:, yo:yo + 512],
                                    pbig2[:, 1024:1536], OP.mult)
            nc.vector.tensor_scalar(xn2[:, off:off + 512], scr[:, :],
                                    w8a[:, fc:fc + 1], None, OP.mult)
            last = nc.vector.tensor_add(xsb[:, off:off + 512], xsb[:, off:off + 512],
                                        yinner[:, yo:yo + 512])
        ctx.tag(last, "x23")
        nc.tensor.wait_ge(ctx.sem("x23"), ctx.cnt["x23"])
        nc.scalar.wait_ge(ctx.sem("x23"), ctx.cnt["x23"])
    ctx.bar()

    for tt in range(TOK // 512):
        def ld_w1(hk, buf):
            return stream_w(w1, hk, 8, buf)

        def mm_w1(hk, buf):
            bo = buf * 4096
            pb = pbig[:, (hk % 2) * 512:(hk % 2) * 512 + 512]
            ctx.lagwait(nc.tensor, "f2", 1)
            last = None
            for kc in range(8):
                last = nc.tensor.matmul(pb, wslab[:, bo + kc * 128: bo + (kc + 1) * 128],
                                        xn2[:, kc * TOK + tt * 512: kc * TOK + (tt + 1) * 512],
                                        start=(kc == 0), stop=(kc == 7))
            ctx.tag(last, "f1")
            nc.scalar.wait_ge(ctx.sem("f1"), ctx.cnt["f1"])
            aa = nc.scalar.activation(hbuf[:, hk * 512:(hk + 1) * 512], pb, AF.Relu,
                                      bias=0.0, scale=1.0)
            ctx.tag(aa, "f2")

        _mm_stream(ctx, list(range(32)), ld_w1, mm_w1, "f1")
        nc.tensor.wait_ge(ctx.sem("f2"), ctx.cnt["f2"])
        ctx.bar()
        def ld_w2(fc, buf):
            return stream_w(w2, fc, 32, buf)

        def mm_w2(fc, buf):
            bo = buf * 4096
            pb = pbig[:, (fc % 2) * 512:(fc % 2) * 512 + 512]
            ctx.lagwait(nc.tensor, "f4", 1)
            last = None
            for kc in range(32):
                last = nc.tensor.matmul(pb, wslab[:, bo + kc * 128: bo + (kc + 1) * 128],
                                        hbuf[:, kc * 512: (kc + 1) * 512],
                                        start=(kc == 0), stop=(kc == 31))
            ctx.tag(last, "f3")
            nc.vector.wait_ge(ctx.sem("f3"), ctx.cnt["f3"])
            off = fc * TOK + tt * 512
            vv = nc.vector.tensor_add(xsb[:, off:off + 512], xsb[:, off:off + 512], pb)
            ctx.tag(vv, "f4")

        _mm_stream(ctx, list(range(8)), ld_w2, mm_w2, "f3")
        ctx.wait(nc.vector, "f4")
        ctx.bar()

    xnb = pool[:, 0:16384]
    for tt in range(TOK // 512):
        ts = slice(tt * 512, (tt + 1) * 512)
        _emit_ln_sums(ctx, xsb, 8, TOK, scr, pbig2[0:1, 0:512], pbig2[0:1, 512:1024], ts)
        rO = {"mean": t_mean, "var": t_var, "r1": t_r, "tmp": t_tmp, "sl": slice(0, 512)}
        _rows_chain_ln(ctx, pbig2[0:1, 0:512], pbig2[0:1, 512:1024], rO)
        nc.vector.tensor_copy(r_rbf[0:1, ts], t_r[0:1, :])
        nc.vector.tensor_mul(t_tmp[0:1, :], t_mean[0:1, :], t_r[0:1, :])
        ctx.bar()
        nc.vector.tensor_scalar_mul(r_mbrb[0:1, ts], t_tmp[0:1, :], -1.0)
        ctx.bar()
    ctx.bar()
    for fc in range(8):
        stream_w(cbt, fc, 8, 0)
        ctx.wait(nc.tensor, "wl0")
        for kc in range(8):
            nc.tensor.matmul(pbig[0:128, fc:fc + 1],
                             wslab[:, kc * 128:(kc + 1) * 128],
                             ctx.c("c_onescol_b")[:, 0:1], start=(kc == 0), stop=(kc == 7))
        ctx.bar()
    nc.vector.tensor_copy(cwcol[:, 0:8], pbig[:, 0:8])
    ctx.bar()
    for j in range(8):
        mt = nc.tensor.transpose(ptb[0:1, 0:128], cwcol[:, j:j + 1], ctx.c("c_I")[:])
        ctx.tag(mt, "cwt")
        nc.vector.wait_ge(ctx.sem("cwt"), ctx.cnt["cwt"])
        vv = nc.vector.tensor_copy(cwrow[0:1, j * 128:(j + 1) * 128], ptb[0:1, 0:128])
        ctx.tag(vv, "cwc8")
        nc.tensor.wait_ge(ctx.sem("cwc8"), ctx.cnt["cwc8"])
    ctx.bar()
    for tt in range(TOK // 512):
        ts = slice(tt * 512, (tt + 1) * 512)
        m = nc.tensor.matmul(pbig2[:, 1024:1536], ctx.c("c_onesrow_f")[0:1, :],
                             r_rbf[0:1, ts], start=True, stop=True)
        ctx.tag(m, "bcb")
        nc.vector.wait_ge(ctx.sem("bcb"), ctx.cnt["bcb"])
        last = None
        for fc in range(8):
            off = fc * TOK + tt * 512
            last = nc.vector.tensor_tensor(xnb[:, off:off + 512], xsb[:, off:off + 512],
                                           pbig2[:, 1024:1536], OP.mult)
        ctx.tag(last, "xnb")
        nc.tensor.wait_ge(ctx.sem("xnb"), ctx.cnt["xnb"])
    ctx.bar()
    for tt in range(TOK // 512):
        ts = slice(tt * 512, (tt + 1) * 512)
        def ld_cb(fc, buf):
            return stream_w(cbt, fc, 8, buf)

        def mm_cb(fc, buf):
            bo = buf * 4096
            pb = pbig[:, (fc % 2) * 512:(fc % 2) * 512 + 512]
            ctx.lagwait(nc.tensor, "cb2", 1)
            for kc in range(8):
                nc.tensor.matmul(pb, wslab[:, bo + kc * 128: bo + (kc + 1) * 128],
                                 xnb[:, kc * TOK + tt * 512: kc * TOK + (tt + 1) * 512],
                                 start=(kc == 0), stop=False)
            mmm = nc.tensor.matmul(pb, cwrow[0:1, fc * 128:(fc + 1) * 128],
                                   r_mbrb[0:1, ts], start=False, stop=True)
            ctx.tag(mmm, "cb1")
            nc.vector.wait_ge(ctx.sem("cb1"), ctx.cnt["cb1"])
            off = fc * TOK + tt * 512
            nc.vector.tensor_scalar(scr3b[:, (fc % 2) * 512:(fc % 2) * 512 + 512], pb,
                                    w8b[:, fc:fc + 1], w8c[:, fc:fc + 1],
                                    OP.mult, OP.add)
            vv = nc.vector.tensor_add(xsb[:, off:off + 512], xsb[:, off:off + 512],
                                      scr3b[:, (fc % 2) * 512:(fc % 2) * 512 + 512])
            ctx.tag(vv, "cb2")

        _mm_stream(ctx, list(range(8)), ld_cb, mm_cb, "cb1")
        ctx.wait(nc.vector, "cb2")
    ctx.bar()
    ctx.dma("sq", o_x4.rearrange("(f p) t -> p f t", p=128),
            xsb[:].rearrange("p (f t) -> p f t", f=8), "st")
    ctx.wait(nc.sync, "st")
    ctx.bar()
    return nc


def emit_p4(ctx, EX):
    nc = ctx.nc
    xq = EX["x4q"]
    n1w8 = EX["n1w8"]
    o_xn4 = EX["o_xn4"]
    o_rows = EX["o_rows4"]

    xsb = ctx.sb("xsb4", [128, 8 * TOK], F32)
    scr = ctx.sb("scr4", [128, 512], F32)
    w8 = ctx.sb("w84", [128, 8], F32)
    xn4 = ctx.sb("xn4", [128, 8 * TOK], BF16)
    r_mean = ctx.sb("r_mean4", [1, TOK], F32)
    r_var = ctx.sb("r_var4", [1, TOK], F32)
    r_r1 = ctx.sb("r_r14", [1, TOK], F32)
    r_tmp = ctx.sb("r_tmp4", [1, TOK], F32)
    r_s = ctx.sb("r_s4", [1, TOK], F32)
    r_ms = ctx.sb("r_ms4", [1, TOK], F32)
    r_mr1 = ctx.sb("r_mr14", [1, TOK], F32)
    psSx = ctx.pbig2[0:1, 0:512]
    psSxx = ctx.pbig2[0:1, 512:1024]
    psB = ctx.pbig2[0:128, 1024:1536]

    ctx.dma("sq", xsb[:].rearrange("p (f t) -> p f t", f=8),
            xq.rearrange("(f p) t -> p f t", p=128), "ld")
    ctx.dma("sq", w8[:], n1w8, "ld")
    ctx.wait(nc.sync, "ld")
    ctx.bar()
    rO = {"mean": r_mean, "var": r_var, "r1": r_r1, "tmp": r_tmp}
    for tt in range(TOK // 512):
        tsl = slice(tt * 512, (tt + 1) * 512)
        _emit_ln_sums(ctx, xsb, 8, TOK, scr, psSx, psSxx, tsl)
        rO["sl"] = tsl
        _rows_chain_ln(ctx, psSx, psSxx, rO)
    ctx.bar()
    nc.vector.tensor_mul(r_tmp[0:1, :], r_r1[0:1, :], r_r1[0:1, :])
    ctx.bar()
    nc.vector.tensor_mul(r_tmp[0:1, :], r_tmp[0:1, :], r_var[0:1, :])
    ctx.bar()
    nc.scalar.activation(r_tmp[0:1, :], r_tmp[0:1, :], AF.Sqrt,
                         bias=ctx.cc(1e-5)[0:1, :], scale=1.0)
    ctx.bar()
    nc.vector.reciprocal(r_s[0:1, :], r_tmp[0:1, :])
    ctx.bar()
    nc.vector.tensor_mul(r_s[0:1, :], r_s[0:1, :], r_r1[0:1, :])
    ctx.bar()
    nc.vector.tensor_mul(r_ms[0:1, :], r_s[0:1, :], r_mean[0:1, :])
    nc.vector.tensor_mul(r_mr1[0:1, :], r_r1[0:1, :], r_mean[0:1, :])
    ctx.bar()
    onesrow = ctx.c("c_onesrow_f")
    for tt in range(TOK // 512):
        tsl = slice(tt * 512, (tt + 1) * 512)
        m = nc.tensor.matmul(psB[:, 0:512], onesrow[0:1, :], r_s[0:1, tsl],
                             start=True, stop=True)
        ctx.tag(m, "bc_ev")
        nc.vector.wait_ge(ctx.sem("bc_ev"), ctx.cnt["bc_ev"])
        last = None
        for fc in range(8):
            off = fc * TOK + tt * 512
            nc.vector.tensor_tensor(scr[:, 0:512], xsb[:, off:off + 512],
                                    psB[:, 0:512], OP.mult)
            last = nc.vector.tensor_scalar(xn4[:, off:off + 512], scr[:, 0:512],
                                           w8[:, fc:fc + 1], None, OP.mult)
        ctx.tag(last, "xn_ev")
        nc.tensor.wait_ge(ctx.sem("xn_ev"), ctx.cnt["xn_ev"])
        nc.scalar.wait_ge(ctx.sem("xn_ev"), ctx.cnt["xn_ev"])
    ctx.bar()
    ctx.dma("sq", o_xn4.rearrange("(f p) t -> p f t", p=128),
            xn4[:].rearrange("p (f t) -> p f t", f=8), "st")
    ctx.dma("sq", o_rows[0:1, :], r_s[0:1, :], "st")
    ctx.dma("sq", o_rows[1:2, :], r_ms[0:1, :], "st")
    ctx.dma("sq", o_rows[2:3, :], r_r1[0:1, :], "st")
    ctx.dma("sq", o_rows[3:4, :], r_mr1[0:1, :], "st")
    ctx.wait(nc.sync, "st")
    ctx.bar()
    return nc


def emit_p5(ctx, EX):
    nc = ctx.nc
    xn4f = EX["xn4f"]
    mscf4 = EX["mscf4"]
    win_z = EX["win_z"]
    win_x = EX["win_x"]
    win_B = EX["win_B"]
    win_C = EX["win_C"]
    win_dt = EX["win_dt"]
    win_th = EX["win_th"]
    n1w8 = EX["n1w8"]
    alog8 = EX["alog8"]
    dtb8 = EX["dtb8"]
    d8 = EX["d8"]
    bnw = EX["bnw"]
    cnw = EX["cnw"]
    bbias = EX["bbias"]
    cbias = EX["cbias"]
    o_yg = EX["o_yg"]
    o_dbga = EX["o_dbga"]
    o_dbgb = EX["o_dbgb"]
    import os
    _DBGB = os.environ.get("P5_DBGB", "")

    wz = ctx.sb("wz", [128, 8 * 512], BF16)
    wx = ctx.sb("wx", [128, 8 * 512], BF16)
    wB = ctx.sb("wB", [128, 8 * 512], BF16)
    wC = ctx.sb("wC", [128, 8 * 512], BF16)
    wdt = ctx.sb("wdt", [128, 8 * 8], BF16)
    wth = ctx.sb("wth", [128, 8 * 256], BF16)
    n1b = ctx.sb("n1b", [128, 8], BF16)
    n1f = ctx.sb("n1f", [128, 8], F32)
    msneg = ctx.sb("msneg5", [1, S], BF16)
    xt = [ctx.sb(f"xt5{i}", [128, 8 * 512], BF16) for i in range(2)]
    btok = ctx.sb("btok", [128, 2048], BF16)
    ctok = ctx.sb("ctok", [128, 2048], BF16)
    cfm = ctx.sb("cfm", [128, 2048], BF16)
    xg = ctx.sb("xg", [128, 2048], BF16)
    xw = ctx.sb("xw", [128, 2048], BF16)
    zsil = ctx.sb("zsil", [128, 2048], BF16)
    ygt = ctx.sb("ygt", [128, 2048], BF16)
    rdt = ctx.sb("rdt", [8, 512], F32)
    rtmp = ctx.sb("rtmp", [8, 512], F32)
    rtmp2 = ctx.sb("rtmp2", [8, 512], F32)
    rA = ctx.sb("rA", [8, 512], F32)
    rgam = ctx.sb("rgam", [8, 512], F32)
    rF = ctx.sb("rF", [8, 512], F32)
    rG = ctx.sb("rG", [8, 512], F32)
    rea = ctx.sb("rea", [8, 512], F32)
    ren = ctx.sb("ren", [8, 512], F32)
    rvin = ctx.sb("rvin", [8, 512], F32)
    ru = ctx.sb("ru", [8, 512], F32)
    rub = ctx.sb("rub", [8, 512], BF16)
    reab = ctx.sb("reab", [8, 512], BF16)
    rdsv = ctx.sb("rdsv", [8, 512], F32)
    rones = ctx.sb("rones", [8, 512], F32)
    ramp8 = ctx.sb("ramp8", [8, 512], F32)
    eaend = ctx.sb("eaend", [8, 8], F32)
    prevend = ctx.sb("prevend", [8, 8], F32)
    carryF = ctx.sb("carryF", [8, 1], F32)
    expal = ctx.sb("expal", [8, 1], F32)
    dtbc = ctx.sb("dtbc", [8, 1], F32)
    dttm = ctx.sb("dttm", [128, 32], F32)
    gamtm = ctx.sb("gamtm", [128, 32], F32)
    dswtm = ctx.sb("dswtm", [128, 32], F32)
    skiptok = ctx.sb("skiptok", [128, 8], F32)
    skipfm = ctx.sb("skipfm", [8, 512], F32)
    d8bc = ctx.sb("d8bc", [128, 8], F32)
    bnw128 = ctx.sb("bnw128", [128, 64], F32)
    cnw128 = ctx.sb("cnw128", [128, 64], F32)
    bb128 = ctx.sb("bb128", [128, 512], F32)
    cb128 = ctx.sb("cb128", [128, 512], F32)
    dthe = ctx.sb("dthe", [128, 256], F32)
    scrA = ctx.sb("scrA5", [128, 512], F32)
    scrB = ctx.sb("scrB5", [128, 512], F32)
    ssin = ctx.sb("ssin", [128, 256], F32)
    scos = ctx.sb("scos", [128, 256], F32)
    carryTh = ctx.sb("carryTh", [1, 256], F32)
    rstd8 = ctx.sb("rstd8", [128, 8], F32)
    rstd8b = ctx.sb("rstd8b", [128, 8], F32)
    sq5 = ctx.sb("sq5", [128, 512], BF16)
    FS = ctx.sb("FS", [128, 256], F32)
    FSb = ctx.sb("FSb", [128, 256], BF16)
    T1sb = ctx.sb("T1sb", [128, 256], BF16)
    bva = ctx.sb("bva", [8, 2048], BF16)
    ubcA = ctx.sb("ubcA", [128, 2048], BF16)
    eabcA = ctx.sb("eabcA", [128, 2048], BF16)
    skbcA = ctx.sb("skbcA", [128, 2048], BF16)
    decA = ctx.sb("decA", [128, 2048], BF16)
    cwr = ctx.sb("cwr", [1, 14 * 128], BF16)
    cwz = ctx.sb("cwz", [1, 4 * 128], BF16)
    cwdt = ctx.sb("cwdt", [1, 8], BF16)
    cwc = ctx.sb("cwc", [128, 16], BF16)

    pA = ctx.pbig
    pB = ctx.pbig2
    ptb = pA[:, 1792:1856].bitcast(BF16)


    for w_, ap_, ncol in [(wz, win_z, 512), (wx, win_x, 512), (wB, win_B, 512),
                          (wC, win_C, 512), (wdt, win_dt, 8), (wth, win_th, 256)]:
        ctx.dma("gq", w_[:].rearrange("p (kc m) -> p kc m", kc=8),
                ap_.rearrange("(kc p) m -> p kc m", p=128), "gl")
    for rr in range(4):
        ctx.dma("sq", msneg[0:1, rr * TOK:(rr + 1) * TOK], mscf4[rr], "ld")
    ctx.dma("sq", n1f[:], n1w8, "ld")
    ctx.dma("sq", expal[:], alog8, "ld")
    ctx.dma("sq", dtbc[:], dtb8, "ld")
    ctx.wait(nc.sync, "ld")
    ctx.wait(nc.gpsimd, "gl")
    ctx.bar()
    ctx.dma("gq", ramp8[:], ctx.cdram["c_ramp"][0:1, :].partition_broadcast(8), "g2")
    ctx.dma("gq", d8bc[:], d8[0:1, :].partition_broadcast(128), "g2")
    ctx.dma("gq", bnw128[:], bnw[0:1, :].partition_broadcast(128), "g2")
    ctx.dma("gq", cnw128[:], cnw[0:1, :].partition_broadcast(128), "g2")
    ctx.dma("gq", bb128[:], bbias[0:1, :].partition_broadcast(128), "g2")
    ctx.dma("gq", cb128[:], cbias[0:1, :].partition_broadcast(128), "g2")
    ctx.wait(nc.gpsimd, "g2")
    ctx.bar()
    nc.vector.memset(rones[:], 1.0)
    nc.vector.memset(carryF[:], 0.0)
    nc.vector.memset(carryTh[:], 0.0)
    nc.vector.memset(FS[:], 0.0)
    nc.vector.tensor_copy(n1b[:], n1f[:])
    nc.vector.tensor_scalar_mul(msneg[0:1, :], msneg[0:1, :], -1.0)
    ctx.bar()
    nc.scalar.activation(expal[:], expal[:], AF.Exp, bias=0.0, scale=1.0)
    ctx.bar()
    nc.vector.tensor_scalar_mul(expal[:], expal[:], -1.0)
    ctx.bar()

    order = [(wx, 4, 0), (wB, 4, 4), (wC, 4, 8), (wth, 2, 12)]
    for w_, nch, base in order:
        for mi in range(nch):
            for kc in range(8):
                wid = 512 if w_ is not wth else 256
                nc.tensor.matmul(pA[0:128, base + mi: base + mi + 1],
                                 w_[:, kc * wid + mi * 128: kc * wid + (mi + 1) * 128],
                                 n1b[:, kc:kc + 1], start=(kc == 0), stop=(kc == 7))
    ctx.bar()
    nc.vector.tensor_copy(cwc[:, 0:14], pA[0:128, 0:14])
    ctx.bar()
    for j in range(14):
        nc.tensor.transpose(ptb[0:1, 0:128], cwc[:, j:j + 1], ctx.c("c_I")[:])
        ctx.bar()
        nc.vector.tensor_copy(cwr[0:1, j * 128:(j + 1) * 128], ptb[0:1, 0:128])
        ctx.bar()
    for mi in range(4):
        for kc in range(8):
            nc.tensor.matmul(pA[0:128, mi:mi + 1],
                             wz[:, kc * 512 + mi * 128: kc * 512 + (mi + 1) * 128],
                             n1b[:, kc:kc + 1], start=(kc == 0), stop=(kc == 7))
    ctx.bar()
    nc.vector.tensor_copy(cwc[:, 0:4], pA[0:128, 0:4])
    ctx.bar()
    for j in range(4):
        nc.tensor.transpose(ptb[0:1, 0:128], cwc[:, j:j + 1], ctx.c("c_I")[:])
        ctx.bar()
        nc.vector.tensor_copy(cwz[0:1, j * 128:(j + 1) * 128], ptb[0:1, 0:128])
        ctx.bar()
    for kc in range(8):
        nc.tensor.matmul(pA[0:8, 0:1], wdt[:, kc * 8:(kc + 1) * 8], n1b[:, kc:kc + 1],
                         start=(kc == 0), stop=(kc == 7))
    ctx.bar()
    nc.vector.tensor_copy(cwc[0:8, 0:1], pA[0:8, 0:1])
    ctx.bar()
    nc.tensor.transpose(ptb[0:1, 0:8], cwc[0:8, 0:1], ctx.c("c_I")[0:8, 0:8])
    ctx.bar()
    nc.vector.tensor_copy(cwdt[0:1, 0:8], ptb[0:1, 0:8])
    ctx.bar()

    TPI = 2 * math.pi

    for tile in range(16):
        gts = tile * 512
        rank = gts // TOK
        toff = gts % TOK
        xb = xt[tile % 2]
        ctx.dma("sq", xb[:].rearrange("p (kc t) -> p kc t", kc=8),
                xn4f[rank, :, toff:toff + 512].rearrange("(kc p) t -> p kc t", p=128),
                "xl")
        ctx.wait(nc.sync, "xl")
        ctx.bar()
        msl = msneg[0:1, gts:gts + 512]

        for kc in range(8):
            nc.tensor.matmul(pB[0:8, 0:512], wdt[:, kc * 8:(kc + 1) * 8],
                             xb[:, kc * 512:(kc + 1) * 512], start=(kc == 0), stop=False)
        _i = nc.tensor.matmul(pB[0:8, 0:512], cwdt[0:1, 0:8], msl, start=False, stop=True)
        ctx.tag(_i, "dtmm")
        nc.vector.wait_ge(ctx.sem("dtmm"), ctx.cnt["dtmm"])
        lv = nc.vector.tensor_scalar(rdt[:, :], pB[0:8, 0:512], dtbc[:, 0:1], None, OP.add)
        ctx.tag(lv, "gv")
        nc.scalar.wait_ge(ctx.sem("gv"), ctx.cnt["gv"])
        nc.scalar.activation(rtmp[:, :], rdt[:, :], AF.Abs, bias=0.0, scale=1.0)
        nc.scalar.activation(rtmp[:, :], rtmp[:, :], AF.Exp, bias=0.0, scale=-1.0)
        la = nc.scalar.activation(rtmp[:, :], rtmp[:, :], AF.Ln,
                                  bias=ctx.c("c_onescol_f")[0:8, 0:1], scale=1.0)
        ctx.tag(la, "gs")
        nc.vector.tensor_scalar_max(rtmp2[:, :], rdt[:, :], 0.0)
        nc.vector.wait_ge(ctx.sem("gs"), ctx.cnt["gs"])
        nc.vector.tensor_add(rdt[:, :], rtmp[:, :], rtmp2[:, :])
        lv = nc.vector.tensor_scalar(rA[:, :], rdt[:, :], expal[:, 0:1], None, OP.mult)
        ctx.tag(lv, "gv")
        nc.scalar.wait_ge(ctx.sem("gv"), ctx.cnt["gv"])
        la = nc.scalar.activation(rtmp[:, :], rA[:, :], AF.Exp, bias=0.0, scale=1.0)
        ctx.tag(la, "gs")
        nc.vector.wait_ge(ctx.sem("gs"), ctx.cnt["gs"])
        nc.vector.tensor_scalar(rtmp[:, :], rtmp[:, :], -1.0, None, OP.add)
        nc.vector.tensor_scalar(rtmp2[:, :], rA[:, :], 1e-6, None, OP.add)
        lv = nc.vector.reciprocal(rtmp2[:, :], rtmp2[:, :])
        ctx.tag(lv, "gv")
        nc.vector.wait_ge(ctx.sem("gv"), ctx.cnt["gv"])
        nc.vector.tensor_mul(rgam[:, :], rtmp[:, :], rtmp2[:, :])
        nc.vector.tensor_scalar(rgam[:, :], rgam[:, :], 0.5, 1.0, OP.mult, OP.add)
        nc.vector.tensor_tensor_scan(rF[:, :], rones[:, :], rA[:, :], carryF[:, 0:1],
                                     OP.mult, OP.add)
        nc.vector.tensor_copy(prevend[:, 0:1], carryF[:, 0:1])
        nc.vector.tensor_copy(prevend[:, 1:8],
                              rF[:, :].rearrange("p (c l) -> p c l", c=8)[:, 0:7, 63:64]
                              .rearrange("p c l -> p (c l)"))
        nc.vector.tensor_copy(carryF[:, 0:1], rF[:, 511:512])
        pv = prevend[:, :].unsqueeze(2).broadcast_to((8, 8, 64))
        lv = nc.vector.tensor_tensor(rG[:, :].rearrange("p (c l) -> p c l", c=8),
                                     rF[:, :].rearrange("p (c l) -> p c l", c=8), pv, OP.subtract)
        ctx.tag(lv, "gv")
        nc.scalar.wait_ge(ctx.sem("gv"), ctx.cnt["gv"])
        nc.scalar.activation(rea[:, :], rG[:, :], AF.Exp, bias=0.0, scale=1.0)
        la = nc.scalar.activation(ren[:, :], rG[:, :], AF.Exp, bias=0.0, scale=-1.0)
        ctx.tag(la, "gs")
        nc.vector.wait_ge(ctx.sem("gs"), ctx.cnt["gs"])
        nc.vector.tensor_copy(eaend[:, :],
                              rea[:, :].rearrange("p (c l) -> p c l", c=8)[:, :, 63:64]
                              .rearrange("p c l -> p (c l)"))
        for c in range(8):
            nc.vector.tensor_tensor_scan(rvin[:, c * 64:(c + 1) * 64],
                                         rones[:, 0:64], ren[:, c * 64:(c + 1) * 64],
                                         0.0, OP.mult, OP.add)
        nc.vector.tensor_sub(rtmp[:, :], rvin[:, :], ren[:, :])
        nc.vector.tensor_mul(rtmp[:, :], rtmp[:, :], rea[:, :])
        nc.vector.tensor_add(ru[:, :], rtmp[:, :], ramp8[:, :])
        nc.vector.tensor_copy(rub[:, :], ru[:, :])
        nc.vector.tensor_copy(reab[:, :], rea[:, :])
        ev_ = eaend[:, :].unsqueeze(2).broadcast_to((8, 8, 64))
        lv = nc.vector.tensor_tensor(rdsv[:, :].rearrange("p (c l) -> p c l", c=8),
                                     ren[:, :].rearrange("p (c l) -> p c l", c=8), ev_, OP.mult)
        ctx.tag(lv, "gv")
        nc.tensor.wait_ge(ctx.sem("gv"), ctx.cnt["gv"])
        if tile == 0 and _DBGB == "rows":
            for ri, rr in enumerate([rdt, rA, rF, rG, rea, ren, rvin, ru, rgam, rdsv]):
                ctx.dma("sq", o_dbga[ri * 8:(ri + 1) * 8, :], rr[:, :], "dbg")
            ctx.dma("sq", o_dbga[80:88, :], eaend[:, 0:8].broadcast_to((8, 8)) if False else ru[:, :], "dbg")
            ctx.wait(nc.sync, "dbg")
            ctx.bar()
        for tc in range(4):
            for src, dst in [(rdt, dttm), (rgam, gamtm), (rdsv, dswtm)]:
                mt = nc.tensor.transpose(pB[0:128, 512:520].bitcast(F32)[:, 0:8] if False else pA[0:128, 1024:1032],
                                         src[:, tc * 128:(tc + 1) * 128], ctx.c("c_If")[0:8, 0:8])
                ctx.tag(mt, "ttm")
                nc.vector.wait_ge(ctx.sem("ttm"), ctx.cnt["ttm"])
                vv = nc.vector.tensor_copy(dst[:, tc * 8:(tc + 1) * 8], pA[0:128, 1024:1032])
                ctx.tag(vv, "ttc")
                nc.tensor.wait_ge(ctx.sem("ttc"), ctx.cnt["ttc"])
        ctx.bar()

        for mz in range(4):
            pb = pA[:, (mz % 2) * 512:(mz % 2) * 512 + 512]
            for kc in range(8):
                nc.tensor.matmul(pb, wz[:, kc * 512 + mz * 128: kc * 512 + (mz + 1) * 128],
                                 xb[:, kc * 512:(kc + 1) * 512], start=(kc == 0), stop=False)
            _i = nc.tensor.matmul(pb, cwz[0:1, mz * 128:(mz + 1) * 128], msl,
                                  start=False, stop=True)
            ctx.tag(_i, "zmm")
            nc.scalar.wait_ge(ctx.sem("zmm"), ctx.cnt["zmm"])
            aa = nc.scalar.activation(zsil[:, mz * 512:(mz + 1) * 512], pb, AF.Silu,
                                      bias=0.0, scale=1.0)
            ctx.tag(aa, "zsl")
            nc.tensor.wait_ge(ctx.sem("zsl"), ctx.cnt["zsl"])
        ctx.bar()

        for tc in range(4):
            tsl = slice(tc * 128, (tc + 1) * 128)
            if "gs" in ctx.cnt:
                nc.tensor.wait_ge(ctx.sem("gs"), ctx.cnt["gs"])
            for kc in range(8):
                st = (kc == 0)
                sp = (kc == 7)
                lhs = xb[:, kc * 512 + tc * 128: kc * 512 + (tc + 1) * 128]
                nc.tensor.matmul(pA[:, 0:512], lhs, wx[:, kc * 512:(kc + 1) * 512], start=st, stop=False)
                nc.tensor.matmul(pA[:, 512:1024], lhs, wB[:, kc * 512:(kc + 1) * 512], start=st, stop=False)
                nc.tensor.matmul(pA[:, 1024:1536], lhs, wC[:, kc * 512:(kc + 1) * 512], start=st, stop=False)
                nc.tensor.matmul(pA[:, 1536:1792], lhs, wth[:, kc * 256:(kc + 1) * 256], start=st, stop=False)
            mstat = msneg[0:1, gts + tc * 128: gts + (tc + 1) * 128]
            for regi, (off, cwbase, wd) in enumerate([(0, 0, 512), (512, 4, 512), (1024, 8, 512)]):
                nc.tensor.matmul(pA[:, off:off + wd], mstat,
                                 cwr[0:1, cwbase * 128:(cwbase + 4) * 128][0:1, 0:wd],
                                 start=False, stop=True)
            _i = nc.tensor.matmul(pA[:, 1536:1792], mstat, cwr[0:1, 12 * 128:14 * 128],
                                  start=False, stop=True)
            ctx.tag(_i, "swm")
            nc.vector.wait_ge(ctx.sem("swm"), ctx.cnt["swm"])
            if tile == 0 and _DBGB in ("braw", "craw", "xraw", "thraw"):
                srcp = {"braw": pA[:, 512:1024], "craw": pA[:, 1024:1536],
                        "xraw": pA[:, 0:512], "thraw": pA[:, 1536:1792]}[_DBGB]
                wd = 256 if _DBGB == "thraw" else 512
                nc.vector.tensor_copy(ygt[:, tc * 512: tc * 512 + wd].bitcast(F32)[:, 0:wd] if False
                                      else scrB[:, 0:wd], srcp)
                ctx.bar()
                ctx.dma("sq", o_dbgb[:, tc * 512: tc * 512 + wd], scrB[:, 0:wd], "dbg3")
                ctx.wait(nc.sync, "dbg3")
                ctx.bar()

            dtv_ = dttm[:, tc * 8:(tc + 1) * 8].unsqueeze(2).broadcast_to((128, 8, 32))
            lv = nc.vector.tensor_tensor(dthe[:, :].rearrange("p (h j) -> p h j", h=8),
                                         pA[:, 1536:1792].rearrange("p (h j) -> p h j", h=8),
                                         dtv_, OP.mult)
            ctx.tag(lv, "dth")
            nc.tensor.wait_ge(ctx.sem("dth"), ctx.cnt["dth"])
            m1 = nc.tensor.matmul(pB[:, 512:768], ctx.c("c_tril")[:], dthe[:, :],
                                  start=True, stop=False)
            m2 = nc.tensor.matmul(pB[:, 512:768], ctx.c("c_onesrow_f")[0:1, :],
                                  carryTh[0:1, :], start=False, stop=True)
            m3 = nc.tensor.matmul(pB[0:1, 768:1024], ctx.c("c_onescol_f")[:, 0:1],
                                  dthe[:, :], start=True, stop=True)
            ctx.tag(m3, "cum")
            nc.vector.wait_ge(ctx.sem("cum"), ctx.cnt["cum"])
            if tile == 0 and _DBGB == "cum":
                nc.vector.tensor_copy(scrB[:, 0:256], pB[:, 512:768])
                ctx.bar()
                ctx.dma("sq", o_dbgb[:, tc * 512: tc * 512 + 256], scrB[:, 0:256], "dbg3")
                ctx.wait(nc.sync, "dbg3")
                ctx.bar()
            nc.vector.tensor_add(carryTh[0:1, :], carryTh[0:1, :], pB[0:1, 768:1024])
            K_ = 1.5 * (2 ** 23)
            nc.vector.tensor_scalar(scrA[:, 0:256], pB[:, 512:768], 1.0 / TPI, K_,
                                    OP.mult, OP.add)
            nc.vector.tensor_scalar(scrA[:, 0:256], scrA[:, 0:256], -K_, None, OP.add)
            nc.vector.scalar_tensor_tensor(scrB[:, 0:256], scrA[:, 0:256], -TPI,
                                           pB[:, 512:768], OP.mult, OP.add)
            nc.vector.tensor_scalar_min(scrB[:, 0:256], scrB[:, 0:256], float(math.pi))
            lv = nc.vector.tensor_scalar_max(scrB[:, 0:256], scrB[:, 0:256], float(-math.pi))
            ctx.tag(lv, "gv")
            nc.scalar.wait_ge(ctx.sem("gv"), ctx.cnt["gv"])
            la = nc.scalar.activation(ssin[:, :], scrB[:, 0:256], AF.Sin, bias=0.0, scale=1.0)
            ctx.tag(la, "gs")
            nc.vector.tensor_scalar(scrA[:, 0:256], scrB[:, 0:256], float(math.pi / 2), None, OP.add)
            nc.vector.wait_ge(ctx.sem("gs"), ctx.cnt["gs"])
            nc.vector.tensor_scalar(scrB[:, 0:256], scrA[:, 0:256], float(math.pi), None, OP.is_gt)
            lv = nc.vector.scalar_tensor_tensor(scrA[:, 0:256], scrB[:, 0:256], -TPI,
                                                scrA[:, 0:256], OP.mult, OP.add)
            ctx.tag(lv, "gv")
            nc.scalar.wait_ge(ctx.sem("gv"), ctx.cnt["gv"])
            la = nc.scalar.activation(scos[:, :], scrA[:, 0:256], AF.Sin, bias=0.0, scale=1.0)
            ctx.tag(la, "gs")

            for (pso, w128, bias128, dstt) in [(512, bnw128, bb128, btok),
                                               (1024, cnw128, cb128, ctok)]:
                ps_ = pA[:, pso:pso + 512]
                la = nc.scalar.activation(sq5[:, :], ps_, AF.Square, bias=0.0, scale=1.0)
                ctx.tag(la, "gs")
                nc.vector.wait_ge(ctx.sem("gs"), ctx.cnt["gs"])
                lv = nc.vector.tensor_reduce(rstd8[:, :].rearrange("p (h o) -> p h o", h=8),
                                              sq5[:, :].rearrange("p (h n) -> p h n", h=8),
                                              AX, OP.add)
                ctx.tag(lv, "gv")
                nc.vector.wait_ge(ctx.sem("gv"), ctx.cnt["gv"])
                lv = nc.vector.tensor_scalar_mul(rstd8[:, :], rstd8[:, :], 1.0 / 64.0)
                ctx.tag(lv, "gv")
                nc.scalar.wait_ge(ctx.sem("gv"), ctx.cnt["gv"])
                la = nc.scalar.activation(rstd8b[:, :], rstd8[:, :], AF.Sqrt,
                                           bias=ctx.cc(1e-5)[0:128, 0:1], scale=1.0)
                ctx.tag(la, "gs")
                nc.vector.wait_ge(ctx.sem("gs"), ctx.cnt["gs"])
                lv = nc.vector.reciprocal(rstd8[:, :], rstd8b[:, :])
                ctx.tag(lv, "gv")
                nc.vector.wait_ge(ctx.sem("gv"), ctx.cnt["gv"])
                rsv = rstd8[:, :].unsqueeze(2).broadcast_to((128, 8, 64))
                nc.vector.tensor_tensor(scrA[:, :].rearrange("p (h n) -> p h n", h=8),
                                        ps_.rearrange("p (h n) -> p h n", h=8), rsv, OP.mult)
                wv_ = w128[:, :].unsqueeze(1).broadcast_to((128, 8, 64))
                nc.vector.tensor_tensor(scrA[:, :].rearrange("p (h n) -> p h n", h=8),
                                        scrA[:, :].rearrange("p (h n) -> p h n", h=8), wv_, OP.mult)
                nc.vector.tensor_add(scrA[:, :], scrA[:, :], bias128[:, :])
                bev = scrA[:, :].rearrange("p (h t) -> p h t", h=8)[:, :, 0:32]
                bod = scrA[:, :].rearrange("p (h t) -> p h t", h=8)[:, :, 32:64]
                cv = scos[:, :].rearrange("p (h j) -> p h j", h=8)
                sv = ssin[:, :].rearrange("p (h j) -> p h j", h=8)
                dev = dstt[:, tc * 512:(tc + 1) * 512].rearrange("p (h t) -> p h t", h=8)[:, :, 0:32]
                dod = dstt[:, tc * 512:(tc + 1) * 512].rearrange("p (h t) -> p h t", h=8)[:, :, 32:64]
                nc.vector.tensor_tensor(scrB[:, 0:256].rearrange("p (h j) -> p h j", h=8), bev, cv, OP.mult)
                nc.vector.tensor_tensor(scrB[:, 256:512].rearrange("p (h j) -> p h j", h=8), bod, sv, OP.mult)
                nc.vector.tensor_sub(scrB[:, 0:256], scrB[:, 0:256], scrB[:, 256:512])
                nc.vector.tensor_copy(dev, scrB[:, 0:256].rearrange("p (h j) -> p h j", h=8))
                nc.vector.tensor_tensor(scrB[:, 0:256].rearrange("p (h j) -> p h j", h=8), bev, sv, OP.mult)
                nc.vector.tensor_tensor(scrB[:, 256:512].rearrange("p (h j) -> p h j", h=8), bod, cv, OP.mult)
                nc.vector.tensor_add(scrB[:, 0:256], scrB[:, 0:256], scrB[:, 256:512])
                nc.vector.tensor_copy(dod, scrB[:, 0:256].rearrange("p (h j) -> p h j", h=8))

            gv_ = gamtm[:, tc * 8:(tc + 1) * 8].unsqueeze(2).broadcast_to((128, 8, 64))
            nc.vector.tensor_tensor(xg[:, tc * 512:(tc + 1) * 512].rearrange("p (h q) -> p h q", h=8),
                                    pA[:, 0:512].rearrange("p (h q) -> p h q", h=8), gv_, OP.mult)
            dv_ = dswtm[:, tc * 8:(tc + 1) * 8].unsqueeze(2).broadcast_to((128, 8, 64))
            nc.vector.tensor_tensor(xw[:, tc * 512:(tc + 1) * 512].rearrange("p (h q) -> p h q", h=8),
                                    xg[:, tc * 512:(tc + 1) * 512].rearrange("p (h q) -> p h q", h=8),
                                    dv_, OP.mult)
            lv = nc.vector.tensor_reduce(skiptok[:, :].rearrange("p (h o) -> p h o", h=8),
                                         pA[:, 0:512].rearrange("p (h q) -> p h q", h=8), AX, OP.add)
            ctx.tag(lv, "gv")
            nc.vector.wait_ge(ctx.sem("gv"), ctx.cnt["gv"])
            lv = nc.vector.tensor_mul(skiptok[:, :], skiptok[:, :], d8bc[:, :])
            ctx.tag(lv, "gv")
            nc.tensor.wait_ge(ctx.sem("gv"), ctx.cnt["gv"])
            mt = nc.tensor.transpose(pA[0:8, 1792:1920], skiptok[:, :], ctx.c("c_If")[0:128, 0:128])
            ctx.tag(mt, "skt")
            nc.vector.wait_ge(ctx.sem("skt"), ctx.cnt["skt"])
            lv = nc.vector.tensor_copy(skipfm[0:8, tc * 128:(tc + 1) * 128], pA[0:8, 1792:1920])
            ctx.tag(lv, "gv")
            nc.tensor.wait_ge(ctx.sem("gv"), ctx.cnt["gv"])
            for j in range(4):
                mt = nc.tensor.transpose(ptb[:, 0:128],
                                         ctok[:, tc * 512 + j * 128: tc * 512 + (j + 1) * 128],
                                         ctx.c("c_I")[:])
                ctx.tag(mt, "cft")
                nc.vector.wait_ge(ctx.sem("cft"), ctx.cnt["cft"])
                vv = nc.vector.tensor_copy(cfm[:, (j * 4 + tc) * 128:(j * 4 + tc + 1) * 128],
                                           ptb[:, 0:128])
                ctx.tag(vv, "cfc")
                nc.tensor.wait_ge(ctx.sem("cfc"), ctx.cnt["cfc"])
        ctx.bar()

        mv = ctx.c("c_mask8")[:, :].rearrange("p (j l) -> p j l", j=4) \
            .unsqueeze(1).broadcast_to((8, 8, 4, 64))
        bva4 = bva[:, :].rearrange("p (c j l) -> p c j l", c=8, j=4)
        lv = nc.vector.tensor_tensor(bva4,
                                     rub[:, :].rearrange("p (c l) -> p c l", c=8)
                                     .unsqueeze(2).broadcast_to((8, 8, 4, 64)),
                                     mv, OP.mult)
        ctx.tag(lv, "bv")
        nc.tensor.wait_ge(ctx.sem("bv"), ctx.cnt["bv"])
        for j4 in range(4):
            mm = nc.tensor.matmul(pA[:, j4 * 512:(j4 + 1) * 512], ctx.c("c_aexp")[:, :],
                                  bva[:, j4 * 512:(j4 + 1) * 512], start=True, stop=True)
        ctx.tag(mm, "bvm")
        nc.vector.wait_ge(ctx.sem("bvm"), ctx.cnt["bvm"])
        lv = nc.vector.tensor_copy(ubcA[:, :], pA[:, 0:2048])
        ctx.tag(lv, "bvc")
        lv = nc.vector.tensor_tensor(bva4,
                                     reab[:, :].rearrange("p (c l) -> p c l", c=8)
                                     .unsqueeze(2).broadcast_to((8, 8, 4, 64)),
                                     mv, OP.mult)
        ctx.tag(lv, "bv")
        nc.tensor.wait_ge(ctx.sem("bv"), ctx.cnt["bv"])
        nc.tensor.wait_ge(ctx.sem("bvc"), ctx.cnt["bvc"])
        for j4 in range(4):
            mm = nc.tensor.matmul(pA[:, j4 * 512:(j4 + 1) * 512], ctx.c("c_aexp")[:, :],
                                  bva[:, j4 * 512:(j4 + 1) * 512], start=True, stop=True)
        ctx.tag(mm, "bvm")
        nc.vector.wait_ge(ctx.sem("bvm"), ctx.cnt["bvm"])
        lv = nc.vector.tensor_copy(eabcA[:, :], pA[:, 0:2048])
        ctx.tag(lv, "bvc")
        lv = nc.vector.tensor_tensor(bva4,
                                     skipfm[:, :].rearrange("p (c l) -> p c l", c=8)
                                     .unsqueeze(2).broadcast_to((8, 8, 4, 64)),
                                     mv, OP.mult)
        ctx.tag(lv, "bv")
        nc.tensor.wait_ge(ctx.sem("bv"), ctx.cnt["bv"])
        nc.tensor.wait_ge(ctx.sem("bvc"), ctx.cnt["bvc"])
        for j4 in range(4):
            mm = nc.tensor.matmul(pA[:, j4 * 512:(j4 + 1) * 512], ctx.c("c_aexp")[:, :],
                                  bva[:, j4 * 512:(j4 + 1) * 512], start=True, stop=True)
        ctx.tag(mm, "bvm")
        nc.vector.wait_ge(ctx.sem("bvm"), ctx.cnt["bvm"])
        lv = nc.vector.tensor_copy(skbcA[:, :], pA[:, 0:2048])
        ctx.tag(lv, "bvc")
        lv = nc.vector.tensor_tensor(bva4,
                                     eaend[:, :].unsqueeze(2).unsqueeze(3)
                                     .broadcast_to((8, 8, 4, 64)),
                                     mv, OP.mult)
        ctx.tag(lv, "bv")
        nc.tensor.wait_ge(ctx.sem("bv"), ctx.cnt["bv"])
        nc.tensor.wait_ge(ctx.sem("bvc"), ctx.cnt["bvc"])
        for j4 in range(4):
            mm = nc.tensor.matmul(pA[:, j4 * 512:(j4 + 1) * 512], ctx.c("c_aexp")[:, :],
                                  bva[:, j4 * 512:(j4 + 1) * 512], start=True, stop=True)
        ctx.tag(mm, "bvm")
        nc.vector.wait_ge(ctx.sem("bvm"), ctx.cnt["bvm"])
        lv = nc.vector.tensor_copy(decA[:, :], pA[:, 0:2048])
        ctx.tag(lv, "bvc")
        nc.tensor.wait_ge(ctx.sem("bvc"), ctx.cnt["bvc"])

        for c in range(8):
            tcb = c // 2
            pr = (c % 2) * 64
            if c > 0:
                nc.tensor.wait_ge(ctx.sem("fsu"), ctx.cnt["fsu"])
            last = None
            for h in range(8):
                lhs = btok[pr:pr + 64, tcb * 512 + h * 64: tcb * 512 + (h + 1) * 64]
                o1 = pB[(h % 2) * 64:(h % 2) * 64 + 64, 0 + (h // 2) * 64: 0 + (h // 2) * 64 + 64]
                o2 = pB[(h % 2) * 64:(h % 2) * 64 + 64, 256 + (h // 2) * 64: 256 + (h // 2) * 64 + 64]
                nc.tensor.matmul(o1, lhs, xg[pr:pr + 64, tcb * 512 + h * 64: tcb * 512 + (h + 1) * 64],
                                 start=(h < 2), stop=True, skip_group_check=True)
                last = nc.tensor.matmul(o2, lhs, xw[pr:pr + 64, tcb * 512 + h * 64: tcb * 512 + (h + 1) * 64],
                                        start=(h < 2), stop=True, skip_group_check=True)
            ctx.tag(last, "t1s")
            nc.vector.wait_ge(ctx.sem("t1s"), ctx.cnt["t1s"])
            nc.vector.tensor_copy(T1sb[:, :], pB[:, 0:256])
            nc.vector.tensor_mul(FS[:, :], FS[:, :], decA[:, c * 256:(c + 1) * 256])
            nc.vector.tensor_add(FS[:, :], FS[:, :], pB[:, 256:512])
            lv = nc.vector.tensor_copy(FSb[:, :], FS[:, :])
            ctx.tag(lv, "fsu")
            nc.tensor.wait_ge(ctx.sem("fsu"), ctx.cnt["fsu"])
            if c > 0:
                nc.tensor.wait_ge(ctx.sem("ycb"), ctx.cnt["ycb"])
            last = None
            for h in range(8):
                rhs = cfm[(h % 2) * 64:(h % 2) * 64 + 64,
                          ((h // 2) * 4 + tcb) * 128 + pr: ((h // 2) * 4 + tcb) * 128 + pr + 64]
                oy1 = pB[(h % 2) * 64:(h % 2) * 64 + 64, 512 + (h // 2) * 64: 512 + (h // 2) * 64 + 64]
                oy2 = pB[(h % 2) * 64:(h % 2) * 64 + 64, 768 + (h // 2) * 64: 768 + (h // 2) * 64 + 64]
                nc.tensor.matmul(oy1, T1sb[(h % 2) * 64:(h % 2) * 64 + 64, (h // 2) * 64:(h // 2) * 64 + 64],
                                 rhs, start=(h < 2), stop=True, skip_group_check=True)
                last = nc.tensor.matmul(oy2, FSb[(h % 2) * 64:(h % 2) * 64 + 64, (h // 2) * 64:(h // 2) * 64 + 64],
                                        rhs, start=(h < 2), stop=True, skip_group_check=True)
            ctx.tag(last, "ymm")
            nc.vector.wait_ge(ctx.sem("ymm"), ctx.cnt["ymm"])
            nc.vector.tensor_tensor(scrA[:, 0:256], pB[:, 512:768],
                                    ubcA[:, c * 256:(c + 1) * 256], OP.mult)
            nc.vector.tensor_tensor(scrB[:, 0:256], pB[:, 768:1024],
                                    eabcA[:, c * 256:(c + 1) * 256], OP.mult)
            lv = nc.vector.tensor_add(scrA[:, 0:256], scrA[:, 0:256], scrB[:, 0:256])
            ctx.tag(lv, "ycb")
            nc.vector.tensor_add(scrA[:, 0:256], scrA[:, 0:256],
                                 skbcA[:, c * 256:(c + 1) * 256])
            zv = zsil[:, :].rearrange("p (j t) -> p j t", j=4)[:, :, c * 64:(c + 1) * 64]
            lv = nc.vector.tensor_tensor(ygt[:, :].rearrange("p (j t) -> p j t", j=4)[:, :, c * 64:(c + 1) * 64],
                                         scrA[:, 0:256].rearrange("p (j l) -> p j l", j=4),
                                         zv, OP.mult)
            ctx.tag(lv, "ygd")
        nc.sync.wait_ge(ctx.sem("ygd"), ctx.cnt["ygd"])
        ctx.bar()
        if tile == 0 and _DBGB in ("btok", "ctok", "cfm", "xg", "xw", "zsil", "ygt"):
            srcmap = {"btok": btok, "ctok": ctok, "cfm": cfm, "xg": xg, "xw": xw,
                      "zsil": zsil, "ygt": ygt}
            ctx.dma("gq", o_dbgb, srcmap[_DBGB][:, 0:2048], "dbg2")
            ctx.wait(nc.gpsimd, "dbg2")
            ctx.bar()
        ctx.dma("sq", o_yg.rearrange("(j q) t -> q j t", j=4)[:, :, gts:gts + 512],
                ygt[:].rearrange("q (j t) -> q j t", j=4), "st")
        ctx.wait(nc.sync, "st")
        ctx.bar()
    return nc


def emit_p6(ctx, EX):
    nc = ctx.nc
    ygq = EX["ygq"]
    xq = EX["x4q"]
    rows4 = EX["rows4"]
    wout = EX["wout"]
    wg = EX["wg"]
    wu = EX["wu"]
    wd = EX["wd"]
    n2w8 = EX["n2w8"]
    o_x5 = EX["o_x5"]

    xsb = ctx.sb("xsb6", [128, 8 * TOK], F32)
    yin2 = ctx.sb("yin2", [128, 8 * 512], F32)
    tbuf = ctx.sb("tbuf6", [128, 20 * 512], BF16)
    pool = ctx.sb("pool6", [128, 16384], BF16)
    ygt = ctx.sb("ygt6", [128, 16 * 512], BF16)
    wslab = ctx.sb("wslab6", [128, 2 * 4096], BF16)
    scr = ctx.sb("scr6", [128, 512], F32)
    scr6b = ctx.sb("scr6b", [128, 1024], F32)
    gtile = ctx.sb("gtile", [128, 1024], BF16)
    w8a = ctx.sb("w8a6", [128, 8], F32)
    r_r1 = ctx.sb("r_r16", [1, TOK], F32)
    r_mr1n = ctx.sb("r_mr1n6", [1, TOK], BF16)
    t_tmp = ctx.sb("t_tmp6", [1, 512], F32)
    t_r = ctx.sb("t_r6", [1, 512], F32)
    pbig = ctx.pbig
    pbig2 = ctx.pbig2

    ctx.dma("sq", xsb[:].rearrange("p (f t) -> p f t", f=8),
            xq.rearrange("(f p) t -> p f t", p=128), "ld")
    ctx.dma("sq", r_r1[0:1, :], rows4[2:3, :], "ld")
    ctx.dma("gq", r_mr1n[0:1, :], rows4[3:4, :], "gl")
    ctx.dma("sq", w8a[:], n2w8, "ld")
    ctx.wait(nc.sync, "ld")
    ctx.wait(nc.gpsimd, "gl")
    ctx.bar()
    nc.vector.tensor_scalar_mul(r_mr1n[0:1, :], r_mr1n[0:1, :], -1.0)
    ctx.bar()

    def stream_w(ap, job, kchunks, buf, off2=0):
        ctx.dma("gq", wslab[:, buf * 4096 + off2: buf * 4096 + off2 + kchunks * 128],
                ap[job * 128:(job + 1) * 128, :], "wl%d" % buf)
        return ctx.cnt["wl%d" % buf]

    xn5 = pool[:, 0:16384]
    for tt in range(TOK // 512):
        ts = slice(tt * 512, (tt + 1) * 512)
        ctx.dma("sq", ygt[:].rearrange("p (kc t) -> p kc t", kc=16),
                ygq[:, ts].rearrange("(kc p) t -> p kc t", p=128), "yl")
        ctx.wait(nc.tensor, "yl")
        m = nc.tensor.matmul(pbig2[:, 1024:1536], ctx.c("c_onesrow_f")[0:1, :],
                             r_r1[0:1, ts], start=True, stop=True)
        def ld_mo(fc, buf):
            return stream_w(wout, fc, 16, buf)

        def mm_mo(fc, buf):
            bo = buf * 4096
            pb = pbig[:, (fc % 2) * 512:(fc % 2) * 512 + 512]
            ctx.lagwait(nc.tensor, "ev6", 1)
            for kc in range(16):
                nc.tensor.matmul(pb, wslab[:, bo + kc * 128: bo + (kc + 1) * 128],
                                 ygt[:, kc * 512:(kc + 1) * 512],
                                 start=(kc == 0), stop=False)
            mmm = nc.tensor.matmul(pb, ctx.c("c_onesrow_b")[0:1, :], r_mr1n[0:1, ts],
                                   start=False, stop=True)
            ctx.tag(mmm, "mo")
            nc.vector.wait_ge(ctx.sem("mo"), ctx.cnt["mo"])
            off = fc * TOK + tt * 512
            nc.vector.tensor_tensor(scr6b[:, (fc % 2) * 512:(fc % 2) * 512 + 512],
                                    xsb[:, off:off + 512], pbig2[:, 1024:1536], OP.mult)
            vv = nc.vector.tensor_add(yin2[:, fc * 512:(fc + 1) * 512],
                                      scr6b[:, (fc % 2) * 512:(fc % 2) * 512 + 512], pb)
            ctx.tag(vv, "ev6")

        _mm_stream(ctx, list(range(8)), ld_mo, mm_mo, "mo")
        ctx.bar()
        _emit_ln_sums(ctx, yin2, 8, 512, scr, pbig2[0:1, 0:512], pbig2[0:1, 512:1024],
                      slice(0, 512))
        nc.vector.wait_ge(ctx.sem("mm_ev"), ctx.cnt["mm_ev"])
        lv = nc.vector.tensor_scalar_mul(t_tmp[0:1, :], pbig2[0:1, 512:1024], 1.0 / D)
        ctx.tag(lv, "gv")
        nc.scalar.wait_ge(ctx.sem("gv"), ctx.cnt["gv"])
        nc.tensor.wait_ge(ctx.sem("gv"), ctx.cnt["gv"])
        la = nc.scalar.activation(t_tmp[0:1, :], t_tmp[0:1, :], AF.Sqrt,
                                  bias=ctx.cc(1e-5)[0:1, :], scale=1.0)
        ctx.tag(la, "gs")
        nc.vector.wait_ge(ctx.sem("gs"), ctx.cnt["gs"])
        lv = nc.vector.reciprocal(t_r[0:1, :], t_tmp[0:1, :])
        ctx.tag(lv, "gv")
        nc.tensor.wait_ge(ctx.sem("gv"), ctx.cnt["gv"])
        m = nc.tensor.matmul(pbig2[:, 1024:1536], ctx.c("c_onesrow_f")[0:1, :],
                             t_r[0:1, :], start=True, stop=True)
        ctx.tag(m, "bc6")
        nc.vector.wait_ge(ctx.sem("bc6"), ctx.cnt["bc6"])
        last = None
        for fc in range(8):
            off = fc * TOK + tt * 512
            yo = fc * 512
            nc.vector.tensor_tensor(scr[:, :], yin2[:, yo:yo + 512],
                                    pbig2[:, 1024:1536], OP.mult)
            nc.vector.tensor_scalar(xn5[:, off:off + 512], scr[:, :],
                                    w8a[:, fc:fc + 1], None, OP.mult)
            last = nc.vector.tensor_add(xsb[:, off:off + 512], xsb[:, off:off + 512],
                                        yin2[:, yo:yo + 512])
        ctx.tag(last, "x26")
        nc.tensor.wait_ge(ctx.sem("x26"), ctx.cnt["x26"])
        nc.scalar.wait_ge(ctx.sem("x26"), ctx.cnt["x26"])
    ctx.bar()

    for tt in range(TOK // 512):
        def ld_gu(hk, buf):
            stream_w(wg, hk, 8, buf)
            return stream_w(wu, hk, 8, buf, off2=1024)

        def mm_gu(hk, buf):
            bo = buf * 4096
            pb = pbig[:, (hk % 2) * 512:(hk % 2) * 512 + 512]
            pb2 = pbig[:, 1024 + (hk % 2) * 512: 1024 + (hk % 2) * 512 + 512]
            ctx.lagwait(nc.tensor, "g4", 1)
            last = None
            for kc in range(8):
                last = nc.tensor.matmul(pb, wslab[:, bo + kc * 128: bo + (kc + 1) * 128],
                                        xn5[:, kc * TOK + tt * 512: kc * TOK + (tt + 1) * 512],
                                        start=(kc == 0), stop=(kc == 7))
            ctx.tag(last, "g1")
            last = None
            for kc in range(8):
                last = nc.tensor.matmul(pb2, wslab[:, bo + 1024 + kc * 128: bo + 1024 + (kc + 1) * 128],
                                        xn5[:, kc * TOK + tt * 512: kc * TOK + (tt + 1) * 512],
                                        start=(kc == 0), stop=(kc == 7))
            ctx.tag(last, "g2")
            nc.scalar.wait_ge(ctx.sem("g1"), ctx.cnt["g1"])
            aa = nc.scalar.activation(gtile[:, (hk % 2) * 512:(hk % 2) * 512 + 512], pb,
                                      AF.Silu, bias=0.0, scale=1.0)
            ctx.tag(aa, "g3")
            nc.vector.wait_ge(ctx.sem("g3"), ctx.cnt["g3"])
            nc.vector.wait_ge(ctx.sem("g2"), ctx.cnt["g2"])
            vv = nc.vector.tensor_tensor(tbuf[:, hk * 512:(hk + 1) * 512], pb2,
                                         gtile[:, (hk % 2) * 512:(hk % 2) * 512 + 512], OP.mult)
            ctx.tag(vv, "g4")

        _mm_stream(ctx, list(range(20)), ld_gu, mm_gu, "g4")
        ctx.bar()
        nc.tensor.wait_ge(ctx.sem("g4"), ctx.cnt["g4"])

        def ld_wd(fc, buf):
            return stream_w(wd, fc, 20, buf)

        def mm_wd(fc, buf):
            bo = buf * 4096
            pb = pbig[:, (fc % 2) * 512:(fc % 2) * 512 + 512]
            ctx.lagwait(nc.tensor, "g6", 1)
            last = None
            for kc in range(20):
                last = nc.tensor.matmul(pb, wslab[:, bo + kc * 128: bo + (kc + 1) * 128],
                                        tbuf[:, kc * 512:(kc + 1) * 512],
                                        start=(kc == 0), stop=(kc == 19))
            ctx.tag(last, "g5")
            nc.vector.wait_ge(ctx.sem("g5"), ctx.cnt["g5"])
            off = fc * TOK + tt * 512
            vv = nc.vector.tensor_add(xsb[:, off:off + 512], xsb[:, off:off + 512], pb)
            ctx.tag(vv, "g6")

        _mm_stream(ctx, list(range(8)), ld_wd, mm_wd, "g5")
        ctx.bar()
    ctx.dma("sq", o_x5.rearrange("(f p) t -> p f t", p=128),
            xsb[:].rearrange("p (f t) -> p f t", f=8), "st")
    ctx.wait(nc.sync, "st")
    ctx.bar()
    return nc


def emit_p7(ctx, EX):
    nc = ctx.nc
    absp = EX["absp_all"]
    wup_sl = EX["wupT_sl"]
    wdn_sl = EX["wdnT_sl"]
    o_up = EX["o_wtup"]
    o_dn = EX["o_wtdn"]

    ab = ctx.sb("ab", [8, 2], F32)
    thr = ctx.sb("thr", [1, 2], F32)
    thrc = ctx.sb("thrc", [128, 2], F32)
    wld = ctx.sb("wld7", [128, 1024], F32)
    aw = ctx.sb("aw7", [128, 1024], F32)
    sg = ctx.sb("sg7", [128, 1024], F32)
    msk = ctx.sb("msk7", [128, 1024], F32)
    wt = ctx.sb("wt7", [128, 1024], BF16)
    ps = ctx.pbig[0:128, 0:512]

    ctx.dma("sq", ab[:], absp, "ld")
    ctx.wait(nc.sync, "ld")
    ctx.bar()
    nc.vector.tensor_reduce(thr[0:1, 0:2].rearrange("p (a b) -> p a b", a=2) if False
                            else thr[0:1, 0:2], ab[0:1, 0:2], AX, OP.add) if False else None
    m = nc.tensor.matmul(ps[0:1, 0:2], ctx.c("c_onescol_f")[0:8, 0:1], ab[:, :],
                         start=True, stop=True)
    ctx.tag(m, "s7")
    nc.vector.wait_ge(ctx.sem("s7"), ctx.cnt["s7"])
    nc.vector.tensor_copy(thr[0:1, 0:2], ps[0:1, 0:2])
    ctx.bar()
    nc.vector.tensor_scalar_mul(thr[0:1, 0:1], thr[0:1, 0:1], 0.5 * 0.5 / (D * DFF))
    nc.vector.tensor_scalar_mul(thr[0:1, 1:2], thr[0:1, 1:2], 0.5 * 0.5 / (D * DFF))
    ctx.bar()
    m = nc.tensor.matmul(ps[:, 0:2], ctx.c("c_onesrow_f")[0:1, :], thr[0:1, 0:2],
                         start=True, stop=True)
    ctx.tag(m, "s8")
    nc.vector.wait_ge(ctx.sem("s8"), ctx.cnt["s8"])
    nc.vector.tensor_copy(thrc[:, 0:2], ps[:, 0:2])
    ctx.bar()
    for mi, (ap, oap, col) in enumerate(
            [(wup_sl, o_up, 0), (wdn_sl, o_dn, 1)]):
        nr, ncol = ap.shape[0] // 128, ap.shape[1] // 1024
        blocks = [(ap[r * 128:(r + 1) * 128, c * 1024:(c + 1) * 1024],
                   oap[r * 128:(r + 1) * 128, c * 1024:(c + 1) * 1024])
                  for r in range(nr) for c in range(ncol)]
        cols = 1024
        for k in range(len(blocks)):
            ctx.dma("sq", wld[:, 0:cols], blocks[k][0], "wl")
            ctx.wait(nc.scalar, "wl")
            ctx.wait(nc.vector, "wl")
            a1 = nc.scalar.activation(aw[:, 0:cols], wld[:, 0:cols], AF.Abs, bias=0.0, scale=1.0)
            a2 = nc.scalar.activation(sg[:, 0:cols], wld[:, 0:cols], AF.Sign, bias=0.0, scale=1.0)
            ctx.tag(a2, "t7")
            nc.vector.wait_ge(ctx.sem("t7"), ctx.cnt["t7"])
            nc.vector.tensor_scalar(msk[:, 0:cols], aw[:, 0:cols], thrc[:, col:col + 1],
                                    None, OP.is_gt)
            vv = nc.vector.tensor_tensor(wt[:, 0:cols], msk[:, 0:cols], sg[:, 0:cols], OP.mult)
            ctx.tag(vv, "t8")
            nc.sync.wait_ge(ctx.sem("t8"), ctx.cnt["t8"])
            nc.scalar.wait_ge(ctx.sem("t8"), ctx.cnt["t8"])
            ctx.dma("sq", blocks[k][1], wt[:, 0:cols], "st")
    ctx.wait(nc.sync, "st")
    ctx.bar()
    return nc


def emit_p8(ctx, EX):
    nc = ctx.nc
    xq = EX["x5q"]
    wtup = EX["wtupT"]
    wtdn = EX["wtdnT"]
    o_out = EX["o_out"]

    xsb = ctx.sb("xsb8", [128, 8 * TOK], F32)
    hbuf = ctx.sb("hbuf8", [128, 32 * 512], BF16)
    pool = ctx.sb("pool8", [128, 16384], BF16)
    wslab = ctx.sb("wslab8", [128, 2 * 4096], BF16)
    scr = ctx.sb("scr8", [128, 512], F32)
    cwrow = ctx.sb("cwrow8", [1, 32 * 128], BF16)
    cwcol = ctx.sb("cwcol8", [128, 32], BF16)
    r_sd = ctx.sb("r_sd", [1, TOK], F32)
    r_msdn = ctx.sb("r_msdn", [1, TOK], BF16)
    t_mean = ctx.sb("t_mean8", [1, 512], F32)
    t_var = ctx.sb("t_var8", [1, 512], F32)
    t_tmp = ctx.sb("t_tmp8", [1, 512], F32)
    t_r = ctx.sb("t_r8", [1, 512], F32)
    pbig = ctx.pbig
    pbig2 = ctx.pbig2
    ptb = ctx.ptb

    ctx.dma("sq", xsb[:].rearrange("p (f t) -> p f t", f=8),
            xq.rearrange("(f p) t -> p f t", p=128), "ld")
    ctx.wait(nc.sync, "ld")
    ctx.bar()

    def stream_w(ap, job, width, buf):
        ctx.dma("gq", wslab[:, buf * 4096: buf * 4096 + width],
                ap[job * 128:(job + 1) * 128, :], "wl%d" % buf)
        return ctx.cnt["wl%d" % buf]

    xnd = pool[:, 0:16384]
    for tt in range(TOK // 512):
        ts = slice(tt * 512, (tt + 1) * 512)
        nc.scalar.wait_ge(ctx.sem("xn8"), ctx.cnt.get("xn8", 0))
        _emit_ln_sums(ctx, xsb, 8, TOK, scr, pbig2[0:1, 0:512], pbig2[0:1, 512:1024], ts)
        rO = {"mean": t_mean, "var": t_var, "r1": t_r, "tmp": t_tmp, "sl": slice(0, 512)}
        _rows_chain_ln(ctx, pbig2[0:1, 0:512], pbig2[0:1, 512:1024], rO)
        nc.vector.tensor_mul(t_tmp[0:1, :], t_r[0:1, :], t_r[0:1, :])
        lv = nc.vector.tensor_mul(t_tmp[0:1, :], t_tmp[0:1, :], t_var[0:1, :])
        ctx.tag(lv, "gv")
        nc.scalar.wait_ge(ctx.sem("gv"), ctx.cnt["gv"])
        la = nc.scalar.activation(t_tmp[0:1, :], t_tmp[0:1, :], AF.Sqrt, bias=0.0, scale=1.0)
        ctx.tag(la, "gs")
        nc.vector.wait_ge(ctx.sem("gs"), ctx.cnt["gs"])
        nc.vector.tensor_scalar(t_tmp[0:1, :], t_tmp[0:1, :], 1e-8, None, OP.add)
        lv = nc.vector.reciprocal(t_tmp[0:1, :], t_tmp[0:1, :])
        ctx.tag(lv, "gv")
        nc.vector.wait_ge(ctx.sem("gv"), ctx.cnt["gv"])
        nc.vector.tensor_mul(t_tmp[0:1, :], t_tmp[0:1, :], t_r[0:1, :])
        nc.vector.tensor_copy(r_sd[0:1, ts], t_tmp[0:1, :])
        nc.vector.tensor_mul(t_tmp[0:1, :], t_tmp[0:1, :], t_mean[0:1, :])
        lv = nc.vector.tensor_scalar_mul(r_msdn[0:1, ts], t_tmp[0:1, :], -1.0)
        ctx.tag(lv, "gv")
        nc.tensor.wait_ge(ctx.sem("gv"), ctx.cnt["gv"])
        m = nc.tensor.matmul(pbig2[:, 1024:1536], ctx.c("c_onesrow_f")[0:1, :],
                             r_sd[0:1, ts], start=True, stop=True)
        ctx.tag(m, "bc8")
        nc.vector.wait_ge(ctx.sem("bc8"), ctx.cnt["bc8"])
        last = None
        for fc in range(8):
            off = fc * TOK + tt * 512
            last = nc.vector.tensor_tensor(xnd[:, off:off + 512], xsb[:, off:off + 512],
                                           pbig2[:, 1024:1536], OP.mult)
        ctx.tag(last, "xn8")
        nc.tensor.wait_ge(ctx.sem("xn8"), ctx.cnt["xn8"])
    ctx.bar()
    def ld_cs(fc, buf):
        return stream_w(wtup, fc, 1024, buf)

    def mm_cs(fc, buf):
        bo = buf * 4096
        last = None
        for kc in range(8):
            last = nc.tensor.matmul(pbig[0:128, fc % 16: fc % 16 + 1],
                                    wslab[:, bo + kc * 128: bo + (kc + 1) * 128],
                                    ctx.c("c_onescol_b")[:, 0:1],
                                    start=(kc == 0 and fc % 16 == 0), stop=(kc == 7),
                                    skip_group_check=True)
        ctx.tag(last, "cs8")
        if fc % 16 == 15:
            nc.vector.wait_ge(ctx.sem("cs8"), ctx.cnt["cs8"])
            nc.vector.tensor_copy(cwcol[:, (fc // 16) * 16:(fc // 16) * 16 + 16],
                                  pbig[:, 0:16])
            ctx.tag(nc.vector.tensor_copy(scr[0:1, 0:1], cwcol[0:1, 0:1]), "cv8")
            nc.tensor.wait_ge(ctx.sem("cv8"), ctx.cnt["cv8"])

    _mm_stream(ctx, list(range(32)), ld_cs, mm_cs, "cs8")
    ctx.bar()
    for j in range(32):
        mt = nc.tensor.transpose(ptb[0:1, 0:128], cwcol[:, j:j + 1], ctx.c("c_I")[:])
        ctx.tag(mt, "cwt")
        nc.vector.wait_ge(ctx.sem("cwt"), ctx.cnt["cwt"])
        vv = nc.vector.tensor_copy(cwrow[0:1, j * 128:(j + 1) * 128], ptb[0:1, 0:128])
        ctx.tag(vv, "cwc8")
        nc.tensor.wait_ge(ctx.sem("cwc8"), ctx.cnt["cwc8"])
    ctx.bar()

    for tt in range(TOK // 512):
        ts = slice(tt * 512, (tt + 1) * 512)
        def ld_up(hk, buf):
            return stream_w(wtup, hk, 1024, buf)

        def mm_up(hk, buf):
            bo = buf * 4096
            pb = pbig[:, (hk % 2) * 512:(hk % 2) * 512 + 512]
            ctx.lagwait(nc.tensor, "u2", 1)
            for kc in range(8):
                nc.tensor.matmul(pb, wslab[:, bo + kc * 128: bo + (kc + 1) * 128],
                                 xnd[:, kc * TOK + tt * 512: kc * TOK + (tt + 1) * 512],
                                 start=(kc == 0), stop=False)
            last = nc.tensor.matmul(pb, cwrow[0:1, hk * 128:(hk + 1) * 128],
                                    r_msdn[0:1, ts], start=False, stop=True)
            ctx.tag(last, "u1")
            nc.scalar.wait_ge(ctx.sem("u1"), ctx.cnt["u1"])
            aa = nc.scalar.activation(hbuf[:, hk * 512:(hk + 1) * 512], pb, AF.Gelu,
                                      bias=0.0, scale=1.0)
            ctx.tag(aa, "u2")

        _mm_stream(ctx, list(range(32)), ld_up, mm_up, "u1")
        nc.tensor.wait_ge(ctx.sem("u2"), ctx.cnt["u2"])
        ctx.bar()
        for i32 in range(1):
            pass
        for hk in range(32):
            aa = nc.scalar.activation(scr[:, :], hbuf[:, hk * 512:(hk + 1) * 512],
                                      AF.Square, bias=0.0, scale=1.0)
            ctx.tag(aa, "h1")
            nc.tensor.wait_ge(ctx.sem("h1"), ctx.cnt["h1"])
            last = nc.tensor.matmul(pbig2[0:1, 0:512], ctx.c("c_onescol_f")[:, 0:1],
                                    scr[:, :], start=(hk == 0), stop=(hk == 31))
            ctx.tag(last, "h2")
            nc.scalar.wait_ge(ctx.sem("h2"), ctx.cnt["h2"])
        ctx.bar()
        nc.vector.tensor_scalar_mul(t_tmp[0:1, :], pbig2[0:1, 0:512], 1.0 / DFF)
        ctx.bar()
        nc.scalar.activation(t_tmp[0:1, :], t_tmp[0:1, :], AF.Sqrt, bias=0.0, scale=1.0)
        ctx.bar()
        nc.vector.tensor_scalar(t_tmp[0:1, :], t_tmp[0:1, :], 1e-8, None, OP.add)
        ctx.bar()
        nc.vector.reciprocal(t_r[0:1, :], t_tmp[0:1, :])
        ctx.bar()
        m = nc.tensor.matmul(pbig2[:, 1024:1536], ctx.c("c_onesrow_f")[0:1, :],
                             t_r[0:1, :], start=True, stop=True)
        ctx.tag(m, "h3")
        nc.vector.wait_ge(ctx.sem("h3"), ctx.cnt["h3"])
        last = None
        for hk in range(32):
            last = nc.vector.tensor_tensor(hbuf[:, hk * 512:(hk + 1) * 512],
                                           hbuf[:, hk * 512:(hk + 1) * 512],
                                           pbig2[:, 1024:1536], OP.mult)
        ctx.tag(last, "h4")
        nc.tensor.wait_ge(ctx.sem("h4"), ctx.cnt["h4"])
        ctx.bar()
        def ld_dn(fc, buf):
            return stream_w(wtdn, fc, 4096, buf)

        def mm_dn(fc, buf):
            bo = buf * 4096
            pb = pbig[:, (fc % 2) * 512:(fc % 2) * 512 + 512]
            ctx.lagwait(nc.tensor, "d2", 1)
            last = None
            for kc in range(32):
                last = nc.tensor.matmul(pb, wslab[:, bo + kc * 128: bo + (kc + 1) * 128],
                                        hbuf[:, kc * 512:(kc + 1) * 512],
                                        start=(kc == 0), stop=(kc == 31))
            ctx.tag(last, "d1")
            nc.vector.wait_ge(ctx.sem("d1"), ctx.cnt["d1"])
            off = fc * TOK + tt * 512
            vv = nc.vector.tensor_add(xsb[:, off:off + 512], xsb[:, off:off + 512], pb)
            ctx.tag(vv, "d2")

        _mm_stream(ctx, list(range(8)), ld_dn, mm_dn, "d1")
        ctx.bar()
    ctx.dma("sq", o_out.rearrange("(f p) t -> p f t", p=128),
            xsb[:].rearrange("p (f t) -> p f t", f=8), "st")
    ctx.wait(nc.sync, "st")
    ctx.bar()
    return nc

_EXT_SHAPES = {
    "xq": ([D, TOK], F32), "ln1w8": ([128, 8], F32), "ln2w8": ([128, 8], F32),
    "n1w8": ([128, 8], F32), "n2w8": ([128, 8], F32), "scb8": ([128, 8], F32),
    "bias8": ([128, 8], F32), "wq4": ([D, 256], F32), "wk4": ([D, 256], F32),
    "wv4": ([D, 256], F32), "scale4": ([4, 1], F32), "scale22": ([2, 2], F32),
    "scale4r": ([1, 4], F32), "relb4": ([NB, 4], F32), "wo": ([8 * 128, D], F32),
    "w1": ([32 * 128, D], F32), "w2": ([8 * 128, FFN], F32), "cbt": ([8 * 128, D], F32),
    "win_z": ([D, 512], F32), "win_x": ([D, 512], F32), "win_B": ([D, 512], F32),
    "win_C": ([D, 512], F32), "win_dt": ([D, 8], F32), "win_th": ([D, 256], F32),
    "alog8": ([8, 1], F32), "dtb8": ([8, 1], F32), "d8": ([1, 8], F32),
    "bnw": ([1, 64], F32), "cnw": ([1, 64], F32), "bbias": ([1, 512], F32),
    "cbias": ([1, 512], F32), "wout": ([8 * 128, 2048], F32), "wg": ([20 * 128, D], F32),
    "wu": ([20 * 128, D], F32), "wd": ([8 * 128, DMLP], F32),
    "wupT_sl": ([D, 1024], F32), "wdnT_sl": ([256, DFF], F32),
}
_ALL_CONSTS = ["c_sel", "c_maskrow", "c_J", "c_I", "c_If", "c_onescol_f",
               "c_onescol_b", "c_onesrow_f", "c_onesrow_b", "c_sel64", "c_exp2",
               "c_tril", "c_ramp", "c_mask8", "c_aexp"]


def build_fused():
    import os
    _UPTO = int(os.environ.get("FUSE_UPTO", "99"))
    nc = bass.Bass("TRN2", target_bir_lowering=False, debug=False,
                   detect_race_conections=False) if False else bass.Bass(
        "TRN2", target_bir_lowering=False, debug=False, detect_race_conditions=False,
        num_swdge_queues=2)
    ctx = Ctx(nc)
    E = {}
    for n, (sh, dt) in _EXT_SHAPES.items():
        E[n] = nc.dram_tensor(n, sh, dt, kind="ExternalInput").ap()
    o_out = nc.dram_tensor("o_out", [D, TOK], F32, kind="ExternalOutput").ap()

    ti = lambda n, sh, dt: nc.dram_tensor(n, sh, dt, kind="Internal")
    s_ag1 = ti("s_ag1", [1032, TOK], BF16)
    g_ag1 = ti("g_ag1", [4 * 1032, TOK], BF16)
    s_absp = ti("s_absp", [1, 2], F32)
    g_absp = ti("g_absp", [8, 2], F32)
    s_rows1 = ti("s_rows1", [4, TOK], F32)
    s_wtup = ti("s_wtup", [D, 1024], BF16)
    s_wtdn = ti("s_wtdn", [256, DFF], BF16)
    g_wtup = ti("g_wtup", [4 * D, 1024], BF16)
    g_wtdn = ti("g_wtdn", [4 * 256, DFF], BF16)
    s_y = ti("s_y", [256, S], BF16)
    g_y = ti("g_y", [1024, S], BF16)
    s_yq = ti("s_yq", [1024, TOK], BF16)
    s_x4 = ti("s_x4", [D, TOK], F32)
    s_rows4 = ti("s_rows4", [4, TOK], F32)
    s_ag4 = ti("s_ag4", [1032, TOK], BF16)
    g_ag4 = ti("g_ag4", [4 * 1032, TOK], BF16)
    s_yg = ti("s_yg", [512, S], BF16)
    g_yg = ti("g_yg", [2048, S], BF16)
    s_ygq = ti("s_ygq", [2048, TOK], BF16)
    s_x5 = ti("s_x5", [D, TOK], F32)
    s_dbga = ti("s_dbga", [88, 512], F32)
    s_dbgb = ti("s_dbgb", [128, 2048], F32)

    ctx.load_consts(_ALL_CONSTS)
    ctx.prep_consts([1e-5, 1e-6, 1e-8])
    ctx.pbig = nc.alloc_psum_tensor("pbig", [128, 2048], F32)
    ctx.pbig2 = nc.alloc_psum_tensor("pbig2", [128, 1536], F32)
    ctx.ptb = nc.alloc_psum_tensor("ptb", [128, 256], BF16)
    ctx.bar()
    ctx.make_pool()
    QUAD = [[0, 1, 2, 3], [4, 5, 6, 7]]
    GLOB = [list(range(8))]
    ncoll = [0]

    def coll(kind, groups, in_ap, out_ap):
        ncoll[0] += 1
        cc = ctx.sem(f"cc{ncoll[0]}")
        nc.gpsimd.collective_compute(kind, OP.bypass, replica_groups=groups,
                                     ins=[in_ap], outs=[out_ap]).then_inc(cc)
        nc.gpsimd.wait_ge(cc, 1)
        ctx.bar()

    ag1v = g_ag1.ap().rearrange("(r d) t -> r d t", d=1032)

    ctx.pool_reset()
    emit_p1(ctx, {"xq": E["xq"], "ln1w8": E["ln1w8"], "wupT_sl": E["wupT_sl"],
                  "wdnT_sl": E["wdnT_sl"], "o_xn1": s_ag1.ap()[0:1024, :],
                  "o_rows": s_rows1.ap(), "o_absp": s_absp.ap()})
    ctx.bar()
    ctx.dma("gq", s_ag1.ap()[1024:1025, :], s_rows1.ap()[1:2, :], "gms")
    ctx.wait(nc.gpsimd, "gms")
    ctx.bar()
    coll("AllGather", QUAD, s_ag1.ap(), g_ag1.ap())
    coll("AllGather", GLOB, s_absp.ap(), g_absp.ap())
    if _UPTO <= 1:
        ctx.dma("sq", o_out.rearrange("(f p) t -> p f t", p=128),
                ctx.pool_t[:, 0:16 * TOK].bitcast(F32).rearrange("p (f t) -> p f t", f=8), "fin")
        ctx.wait(nc.sync, "fin")
        ctx.bar()
        return nc

    ctx.pool_reset()
    emit_p7(ctx, {"absp_all": g_absp.ap(), "wupT_sl": E["wupT_sl"],
                  "wdnT_sl": E["wdnT_sl"], "o_wtup": s_wtup.ap(),
                  "o_wtdn": s_wtdn.ap()})
    ctx.bar()
    coll("AllGather", QUAD, s_wtup.ap(), g_wtup.ap())
    coll("AllGather", QUAD, s_wtdn.ap(), g_wtdn.ap())
    if _UPTO == 15:
        ctx.dma("sq", o_out.rearrange("(f p) t -> p f t", p=128),
                ctx.pool_t[:, 0:16 * TOK].bitcast(F32).rearrange("p (f t) -> p f t", f=8), "fin")
        ctx.wait(nc.sync, "fin")
        ctx.bar()
        return nc

    ctx.pool_reset()
    emit_p2(ctx, {"xn1f": ag1v[:, 0:1024, :], "msf4": ag1v[:, 1024:1025, :],
                  "wq4": E["wq4"], "wk4": E["wk4"], "wv4": E["wv4"],
                  "ln1w8": E["ln1w8"], "scale4": E["scale4"],
                  "scale22": E["scale22"], "scale4r": E["scale4r"],
                  "relb4": E["relb4"], "o_y": s_y.ap()})
    ctx.bar()
    coll("AllGather", QUAD, s_y.ap(), g_y.ap())
    pid = nc.gpsimd.partition_id()
    qoff = (pid % 4) * TOK
    src = bass.AP(g_y.ap().tensor, qoff, [[S, 1024], [1, TOK]])
    ctx.dma("gq", s_yq.ap(), src, "gsl")
    ctx.wait(nc.gpsimd, "gsl")
    ctx.bar()
    if _UPTO <= 2:
        ctx.dma("sq", o_out.rearrange("(f p) t -> p f t", p=128),
                ctx.pool_t[:, 0:16 * TOK].bitcast(F32).rearrange("p (f t) -> p f t", f=8), "fin")
        ctx.wait(nc.sync, "fin")
        ctx.bar()
        return nc

    ctx.pool_reset()
    emit_p3(ctx, {"yq": s_yq.ap(), "xq": E["xq"], "rows1": s_rows1.ap(),
                  "wo": E["wo"], "w1": E["w1"], "w2": E["w2"], "cbt": E["cbt"],
                  "ln2w8": E["ln2w8"], "scb8": E["scb8"], "bias8": E["bias8"],
                  "o_x4": s_x4.ap()})
    ctx.bar()
    if _UPTO <= 3:
        ctx.dma("sq", o_out.rearrange("(f p) t -> p f t", p=128),
                ctx.pool_t[:, 0:16 * TOK].bitcast(F32).rearrange("p (f t) -> p f t", f=8), "fin")
        ctx.wait(nc.sync, "fin")
        ctx.bar()
        return nc

    ctx.pool_reset()
    emit_p4(ctx, {"x4q": s_x4.ap(), "n1w8": E["n1w8"],
                  "o_xn4": s_ag4.ap()[0:1024, :], "o_rows4": s_rows4.ap()})
    ctx.bar()
    ctx.dma("gq", s_ag4.ap()[1024:1025, :], s_rows4.ap()[1:2, :], "gms")
    ctx.wait(nc.gpsimd, "gms")
    ctx.bar()
    coll("AllGather", QUAD, s_ag4.ap(), g_ag4.ap())
    if _UPTO <= 4:
        ctx.dma("sq", o_out.rearrange("(f p) t -> p f t", p=128),
                ctx.pool_t[:, 0:16 * TOK].bitcast(F32).rearrange("p (f t) -> p f t", f=8), "fin")
        ctx.wait(nc.sync, "fin")
        ctx.bar()
        return nc
    ag4v = g_ag4.ap().rearrange("(r d) t -> r d t", d=1032)

    ctx.pool_reset()
    emit_p5(ctx, {"xn4f": ag4v[:, 0:1024, :], "mscf4": ag4v[:, 1024:1025, :],
                  "win_z": E["win_z"], "win_x": E["win_x"], "win_B": E["win_B"],
                  "win_C": E["win_C"], "win_dt": E["win_dt"], "win_th": E["win_th"],
                  "n1w8": E["n1w8"], "alog8": E["alog8"], "dtb8": E["dtb8"],
                  "d8": E["d8"], "bnw": E["bnw"], "cnw": E["cnw"],
                  "bbias": E["bbias"], "cbias": E["cbias"], "o_yg": s_yg.ap(),
                  "o_dbga": s_dbga.ap(), "o_dbgb": s_dbgb.ap()})
    ctx.bar()
    coll("AllGather", QUAD, s_yg.ap(), g_yg.ap())
    src = bass.AP(g_yg.ap().tensor, qoff, [[S, 2048], [1, TOK]])
    ctx.dma("gq", s_ygq.ap(), src, "gsl")
    ctx.wait(nc.gpsimd, "gsl")
    ctx.bar()
    if _UPTO == 55:
        ctx.dma("sq", o_out.rearrange("a b -> (a b)"),
                s_ygq.ap().bitcast(F32).rearrange("a b -> (a b)"), "fin")
        ctx.wait(nc.sync, "fin")
        ctx.bar()
        return nc
    if _UPTO <= 5:
        ctx.dma("sq", o_out.rearrange("(f p) t -> p f t", p=128),
                ctx.pool_t[:, 0:16 * TOK].bitcast(F32).rearrange("p (f t) -> p f t", f=8), "fin")
        ctx.wait(nc.sync, "fin")
        ctx.bar()
        return nc

    ctx.pool_reset()
    emit_p6(ctx, {"ygq": s_ygq.ap(), "x4q": s_x4.ap(), "rows4": s_rows4.ap(),
                  "wout": E["wout"], "wg": E["wg"], "wu": E["wu"], "wd": E["wd"],
                  "n2w8": E["n2w8"], "o_x5": s_x5.ap()})
    ctx.bar()
    if _UPTO <= 6:
        ctx.dma("sq", o_out, s_x5.ap(), "fin")
        ctx.wait(nc.sync, "fin")
        ctx.bar()
        return nc

    ctx.pool_reset()
    emit_p8(ctx, {"x5q": s_x5.ap(),
                  "wtupT": g_wtup.ap(),
                  "wtdnT": g_wtdn.ap(),
                  "o_out": o_out})
    ctx.bar()
    return nc


def kernel(**inputs):
    maps = _stage_inputs(inputs)
    nc = build_fused()
    keys = list(_EXT_SHAPES) + _ALL_CONSTS
    ins = [{k: m[k] for k in keys} for m in maps]
    res = run_bass_kernel_spmd(nc, ins, core_ids=list(range(NCORES)))
    out = np.empty((B, S, D), np.float32)
    for c in range(NCORES):
        b, q = c // 4, c % 4
        out[b, TOK * q: TOK * (q + 1), :] = np.asarray(res.results[c]["o_out"]).T
    return out
